# version 06
# baseline (speedup 1.0000x reference)
import math
import numpy as np
import ml_dtypes
import concourse.bass as bass
import concourse.mybir as mybir
from concourse.bass_utils import run_bass_kernel_spmd

F32 = mybir.dt.float32
BF16 = mybir.dt.bfloat16
I32 = mybir.dt.int32
AF = mybir.ActivationFunctionType
ALU = mybir.AluOpType
AX = mybir.AxisListType
BF = ml_dtypes.bfloat16

B, S, D = 2, 8192, 1024
OWN = 2048
HALO = 256
EXT = OWN + 2 * HALO
NCORE = 8
H, W, STRIDE = 16, 512, 256
HD = 64
N_EXP, D_MOE = 8, 4096
FFN1_H = 3072
EPS_LN = 1e-5
EPS_MP = 1e-4
TRUNC_L0B = False


class SemW:

    __slots__ = ("h", "base")

    def __init__(self, h, base):
        self.h = h
        self.base = base


class Pl:
    ENGS = ("sync", "vector", "scalar", "tensor", "gpsimd")

    def __init__(self, nc):
        self.nc = nc
        self.recs = {e: [] for e in self.ENGS}
        self.cnt = {}
        self.bar_sem = None
        self.rsync = None
        self.junk_ap = None

    def _h(self, s):
        return (s.h, s.base) if isinstance(s, SemW) else (s, 0)

    def val(self, s):
        h, base = self._h(s)
        return self.cnt.get(id(h), 0) - base

    def bump(self, s, n):
        h, _ = self._h(s)
        self.cnt[id(h)] = self.cnt.get(id(h), 0) + n
        return self.cnt[id(h)]

    def _tw(self, waits):
        out = []
        for sm, v in waits:
            h, base = self._h(sm)
            out.append((h, base + v))
        return out

    def _ti(self, inc):
        if inc is None:
            return None
        h, _ = self._h(inc[0])
        return (h, inc[1])

    def add(self, eng, build, waits=(), inc=None):
        if inc is not None:
            self.bump(inc[0], inc[1])
        self.recs[eng].append({"build": build, "waits": self._tw(waits),
                               "inc": self._ti(inc)})

    def raw_wait(self, eng, sm, v):
        self.recs[eng].append({"build": None, "waits": self._tw([(sm, v)]),
                               "inc": None})

    def add_r(self, eng, build, waits=(), inc=None):
        rs = self.rsync[eng]
        self.add(eng, build, waits=waits, inc=(rs, 1))
        self.raw_wait(eng, rs, self.val(rs))
        if inc is not None:
            self.bump(inc[0], inc[1])
            self.recs[eng].append({"build": None, "waits": [],
                                   "inc": None,
                                   "postinc": self._ti(inc)})

    def flush(self, eng):
        rs = self.rsync[eng]
        recs = self.recs[eng]
        prev = None
        for r in reversed(recs):
            if r["build"] is not None:
                prev = r
                break
        if prev is None:
            return
        if prev["inc"] is None:
            prev["inc"] = (rs, 1)
            self.bump(rs, 1)
            self.raw_wait(eng, rs, self.val(rs))
        else:
            self.raw_wait(eng, prev["inc"][0], self.val(prev["inc"][0]))

    def add_red(self, eng, build, waits=(), inc=None):
        self.flush(eng)
        self.add_r(eng, build, waits=waits, inc=inc)

    def barrier(self):
        b = self.bar_sem
        ja = self.junk_ap
        for _ in range(5):
            self.bump(b, 1)
        tgt = self.val(b)
        self.recs["vector"].append({
            "build": lambda eo: eo.memset(ja[:, 0:1], 0.0),
            "waits": [], "inc": (b, 1), "postwait": (b, tgt)})
        self.recs["scalar"].append({
            "build": lambda eo: eo.memzero(ja[:, 1:2]),
            "waits": [], "inc": (b, 1), "postwait": (b, tgt)})
        self.recs["tensor"].append({
            "build": lambda eo: eo.nop(),
            "waits": [], "inc": (b, 1), "postwait": (b, tgt)})
        for e in ("sync", "gpsimd"):
            self.recs[e].append({
                "build": None, "waits": [], "inc": None,
                "postinc": (b, 1), "postwait": (b, tgt)})

    def _run(self, eng, eo):
        for r in self.recs[eng]:
            for sm, v in r["waits"]:
                eo.wait_ge(sm, v)
            if r["build"] is not None:
                ins = r["build"](eo)
                if r["inc"] is not None:
                    ins.then_inc(r["inc"][0], r["inc"][1])
            if "postinc" in r:
                eo.sem_inc(r["postinc"][0], r["postinc"][1])
            if "postwait" in r:
                eo.wait_ge(r["postwait"][0], r["postwait"][1])

    def finalize(self):
        with self.nc.Block() as block:
            @block.sync
            def _(eng):
                self._run("sync", eng)

            @block.vector
            def _(eng):
                self._run("vector", eng)

            @block.scalar
            def _(eng):
                self._run("scalar", eng)

            @block.tensor
            def _(eng):
                self._run("tensor", eng)

            @block.gpsimd
            def _(eng):
                self._run("gpsimd", eng)


def seg(ap2d, e):
    return ap2d.rearrange("p (s e) -> p s e", e=e)


def bcast(ap2d, e):
    return bass.AP(tensor=ap2d.tensor, offset=ap2d.offset,
                   ap=[list(ap2d.ap[0]), list(ap2d.ap[1]), [0, e]])


class Ctx:

    def __init__(self, nc):
        self.nc = nc
        self.p = Pl(nc)
        self.exits = []
        self.sems = {}
        self.uid = 0
        base = (nc.sbuf_base + 63) & ~63
        self.freelist = [(base, nc.sbuf_top - base)]
        self.sizes = {}
        self.hiwater = 0

    def _name(self, base):
        self.uid += 1
        return f"{base}_{self.uid}"

    SEM_POOL = 78

    def sem(self, name=None, swdge=False):
        if swdge:
            cm = self.nc.semaphore(self._name("swdge"))
            h = cm.__enter__()
            self.exits.append((None, cm))
            return h
        if name is None:
            self.anon_cnt = getattr(self, "anon_cnt", 0)
            self.anon_pool = getattr(self, "anon_pool", [])
            if self.anon_cnt >= self.SEM_POOL:
                h = self.anon_pool[self.anon_cnt % self.SEM_POOL]
                self.anon_cnt += 1
                return SemW(h, self.p.val(h))
            name = self._name("sem")
            cm = self.nc.semaphore(self._name(name))
            h = cm.__enter__()
            self.exits.append((None, cm))
            self.anon_pool.append(h)
            self.anon_cnt += 1
            return h
        if name not in self.sems:
            cm = self.nc.semaphore(self._name(name))
            self.sems[name] = cm.__enter__()
            self.exits.append((None, cm))
        return self.sems[name]

    def sbuf(self, name, shape, dtype):
        nb = int(np.prod(shape[1:])) * mybir.dt.size(dtype)
        nb = (nb + 63) & ~63
        for i, (off, sz) in enumerate(self.freelist):
            if sz >= nb:
                if sz == nb:
                    self.freelist.pop(i)
                else:
                    self.freelist[i] = (off + nb, sz - nb)
                t = self.nc.alloc_sbuf_tensor_at(
                    self._name(name), shape, dtype, offset=off)
                self.sizes[id(t)] = (off, nb)
                used = self.nc.sbuf_top - sum(s for _, s in self.freelist) \
                    - ((self.nc.sbuf_base + 63) & ~63)
                self.hiwater = max(self.hiwater, used)
                return t
        raise MemoryError(
            f"sbuf arena out of space for {name} {shape} ({nb}B); "
            f"free={[s for _, s in self.freelist]}")

    def psum(self, name, shape, dtype=F32):
        cm = self.nc.psum_tensor(self._name(name), shape, dtype)
        t = cm.__enter__()
        self.exits.append((id(t), cm))
        return t

    def ccol(self, val):
        val = float(val)
        if val not in self.cmap:
            assert len(self.cvals) < self.cpool.shape[1]
            self.cmap[val] = len(self.cvals)
            self.cvals.append(val)
        return self.cpool[:, self.cmap[val]:self.cmap[val] + 1]

    def free(self, *tensors):
        for t in tensors:
            if isinstance(t, (list, tuple)):
                self.free(*reversed(t))
                continue
            key = id(t)
            if key in self.sizes:
                off, nb = self.sizes.pop(key)
                self.freelist.append((off, nb))
                self.freelist.sort()
                merged = []
                for o, s in self.freelist:
                    if merged and merged[-1][0] + merged[-1][1] == o:
                        merged[-1] = (merged[-1][0], merged[-1][1] + s)
                    else:
                        merged.append((o, s))
                self.freelist = [tuple(x) for x in merged]
                continue
            for i, (k, cm) in enumerate(self.exits):
                if k == key:
                    cm.__exit__(None, None, None)
                    self.exits.pop(i)
                    break
            else:
                raise KeyError("free: tensor not found")

    def close(self):
        for _, cm in reversed(self.exits):
            cm.__exit__(None, None, None)
        self.exits = []


def make_ctx(name):
    nc = bass.Bass("TRN2", target_bir_lowering=False)
    ctx = Ctx(nc)
    ctx.p.bar_sem = ctx.sem("bar")
    ctx.p.rsync = {e: ctx.sem(f"rsync_{e}")
                   for e in ("sync", "vector", "scalar", "tensor",
                             "gpsimd")}
    ident = nc.inline_tensor(np.eye(128, dtype=BF), name=ctx._name("identbf")).ap()
    ident32 = nc.inline_tensor(np.eye(128, dtype=np.float32),
                               name=ctx._name("identf32")).ap()
    ones = nc.inline_tensor(np.ones((1, 512), dtype=BF), name=ctx._name("onesbf")).ap()
    ctx.ident_sb = ctx.sbuf("ident_sb", [128, 128], BF16)
    ctx.ident_f32 = ctx.sbuf("ident_f32", [128, 128], F32)
    ctx.ones_sb = ctx.sbuf("ones_sb", [1, 512], BF16)
    ld = ctx.sem("ldconst")
    ctx.p.add("sync", lambda e: e.dma_start(out=ctx.ident_sb[:], in_=ident),
              inc=(ld, 16))
    ctx.p.add("sync", lambda e: e.dma_start(out=ctx.ident_f32[:], in_=ident32),
              inc=(ld, 16))
    ctx.p.add("sync", lambda e: e.dma_start(out=ctx.ones_sb[:], in_=ones),
              inc=(ld, 16))
    ctx.cvals, ctx.cmap = [], {}
    ctx.cpool = ctx.sbuf("cpool", [128, 32], F32)
    ctx.junk = ctx.sbuf("junk", [128, 4], F32)
    ctx.p.junk_ap = ctx.junk

    def _ld_cpool(e):
        vals = ctx.cvals or [0.0]
        arr = np.tile(np.asarray(vals, np.float32), (128, 1))
        return e.dma_start(out=ctx.cpool[:, 0:len(vals)],
                           in_=nc.inline_tensor(arr, name=ctx._name("cpool")).ap())
    ctx.p.add("sync", _ld_cpool, inc=(ld, 16))
    ctx.p.raw_wait("tensor", ld, 64)
    ctx.p.raw_wait("vector", ld, 64)
    ctx.p.raw_wait("scalar", ld, 64)
    ctx.p.barrier()
    return ctx


def load_weight_bf16(ctx, name, dram_ap, kin, nout, scale=None):
    p = ctx.p
    nk = (kin + 127) // 128
    ND = 4
    stage = [ctx.sbuf(f"wst_{name}", [128, nout], F32) for _ in range(ND)]
    out = [ctx.sbuf(f"w_{name}{k}", [128, nout], BF16) for k in range(nk)]
    s_lds = [ctx.sem() for _ in range(ND)]
    s_cv = ctx.sem()
    for k in range(nk):
        buf = stage[k % ND]
        s_ld = s_lds[k % ND]
        w = [(s_cv, k - ND + 1)] if k >= ND else []
        p.add("sync", lambda e, buf=buf, k=k: e.dma_start(
            out=buf[:], in_=dram_ap[k * 128:(k + 1) * 128, :]),
            waits=w, inc=(s_ld, 16))
        wv = [(s_ld, 16 * (k // ND + 1))]
        if scale is None:
            p.add("vector", lambda e, k=k, buf=buf: e.tensor_copy(
                out=out[k][:], in_=buf[:]), waits=wv, inc=(s_cv, 1))
        else:
            p.add("vector", lambda e, k=k, buf=buf: e.tensor_scalar(
                out=out[k][:], in0=buf[:], scalar1=scale, scalar2=None,
                op0=ALU.mult), waits=wv, inc=(s_cv, 1))
    p.raw_wait("tensor", s_cv, nk)
    p.raw_wait("scalar", s_cv, nk)
    p.raw_wait("sync", s_cv, nk)
    p.raw_wait("gpsimd", s_cv, nk)
    ctx.free(stage)
    return out


def load_bias_bf16(ctx, name, dram_ap, n, scale=None):
    p = ctx.p
    stage = ctx.sbuf(f"bst_{name}", [1, n], F32)
    out = ctx.sbuf(f"b_{name}", [1, n], BF16)
    s_ld = ctx.sem()
    s_cv = ctx.sem()
    p.add("sync", lambda e: e.dma_start(out=stage[:], in_=dram_ap),
          inc=(s_ld, 16))
    if scale is None:
        p.add("vector", lambda e: e.tensor_copy(out=out[:], in_=stage[:]),
              waits=[(s_ld, 16)], inc=(s_cv, 1))
    else:
        p.add("vector", lambda e: e.tensor_scalar(
            out=out[:], in0=stage[:], scalar1=scale, scalar2=None,
            op0=ALU.mult), waits=[(s_ld, 16)], inc=(s_cv, 1))
    p.raw_wait("tensor", s_cv, 1)
    p.raw_wait("sync", s_cv, 1)
    p.raw_wait("gpsimd", s_cv, 1)
    ctx.free(stage)
    return out


def emit_ln_to_fm(ctx, x_dram, nrows, tag, keep_tm=False, keep_stats=False):
    p = ctx.p
    nt = nrows // 128
    xt = [ctx.sbuf(f"lnx_{tag}", [128, 1024], F32) for _ in range(nt)]
    fm = [ctx.sbuf(f"xnfm_{tag}{c}", [128, nrows], BF16) for c in range(8)]
    nmr = ctx.sbuf(f"lnnmr_{tag}", [128, nt], F32)
    nm = ctx.sbuf(f"lnnm_{tag}", [128, nt], F32)
    ssq = ctx.sbuf(f"lnssq_{tag}", [128, nt], F32)
    rstd_a = ctx.sbuf(f"lnrstda_{tag}", [128, nt], F32)
    sq = [ctx.sbuf(f"lnsq_{tag}", [128, 1024], F32) for _ in range(2)]
    rstd = ctx.sbuf(f"lnrstd_{tag}", [128, nt], F32)
    if keep_tm:
        tmb = [ctx.sbuf(f"lntm_{tag}", [128, 1024], BF16) for _ in range(nt)]
    else:
        tmb = [ctx.sbuf(f"lntm_{tag}", [128, 1024], BF16) for _ in range(2)]
    tp = [ctx.psum(f"lntp_{tag}", [128, 128], BF16) for _ in range(2)]
    s_ld = ctx.sem()
    s_st = ctx.sem()
    s_sq = ctx.sem()
    s_no = ctx.sem()
    s_tp = ctx.sem()
    s_ev = ctx.sem()
    for i in range(nt):
        p.add("sync", lambda e, i=i: e.dma_start(
            out=xt[i][:], in_=x_dram[i * 128:(i + 1) * 128, :]),
            inc=(s_ld, 16))
    for i in range(nt):
        last = (i == nt - 1)
        fn = p.add_r if last else p.add
        fn("vector", lambda e, i=i: e.reduce_sum(
            out=nmr[:, i:i + 1], in_=xt[i][:], axis=AX.X),
            waits=[(s_ld, 16 * nt)])
    p.add("vector", lambda e: e.tensor_scalar(
        out=nm[:], in0=nmr[:], scalar1=-1.0 / 1024.0, scalar2=None,
        op0=ALU.mult), inc=(s_st, 1))
    for i in range(nt):
        p.add_r("scalar", lambda e, i=i: e.activation(
            out=sq[i % 2][:], in_=xt[i][:], func=AF.Square,
            bias=nm[:, i:i + 1], scale=1.0, accum_out=ssq[:, i:i + 1]),
            waits=[(s_st, 1)], inc=(s_sq, 1))
    c_eps, c_inv, c_half = ctx.ccol(EPS_LN), ctx.ccol(1.0 / 1024.0), \
        ctx.ccol(-0.5)
    p.add_r("scalar", lambda e: e.activation(
        out=rstd_a[:], in_=ssq[:], func=AF.Ln, bias=c_eps, scale=c_inv))
    p.add("scalar", lambda e: e.activation(
        out=rstd[:], in_=rstd_a[:], func=AF.Exp, scale=c_half),
        inc=(s_sq, 1))
    p.raw_wait("vector", s_sq, nt + 1)
    for i in range(nt):
        t = tmb[i] if keep_tm else tmb[i % 2]
        wv = []
        if not keep_tm and i >= 2:
            wv.append((s_tp, 8 * (i - 1)))
        p.add("vector", lambda e, i=i, t=t: e.tensor_scalar(
            out=t[:], in0=xt[i][:], scalar1=nm[:, i:i + 1],
            scalar2=rstd[:, i:i + 1], op0=ALU.add, op1=ALU.mult),
            waits=wv, inc=(s_no, 1))
        for c in range(8):
            j = i * 8 + c
            wv = [(s_no, i + 1)]
            if j >= 2:
                wv.append((s_ev, j - 1))
            p.add("tensor", lambda e, c=c, t=t, j=j: e.transpose(
                tp[j % 2][:], t[:, c * 128:(c + 1) * 128], ctx.ident_sb[:]),
                waits=wv, inc=(s_tp, 1))
            p.add("vector", lambda e, i=i, c=c, j=j: e.tensor_copy(
                out=fm[c][:, i * 128:(i + 1) * 128], in_=tp[j % 2][:]),
                waits=[(s_tp, j + 1)], inc=(s_ev, 1))
    p.barrier()
    if keep_stats:
        ctx.ln_stats = (xt, nm, rstd)
        ctx.ln_stats_free = lambda: ctx.free(xt, nmr, nm, ssq, rstd_a, sq,
                                             rstd)
        ctx.free(tp)
    else:
        ctx.free(xt, nmr, nm, ssq, rstd_a, sq, rstd, tp)
    if keep_tm:
        return fm, tmb
    ctx.free(tmb)
    return fm, None


def emit_gemm_fm_fm(ctx, xfm, T, wsb, nw, bias_row, out, tag, tsub=512,
                    evac=None):
    p = ctx.p
    ps = [ctx.psum(f"gm_{tag}", [128, tsub], F32) for _ in range(2)]
    s_mm = ctx.sem()
    s_ev = ctx.sem()
    ntt = (T + tsub - 1) // tsub
    it = 0
    for n in range(nw // 128):
        for tt in range(ntt):
            t0, t1 = tt * tsub, min((tt + 1) * tsub, T)
            pb = ps[it % 2]
            wv = [(s_ev, it - 1)] if it >= 2 else []
            for k in range(8):
                last = (k == 7) and bias_row is None
                p.add("tensor", lambda e, k=k, n=n, t0=t0, t1=t1, pb=pb,
                      last=last: e.matmul(pb[:, 0:t1 - t0],
                                          wsb[k][:, n * 128:(n + 1) * 128],
                                          xfm[k][:, t0:t1], start=(k == 0),
                                          stop=last),
                      waits=wv if k == 0 else [],
                      inc=(s_mm, 1) if last else None)
            if bias_row is not None:
                p.add("tensor", lambda e, n=n, t0=t0, t1=t1, pb=pb: e.matmul(
                    pb[:, 0:t1 - t0], bias_row[:, n * 128:(n + 1) * 128],
                    ctx.ones_sb[:, 0:t1 - t0], start=False, stop=True),
                    inc=(s_mm, 1))
            it += 1
            if evac is None:
                p.add("vector", lambda e, n=n, t0=t0, t1=t1, pb=pb, v=it:
                      e.tensor_copy(out=out[n][:, t0:t1],
                                    in_=pb[:, 0:t1 - t0]),
                      waits=[(s_mm, it)], inc=(s_ev, 1))
            else:
                p.add("vector", lambda e, n=n, t0=t0, t1=t1, pb=pb: evac(
                    e, pb[:, 0:t1 - t0], (n, t0, t1)),
                    waits=[(s_mm, it)], inc=(s_ev, 1))
    p.barrier()
    ctx.free(ps)


def emit_gemm_fm_tm(ctx, xfm, T, wsb, nw, bias_row, tag, evac, nsub=512,
                    evac_eng="vector"):
    p = ctx.p
    ps = [ctx.psum(f"gt_{tag}", [128, nsub], F32) for _ in range(2)]
    s_mm = ctx.sem()
    s_ev = ctx.sem()
    nkc = len(xfm)
    it = 0
    for tt in range(T // 128):
        for n in range((nw + nsub - 1) // nsub):
            n0, n1 = n * nsub, min((n + 1) * nsub, nw)
            pb = ps[it % 2]
            wv = [(s_ev, it - 1)] if it >= 2 else []
            for k in range(nkc):
                last = (k == nkc - 1) and bias_row is None
                p.add("tensor", lambda e, k=k, tt=tt, n0=n0, n1=n1, pb=pb,
                      last=last: e.matmul(
                          pb[:, 0:n1 - n0], xfm[k][:, tt * 128:(tt + 1) * 128],
                          wsb[k][:, n0:n1], start=(k == 0), stop=last),
                      waits=wv if k == 0 else [],
                      inc=(s_mm, 1) if last else None)
            if bias_row is not None:
                p.add("tensor", lambda e, tt=tt, n0=n0, n1=n1, pb=pb: e.matmul(
                    pb[:, 0:n1 - n0], ctx.ones_sb[0:1, 0:128],
                    bias_row[:, n0:n1], start=False, stop=True),
                    inc=(s_mm, 1))
            it += 1
            p.add(evac_eng, lambda e, tt=tt, n0=n0, n1=n1, pb=pb: evac(
                e, pb[:, 0:n1 - n0], (tt, n0, n1)),
                waits=[(s_mm, it)], inc=(s_ev, 1))
    p.barrier()
    ctx.free(ps)


def emit_tm_to_fm(ctx, tm_tiles, T, tag, out_dtype=BF16, scale_cols=None,
                  in_ready=None):
    p = ctx.p
    nt = T // 128
    fm = [ctx.sbuf(f"fm_{tag}{c}", [128, T], out_dtype) for c in range(8)]
    tp = [ctx.psum(f"tp_{tag}", [128, 128], out_dtype) for _ in range(2)]
    s_sc = ctx.sem()
    s_tp = ctx.sem()
    s_ev = ctx.sem()
    for i in range(nt):
        if scale_cols is not None:
            p.add("vector", lambda e, i=i: e.tensor_scalar(
                out=tm_tiles[i][:], in0=tm_tiles[i][:],
                scalar1=scale_cols[:, i:i + 1], scalar2=None, op0=ALU.mult),
                waits=in_ready or [], inc=(s_sc, 1))
        for c in range(8):
            j = i * 8 + c
            wv = []
            if scale_cols is not None:
                wv.append((s_sc, i + 1))
            elif in_ready and j == 0:
                wv.extend(in_ready)
            if j >= 2:
                wv.append((s_ev, j - 1))
            p.add("tensor", lambda e, i=i, c=c, j=j: e.transpose(
                tp[j % 2][:], tm_tiles[i][:, c * 128:(c + 1) * 128],
                ctx.ident_sb[:]), waits=wv, inc=(s_tp, 1))
            p.add("vector", lambda e, i=i, c=c, j=j: e.tensor_copy(
                out=fm[c][:, i * 128:(i + 1) * 128], in_=tp[j % 2][:]),
                waits=[(s_tp, j + 1)], inc=(s_ev, 1))
    p.barrier()
    ctx.free(tp)
    return fm


def slot_qrange(j):
    r0 = max(0, 256 * (j - 1))
    r1 = min(OWN, 256 * (j + 1))
    return r0, r1


def build_l0b():
    ctx = make_ctx("l0b")
    nc = ctx.nc
    io = {}
    io["q"] = nc.dram_tensor("q", [8, 128, OWN], BF16,
                             kind="ExternalInput").ap()
    io["k"] = nc.dram_tensor("k", [8, 128, EXT], BF16,
                             kind="ExternalInput").ap()
    io["v"] = nc.dram_tensor("v", [EXT, 1040], BF16,
                             kind="ExternalInput").ap()
    io["posT"] = nc.dram_tensor("posT", [16, 512, 512], F32,
                                kind="ExternalInput").ap()
    io["slot_w"] = nc.dram_tensor("slot_w", [128, 9], F32,
                                  kind="ExternalInput").ap()
    io["invcnt"] = nc.dram_tensor("invcnt", [128, OWN // 128], F32,
                                  kind="ExternalInput").ap()
    io["wo"] = nc.dram_tensor("wo", [1024, 1024], F32,
                              kind="ExternalInput").ap()
    io["bo"] = nc.dram_tensor("bo", [1, 1024], F32,
                              kind="ExternalInput").ap()
    io["s0"] = nc.dram_tensor("s0", [OWN, 1024], BF16,
                              kind="ExternalOutput").ap()
    emit_l0b(ctx, io)
    ctx.p.finalize()
    ctx.close()
    return nc


def emit_l0b(ctx, io):
    nc, p = ctx.nc, ctx.p
    q_d, k_d, v_d, pos_d = io["q"], io["k"], io["v"], io["posT"]
    sw_d, ic_d, wo_d, bo_d = io["slot_w"], io["invcnt"], io["wo"], io["bo"]
    s_o = io["s0"]

    qfm = [ctx.sbuf(f"qfm{c}", [128, OWN], BF16) for c in range(8)]
    kfm = [ctx.sbuf(f"kfm{c}", [128, EXT], BF16) for c in range(8)]
    vtm = [ctx.sbuf("vtm", [128, 1040], BF16) for _ in range(EXT // 128)]
    ctxa = [ctx.sbuf("ctxa", [128, 1024], BF16) for _ in range(OWN // 128)]
    sw = ctx.sbuf("sw", [128, 9], F32)
    ic = ctx.sbuf("ic", [128, OWN // 128], F32)
    s_ld = ctx.sem()
    nld = 0
    for c in range(8):
        p.add("sync", lambda e, c=c: e.dma_start(out=qfm[c][:], in_=q_d[c]),
              inc=(s_ld, 16)); nld += 1
        p.add("sync", lambda e, c=c: e.dma_start(out=kfm[c][:], in_=k_d[c]),
              inc=(s_ld, 16)); nld += 1
    for i in range(EXT // 128):
        p.add("sync", lambda e, i=i: e.dma_start(
            out=vtm[i][:], in_=v_d[i * 128:(i + 1) * 128, :]),
            inc=(s_ld, 16)); nld += 1
    p.add("sync", lambda e: e.dma_start(out=sw[:], in_=sw_d),
          inc=(s_ld, 16)); nld += 1
    p.add("sync", lambda e: e.dma_start(out=ic[:], in_=ic_d),
          inc=(s_ld, 16)); nld += 1
    p.raw_wait("tensor", s_ld, 16 * nld)
    p.raw_wait("vector", s_ld, 16 * nld)
    p.raw_wait("scalar", s_ld, 16 * nld)
    p.barrier()

    pstg = [ctx.sbuf("pstg", [128, 512], F32) for _ in range(2)]
    psb = [[ctx.sbuf("psb", [128, 512], BF16) for _ in range(4)]
           for _ in range(2)]
    sc = [ctx.psum("sc", [128, 512], F32) for _ in range(2)]
    owp = [ctx.psum("owp", [128, 65], F32) for _ in range(4)]
    exp_sb = [[ctx.sbuf("exp", [128, 512], BF16) for _ in range(4)]
              for _ in range(2)]
    rcol = ctx.sbuf("rcol", [128, 2], F32)
    s_plds = [ctx.sem(), ctx.sem()]
    s_pcv = ctx.sem()
    s_sc = ctx.sem()
    s_ex = ctx.sem()
    s_ow = ctx.sem()
    s_mg = ctx.sem()

    npld = [0]

    def load_pos(h):
        buf = h % 2
        for ksub in range(4):
            wv = []
            if npld[0] >= 2:
                wv.append((s_pcv, npld[0] - 1))
            p.add("sync", lambda e, h=h, ksub=ksub, n=npld[0]: e.dma_start(
                out=pstg[n % 2][:],
                in_=pos_d[h, ksub * 128:(ksub + 1) * 128, :]),
                waits=wv, inc=(s_plds[npld[0] % 2], 16))
            wv2 = [(s_plds[npld[0] % 2], 16 * (npld[0] // 2 + 1))]
            if h >= 2:
                wv2.append((s_sc, sc_cnt_at_head_end[h - 2]))
            p.add("vector", lambda e, buf=buf, ksub=ksub, n=npld[0]:
                  e.tensor_copy(out=psb[buf][ksub][:], in_=pstg[n % 2][:]),
                  waits=wv2, inc=(s_pcv, 1))
            npld[0] += 1

    sc_cnt_at_head_end = {}
    cnt = 0
    for h in range(16):
        cnt += 9 * 4
        sc_cnt_at_head_end[h] = cnt

    ow_tile_last_mg = [0, 0, 0, 0]
    first_touch = {}
    gk = [0]
    gow = [0]
    ow_end_of_slot = {}

    def emit_scores(h, j, sl):
        ch, rb = h // 2, (h % 2) * 64
        r0, r1 = slot_qrange(j)
        qlen = r1 - r0
        qoff = r0 + 256 - 256 * j
        for ksub in range(4):
            scb = sc[gk[0] % 2]
            wv = [(s_pcv, 4 * h + ksub + 1)]
            if gk[0] >= 2:
                wv.append((s_ex, gk[0] - 1))
            p.add("tensor", lambda e, ch=ch, rb=rb, j=j, ksub=ksub,
                  r0=r0, qlen=qlen, scb=scb: e.matmul(
                      scb[:, 0:qlen],
                      kfm[ch][rb:rb + 64,
                              256 * j + 128 * ksub:256 * j + 128 * ksub + 128],
                      qfm[ch][rb:rb + 64, r0:r0 + qlen],
                      start=True, stop=False), waits=wv)
            p.add("tensor", lambda e, h=h, ksub=ksub, qoff=qoff,
                  qlen=qlen, scb=scb: e.matmul(
                      scb[:, 0:qlen], ctx.ident_sb[:],
                      psb[h % 2][ksub][:, qoff:qoff + qlen],
                      start=False, stop=True), inc=(s_sc, 1))
            ebuf = exp_sb[sl % 2][ksub]
            wv = [(s_sc, p.val(s_sc))]
            if sl >= 2:
                wv.append((s_ow, ow_end_of_slot[sl - 2]))
            p.add("scalar", lambda e, ebuf=ebuf, qlen=qlen, scb=scb:
                  e.activation(out=ebuf[:, 0:qlen], in_=scb[:, 0:qlen],
                               func=AF.Exp), waits=wv, inc=(s_ex, 1))
            gk[0] += 1

    def emit_ow(h, j, sl):
        r0, r1 = slot_qrange(j)
        qlen = r1 - r0
        exps = exp_sb[sl % 2]
        ex_done = 4 * (sl + 1)
        for qq in range(qlen // 128):
            ti = r0 // 128 + qq
            ob = owp[gow[0] % 4]
            wv = [(s_ex, ex_done), (s_mg, ow_tile_last_mg[gow[0] % 4])]
            for ksub in range(4):
                p.add("tensor", lambda e, exps=exps, ksub=ksub, qq=qq,
                      j=j, h=h, ob=ob: e.matmul(
                          ob[:],
                          exps[ksub][:, qq * 128:qq * 128 + 128],
                          vtm[2 * j + ksub][:, 65 * h:65 * h + 65],
                          start=(ksub == 0), stop=(ksub == 3)),
                      waits=wv if ksub == 0 else [],
                      inc=(s_ow, 1) if ksub == 3 else None)
            gow[0] += 1
            wv = [(s_ow, p.val(s_ow))]
            p.add_r("vector", lambda e, ob=ob: e.reciprocal(
                out=rcol[:, 0:1], in_=ob[:, 64:65]), waits=wv)
            p.add_r("vector", lambda e, j=j: e.tensor_tensor(
                out=rcol[:, 1:2], in0=rcol[:, 0:1], in1=sw[:, j:j + 1],
                op=ALU.mult))
            ft = (ti, h)
            if ft not in first_touch:
                first_touch[ft] = True
                p.add("vector", lambda e, ob=ob, ti=ti, h=h:
                      e.tensor_scalar(
                          out=ctxa[ti][:, 64 * h:64 * h + 64],
                          in0=ob[:, 0:64], scalar1=rcol[:, 1:2],
                          scalar2=None, op0=ALU.mult), inc=(s_mg, 1))
            else:
                p.add("vector", lambda e, ob=ob, ti=ti, h=h:
                      e.scalar_tensor_tensor(
                          out=ctxa[ti][:, 64 * h:64 * h + 64],
                          in0=ob[:, 0:64], scalar=rcol[:, 1:2],
                          in1=ctxa[ti][:, 64 * h:64 * h + 64],
                          op0=ALU.mult, op1=ALU.add), inc=(s_mg, 1))
            ow_tile_last_mg[(gow[0] - 1) % 4] = p.val(s_mg)
        ow_end_of_slot[sl] = p.val(s_ow)

    load_pos(0)
    load_pos(1)
    slots_list = [(h, j) for h in range(16) for j in range(9)]
    for sl, (h, j) in enumerate(slots_list):
        emit_scores(h, j, sl)
        if sl >= 1:
            emit_ow(*slots_list[sl - 1], sl - 1)
        if j == 8 and h + 2 < 16:
            load_pos(h + 2)
    emit_ow(*slots_list[-1], len(slots_list) - 1)
    p.barrier()
    ctx.free(qfm, kfm, vtm, pstg, psb, exp_sb, owp, sc)

    ctx_fm = emit_tm_to_fm(ctx, ctxa, OWN, "cx", scale_cols=ic)
    ctx.free(ctxa)
    wo_sb = load_weight_bf16(ctx, "wo", wo_d, 1024, 1024)
    bo_sb = load_bias_bf16(ctx, "bo", bo_d, 1024)
    p.barrier()
    s0 = [ctx.sbuf("s0", [128, 1024], BF16) for _ in range(OWN // 128)]

    def s_evac(e, ps_ap, key):
        tt, n0, n1 = key
        return e.tensor_copy(out=s0[tt][:, n0:n1], in_=ps_ap)
    emit_gemm_fm_tm(ctx, ctx_fm, OWN, wo_sb, 1024, bo_sb, "s0", s_evac)
    st = ctx.sem(swdge=True)
    for i in range(OWN // 128):
        p.add("gpsimd", lambda e, i=i: e.dma_start(
            out=s_o[i * 128:(i + 1) * 128, :], in_=s0[i][:]), inc=(st, 16))
    p.raw_wait("gpsimd", st, 16 * (OWN // 128))
    p.barrier()
    ctx.free(s0, ctx_fm, wo_sb, bo_sb, sw, ic)


def store_and_free(ctx, pairs, also_free=()):
    p = ctx.p
    st = ctx.sem(swdge=True)
    n = 0
    for dram_ap, sb in pairs:
        p.add("gpsimd", lambda e, d=dram_ap, s=sb: e.dma_start(out=d, in_=s),
              inc=(st, 16))
        n += 1
    p.raw_wait("gpsimd", st, 16 * n)
    p.barrier()
    frees = []
    for _, sb in pairs:
        pass
    for t in also_free:
        ctx.free(t)


def build_l0a():
    ctx = make_ctx("l0a")
    nc = ctx.nc
    io = {}
    io["x_ext"] = nc.dram_tensor("x_ext", [EXT, 1024], F32,
                                 kind="ExternalInput").ap()
    for nm_ in ("wq", "wk", "wv", "wi", "wiT"):
        io[nm_] = nc.dram_tensor(nm_, [1024, 1024], F32,
                                 kind="ExternalInput").ap()
    for nm_ in ("bq", "bk", "bv", "bi"):
        io[nm_] = nc.dram_tensor(nm_, [1, 1024], F32,
                                 kind="ExternalInput").ap()
    io["wg"] = nc.dram_tensor("wg", [1024, 8], F32, kind="ExternalInput").ap()
    io["bg"] = nc.dram_tensor("bg", [1, 8], F32, kind="ExternalInput").ap()
    io["b_in_col"] = nc.dram_tensor("b_in_col", [1024, 1], F32,
                                    kind="ExternalInput").ap()
    io["q"] = nc.dram_tensor("q", [8, 128, OWN], BF16,
                             kind="ExternalOutput").ap()
    io["k"] = nc.dram_tensor("k", [8, 128, OWN], BF16,
                             kind="ExternalOutput").ap()
    io["v"] = nc.dram_tensor("v", [OWN, 1040], BF16,
                             kind="ExternalOutput").ap()
    io["t"] = nc.dram_tensor("t", [8, 128, OWN], BF16,
                             kind="ExternalOutput").ap()
    io["route"] = nc.dram_tensor("route", [OWN, 8], F32,
                                 kind="ExternalOutput").ap()
    emit_l0a(ctx, io, ext_kv=False)
    ctx.p.finalize()
    ctx.close()
    return nc


def emit_l0a(ctx, io, ext_kv):
    nc, p = ctx.nc, ctx.p
    x = io["x_ext"]
    wq_d, wk_d, wv_d, wi_d = io["wq"], io["wk"], io["wv"], io["wi"]
    bq_d, bk_d, bv_d, bi_d = io["bq"], io["bk"], io["bv"], io["bi"]
    wg_d, bg_d, wiT_d, bic_d = io["wg"], io["bg"], io["wiT"], io["b_in_col"]
    q_o, k_o, v_o, t_o, r_o = io["q"], io["k"], io["v"], io["t"], io["route"]

    SCL = 1.0 / 8.0
    xn, _ = emit_ln_to_fm(ctx, x, EXT, "a", keep_stats=True)
    xn_own = [c[:, HALO:HALO + OWN] for c in xn]

    wk_sb = load_weight_bf16(ctx, "wk", wk_d, 1024, 1024)
    bk_sb = load_bias_bf16(ctx, "bk", bk_d, 1024)
    p.barrier()
    kfm = [ctx.sbuf(f"kfm{c}", [128, EXT], BF16) for c in range(8)]
    emit_gemm_fm_fm(ctx, xn, EXT, wk_sb, 1024, bk_sb, kfm, "k")
    if ext_kv:
        store_and_free(ctx, [(k_o[c], kfm[c][:]) for c in range(8)])
    else:
        store_and_free(ctx, [(k_o[c], kfm[c][:, HALO:HALO + OWN])
                             for c in range(8)])
    ctx.free(kfm, wk_sb, bk_sb)

    wq_sb = load_weight_bf16(ctx, "wq", wq_d, 1024, 1024, scale=SCL)
    bq_sb = load_bias_bf16(ctx, "bq", bq_d, 1024, scale=SCL)
    p.barrier()
    qfm = [ctx.sbuf(f"qfm{c}", [128, OWN], BF16) for c in range(8)]
    emit_gemm_fm_fm(ctx, xn_own, OWN, wq_sb, 1024, bq_sb, qfm, "q")
    store_and_free(ctx, [(q_o[c], qfm[c][:]) for c in range(8)])
    ctx.free(qfm, wq_sb, bq_sb)

    wv_sb = load_weight_bf16(ctx, "wv", wv_d, 1024, 1024)
    bv_sb = load_bias_bf16(ctx, "bv", bv_d, 1024)
    p.barrier()
    VR = EXT if ext_kv else OWN
    xn_v = xn if ext_kv else xn_own
    nvt = VR // 128
    vtm = [ctx.sbuf("vtm", [128, 1040], BF16) for _ in range(nvt)]
    s_vz = ctx.sem()
    for i in range(nvt):
        p.add("vector", lambda e, i=i: e.memset(
            vtm[i].rearrange("p (h x) -> p h x", x=65)[:, :, 64:65], 1.0),
            inc=(s_vz, 1))
    p.raw_wait("vector", s_vz, nvt)

    def v_evac(e, ps_ap, key):
        tt, n0, n1 = key
        hbase = n0 // 64
        dst = vtm[tt].rearrange("p (h x) -> p h x", x=65)[
            :, hbase:hbase + (n1 - n0) // 64, 0:64]
        return e.tensor_copy(
            out=dst, in_=ps_ap.rearrange("p (h x) -> p h x", x=64))
    emit_gemm_fm_tm(ctx, xn_v, VR, wv_sb, 1024, bv_sb, "v", v_evac)
    store_and_free(ctx, [(v_o[i * 128:(i + 1) * 128, :], vtm[i][:])
                         for i in range(nvt)])
    ctx.free(vtm, wv_sb, bv_sb)

    wi_sb = load_weight_bf16(ctx, "wi", wi_d, 1024, 1024)
    bi_sb = load_bias_bf16(ctx, "bi", bi_d, 1024)
    p.barrier()
    tfm = [ctx.sbuf(f"tfm{c}", [128, OWN], BF16) for c in range(8)]
    emit_gemm_fm_fm(ctx, xn_own, OWN, wi_sb, 1024, bi_sb, tfm, "t")
    store_and_free(ctx, [(t_o[c], tfm[c][:]) for c in range(8)])
    ctx.free(wi_sb, bi_sb, xn)

    xt_ln, nm_ln, rstd_ln = ctx.ln_stats
    ntt = OWN // 128
    route = ctx.sbuf("route_sb", [128, 8 * ntt], F32)
    wg32 = [ctx.sbuf("wg32", [128, 8], F32) for _ in range(8)]
    bg32 = ctx.sbuf("bg32", [1, 8], F32)
    bic = [ctx.sbuf("bic", [128, 1], F32) for _ in range(8)]
    onesr32 = ctx.sbuf("onesr32", [1, 128], F32)
    bcomb = ctx.sbuf("bcomb", [1, 8], F32)
    wcomb = [ctx.sbuf("wcomb", [128, 8], F32) for _ in range(8)]
    s_ld2 = ctx.sem()
    nld2 = 0
    p.add("vector", lambda e: e.memset(onesr32[:], 1.0))
    for k in range(8):
        p.add("sync", lambda e, k=k: e.dma_start(
            out=wg32[k][:], in_=wg_d[k * 128:(k + 1) * 128, :]),
            inc=(s_ld2, 16)); nld2 += 1
        p.add("sync", lambda e, k=k: e.dma_start(
            out=bic[k][:], in_=bic_d[k * 128:(k + 1) * 128, :]),
            inc=(s_ld2, 16)); nld2 += 1
    p.add("sync", lambda e: e.dma_start(out=bg32[:], in_=bg_d),
          inc=(s_ld2, 16)); nld2 += 1
    p.raw_wait("tensor", s_ld2, 16 * nld2)
    p.barrier()
    bc_ps = ctx.psum("bc", [1, 8], F32)
    s_wc = ctx.sem()
    for k in range(8):
        p.add("tensor", lambda e, k=k: e.matmul(
            bc_ps[:], bic[k][:], wg32[k][:], start=(k == 0), stop=False))
    p.add("tensor", lambda e: e.matmul(
        bc_ps[:], onesr32[0:1, 0:1], bg32[:], start=False, stop=True),
        inc=(s_wc, 1))
    p.add("vector", lambda e: e.tensor_copy(out=bcomb[:], in_=bc_ps[:]),
          waits=[(s_wc, 1)], inc=(s_wc, 1))
    ctx.free(bc_ps)
    wiT_sb = [ctx.sbuf("wiTs", [128, 1024], F32) for _ in range(2)]
    s_wl = [ctx.sem(), ctx.sem()]
    s_wi = ctx.sem()
    wc_ps = [ctx.psum("wcp", [128, 8], F32) for _ in range(8)]
    for k in range(8):
        p.add("sync", lambda e, k=k: e.dma_start(
            out=wiT_sb[k % 2][:], in_=wiT_d[k * 128:(k + 1) * 128, :]),
            waits=[(s_wi, k - 1)] if k >= 2 else [],
            inc=(s_wl[k % 2], 16))
        for m in range(8):
            wvm = [(s_wl[k % 2], 16 * (k // 2 + 1))] if m == 0 else []
            if k == 0 and m == 0:
                wvm.append((s_wc, 2))
            p.add("tensor", lambda e, k=k, m=m: e.matmul(
                wc_ps[m][:], wiT_sb[k % 2][:, m * 128:(m + 1) * 128],
                wg32[k][:], start=(k == 0), stop=(k == 7)),
                waits=wvm,
                inc=(s_wi, 1) if m == 7 else None)
    p.raw_wait("vector", s_wi, 8)
    p.add("vector", lambda e: e.tensor_copy(out=ctx.junk[:, 2:3],
                                            in_=ctx.cpool[:, 0:1]),
          inc=(s_wc, 1))
    for m in range(8):
        p.add("vector", lambda e, m=m: e.tensor_copy(
            out=wcomb[m][:], in_=wc_ps[m][:]), inc=(s_wc, 1))
    p.barrier()
    ctx.free(wc_ps, wiT_sb)

    lgw = ctx.sbuf("rt_lgw", [128, 8 * ntt], F32)
    xn32 = [ctx.sbuf("xn32", [128, 1024], F32) for _ in range(2)]
    xfm32 = [ctx.sbuf("xfm32", [128, 128], F32) for _ in range(8)]
    tp32 = [ctx.psum("tp32", [128, 128], F32) for _ in range(2)]
    lg = [ctx.psum("lg", [128, 8], F32) for _ in range(2)]
    s_n32 = ctx.sem()
    s_t32 = ctx.sem()
    s_x32 = ctx.sem()
    s_lgm = ctx.sem()
    s_rt = ctx.sem()
    for tt in range(ntt):
        i_ext = tt + HALO // 128
        wvn = [(s_t32, 8 * (tt - 1))] if tt >= 2 else []
        p.add("vector", lambda e, tt=tt, i=i_ext: e.tensor_scalar(
            out=xn32[tt % 2][:], in0=xt_ln[i][:], scalar1=nm_ln[:, i:i + 1],
            scalar2=rstd_ln[:, i:i + 1], op0=ALU.add, op1=ALU.mult),
            waits=wvn, inc=(s_n32, 1))
        for c in range(8):
            j = tt * 8 + c
            wv = [(s_n32, tt + 1)]
            if j >= 2:
                wv.append((s_x32, j - 1))
            p.add("tensor", lambda e, tt=tt, c=c, j=j: e.transpose(
                tp32[j % 2][:], xn32[tt % 2][:, c * 128:(c + 1) * 128],
                ctx.ident_f32[:]), waits=wv, inc=(s_t32, 1))
            p.add("vector", lambda e, c=c, j=j, tt=tt: e.tensor_copy(
                out=xfm32[c][:], in_=tp32[j % 2][:]),
                waits=[(s_t32, j + 1), (s_lgm, tt)], inc=(s_x32, 1))
        pb = lg[tt % 2]
        wv0 = [(s_x32, 8 * (tt + 1))]
        if tt >= 2:
            wv0.append((s_rt, p.val(s_rt)))
        for k in range(8):
            p.add("tensor", lambda e, k=k, pb=pb: e.matmul(
                pb[:], xfm32[k][:], wcomb[k][:], start=(k == 0), stop=False),
                waits=wv0 if k == 0 else [])
        p.add("tensor", lambda e, pb=pb: e.matmul(
            pb[:], onesr32[:], bcomb[:], start=False, stop=True),
            inc=(s_lgm, 1))
        fn = p.add_r if tt == ntt - 1 else p.add
        fn("vector", lambda e, tt=tt, pb=pb: e.tensor_copy(
            out=lgw[:, tt * 8:(tt + 1) * 8], in_=pb[:]),
            waits=[(s_lgm, tt + 1)], inc=(s_rt, 1))
    exps = ctx.sbuf("rt_exp", [128, 8 * ntt], F32)
    prb = ctx.sbuf("rt_prb", [128, 8 * ntt], F32)
    msk = ctx.sbuf("rt_msk", [128, 8 * ntt], F32)
    pm2 = ctx.sbuf("rt_pm2", [128, 8 * ntt], F32)
    ru = ctx.sbuf("rt_ru", [128, 8 * ntt], F32)
    mcol = ctx.sbuf("rt_mcol", [128, 6 * ntt], F32)
    mx, ssum, rs_, m1, m2, den = [mcol[:, i * ntt:(i + 1) * ntt]
                                  for i in range(6)]
    A = p.add_r
    A("vector", lambda e: e.reduce_max(out=seg(mx, 1), in_=seg(lgw[:], 8),
                                       axis=AX.X))
    A("vector", lambda e: e.tensor_tensor(out=seg(exps[:], 8),
                                          in0=seg(lgw[:], 8),
                                          in1=bcast(mx, 8),
                                          op=ALU.subtract), inc=(s_rt, 1))
    p.add_r("scalar", lambda e: e.activation(out=exps[:], in_=exps[:],
                                             func=AF.Exp),
            waits=[(s_rt, p.val(s_rt))], inc=(s_rt, 1))
    p.raw_wait("vector", s_rt, p.val(s_rt))
    A("vector", lambda e: e.reduce_sum(out=seg(ssum, 1), in_=seg(exps[:], 8),
                                       axis=AX.X))
    A("vector", lambda e: e.reciprocal(out=rs_, in_=ssum))
    A("vector", lambda e: e.tensor_tensor(out=seg(prb[:], 8),
                                          in0=seg(exps[:], 8),
                                          in1=bcast(rs_, 8), op=ALU.mult))
    A("vector", lambda e: e.reduce_max(out=seg(m1, 1), in_=seg(prb[:], 8),
                                       axis=AX.X))
    A("vector", lambda e: e.tensor_tensor(out=seg(msk[:], 8),
                                          in0=seg(prb[:], 8),
                                          in1=bcast(m1, 8), op=ALU.is_lt))
    A("vector", lambda e: e.tensor_tensor(out=pm2[:], in0=msk[:], in1=prb[:],
                                          op=ALU.mult))
    A("vector", lambda e: e.reduce_max(out=seg(m2, 1), in_=seg(pm2[:], 8),
                                       axis=AX.X))
    A("vector", lambda e: e.tensor_tensor(out=seg(msk[:], 8),
                                          in0=seg(prb[:], 8),
                                          in1=bcast(m2, 8), op=ALU.is_ge))
    A("vector", lambda e: e.tensor_tensor(out=ru[:], in0=msk[:], in1=prb[:],
                                          op=ALU.mult))
    A("vector", lambda e: e.reduce_sum(out=seg(den, 1), in_=seg(ru[:], 8),
                                       axis=AX.X))
    A("vector", lambda e: e.reciprocal(out=rs_, in_=den))
    A("vector", lambda e: e.tensor_tensor(out=seg(route[:], 8),
                                          in0=seg(ru[:], 8),
                                          in1=bcast(rs_, 8), op=ALU.mult))
    p.barrier()
    ctx.ln_stats_free()
    p.barrier()
    st = ctx.sem(swdge=True)
    for i in range(OWN // 128):
        p.add("gpsimd", lambda e, i=i: e.dma_start(
            out=r_o[i * 128:(i + 1) * 128, :],
            in_=route[:, i * 8:(i + 1) * 8]), inc=(st, 16))
    p.raw_wait("gpsimd", st, 16 * (OWN // 128))
    p.barrier()
    ctx.free(lg, tp32, tfm, route, lgw, exps, prb, msk, pm2, ru, mcol,
             xn32, xfm32, wg32, bg32, bic, onesr32, bcomb, wcomb)


def build_l0c():
    ctx = make_ctx("l0c")
    nc = ctx.nc
    io = {}
    io["t"] = nc.dram_tensor("t", [8, 128, OWN], BF16,
                             kind="ExternalInput").ap()
    io["route"] = nc.dram_tensor("route", [OWN, 8], F32,
                                 kind="ExternalInput").ap()
    io["w1"] = nc.dram_tensor("w1", [8, 1024, 4096], F32,
                              kind="ExternalInput").ap()
    io["b1"] = nc.dram_tensor("b1", [8, 4096], F32,
                              kind="ExternalInput").ap()
    io["w2"] = nc.dram_tensor("w2", [8, 4096, 1024], F32,
                              kind="ExternalInput").ap()
    io["b2"] = nc.dram_tensor("b2", [8, 1024], F32,
                              kind="ExternalInput").ap()
    io["wout"] = nc.dram_tensor("wout", [1024, 1024], F32,
                                kind="ExternalInput").ap()
    io["bout"] = nc.dram_tensor("bout", [1, 1024], F32,
                                kind="ExternalInput").ap()
    io["s0"] = nc.dram_tensor("s0", [OWN, 1024], BF16,
                              kind="ExternalInput").ap()
    io["x"] = nc.dram_tensor("x", [OWN, 1024], F32,
                             kind="ExternalInput").ap()
    io["x1"] = nc.dram_tensor("x1", [OWN, 1024], F32,
                              kind="ExternalOutput").ap()
    emit_l0c(ctx, io)
    ctx.p.finalize()
    ctx.close()
    return nc


def emit_l0c(ctx, io):
    nc, p = ctx.nc, ctx.p
    t_d, r_d = io["t"], io["route"]
    w1_d, b1_d, w2_d, b2_d = io["w1"], io["b1"], io["w2"], io["b2"]
    wo_d, bo_d = io["wout"], io["bout"]
    s0_d, x_d, x1_d = io["s0"], io["x"], io["x1"]

    NT = OWN // 128
    tfm = [ctx.sbuf(f"tfm{c}", [128, OWN], BF16) for c in range(8)]
    acc = [ctx.sbuf("acc", [128, 1024], BF16) for _ in range(NT)]
    rts = ctx.sbuf("rts", [128, 8 * NT], F32)
    s_ld = ctx.sem()
    nld = 0
    for c in range(8):
        p.add("sync", lambda e, c=c: e.dma_start(out=tfm[c][:], in_=t_d[c]),
              inc=(s_ld, 16)); nld += 1
    for i in range(NT):
        p.add("sync", lambda e, i=i: e.dma_start(
            out=rts[:, i * 8:(i + 1) * 8], in_=r_d[i * 128:(i + 1) * 128, :]),
            inc=(s_ld, 16)); nld += 1
    p.raw_wait("tensor", s_ld, 16 * nld)
    p.raw_wait("vector", s_ld, 16 * nld)
    p.barrier()

    slots = [[ctx.sbuf("w1q", [128, 1024], BF16) for _ in range(8)] +
             [ctx.sbuf("w2q", [128, 1024], BF16) for _ in range(8)] +
             [ctx.sbuf("b1q", [1, 1024], BF16), ctx.sbuf("b2r", [1, 1024],
                                                         BF16)]
             for _ in range(2)]
    WND = 4
    wstg = [ctx.sbuf("wstg", [128, 1024], F32) for _ in range(WND)]
    s_wld = [ctx.sem() for _ in range(WND)]
    s_wcv = ctx.sem()
    s_mm = ctx.sem()
    s_rel = ctx.sem()
    s_eo = ctx.sem()
    mid_ps = [ctx.psum("midp", [128, 512], F32) for _ in range(2)]
    eo_ps = [ctx.psum("eop", [128, 1024], F32) for _ in range(2)]
    mid_sb = [[ctx.sbuf("mid", [128, 512], BF16) for _ in range(8)]
              for _ in range(2)]
    nw = [0]
    mm_end_of_job = {}
    mid_evac_cnt = {}
    eo_evac_cnt = {}
    NPART = 18

    def load_job(j):
        e_, qq = divmod(j, 4)
        for part in range(NPART):
            g = nw[0]
            wv = [(s_wcv, g - WND + 1)] if g >= WND else []
            if part < 8:
                src = w1_d[e_, part * 128:(part + 1) * 128,
                           qq * 1024:(qq + 1) * 1024]
                dst_stage = wstg[g % WND][:]
                cast_src = wstg[g % WND][:]
            elif part < 16:
                kk = part - 8
                src = w2_d[e_, qq * 1024 + kk * 128:
                           qq * 1024 + kk * 128 + 128, :]
                dst_stage = wstg[g % WND][:]
                cast_src = wstg[g % WND][:]
            elif part == 16:
                src = b1_d[e_:e_ + 1, qq * 1024:(qq + 1) * 1024]
                dst_stage = wstg[g % WND][0:1, :]
                cast_src = wstg[g % WND][0:1, :]
            else:
                src = b2_d[e_:e_ + 1, :]
                dst_stage = wstg[g % WND][0:1, :]
                cast_src = wstg[g % WND][0:1, :]
            p.add("sync", lambda e, src=src, d=dst_stage: e.dma_start(
                out=d, in_=src), waits=wv, inc=(s_wld[g % WND], 16))
            wv2 = [(s_wld[g % WND], 16 * (g // WND + 1))]
            if j >= 2:
                wv2.append((s_mm, mm_end_of_job[j - 2]))
            dst = slots[j % 2][part][:] if part < 16 else \
                slots[j % 2][part][0:1, :]
            p.add("vector", lambda e, dst=dst, csrc=cast_src: e.tensor_copy(
                out=dst, in_=csrc), waits=wv2, inc=(s_wcv, 1))
            nw[0] += 1

    load_job(0)
    load_job(1)
    mid_cnt = [0]
    eo_done = {}
    for j in range(32):
        e_, qq = divmod(j, 4)
        sl = slots[j % 2]
        wcv_need = NPART * (j + 1)
        for tth in range(8):
            t0 = tth * 256
            for n in range(8):
                gmid = mid_cnt[0]
                mp = mid_ps[gmid % 2]
                wv = [(s_wcv, wcv_need)]
                if gmid >= 2:
                    wv.append((s_rel, gmid - 1))
                for k in range(8):
                    p.add("tensor", lambda e, sl=sl, k=k, n=n, t0=t0, mp=mp:
                          e.matmul(mp[:, 0:256],
                                   sl[k][:, n * 128:(n + 1) * 128],
                                   tfm[k][:, t0:t0 + 256],
                                   start=(k == 0), stop=False),
                          waits=wv if k == 0 else [])
                p.add("tensor", lambda e, sl=sl, n=n, mp=mp: e.matmul(
                    mp[:, 0:256], sl[16][:, n * 128:(n + 1) * 128],
                    ctx.ones_sb[:, 0:256], start=False, stop=True),
                    inc=(s_mm, 1))
                p.add("scalar", lambda e, mp=mp, gmid=gmid: e.activation(
                    out=mid_sb[gmid % 2][0][:, 0:256], in_=mp[:, 0:256],
                    func=AF.Relu),
                    waits=[(s_mm, p.val(s_mm))], inc=(s_rel, 1))
                mid_cnt[0] += 1
                wv = [(s_rel, p.val(s_rel))]
                if n == 0:
                    geo = j * 8 + tth
                    if geo >= 1:
                        wv.append((s_eo, eo_done[geo - 1]))
                first_mm = True
                for tq2 in range(2):
                    ep = eo_ps[tq2]
                    for half in range(2):
                        last = (n == 7 and qq != 0)
                        p.add("tensor", lambda e, gmid=gmid, tq2=tq2,
                              half=half, n=n, ep=ep, sl=sl, last=last:
                              e.matmul(
                                  ep[:, half * 512:(half + 1) * 512],
                                  mid_sb[gmid % 2][0][:, tq2 * 128:
                                                      tq2 * 128 + 128],
                                  sl[8 + n][:, half * 512:(half + 1) * 512],
                                  start=(n == 0), stop=last),
                              waits=wv if first_mm else [],
                              inc=(s_mm, 1)
                              if (last and tq2 == 1 and half == 1) else None)
                        first_mm = False
            if qq == 0:
                for tq2 in range(2):
                    ep = eo_ps[tq2]
                    for half in range(2):
                        p.add("tensor", lambda e, sl=sl, half=half, ep=ep:
                              e.matmul(
                                  ep[:, half * 512:(half + 1) * 512],
                                  ctx.ones_sb[0:1, 0:128],
                                  sl[17][0:1, half * 512:(half + 1) * 512],
                                  start=False, stop=True),
                              inc=(s_mm, 1) if half == 1 else None)
            mmv = p.val(s_mm)
            for tq2 in range(2):
                ep = eo_ps[tq2]
                ti = tth * 2 + tq2
                rcol = rts[:, ti * 8 + e_:ti * 8 + e_ + 1]
                if j == 0:
                    p.add("vector", lambda e, ep=ep, ti=ti, rcol=rcol:
                          e.tensor_scalar(out=acc[ti][:], in0=ep[:],
                                          scalar1=rcol, scalar2=None,
                                          op0=ALU.mult),
                          waits=[(s_mm, mmv)], inc=(s_eo, 1))
                else:
                    p.add("vector", lambda e, ep=ep, ti=ti, rcol=rcol:
                          e.scalar_tensor_tensor(out=acc[ti][:], in0=ep[:],
                                                 scalar=rcol, in1=acc[ti][:],
                                                 op0=ALU.mult, op1=ALU.add),
                          waits=[(s_mm, mmv)], inc=(s_eo, 1))
            eo_done[j * 8 + tth] = p.val(s_eo)
        mm_end_of_job[j] = p.val(s_mm)
        if j + 2 < 32:
            load_job(j + 2)
    p.barrier()
    ctx.free(mid_sb, wstg, slots, eo_ps, mid_ps, tfm)

    acc_fm = emit_tm_to_fm(ctx, acc, OWN, "mo")
    ctx.free(acc)
    wo_sb = load_weight_bf16(ctx, "wout", wo_d, 1024, 1024)
    bo_sb = load_bias_bf16(ctx, "bout", bo_d, 1024)
    p.barrier()
    g_tm = [ctx.sbuf("g", [128, 1024], BF16) for _ in range(NT)]

    def g_evac(e, ps_ap, key):
        tt, n0, n1 = key
        return e.activation(out=g_tm[tt][:, n0:n1], in_=ps_ap, func=AF.Silu)
    emit_gemm_fm_tm(ctx, acc_fm, OWN, wo_sb, 1024, bo_sb, "g", g_evac,
                    evac_eng="scalar")
    ctx.free(wo_sb, bo_sb, acc_fm)

    xs = [ctx.sbuf("xs", [128, 1024], F32) for _ in range(2)]
    ss = [ctx.sbuf("ss", [128, 1024], BF16) for _ in range(2)]
    x1t = [ctx.sbuf("x1t", [128, 1024], F32) for _ in range(2)]
    s_xl = [ctx.sem(), ctx.sem()]
    s_cb = ctx.sem()
    s_st2 = [ctx.sem(swdge=True), ctx.sem(swdge=True)]
    for i in range(NT):
        wvl = [(s_cb, 2 * (i - 2) + 2)] if i >= 2 else []
        p.add("sync", lambda e, i=i: e.dma_start(
            out=xs[i % 2][:], in_=x_d[i * 128:(i + 1) * 128, :]),
            waits=wvl, inc=(s_xl[i % 2], 16))
        p.add("sync", lambda e, i=i: e.dma_start(
            out=ss[i % 2][:], in_=s0_d[i * 128:(i + 1) * 128, :]),
            inc=(s_xl[i % 2], 16))
        wv = [(s_xl[i % 2], 32 * (i // 2 + 1))]
        if i >= 2:
            wv.append((s_st2[i % 2], 16 * (i // 2)))
        p.add("vector", lambda e, i=i: e.tensor_tensor(
            out=x1t[i % 2][:], in0=ss[i % 2][:], in1=g_tm[i][:],
            op=ALU.mult), waits=wv, inc=(s_cb, 1))
        p.flush("vector")
        p.add("vector", lambda e, i=i: e.tensor_tensor(
            out=x1t[i % 2][:], in0=x1t[i % 2][:], in1=xs[i % 2][:],
            op=ALU.add), inc=(s_cb, 1))
        p.add("gpsimd", lambda e, i=i: e.dma_start(
            out=x1_d[i * 128:(i + 1) * 128, :], in_=x1t[i % 2][:]),
            waits=[(s_cb, 2 * i + 2)], inc=(s_st2[i % 2], 16))
    p.raw_wait("gpsimd", s_st2[0], 16 * (NT // 2))
    p.raw_wait("gpsimd", s_st2[1], 16 * (NT // 2))
    p.barrier()
    ctx.free(xs, ss, x1t, g_tm, rts)


def emit_fm_to_tm(ctx, fm_chunks, T, tag, out_dtype=BF16):
    p = ctx.p
    nt = T // 128
    tm = [ctx.sbuf(f"tm_{tag}", [128, 1024], out_dtype) for _ in range(nt)]
    tp = [ctx.psum(f"tp_{tag}", [128, 128], out_dtype) for _ in range(2)]
    s_tp = ctx.sem()
    s_ev = ctx.sem()
    j = 0
    for i in range(nt):
        for c in range(8):
            wv = [(s_ev, j - 1)] if j >= 2 else []
            p.add("tensor", lambda e, i=i, c=c, j=j: e.transpose(
                tp[j % 2][:], fm_chunks[c][:, i * 128:(i + 1) * 128],
                ctx.ident_sb[:]), waits=wv, inc=(s_tp, 1))
            p.add("vector", lambda e, i=i, c=c, j=j: e.tensor_copy(
                out=tm[i][:, c * 128:(c + 1) * 128], in_=tp[j % 2][:]),
                waits=[(s_tp, j + 1)], inc=(s_ev, 1))
            j += 1
    p.barrier()
    ctx.free(tp)
    return tm


def load_weight_mp(ctx, name, dram_ap, gain_sb, cmul):
    p, nc = ctx.p, ctx.nc
    out = [ctx.sbuf(f"w_{name}{k}", [128, 1024], BF16) for k in range(8)]
    fcol = ctx.sbuf(f"fc_{name}", [128, 8], F32)
    stage = [ctx.sbuf(f"wst_{name}", [128, 1024], F32) for _ in range(2)]
    sq = [ctx.sbuf(f"sq_{name}", [128, 1024], F32) for _ in range(2)]
    frow = ctx.sbuf(f"fr_{name}", [1, 1024], F32)
    s2p = [ctx.psum(f"s2_{name}", [1, 512], F32) for _ in range(2)]
    fcp = ctx.psum(f"fcp_{name}", [128, 8], F32)
    s_lds = [ctx.sem(), ctx.sem()]
    s_cv = ctx.sem()
    s_sq = ctx.sem()
    s_s2 = ctx.sem()
    s_f = ctx.sem()
    for k in range(8):
        buf = stage[k % 2]
        s_ld = s_lds[k % 2]
        w = [(s_sq, 2 * (k - 2) + 2), (s_cv, k - 1)] if k >= 2 else []
        p.add("sync", lambda e, buf=buf, k=k: e.dma_start(
            out=buf[:], in_=dram_ap[k * 128:(k + 1) * 128, :]),
            waits=w, inc=(s_ld, 16))
        wvc = [(s_ld, 16 * (k // 2 + 1))]
        p.add("vector", lambda e, k=k, buf=buf: e.tensor_copy(
            out=out[k][:], in_=buf[:]), waits=wvc, inc=(s_cv, 1))
        p.add("scalar", lambda e, k=k, buf=buf: e.activation(
            out=sq[k % 2][:], in_=buf[:], func=AF.Square),
            waits=wvc + ([(s_s2, 2 * (k - 1))] if k >= 2 else []),
            inc=(s_sq, 2))
        for hh in range(2):
            p.add("tensor", lambda e, k=k, hh=hh: e.matmul(
                s2p[hh][:], ctx.ones_col128[:, 0:1],
                sq[k % 2][:, hh * 512:(hh + 1) * 512],
                start=(k == 0), stop=(k == 7)),
                waits=[(s_sq, 2 * k + 2)] if hh == 0 else [],
                inc=(s_s2, 1))
    c_inv = ctx.ccol(1.0 / 1024.0)
    p.add("scalar", lambda e: e.activation(
        out=frow[:, 0:512], in_=s2p[0][:], func=AF.Sqrt,
        scale=c_inv[0:1, :]), waits=[(s_s2, 16)])
    p.add_r("scalar", lambda e: e.activation(
        out=frow[:, 512:1024], in_=s2p[1][:], func=AF.Sqrt,
        scale=c_inv[0:1, :]))
    p.add("vector", lambda e: e.tensor_scalar(
        out=frow[:], in0=frow[:], scalar1=EPS_MP, scalar2=None, op0=ALU.add),
        waits=[(p.rsync["scalar"], p.val(p.rsync["scalar"]))])
    p.flush("vector")
    p.add_r("vector", lambda e: e.reciprocal(out=frow[:], in_=frow[:]))
    p.add("vector", lambda e: e.tensor_scalar(
        out=frow[:], in0=frow[:], scalar1=gain_sb[0:1, 0:1], scalar2=cmul,
        op0=ALU.mult, op1=ALU.mult), inc=(s_f, 1))
    for c in range(8):
        p.add("tensor", lambda e, c=c: e.matmul(
            fcp[:, c:c + 1], frow[:, c * 128:(c + 1) * 128],
            ctx.ones_f32[0:1, 0:1], start=True, stop=True),
            waits=[(s_f, p.val(s_f))] if c == 0 else [],
            inc=(s_f, 1) if c == 7 else None)
    p.add("vector", lambda e: e.tensor_copy(out=fcol[:], in_=fcp[:]),
          waits=[(s_f, p.val(s_f))], inc=(s_f, 1))
    p.barrier()
    ctx.free(fcp, s2p, frow, sq, stage)
    return out, fcol


def emit_mp_norm_tm(ctx, tm_tiles, nt, tag, extra=1.0):
    p = ctx.p
    sq = [ctx.sbuf(f"nsq_{tag}", [128, 1024], F32) for _ in range(2)]
    ss = ctx.sbuf(f"nss_{tag}", [128, 16], F32)
    fs = ctx.sbuf(f"nfs_{tag}", [128, 16], F32)
    s_sq = ctx.sem()
    s_ss = ctx.sem()
    c64 = ctx.ccol(1.0 / 64.0)
    for i in range(nt):
        p.add("scalar", lambda e, i=i: e.activation(
            out=sq[i % 2][:], in_=tm_tiles[i][:], func=AF.Square),
            waits=[(s_ss, max(0, 3 * (i - 1)))], inc=(s_sq, 1))
        p.add_r("vector", lambda e, i=i: e.reduce_sum(
            out=seg(ss[:], 1), in_=seg(sq[i % 2][:], 64), axis=AX.X),
            waits=[(s_sq, i + 1)])
        p.add_r("scalar", lambda e: e.activation(
            out=fs[:], in_=ss[:], func=AF.Sqrt, scale=c64),
            waits=[(p.rsync["vector"], p.val(p.rsync["vector"]))])
        p.add("vector", lambda e: e.tensor_scalar(
            out=fs[:], in0=fs[:], scalar1=EPS_MP, scalar2=None, op0=ALU.add),
            waits=[(p.rsync["scalar"], p.val(p.rsync["scalar"]))])
        p.flush("vector")
        p.add_r("vector", lambda e: e.reciprocal(out=fs[:], in_=fs[:]))
        if extra != 1.0:
            p.add("vector", lambda e: e.tensor_scalar(
                out=fs[:], in0=fs[:], scalar1=extra, scalar2=None,
                op0=ALU.mult))
            p.flush("vector")
        p.add("vector", lambda e, i=i: e.tensor_tensor(
            out=seg(tm_tiles[i][:], 64), in0=seg(tm_tiles[i][:], 64),
            in1=bcast(fs[:], 64), op=ALU.mult), inc=(s_ss, 3))
        p.flush("vector")
    p.barrier()
    ctx.free(fs, ss, sq)


def build_l1a():
    ctx = make_ctx("l1a")
    nc = ctx.nc
    io = {}
    io["x1"] = nc.dram_tensor("x1", [OWN, 1024], F32,
                              kind="ExternalInput").ap()
    for nm_ in ("wqT", "wkT", "wvT"):
        io[nm_] = nc.dram_tensor(nm_, [1024, 1024], F32,
                                 kind="ExternalInput").ap()
    io["gain"] = nc.dram_tensor("gain", [1, 1], F32,
                                kind="ExternalInput").ap()
    io["xn1"] = nc.dram_tensor("xn1", [OWN, 1024], BF16,
                               kind="ExternalOutput").ap()
    io["v1"] = nc.dram_tensor("v1", [8, 128, OWN], BF16,
                              kind="ExternalOutput").ap()
    io["gram"] = nc.dram_tensor("gram", [128, 512], F32,
                                kind="ExternalOutput").ap()
    emit_l1a(ctx, io)
    ctx.p.finalize()
    ctx.close()
    return nc


def emit_l1a(ctx, io):
    nc, p = ctx.nc, ctx.p
    ctx.ones_col128 = ctx.sbuf("ones_col", [128, 1], F32)
    ctx.ones_f32 = ctx.sbuf("ones_f32", [1, 8], F32)
    p.add("vector", lambda e: e.memset(ctx.ones_col128[:], 1.0))
    p.add("vector", lambda e: e.memset(ctx.ones_f32[:], 1.0))
    x1_d, gn_d = io["x1"], io["gain"]
    wq_d, wk_d, wv_d = io["wqT"], io["wkT"], io["wvT"]
    xn_o, v_o, g_o = io["xn1"], io["v1"], io["gram"]

    gain_sb = ctx.sbuf("gain", [1, 1], F32)
    s_g = ctx.sem()
    p.add("sync", lambda e: e.dma_start(out=gain_sb[:], in_=gn_d),
          inc=(s_g, 16))
    p.raw_wait("vector", s_g, 16)
    p.barrier()

    xn_fm, xn_tm = emit_ln_to_fm(ctx, x1_d, OWN, "l1", keep_tm=True)
    st = ctx.sem(swdge=True)
    nst = [0]
    for i in range(OWN // 128):
        p.add("gpsimd", lambda e, i=i: e.dma_start(
            out=xn_o[i * 128:(i + 1) * 128, :], in_=xn_tm[i][:]),
            inc=(st, 16)); nst[0] += 1

    NT = OWN // 128
    SQS = 1.0 / math.sqrt(float(S))

    def proj(tagn, w_d, cmul=1.0):
        wsb, fcol = load_weight_mp(ctx, tagn, w_d, gain_sb, 1.0 / 32.0 * cmul)
        p.barrier()
        outfm = [ctx.sbuf(f"{tagn}fm{c}", [128, OWN], BF16) for c in range(8)]

        def evac(e, ps_ap, key):
            n, t0, t1 = key
            return e.tensor_scalar(out=outfm[n][:, t0:t1], in0=ps_ap,
                                   scalar1=fcol[:, n:n + 1], scalar2=None,
                                   op0=ALU.mult)
        emit_gemm_fm_fm(ctx, xn_fm, OWN, wsb, 1024, None, None, tagn,
                        evac=evac)
        ctx.free(wsb)
        return outfm, fcol

    vfm, fv = proj("wv", wv_d)
    for c in range(8):
        p.add("gpsimd", lambda e, c=c: e.dma_start(
            out=v_o[c], in_=vfm[c][:]), inc=(st, 16)); nst[0] += 1
    p.raw_wait("gpsimd", st, 16 * nst[0])
    p.barrier()
    ctx.free(vfm, fv)

    qfm, fq = proj("wq", wq_d)
    q_tm = emit_fm_to_tm(ctx, qfm, OWN, "q")
    ctx.free(qfm, fq)
    kfm, fk = proj("wk", wk_d)
    k_tm = emit_fm_to_tm(ctx, kfm, OWN, "k")
    ctx.free(kfm, fk)
    emit_mp_norm_tm(ctx, q_tm, NT, "q")
    emit_mp_norm_tm(ctx, k_tm, NT, "k", extra=SQS)

    gps = [ctx.psum("gram", [128, 64], F32) for _ in range(8)]
    gsb = ctx.sbuf("gramsb", [128, 512], F32)
    s_gm = ctx.sem()
    for h in range(16):
        hp, rb = h // 2, (h % 2) * 64
        for i in range(NT):
            p.add("tensor", lambda e, h=h, hp=hp, rb=rb, i=i: e.matmul(
                gps[hp][rb:rb + 64, :],
                q_tm[i][:, h * 64:h * 64 + 64],
                k_tm[i][:, h * 64:h * 64 + 64],
                start=(i == 0), stop=(i == NT - 1)),
                inc=(s_gm, 1) if i == NT - 1 else None)
    for hp in range(8):
        p.add("vector", lambda e, hp=hp: e.tensor_copy(
            out=gsb[:, hp * 64:(hp + 1) * 64], in_=gps[hp][:]),
            waits=[(s_gm, 16)], inc=(s_gm, 1))
    st2 = ctx.sem(swdge=True)
    p.add("gpsimd", lambda e: e.dma_start(out=g_o, in_=gsb[:]),
          waits=[(s_gm, 24)], inc=(st2, 16))
    p.raw_wait("gpsimd", st2, 16)
    p.barrier()
    ctx.free(gsb, gps, q_tm, k_tm, xn_fm, xn_tm)


def build_l1b():
    ctx = make_ctx("l1b")
    nc = ctx.nc
    io = {}
    io["grams"] = nc.dram_tensor("grams", [4, 128, 512], F32,
                                 kind="ExternalInput").ap()
    io["v1"] = nc.dram_tensor("v1", [8, 128, OWN], BF16,
                              kind="ExternalInput").ap()
    io["xn1"] = nc.dram_tensor("xn1", [OWN, 1024], BF16,
                               kind="ExternalInput").ap()
    io["woT"] = nc.dram_tensor("woT", [1024, 1024], F32,
                               kind="ExternalInput").ap()
    io["gain"] = nc.dram_tensor("gain", [1, 1], F32,
                                kind="ExternalInput").ap()
    io["s1"] = nc.dram_tensor("s1", [OWN, 1024], BF16,
                              kind="ExternalOutput").ap()
    emit_l1b(ctx, io, npart=4)
    ctx.p.finalize()
    ctx.close()
    return nc


def emit_l1b(ctx, io, npart):
    nc, p = ctx.nc, ctx.p
    ctx.ones_col128 = ctx.sbuf("ones_col", [128, 1], F32)
    ctx.ones_f32 = ctx.sbuf("ones_f32", [1, 8], F32)
    p.add("vector", lambda e: e.memset(ctx.ones_col128[:], 1.0))
    p.add("vector", lambda e: e.memset(ctx.ones_f32[:], 1.0))
    gr_d, v_d, xn_d = io["grams"], io["v1"], io["xn1"]
    wo_d, gn_d, s_o = io["woT"], io["gain"], io["s1"]

    NT = OWN // 128
    gain_sb = ctx.sbuf("gain", [1, 1], F32)
    grs = [ctx.sbuf("grs", [128, 512], F32) for _ in range(npart)]
    vfm = [ctx.sbuf(f"vfm{c}", [128, OWN], BF16) for c in range(8)]
    s_ld = ctx.sem()
    nld = 0
    p.add("sync", lambda e: e.dma_start(out=gain_sb[:], in_=gn_d),
          inc=(s_ld, 16)); nld += 1
    for i in range(npart):
        p.add("sync", lambda e, i=i: e.dma_start(
            out=grs[i][:], in_=gr_d[i] if npart > 1 else gr_d),
            inc=(s_ld, 16)); nld += 1
    for c in range(8):
        p.add("sync", lambda e, c=c: e.dma_start(out=vfm[c][:], in_=v_d[c]),
              inc=(s_ld, 16)); nld += 1
    p.raw_wait("vector", s_ld, 16 * nld)
    p.raw_wait("tensor", s_ld, 16 * nld)
    p.barrier()

    att = ctx.sbuf("att", [128, 512], BF16)
    w0 = ctx.sbuf("smw0", [128, 512], F32)
    w1 = ctx.sbuf("smw1", [128, 512], F32)
    mcol = ctx.sbuf("smc", [128, 24], F32)
    if npart > 1:
        p.add("vector", lambda e: e.tensor_tensor(out=w0[:], in0=grs[0][:],
                                                  in1=grs[1][:], op=ALU.add))
        p.add("vector", lambda e: e.tensor_tensor(out=w1[:], in0=grs[2][:],
                                                  in1=grs[3][:], op=ALU.add))
        p.flush("vector")
        p.add("vector", lambda e: e.tensor_tensor(out=w0[:], in0=w0[:],
                                                  in1=w1[:], op=ALU.add))
        p.flush("vector")
    else:
        p.add("vector", lambda e: e.tensor_copy(out=w0[:], in_=grs[0][:]))
        p.flush("vector")
    A = p.add_r
    mx, sm, rc = mcol[:, 0:8], mcol[:, 8:16], mcol[:, 16:24]
    s_sm = ctx.sem()
    A("vector", lambda e: e.reduce_max(out=seg(mx, 1), in_=seg(w0[:], 64),
                                       axis=AX.X))
    A("vector", lambda e: e.tensor_tensor(out=seg(w1[:], 64),
                                          in0=seg(w0[:], 64),
                                          in1=bcast(mx, 64),
                                          op=ALU.subtract), inc=(s_sm, 1))
    p.add_r("scalar", lambda e: e.activation(out=w1[:], in_=w1[:],
                                             func=AF.Exp),
            waits=[(s_sm, 1)], inc=(s_sm, 1))
    p.raw_wait("vector", s_sm, 2)
    A("vector", lambda e: e.reduce_sum(out=seg(sm, 1), in_=seg(w1[:], 64),
                                       axis=AX.X))
    A("vector", lambda e: e.reciprocal(out=rc, in_=sm))
    A("vector", lambda e: e.tensor_tensor(out=seg(att[:], 64),
                                          in0=seg(w1[:], 64),
                                          in1=bcast(rc, 64), op=ALU.mult))
    p.barrier()

    atT = ctx.sbuf("atT", [128, 512], BF16)
    tp = [ctx.psum("atp", [128, 64], BF16) for _ in range(2)]
    s_tp = ctx.sem()
    s_te = ctx.sem()
    for h in range(16):
        rb, cb = (h % 2) * 64, (h // 2) * 64
        wv = [(s_te, h - 1)] if h >= 2 else []
        p.add("tensor", lambda e, h=h, rb=rb, cb=cb: e.transpose(
            tp[h % 2][rb:rb + 64, :], att[rb:rb + 64, cb:cb + 64],
            ctx.ident_sb[rb:rb + 64, rb:rb + 64]), waits=wv, inc=(s_tp, 1))
        p.add("vector", lambda e, h=h, rb=rb, cb=cb: e.tensor_copy(
            out=atT[rb:rb + 64, cb:cb + 64], in_=tp[h % 2][rb:rb + 64, :]),
            waits=[(s_tp, h + 1)], inc=(s_te, 1))
    p.barrier()
    ctx.free(tp, w0, w1, mcol, grs)

    ofm = [ctx.sbuf(f"ofm{c}", [128, OWN], BF16) for c in range(8)]
    ops_ = [ctx.psum("op", [128, 512], F32) for _ in range(2)]
    s_mm = ctx.sem()
    s_ev = ctx.sem()
    it = 0
    for hp in range(4):
        for tt in range(OWN // 512):
            pb = ops_[it % 2]
            wv = [(s_ev, it - 1)] if it >= 2 else []
            for sub in range(2):
                h = hp * 2 + sub
                rb = (h % 2) * 64
                p.add("tensor", lambda e, h=h, hp=hp, rb=rb, tt=tt, pb=pb:
                      e.matmul(pb[rb:rb + 64, :],
                               atT[rb:rb + 64, (h // 2) * 64:(h // 2) * 64 + 64],
                               vfm[h // 2][rb:rb + 64,
                                           tt * 512:(tt + 1) * 512],
                               start=True, stop=True),
                      waits=wv if sub == 0 else [],
                      inc=(s_mm, 1) if sub == 1 else None)
            it += 1
            p.add("vector", lambda e, hp=hp, tt=tt, pb=pb: e.tensor_copy(
                out=ofm[hp * 2 // 2][:, tt * 512:(tt + 1) * 512]
                if False else ofm[hp][:, tt * 512:(tt + 1) * 512], in_=pb[:]),
                waits=[(s_mm, it)], inc=(s_ev, 1))
    p.barrier()
    for hp in range(4, 8):
        for tt in range(OWN // 512):
            pb = ops_[it % 2]
            wv = [(s_ev, it - 1)] if it >= 2 else []
            for sub in range(2):
                h = hp * 2 + sub
                rb = (h % 2) * 64
                p.add("tensor", lambda e, h=h, rb=rb, tt=tt, pb=pb:
                      e.matmul(pb[rb:rb + 64, :],
                               atT[rb:rb + 64, (h // 2) * 64:(h // 2) * 64 + 64],
                               vfm[h // 2][rb:rb + 64,
                                           tt * 512:(tt + 1) * 512],
                               start=True, stop=True),
                      waits=wv if sub == 0 else [],
                      inc=(s_mm, 1) if sub == 1 else None)
            it += 1
            p.add("vector", lambda e, hp=hp, tt=tt, pb=pb: e.tensor_copy(
                out=ofm[hp][:, tt * 512:(tt + 1) * 512], in_=pb[:]),
                waits=[(s_mm, it)], inc=(s_ev, 1))
    p.barrier()
    ctx.free(ops_, atT, att, vfm)

    wsb, fcol = load_weight_mp(ctx, "wo", wo_d, gain_sb,
                               1.0 / 32.0 * 0.7071067811865476)
    p.barrier()
    opfm = [ctx.sbuf(f"opfm{c}", [128, OWN], BF16) for c in range(8)]

    def evac(e, ps_ap, key):
        n, t0, t1 = key
        return e.tensor_scalar(out=opfm[n][:, t0:t1], in0=ps_ap,
                               scalar1=fcol[:, n:n + 1], scalar2=None,
                               op0=ALU.mult)
    emit_gemm_fm_fm(ctx, ofm, OWN, wsb, 1024, None, None, "op", evac=evac)
    ctx.free(wsb, fcol, ofm)
    op_tm = emit_fm_to_tm(ctx, opfm, OWN, "op")
    ctx.free(opfm)

    xnt = [ctx.sbuf("xnt", [128, 1024], BF16) for _ in range(2)]
    s1t = [ctx.sbuf("s1t", [128, 1024], BF16) for _ in range(2)]
    xsc = [ctx.sbuf("xsc", [128, 1024], F32) for _ in range(2)]
    s_xl = [ctx.sem(), ctx.sem()]
    s_cbA = ctx.sem()
    s_cbB = ctx.sem()
    s_st = [ctx.sem(swdge=True), ctx.sem(swdge=True)]
    c07 = ctx.ccol(0.7071067811865476)
    for i in range(NT):
        wvl = [(s_cbA, i - 1)] if i >= 2 else []
        p.add("sync", lambda e, i=i: e.dma_start(
            out=xnt[i % 2][:], in_=xn_d[i * 128:(i + 1) * 128, :]),
            waits=wvl, inc=(s_xl[i % 2], 16))
        wv = [(s_xl[i % 2], 16 * (i // 2 + 1))]
        if i >= 2:
            wv.append((s_cbB, i - 1))
        p.add("scalar", lambda e, i=i: e.activation(
            out=xsc[i % 2][:], in_=xnt[i % 2][:], func=AF.Copy,
            scale=c07), waits=wv, inc=(s_cbA, 1))
        wv2 = [(s_cbA, i + 1)]
        if i >= 2:
            wv2.append((s_st[i % 2], 16 * (i // 2)))
        p.add("vector", lambda e, i=i: e.tensor_tensor(
            out=s1t[i % 2][:], in0=xsc[i % 2][:], in1=op_tm[i][:],
            op=ALU.add), waits=wv2, inc=(s_cbB, 1))
        p.flush("vector")
        p.add("gpsimd", lambda e, i=i: e.dma_start(
            out=s_o[i * 128:(i + 1) * 128, :], in_=s1t[i % 2][:]),
            waits=[(s_cbB, i + 1)], inc=(s_st[i % 2], 16))
    p.raw_wait("gpsimd", s_st[0], 16 * (NT // 2))
    p.raw_wait("gpsimd", s_st[1], 16 * (NT // 2))
    p.barrier()
    ctx.free(xnt, s1t, xsc, op_tm)


def build_l1c():
    ctx = make_ctx("l1c")
    nc = ctx.nc
    io = {}
    io["xn1"] = nc.dram_tensor("xn1", [OWN, 1024], BF16,
                               kind="ExternalInput").ap()
    io["wgT"] = nc.dram_tensor("wgT", [1024, FFN1_H], I32,
                               kind="ExternalInput").ap()
    io["wuT"] = nc.dram_tensor("wuT", [1024, FFN1_H], I32,
                               kind="ExternalInput").ap()
    io["wdT"] = nc.dram_tensor("wdT", [FFN1_H, 1024], I32,
                               kind="ExternalInput").ap()
    io["x1"] = nc.dram_tensor("x1", [OWN, 1024], F32,
                              kind="ExternalInput").ap()
    io["s1"] = nc.dram_tensor("s1", [OWN, 1024], BF16,
                              kind="ExternalInput").ap()
    io["x2"] = nc.dram_tensor("x2", [OWN, 1024], F32,
                              kind="ExternalOutput").ap()
    emit_l1c(ctx, io)
    ctx.p.finalize()
    ctx.close()
    return nc


def emit_l1c(ctx, io):
    nc, p = ctx.nc, ctx.p
    xn_d, wg_d, wu_d, wd_d = io["xn1"], io["wgT"], io["wuT"], io["wdT"]
    x1_d, s1_d, x2_d = io["x1"], io["s1"], io["x2"]
    NT = OWN // 128
    QS = 1.0 / 127.0

    xnt = [ctx.sbuf("xnt", [128, 1024], BF16) for _ in range(NT)]
    s_ld = ctx.sem()
    for i in range(NT):
        p.add("sync", lambda e, i=i: e.dma_start(
            out=xnt[i][:], in_=xn_d[i * 128:(i + 1) * 128, :]),
            inc=(s_ld, 16))
    p.raw_wait("tensor", s_ld, 16 * NT)
    p.raw_wait("vector", s_ld, 16 * NT)
    p.barrier()
    xn_fm = emit_tm_to_fm(ctx, xnt, OWN, "xn")
    ctx.free(xnt)

    wd_sb = [ctx.sbuf("wd", [128, 1024], BF16) for _ in range(24)]
    wstg = [ctx.sbuf("wstg", [128, 1024], I32) for _ in range(2)]
    s_wld = [ctx.sem(), ctx.sem()]
    s_wcv = ctx.sem()
    for k in range(24):
        g = k
        w = [(s_wcv, k - 1)] if k >= 2 else []
        p.add("sync", lambda e, k=k, g=g: e.dma_start(
            out=wstg[g % 2][:], in_=wd_d[k * 128:(k + 1) * 128, :]),
            waits=w, inc=(s_wld[g % 2], 16))
        p.add("vector", lambda e, k=k, g=g: e.tensor_scalar(
            out=wd_sb[k][:], in0=wstg[g % 2][:], scalar1=QS, scalar2=None,
            op0=ALU.mult), waits=[(s_wld[g % 2], 16 * (g // 2 + 1))],
            inc=(s_wcv, 1))
    p.raw_wait("tensor", s_wcv, 24)
    p.raw_wait("sync", s_wcv, 24)
    p.barrier()

    h_fm = [ctx.sbuf("hfm", [128, 1024], BF16) for _ in range(24)]
    g1 = [ctx.sbuf("g1", [128, 1024], BF16) for _ in range(NT)]
    gu_sb = [[ctx.sbuf("gusb", [128, 128], BF16) for _ in range(16)]
             for _ in range(2)]
    GDEPTH = 8
    gstg = [ctx.sbuf("gstg", [128, 128], I32) for _ in range(GDEPTH)]
    gps = ctx.psum("gate", [128, 1024], F32)
    ups = ctx.psum("up", [128, 1024], F32)
    dps = ctx.psum("down", [128, 1024], F32)
    gsb = ctx.sbuf("gsb", [128, 1024], BF16)
    s_gld = [ctx.sem() for _ in range(8)]
    s_gcv = ctx.sem()
    s_mm = ctx.sem()
    s_svg = ctx.sem()
    s_g1 = ctx.sem()
    s_hv = ctx.sem()
    ng = [0]
    mm_at_gu = {}

    def load_gu(tt, n, slot):
        for part in range(16):
            g = ng[0]
            w = [(s_gcv, g - GDEPTH + 1)] if g >= GDEPTH else []
            src = (wg_d if part < 8 else wu_d)[
                (part % 8) * 128:(part % 8) * 128 + 128,
                n * 128:(n + 1) * 128]
            p.add("sync", lambda e, src=src, g=g: e.dma_start(
                out=gstg[g % GDEPTH][:], in_=src), waits=w,
                inc=(s_gld[g % 8], 16))
            wv2 = [(s_gld[g % 8], 16 * (g // 8 + 1))]
            gun = ng[0] // 16
            if gun >= 2:
                wv2.append((s_mm, mm_at_gu[gun - 2]))
            p.add("vector", lambda e, slot=slot, part=part, g=g:
                  e.tensor_scalar(out=gu_sb[slot][part][:],
                                  in0=gstg[g % GDEPTH][:], scalar1=QS,
                                  scalar2=None, op0=ALU.mult),
                  waits=wv2, inc=(s_gcv, 1))
            ng[0] += 1

    load_gu(0, 0, 0)
    load_gu(0, 1, 1)
    mm_end_down = [0]
    for tt in range(2):
        TT = 1024
        for n in range(24):
            gun = tt * 24 + n
            sl = gu_sb[gun % 2]
            need_cv = 16 * (gun + 1)
            for half in range(2):
                for k in range(8):
                    p.add("tensor", lambda e, k=k, half=half, tt=tt, sl=sl:
                          e.matmul(gps[:, half * 512:(half + 1) * 512],
                                   sl[k][:],
                                   xn_fm[k][:, tt * TT + half * 512:
                                            tt * TT + half * 512 + 512],
                                   start=(k == 0), stop=(k == 7)),
                          waits=[(s_gcv, need_cv), (s_svg, gun),
                                 (s_hv, max(0, 2 * gun))]
                          if (k == 0 and half == 0) else [],
                          inc=(s_mm, 1) if k == 7 else None)
                for k in range(8):
                    p.add("tensor", lambda e, k=k, half=half, tt=tt, sl=sl:
                          e.matmul(ups[:, half * 512:(half + 1) * 512],
                                   sl[8 + k][:],
                                   xn_fm[k][:, tt * TT + half * 512:
                                            tt * TT + half * 512 + 512],
                                   start=(k == 0), stop=(k == 7)),
                          inc=(s_mm, 1) if k == 7 else None)
            wv = [(s_mm, p.val(s_mm)), (s_hv, 2 * gun)]
            p.add_r("scalar", lambda e: e.activation(
                out=gsb[:], in_=gps[:], func=AF.Silu), waits=wv,
                inc=(s_svg, 1))
            wvh = [(s_svg, gun + 1)]
            if tt == 1:
                wvh.append((s_mm, mm_end_down[0]))
            p.add("vector", lambda e, n=n: e.tensor_tensor(
                out=h_fm[n][:], in0=gsb[:], in1=ups[:], op=ALU.mult),
                waits=wvh, inc=(s_hv, 2))
            mm_at_gu[gun] = p.val(s_mm)
            if gun + 2 < 48:
                nxt = gun + 2
                load_gu(nxt // 24, nxt % 24, nxt % 2)
        hv_need = p.val(s_hv)
        for tq in range(8):
            ti = tt * 8 + tq
            wv = [(s_hv, hv_need), (s_g1, ti)]
            im = 0
            for k in range(24):
                for half in range(2):
                    im += 1
                    p.add("tensor", lambda e, k=k, half=half, tq=tq:
                          e.matmul(dps[:, half * 512:(half + 1) * 512],
                                   h_fm[k][:, tq * 128:(tq + 1) * 128],
                                   wd_sb[k][:, half * 512:(half + 1) * 512],
                                   start=(k == 0), stop=(k == 23)),
                          waits=wv if im == 1 else [],
                          inc=(s_mm, 1) if (k == 23 and half == 1) else None)
            p.add_r("scalar", lambda e, ti=ti: e.activation(
                out=g1[ti][:], in_=dps[:], func=AF.Silu),
                waits=[(s_mm, p.val(s_mm))], inc=(s_g1, 1))
        mm_end_down[0] = p.val(s_mm)
    p.barrier()
    ctx.free(gsb, dps, ups, gps, gstg, gu_sb, h_fm, wstg, wd_sb)

    xs = [ctx.sbuf("xs", [128, 1024], F32) for _ in range(2)]
    ss = [ctx.sbuf("ss", [128, 1024], BF16) for _ in range(2)]
    x2t = [ctx.sbuf("x2t", [128, 1024], F32) for _ in range(2)]
    s_xl = [ctx.sem(), ctx.sem()]
    s_cb = ctx.sem()
    s_st2 = [ctx.sem(swdge=True), ctx.sem(swdge=True)]
    for i in range(NT):
        wvl = [(s_cb, 2 * (i - 2) + 2)] if i >= 2 else []
        p.add("sync", lambda e, i=i: e.dma_start(
            out=xs[i % 2][:], in_=x1_d[i * 128:(i + 1) * 128, :]),
            waits=wvl, inc=(s_xl[i % 2], 16))
        p.add("sync", lambda e, i=i: e.dma_start(
            out=ss[i % 2][:], in_=s1_d[i * 128:(i + 1) * 128, :]),
            inc=(s_xl[i % 2], 16))
        wv = [(s_xl[i % 2], 32 * (i // 2 + 1))]
        if i >= 2:
            wv.append((s_st2[i % 2], 16 * (i // 2)))
        p.add("vector", lambda e, i=i: e.tensor_tensor(
            out=x2t[i % 2][:], in0=ss[i % 2][:], in1=g1[i][:],
            op=ALU.mult), waits=wv, inc=(s_cb, 1))
        p.flush("vector")
        p.add("vector", lambda e, i=i: e.tensor_tensor(
            out=x2t[i % 2][:], in0=x2t[i % 2][:], in1=xs[i % 2][:],
            op=ALU.add), inc=(s_cb, 1))
        p.flush("vector")
        p.add("gpsimd", lambda e, i=i: e.dma_start(
            out=x2_d[i * 128:(i + 1) * 128, :], in_=x2t[i % 2][:]),
            waits=[(s_cb, 2 * i + 2)], inc=(s_st2[i % 2], 16))
    p.raw_wait("gpsimd", s_st2[0], 16 * (NT // 2))
    p.raw_wait("gpsimd", s_st2[1], 16 * (NT // 2))
    p.barrier()
    ctx.free(x2t, ss, xs, g1, xn_fm)


def build_fused_a():
    ctx = make_ctx("fuseda")
    nc, p = ctx.nc, ctx.p
    E = lambda n, sh, dt: nc.dram_tensor(n, sh, dt, kind="ExternalInput").ap()
    I = lambda n, sh, dt: nc.dram_tensor(n, sh, dt, kind="Internal").ap()
    O = lambda n, sh, dt: nc.dram_tensor(n, sh, dt,
                                         kind="ExternalOutput").ap()
    io0 = {
        "x_ext": E("x_ext", [EXT, 1024], F32),
        "wq": E("wq", [1024, 1024], F32), "wk": E("wk", [1024, 1024], F32),
        "wv": E("wv", [1024, 1024], F32), "wi": E("wi", [1024, 1024], F32),
        "wiT": E("wiT", [1024, 1024], F32),
        "bq": E("bq", [1, 1024], F32), "bk": E("bk", [1, 1024], F32),
        "bv": E("bv", [1, 1024], F32), "bi": E("bi", [1, 1024], F32),
        "wg": E("wg", [1024, 8], F32), "bg": E("bg", [1, 8], F32),
        "b_in_col": E("b_in_col", [1024, 1], F32),
        "q": I("q_s", [8, 128, OWN], BF16),
        "k": I("k_s", [8, 128, EXT], BF16),
        "v": I("v_s", [EXT, 1040], BF16),
        "t": I("t_s", [8, 128, OWN], BF16),
        "route": I("route_s", [OWN, 8], F32),
    }
    emit_l0a(ctx, io0, ext_kv=True)
    p.barrier()
    io1 = {
        "q": io0["q"], "k": io0["k"], "v": io0["v"],
        "posT": E("posT", [16, 512, 512], F32),
        "slot_w": E("slot_w", [128, 9], F32),
        "invcnt": E("invcnt", [128, OWN // 128], F32),
        "wo": E("wo", [1024, 1024], F32), "bo": E("bo", [1, 1024], F32),
        "s0": I("s0_s", [OWN, 1024], BF16),
    }
    emit_l0b(ctx, io1)
    p.barrier()
    io2 = {
        "t": io0["t"], "route": io0["route"],
        "w1": E("w1", [8, 1024, 4096], F32), "b1": E("b1", [8, 4096], F32),
        "w2": E("w2", [8, 4096, 1024], F32), "b2": E("b2", [8, 1024], F32),
        "wout": E("wout", [1024, 1024], F32), "bout": E("bout", [1, 1024],
                                                        F32),
        "s0": io1["s0"], "x": io0["x_ext"][HALO:HALO + OWN, :],
        "x1": O("x1", [OWN, 1024], F32),
    }
    emit_l0c(ctx, io2)
    p.barrier()
    io3 = {
        "x1": io2["x1"],
        "wqT": E("wqT", [1024, 1024], F32),
        "wkT": E("wkT", [1024, 1024], F32),
        "wvT": E("wvT", [1024, 1024], F32),
        "gain": E("gain", [1, 1], F32),
        "xn1": O("xn1", [OWN, 1024], BF16),
        "v1": O("v1", [8, 128, OWN], BF16),
        "gram": O("gram", [128, 512], F32),
    }
    emit_l1a(ctx, io3)
    p.finalize()
    ctx.close()
    return nc


def build_fused_b():
    ctx = make_ctx("fusedb")
    nc, p = ctx.nc, ctx.p
    E = lambda n, sh, dt: nc.dram_tensor(n, sh, dt, kind="ExternalInput").ap()
    I = lambda n, sh, dt: nc.dram_tensor(n, sh, dt, kind="Internal").ap()
    io4 = {
        "grams": E("grams", [4, 128, 512], F32),
        "v1": E("v1", [8, 128, OWN], BF16),
        "xn1": E("xn1", [OWN, 1024], BF16),
        "woT": E("woT", [1024, 1024], F32),
        "gain": E("gain", [1, 1], F32),
        "s1": I("s1_s", [OWN, 1024], BF16),
    }
    emit_l1b(ctx, io4, npart=4)
    p.barrier()
    io5 = {
        "xn1": io4["xn1"],
        "wgT": E("wgT", [1024, FFN1_H], I32),
        "wuT": E("wuT", [1024, FFN1_H], I32),
        "wdT": E("wdT", [FFN1_H, 1024], I32),
        "x1": E("x1", [OWN, 1024], F32), "s1": io4["s1"],
        "x2": nc.dram_tensor("x2", [OWN, 1024], F32,
                             kind="ExternalOutput").ap(),
    }
    emit_l1c(ctx, io5)
    p.finalize()
    ctx.close()
    return nc


def build_fused():
    ctx = make_ctx("fused")
    nc, p = ctx.nc, ctx.p
    E = lambda n, sh, dt: nc.dram_tensor(n, sh, dt, kind="ExternalInput").ap()
    I = lambda n, sh, dt: nc.dram_tensor(n, sh, dt, kind="Internal").ap()
    io0 = {
        "x_ext": E("x_ext", [EXT, 1024], F32),
        "wq": E("wq", [1024, 1024], F32), "wk": E("wk", [1024, 1024], F32),
        "wv": E("wv", [1024, 1024], F32), "wi": E("wi", [1024, 1024], F32),
        "wiT": E("wiT", [1024, 1024], F32),
        "bq": E("bq", [1, 1024], F32), "bk": E("bk", [1, 1024], F32),
        "bv": E("bv", [1, 1024], F32), "bi": E("bi", [1, 1024], F32),
        "wg": E("wg", [1024, 8], F32), "bg": E("bg", [1, 8], F32),
        "b_in_col": E("b_in_col", [1024, 1], F32),
        "q": I("q_s", [8, 128, OWN], BF16),
        "k": I("k_s", [8, 128, EXT], BF16),
        "v": I("v_s", [EXT, 1040], BF16),
        "t": I("t_s", [8, 128, OWN], BF16),
        "route": I("route_s", [OWN, 8], F32),
    }
    emit_l0a(ctx, io0, ext_kv=True)
    p.barrier()
    io1 = {
        "q": io0["q"], "k": io0["k"], "v": io0["v"],
        "posT": E("posT", [16, 512, 512], F32),
        "slot_w": E("slot_w", [128, 9], F32),
        "invcnt": E("invcnt", [128, OWN // 128], F32),
        "wo": E("wo", [1024, 1024], F32), "bo": E("bo", [1, 1024], F32),
        "s0": I("s0_s", [OWN, 1024], BF16),
    }
    emit_l0b(ctx, io1)
    p.barrier()
    io2 = {
        "t": io0["t"], "route": io0["route"],
        "w1": E("w1", [8, 1024, 4096], F32), "b1": E("b1", [8, 4096], F32),
        "w2": E("w2", [8, 4096, 1024], F32), "b2": E("b2", [8, 1024], F32),
        "wout": E("wout", [1024, 1024], F32), "bout": E("bout", [1, 1024],
                                                        F32),
        "s0": io1["s0"], "x": io0["x_ext"][HALO:HALO + OWN, :],
        "x1": I("x1_s", [OWN, 1024], F32),
    }
    emit_l0c(ctx, io2)
    p.barrier()
    gram_in = I("gram_in", [128, 512], F32)
    gram_out = I("gram_out", [128, 512], F32)
    io3 = {
        "x1": io2["x1"],
        "wqT": E("wqT", [1024, 1024], F32),
        "wkT": E("wkT", [1024, 1024], F32),
        "wvT": E("wvT", [1024, 1024], F32),
        "gain": E("gain", [1, 1], F32),
        "xn1": I("xn1_s", [OWN, 1024], BF16),
        "v1": I("v1_s", [8, 128, OWN], BF16),
        "gram": gram_in,
    }
    emit_l1a(ctx, io3)
    p.barrier()
    s_cc = ctx.sem()
    p.add("gpsimd", lambda e: e.collective_compute(
        "AllReduce", ALU.add, [[0, 1, 2, 3], [4, 5, 6, 7]],
        [gram_in], [gram_out]), inc=(s_cc, 16))
    p.raw_wait("gpsimd", s_cc, 16)
    p.barrier()
    io4 = {
        "grams": gram_out, "v1": io3["v1"], "xn1": io3["xn1"],
        "woT": E("woT", [1024, 1024], F32), "gain": io3["gain"],
        "s1": I("s1_s", [OWN, 1024], BF16),
    }
    emit_l1b(ctx, io4, npart=1)
    p.barrier()
    io5 = {
        "xn1": io3["xn1"],
        "wgT": E("wgT", [1024, FFN1_H], I32),
        "wuT": E("wuT", [1024, FFN1_H], I32),
        "wdT": E("wdT", [FFN1_H, 1024], I32),
        "x1": io2["x1"], "s1": io4["s1"],
        "x2": nc.dram_tensor("x2", [OWN, 1024], F32,
                             kind="ExternalOutput").ap(),
    }
    emit_l1c(ctx, io5)
    p.finalize()
    ctx.close()
    return nc


def _core_slices():
    return [(c // 4, (c % 4) * OWN) for c in range(NCORE)]


def _ext_idx(s0):
    return np.clip(np.arange(s0 - HALO, s0 + OWN + HALO), 0, S - 1)


def run_l0a(inp):
    x = np.asarray(inp["L0_attn_hidden_states"])
    nc = build_l0a()
    in_maps = []
    shared = {
        "wq": np.asarray(inp["L0_attn_Wq"]),
        "wk": np.asarray(inp["L0_attn_Wk"]),
        "wv": np.asarray(inp["L0_attn_Wv"]),
        "wi": np.asarray(inp["L0_ffn_W_in"]),
        "bq": np.asarray(inp["L0_attn_bq"])[None],
        "bk": np.asarray(inp["L0_attn_bk"])[None],
        "bv": np.asarray(inp["L0_attn_bv"])[None],
        "bi": np.asarray(inp["L0_ffn_b_in"])[None],
        "wg": np.asarray(inp["L0_ffn_W_gate"]),
        "bg": np.asarray(inp["L0_ffn_b_gate"])[None],
        "wiT": np.ascontiguousarray(np.asarray(inp["L0_ffn_W_in"]).T),
        "b_in_col": np.ascontiguousarray(
            np.asarray(inp["L0_ffn_b_in"]).reshape(1024, 1)),
    }
    for (b, s0) in _core_slices():
        m = dict(shared)
        m["x_ext"] = np.ascontiguousarray(x[b][_ext_idx(s0)])
        in_maps.append(m)
    return run_bass_kernel_spmd(nc, in_maps, core_ids=list(range(NCORE)))


def _slot_w(s0):
    w = np.zeros((128, 9), np.float32)
    for j in range(9):
        sg = s0 - 256 + 256 * j
        if 0 <= sg <= S - W:
            w[:, j] = 1.0
    if s0 + OWN == S:
        w[:, 7] = 2.0
        w[:, 8] = 0.0
    return w


def _invcnt(s0):
    counts = np.full(S, 2.0, np.float32)
    counts[:256] = 1.0
    counts[7680:7936] = 3.0
    counts[7936:] = 2.0
    sl = counts[s0:s0 + OWN]
    return np.ascontiguousarray(1.0 / sl.reshape(OWN // 128, 128).T)


def run_l0b(per_core, shared):
    nc = build_l0b()
    in_maps = []
    for c, m in enumerate(per_core):
        d = dict(shared)
        d.update(m)
        in_maps.append(d)
    return run_bass_kernel_spmd(nc, in_maps, core_ids=list(range(NCORE)))


def kernel(**inp):
    inp = {k: np.asarray(v) for k, v in inp.items()}
    x = inp["L0_attn_hidden_states"]
    cores = _core_slices()
    R = list(range(NCORE))
    nc_a = build_fused_a()
    nc_b = build_fused_b()
    gainv = np.asarray(inp["L1_attn_gain"], np.float32).reshape(1, 1)
    shared = {
        "wq": inp["L0_attn_Wq"], "wk": inp["L0_attn_Wk"],
        "wv": inp["L0_attn_Wv"], "wi": inp["L0_ffn_W_in"],
        "bq": inp["L0_attn_bq"][None], "bk": inp["L0_attn_bk"][None],
        "bv": inp["L0_attn_bv"][None], "bi": inp["L0_ffn_b_in"][None],
        "wg": inp["L0_ffn_W_gate"], "bg": inp["L0_ffn_b_gate"][None],
        "wiT": np.ascontiguousarray(inp["L0_ffn_W_in"].T),
        "b_in_col": np.ascontiguousarray(
            inp["L0_ffn_b_in"].reshape(1024, 1)),
        "posT": np.ascontiguousarray(
            inp["L0_attn_pos_bias"].transpose(0, 2, 1)),
        "wo": inp["L0_attn_Wo"], "bo": inp["L0_attn_bo"][None],
        "w1": inp["L0_ffn_W1"], "b1": inp["L0_ffn_b1"],
        "w2": inp["L0_ffn_W2"], "b2": inp["L0_ffn_b2"],
        "wout": inp["L0_ffn_W_out"], "bout": inp["L0_ffn_b_out"][None],
        "wqT": np.ascontiguousarray(inp["L1_attn_w_q"].T),
        "wkT": np.ascontiguousarray(inp["L1_attn_w_k"].T),
        "wvT": np.ascontiguousarray(inp["L1_attn_w_v"].T),
        "woT": np.ascontiguousarray(inp["L1_attn_w_o"].T),
        "gain": gainv,
        "wgT": np.ascontiguousarray(inp["L1_ffn_w_gate"].T),
        "wuT": np.ascontiguousarray(inp["L1_ffn_w_up"].T),
        "wdT": np.ascontiguousarray(inp["L1_ffn_w_down"].T),
    }
    akeys = {"wq", "wk", "wv", "wi", "wiT", "bq", "bk", "bv", "bi", "wg",
             "bg", "b_in_col", "posT", "wo", "bo", "w1", "b1", "w2", "b2",
             "wout", "bout", "wqT", "wkT", "wvT", "gain"}
    shared_a = {k_: v_ for k_, v_ in shared.items() if k_ in akeys}
    maps = []
    for (b, s0) in cores:
        m = dict(shared_a)
        m["x_ext"] = np.ascontiguousarray(x[b][_ext_idx(s0)])
        m["slot_w"] = _slot_w(s0)
        m["invcnt"] = _invcnt(s0)
        maps.append(m)
    ra = run_bass_kernel_spmd(nc_a, maps, core_ids=R).results
    bkeys = {"woT", "gain", "wgT", "wuT", "wdT"}
    shared_b = {k_: v_ for k_, v_ in shared.items() if k_ in bkeys}
    maps = []
    for c, (b, s0) in enumerate(cores):
        m = dict(shared_b)
        m["grams"] = np.stack([ra[b * 4 + j]["gram"] for j in range(4)])
        m["v1"] = ra[c]["v1"]
        m["xn1"] = ra[c]["xn1"]
        m["x1"] = ra[c]["x1"]
        maps.append(m)
    rb = run_bass_kernel_spmd(nc_b, maps, core_ids=R).results
    out = np.zeros((B, S, 1024), np.float32)
    for c, (b, s0) in enumerate(cores):
        out[b, s0:s0 + OWN] = rb[c]["x2"]
    return out


def kernel_unfused(**inp):
    inp = {k: np.asarray(v) for k, v in inp.items()}
    x = inp["L0_attn_hidden_states"]
    cores = _core_slices()
    R = list(range(NCORE))

    nc = build_l0a()
    shared = {
        "wq": inp["L0_attn_Wq"], "wk": inp["L0_attn_Wk"],
        "wv": inp["L0_attn_Wv"], "wi": inp["L0_ffn_W_in"],
        "bq": inp["L0_attn_bq"][None], "bk": inp["L0_attn_bk"][None],
        "bv": inp["L0_attn_bv"][None], "bi": inp["L0_ffn_b_in"][None],
        "wg": inp["L0_ffn_W_gate"], "bg": inp["L0_ffn_b_gate"][None],
        "wiT": np.ascontiguousarray(inp["L0_ffn_W_in"].T),
        "b_in_col": np.ascontiguousarray(
            inp["L0_ffn_b_in"].reshape(1024, 1)),
    }
    maps = []
    for (b, s0) in cores:
        m = dict(shared)
        m["x_ext"] = np.ascontiguousarray(x[b][_ext_idx(s0)])
        maps.append(m)
    r = run_bass_kernel_spmd(nc, maps, core_ids=R).results
    q_c = [r[c]["q"] for c in range(NCORE)]
    t_c = [r[c]["t"] for c in range(NCORE)]
    route_c = [r[c]["route"] for c in range(NCORE)]
    k_full = np.zeros((B, 8, 128, S), BF)
    v_full = np.zeros((B, S, 1040), BF)
    for c, (b, s0) in enumerate(cores):
        k_full[b][:, :, s0:s0 + OWN] = r[c]["k"]
        v_full[b][s0:s0 + OWN] = r[c]["v"]

    nc = build_l0b()
    shared = {"posT": np.ascontiguousarray(
        inp["L0_attn_pos_bias"].transpose(0, 2, 1)),
        "wo": inp["L0_attn_Wo"], "bo": inp["L0_attn_bo"][None]}
    maps = []
    for c, (b, s0) in enumerate(cores):
        idx = _ext_idx(s0)
        m = dict(shared)
        m["q"] = q_c[c]
        m["k"] = np.ascontiguousarray(k_full[b][:, :, idx])
        m["v"] = np.ascontiguousarray(v_full[b][idx])
        m["slot_w"] = _slot_w(s0)
        m["invcnt"] = _invcnt(s0)
        maps.append(m)
    r = run_bass_kernel_spmd(nc, maps, core_ids=R).results
    s0_c = [r[c]["s0"] for c in range(NCORE)]

    nc = build_l0c()
    shared = {"w1": inp["L0_ffn_W1"], "b1": inp["L0_ffn_b1"],
              "w2": inp["L0_ffn_W2"], "b2": inp["L0_ffn_b2"],
              "wout": inp["L0_ffn_W_out"], "bout": inp["L0_ffn_b_out"][None]}
    maps = []
    for c, (b, s0) in enumerate(cores):
        m = dict(shared)
        m["t"] = t_c[c]
        m["route"] = route_c[c]
        m["s0"] = s0_c[c]
        m["x"] = np.ascontiguousarray(x[b][s0:s0 + OWN])
        maps.append(m)
    r = run_bass_kernel_spmd(nc, maps, core_ids=R).results
    x1_c = [r[c]["x1"] for c in range(NCORE)]

    nc = build_l1a()
    gainv = np.asarray(inp["L1_attn_gain"], np.float32).reshape(1, 1)
    shared = {"wqT": np.ascontiguousarray(inp["L1_attn_w_q"].T),
              "wkT": np.ascontiguousarray(inp["L1_attn_w_k"].T),
              "wvT": np.ascontiguousarray(inp["L1_attn_w_v"].T),
              "gain": gainv}
    maps = []
    for c, (b, s0) in enumerate(cores):
        m = dict(shared)
        m["x1"] = x1_c[c]
        maps.append(m)
    r = run_bass_kernel_spmd(nc, maps, core_ids=R).results
    xn1_c = [r[c]["xn1"] for c in range(NCORE)]
    v1_c = [r[c]["v1"] for c in range(NCORE)]
    gram_c = [r[c]["gram"] for c in range(NCORE)]

    nc = build_l1b()
    shared = {"woT": np.ascontiguousarray(inp["L1_attn_w_o"].T),
              "gain": gainv}
    maps = []
    for c, (b, s0) in enumerate(cores):
        m = dict(shared)
        m["grams"] = np.stack([gram_c[b * 4 + j] for j in range(4)])
        m["v1"] = v1_c[c]
        m["xn1"] = xn1_c[c]
        maps.append(m)
    r = run_bass_kernel_spmd(nc, maps, core_ids=R).results
    s1_c = [r[c]["s1"] for c in range(NCORE)]

    nc = build_l1c()
    shared = {"wgT": np.ascontiguousarray(inp["L1_ffn_w_gate"].T),
              "wuT": np.ascontiguousarray(inp["L1_ffn_w_up"].T),
              "wdT": np.ascontiguousarray(inp["L1_ffn_w_down"].T)}
    maps = []
    for c, (b, s0) in enumerate(cores):
        m = dict(shared)
        m["xn1"] = xn1_c[c]
        m["x1"] = x1_c[c]
        m["s1"] = s1_c[c]
        maps.append(m)
    r = run_bass_kernel_spmd(nc, maps, core_ids=R).results

    out = np.zeros((B, S, 1024), np.float32)
    for c, (b, s0) in enumerate(cores):
        out[b, s0:s0 + OWN] = r[c]["x2"]
    return out


if __name__ == "__main__":
    pass


# version 07
# speedup vs baseline: 1.1562x; 1.1562x over previous
import math
import numpy as np
import ml_dtypes
import concourse.bass as bass
import concourse.mybir as mybir
from concourse.bass_utils import run_bass_kernel_spmd

F32 = mybir.dt.float32
BF16 = mybir.dt.bfloat16
I32 = mybir.dt.int32
AF = mybir.ActivationFunctionType
ALU = mybir.AluOpType
AX = mybir.AxisListType
BF = ml_dtypes.bfloat16

B, S, D = 2, 8192, 1024
OWN = 2048
HALO = 256
EXT = OWN + 2 * HALO
NCORE = 8
H, W, STRIDE = 16, 512, 256
HD = 64
N_EXP, D_MOE = 8, 4096
FFN1_H = 3072
EPS_LN = 1e-5
EPS_MP = 1e-4
TRUNC_L0B = False


class SemW:

    __slots__ = ("h", "base")

    def __init__(self, h, base):
        self.h = h
        self.base = base


class Pl:
    ENGS = ("sync", "vector", "scalar", "tensor", "gpsimd")

    def __init__(self, nc):
        self.nc = nc
        self.recs = {e: [] for e in self.ENGS}
        self.cnt = {}
        self.bar_sem = None
        self.rsync = None
        self.junk_ap = None

    def _h(self, s):
        return (s.h, s.base) if isinstance(s, SemW) else (s, 0)

    def val(self, s):
        h, base = self._h(s)
        return self.cnt.get(id(h), 0) - base

    def bump(self, s, n):
        h, _ = self._h(s)
        self.cnt[id(h)] = self.cnt.get(id(h), 0) + n
        return self.cnt[id(h)]

    def _tw(self, waits):
        out = []
        for sm, v in waits:
            h, base = self._h(sm)
            out.append((h, base + v))
        return out

    def _ti(self, inc):
        if inc is None:
            return None
        h, _ = self._h(inc[0])
        return (h, inc[1])

    def add(self, eng, build, waits=(), inc=None):
        if inc is not None:
            self.bump(inc[0], inc[1])
        self.recs[eng].append({"build": build, "waits": self._tw(waits),
                               "inc": self._ti(inc)})

    def raw_wait(self, eng, sm, v):
        self.recs[eng].append({"build": None, "waits": self._tw([(sm, v)]),
                               "inc": None})

    def add_r(self, eng, build, waits=(), inc=None):
        rs = self.rsync[eng]
        self.add(eng, build, waits=waits, inc=(rs, 1))
        self.raw_wait(eng, rs, self.val(rs))
        if inc is not None:
            self.bump(inc[0], inc[1])
            self.recs[eng].append({"build": None, "waits": [],
                                   "inc": None,
                                   "postinc": self._ti(inc)})

    def flush(self, eng):
        rs = self.rsync[eng]
        recs = self.recs[eng]
        prev = None
        for r in reversed(recs):
            if r["build"] is not None:
                prev = r
                break
        if prev is None:
            return
        if prev["inc"] is None:
            prev["inc"] = (rs, 1)
            self.bump(rs, 1)
            self.raw_wait(eng, rs, self.val(rs))
        else:
            self.raw_wait(eng, prev["inc"][0], self.val(prev["inc"][0]))

    def add_red(self, eng, build, waits=(), inc=None):
        self.flush(eng)
        self.add_r(eng, build, waits=waits, inc=inc)

    def barrier(self):
        b = self.bar_sem
        ja = self.junk_ap
        for _ in range(5):
            self.bump(b, 1)
        tgt = self.val(b)
        self.recs["vector"].append({
            "build": lambda eo: eo.memset(ja[:, 0:1], 0.0),
            "waits": [], "inc": (b, 1), "postwait": (b, tgt)})
        self.recs["scalar"].append({
            "build": lambda eo: eo.memzero(ja[:, 1:2]),
            "waits": [], "inc": (b, 1), "postwait": (b, tgt)})
        self.recs["tensor"].append({
            "build": lambda eo: eo.nop(),
            "waits": [], "inc": (b, 1), "postwait": (b, tgt)})
        for e in ("sync", "gpsimd"):
            self.recs[e].append({
                "build": None, "waits": [], "inc": None,
                "postinc": (b, 1), "postwait": (b, tgt)})

    def _run(self, eng, eo):
        for r in self.recs[eng]:
            for sm, v in r["waits"]:
                eo.wait_ge(sm, v)
            if r["build"] is not None:
                ins = r["build"](eo)
                if r["inc"] is not None:
                    ins.then_inc(r["inc"][0], r["inc"][1])
            if "postinc" in r:
                eo.sem_inc(r["postinc"][0], r["postinc"][1])
            if "postwait" in r:
                eo.wait_ge(r["postwait"][0], r["postwait"][1])

    def finalize(self):
        with self.nc.Block() as block:
            @block.sync
            def _(eng):
                self._run("sync", eng)

            @block.vector
            def _(eng):
                self._run("vector", eng)

            @block.scalar
            def _(eng):
                self._run("scalar", eng)

            @block.tensor
            def _(eng):
                self._run("tensor", eng)

            @block.gpsimd
            def _(eng):
                self._run("gpsimd", eng)


def seg(ap2d, e):
    return ap2d.rearrange("p (s e) -> p s e", e=e)


def bcast(ap2d, e):
    return bass.AP(tensor=ap2d.tensor, offset=ap2d.offset,
                   ap=[list(ap2d.ap[0]), list(ap2d.ap[1]), [0, e]])


class Ctx:

    def __init__(self, nc):
        self.nc = nc
        self.p = Pl(nc)
        self.exits = []
        self.sems = {}
        self.uid = 0
        base = (nc.sbuf_base + 63) & ~63
        self.freelist = [(base, nc.sbuf_top - base)]
        self.sizes = {}
        self.hiwater = 0

    def _name(self, base):
        self.uid += 1
        return f"{base}_{self.uid}"

    SEM_POOL = 78

    def sem(self, name=None, swdge=False):
        if swdge:
            cm = self.nc.semaphore(self._name("swdge"))
            h = cm.__enter__()
            self.exits.append((None, cm))
            return h
        if name is None:
            self.anon_cnt = getattr(self, "anon_cnt", 0)
            self.anon_pool = getattr(self, "anon_pool", [])
            if self.anon_cnt >= self.SEM_POOL:
                h = self.anon_pool[self.anon_cnt % self.SEM_POOL]
                self.anon_cnt += 1
                return SemW(h, self.p.val(h))
            name = self._name("sem")
            cm = self.nc.semaphore(self._name(name))
            h = cm.__enter__()
            self.exits.append((None, cm))
            self.anon_pool.append(h)
            self.anon_cnt += 1
            return h
        if name not in self.sems:
            cm = self.nc.semaphore(self._name(name))
            self.sems[name] = cm.__enter__()
            self.exits.append((None, cm))
        return self.sems[name]

    def sbuf(self, name, shape, dtype):
        nb = int(np.prod(shape[1:])) * mybir.dt.size(dtype)
        nb = (nb + 63) & ~63
        for i, (off, sz) in enumerate(self.freelist):
            if sz >= nb:
                if sz == nb:
                    self.freelist.pop(i)
                else:
                    self.freelist[i] = (off + nb, sz - nb)
                t = self.nc.alloc_sbuf_tensor_at(
                    self._name(name), shape, dtype, offset=off)
                self.sizes[id(t)] = (off, nb)
                used = self.nc.sbuf_top - sum(s for _, s in self.freelist) \
                    - ((self.nc.sbuf_base + 63) & ~63)
                self.hiwater = max(self.hiwater, used)
                return t
        raise MemoryError(
            f"sbuf arena out of space for {name} {shape} ({nb}B); "
            f"free={[s for _, s in self.freelist]}")

    def psum(self, name, shape, dtype=F32):
        cm = self.nc.psum_tensor(self._name(name), shape, dtype)
        t = cm.__enter__()
        self.exits.append((id(t), cm))
        return t

    def ccol(self, val):
        val = float(val)
        if val not in self.cmap:
            assert len(self.cvals) < self.cpool.shape[1]
            self.cmap[val] = len(self.cvals)
            self.cvals.append(val)
        return self.cpool[:, self.cmap[val]:self.cmap[val] + 1]

    def free(self, *tensors):
        for t in tensors:
            if isinstance(t, (list, tuple)):
                self.free(*reversed(t))
                continue
            key = id(t)
            if key in self.sizes:
                off, nb = self.sizes.pop(key)
                self.freelist.append((off, nb))
                self.freelist.sort()
                merged = []
                for o, s in self.freelist:
                    if merged and merged[-1][0] + merged[-1][1] == o:
                        merged[-1] = (merged[-1][0], merged[-1][1] + s)
                    else:
                        merged.append((o, s))
                self.freelist = [tuple(x) for x in merged]
                continue
            for i, (k, cm) in enumerate(self.exits):
                if k == key:
                    cm.__exit__(None, None, None)
                    self.exits.pop(i)
                    break
            else:
                raise KeyError("free: tensor not found")

    def close(self):
        for _, cm in reversed(self.exits):
            cm.__exit__(None, None, None)
        self.exits = []


def make_ctx(name):
    nc = bass.Bass("TRN2", target_bir_lowering=False)
    ctx = Ctx(nc)
    ctx.p.bar_sem = ctx.sem("bar")
    ctx.p.rsync = {e: ctx.sem(f"rsync_{e}")
                   for e in ("sync", "vector", "scalar", "tensor",
                             "gpsimd")}
    ident = nc.inline_tensor(np.eye(128, dtype=BF), name=ctx._name("identbf")).ap()
    ident32 = nc.inline_tensor(np.eye(128, dtype=np.float32),
                               name=ctx._name("identf32")).ap()
    ones = nc.inline_tensor(np.ones((1, 512), dtype=BF), name=ctx._name("onesbf")).ap()
    ctx.ident_sb = ctx.sbuf("ident_sb", [128, 128], BF16)
    ctx.ident_f32 = ctx.sbuf("ident_f32", [128, 128], F32)
    ctx.ones_sb = ctx.sbuf("ones_sb", [1, 512], BF16)
    ld = ctx.sem("ldconst")
    ctx.p.add("sync", lambda e: e.dma_start(out=ctx.ident_sb[:], in_=ident),
              inc=(ld, 16))
    ctx.p.add("sync", lambda e: e.dma_start(out=ctx.ident_f32[:], in_=ident32),
              inc=(ld, 16))
    ctx.p.add("sync", lambda e: e.dma_start(out=ctx.ones_sb[:], in_=ones),
              inc=(ld, 16))
    ctx.cvals, ctx.cmap = [], {}
    ctx.cpool = ctx.sbuf("cpool", [128, 32], F32)
    ctx.junk = ctx.sbuf("junk", [128, 4], F32)
    ctx.p.junk_ap = ctx.junk

    def _ld_cpool(e):
        vals = ctx.cvals or [0.0]
        arr = np.tile(np.asarray(vals, np.float32), (128, 1))
        return e.dma_start(out=ctx.cpool[:, 0:len(vals)],
                           in_=nc.inline_tensor(arr, name=ctx._name("cpool")).ap())
    ctx.p.add("sync", _ld_cpool, inc=(ld, 16))
    ctx.p.raw_wait("tensor", ld, 64)
    ctx.p.raw_wait("vector", ld, 64)
    ctx.p.raw_wait("scalar", ld, 64)
    ctx.p.barrier()
    return ctx


def load_weight_bf16(ctx, name, dram_ap, kin, nout, scale=None):
    p = ctx.p
    nk = (kin + 127) // 128
    ND = 4
    stage = [ctx.sbuf(f"wst_{name}", [128, nout], F32) for _ in range(ND)]
    out = [ctx.sbuf(f"w_{name}{k}", [128, nout], BF16) for k in range(nk)]
    s_lds = [ctx.sem() for _ in range(ND)]
    s_cv = ctx.sem()
    for k in range(nk):
        buf = stage[k % ND]
        s_ld = s_lds[k % ND]
        w = [(s_cv, k - ND + 1)] if k >= ND else []
        p.add("sync", lambda e, buf=buf, k=k: e.dma_start(
            out=buf[:], in_=dram_ap[k * 128:(k + 1) * 128, :]),
            waits=w, inc=(s_ld, 16))
        wv = [(s_ld, 16 * (k // ND + 1))]
        if scale is None:
            p.add("vector", lambda e, k=k, buf=buf: e.tensor_copy(
                out=out[k][:], in_=buf[:]), waits=wv, inc=(s_cv, 1))
        else:
            p.add("vector", lambda e, k=k, buf=buf: e.tensor_scalar(
                out=out[k][:], in0=buf[:], scalar1=scale, scalar2=None,
                op0=ALU.mult), waits=wv, inc=(s_cv, 1))
    p.raw_wait("tensor", s_cv, nk)
    p.raw_wait("scalar", s_cv, nk)
    p.raw_wait("sync", s_cv, nk)
    p.raw_wait("gpsimd", s_cv, nk)
    ctx.free(stage)
    return out


def load_bias_bf16(ctx, name, dram_ap, n, scale=None):
    p = ctx.p
    stage = ctx.sbuf(f"bst_{name}", [1, n], F32)
    out = ctx.sbuf(f"b_{name}", [1, n], BF16)
    s_ld = ctx.sem()
    s_cv = ctx.sem()
    p.add("sync", lambda e: e.dma_start(out=stage[:], in_=dram_ap),
          inc=(s_ld, 16))
    if scale is None:
        p.add("vector", lambda e: e.tensor_copy(out=out[:], in_=stage[:]),
              waits=[(s_ld, 16)], inc=(s_cv, 1))
    else:
        p.add("vector", lambda e: e.tensor_scalar(
            out=out[:], in0=stage[:], scalar1=scale, scalar2=None,
            op0=ALU.mult), waits=[(s_ld, 16)], inc=(s_cv, 1))
    p.raw_wait("tensor", s_cv, 1)
    p.raw_wait("sync", s_cv, 1)
    p.raw_wait("gpsimd", s_cv, 1)
    ctx.free(stage)
    return out


def emit_ln_to_fm(ctx, x_dram, nrows, tag, keep_tm=False, keep_stats=False):
    p = ctx.p
    nt = nrows // 128
    xt = [ctx.sbuf(f"lnx_{tag}", [128, 1024], F32) for _ in range(nt)]
    fm = [ctx.sbuf(f"xnfm_{tag}{c}", [128, nrows], BF16) for c in range(8)]
    nmr = ctx.sbuf(f"lnnmr_{tag}", [128, nt], F32)
    nm = ctx.sbuf(f"lnnm_{tag}", [128, nt], F32)
    ssq = ctx.sbuf(f"lnssq_{tag}", [128, nt], F32)
    rstd_a = ctx.sbuf(f"lnrstda_{tag}", [128, nt], F32)
    sq = [ctx.sbuf(f"lnsq_{tag}", [128, 1024], F32) for _ in range(2)]
    rstd = ctx.sbuf(f"lnrstd_{tag}", [128, nt], F32)
    if keep_tm:
        tmb = [ctx.sbuf(f"lntm_{tag}", [128, 1024], BF16) for _ in range(nt)]
    else:
        tmb = [ctx.sbuf(f"lntm_{tag}", [128, 1024], BF16) for _ in range(2)]
    tp = [ctx.psum(f"lntp_{tag}", [128, 128], BF16) for _ in range(2)]
    s_ld = ctx.sem()
    s_st = ctx.sem()
    s_sq = ctx.sem()
    s_no = ctx.sem()
    s_tp = ctx.sem()
    s_ev = ctx.sem()
    for i in range(nt):
        p.add("sync", lambda e, i=i: e.dma_start(
            out=xt[i][:], in_=x_dram[i * 128:(i + 1) * 128, :]),
            inc=(s_ld, 16))
    for i in range(nt):
        last = (i == nt - 1)
        fn = p.add_r if last else p.add
        fn("vector", lambda e, i=i: e.reduce_sum(
            out=nmr[:, i:i + 1], in_=xt[i][:], axis=AX.X),
            waits=[(s_ld, 16 * nt)])
    p.add("vector", lambda e: e.tensor_scalar(
        out=nm[:], in0=nmr[:], scalar1=-1.0 / 1024.0, scalar2=None,
        op0=ALU.mult), inc=(s_st, 1))
    for i in range(nt):
        p.add_r("scalar", lambda e, i=i: e.activation(
            out=sq[i % 2][:], in_=xt[i][:], func=AF.Square,
            bias=nm[:, i:i + 1], scale=1.0, accum_out=ssq[:, i:i + 1]),
            waits=[(s_st, 1)], inc=(s_sq, 1))
    c_eps, c_inv, c_half = ctx.ccol(EPS_LN), ctx.ccol(1.0 / 1024.0), \
        ctx.ccol(-0.5)
    p.add_r("scalar", lambda e: e.activation(
        out=rstd_a[:], in_=ssq[:], func=AF.Ln, bias=c_eps, scale=c_inv))
    p.add("scalar", lambda e: e.activation(
        out=rstd[:], in_=rstd_a[:], func=AF.Exp, scale=c_half),
        inc=(s_sq, 1))
    p.raw_wait("vector", s_sq, nt + 1)
    for i in range(nt):
        t = tmb[i] if keep_tm else tmb[i % 2]
        wv = []
        if not keep_tm and i >= 2:
            wv.append((s_tp, 8 * (i - 1)))
        p.add("vector", lambda e, i=i, t=t: e.tensor_scalar(
            out=t[:], in0=xt[i][:], scalar1=nm[:, i:i + 1],
            scalar2=rstd[:, i:i + 1], op0=ALU.add, op1=ALU.mult),
            waits=wv, inc=(s_no, 1))
        for c in range(8):
            j = i * 8 + c
            wv = [(s_no, i + 1)]
            if j >= 2:
                wv.append((s_ev, j - 1))
            p.add("tensor", lambda e, c=c, t=t, j=j: e.transpose(
                tp[j % 2][:], t[:, c * 128:(c + 1) * 128], ctx.ident_sb[:]),
                waits=wv, inc=(s_tp, 1))
            p.add("vector", lambda e, i=i, c=c, j=j: e.tensor_copy(
                out=fm[c][:, i * 128:(i + 1) * 128], in_=tp[j % 2][:]),
                waits=[(s_tp, j + 1)], inc=(s_ev, 1))
    p.barrier()
    if keep_stats:
        ctx.ln_stats = (xt, nm, rstd)
        ctx.ln_stats_free = lambda: ctx.free(xt, nmr, nm, ssq, rstd_a, sq,
                                             rstd)
        ctx.free(tp)
    else:
        ctx.free(xt, nmr, nm, ssq, rstd_a, sq, rstd, tp)
    if keep_tm:
        return fm, tmb
    ctx.free(tmb)
    return fm, None


def emit_gemm_fm_fm(ctx, xfm, T, wsb, nw, bias_row, out, tag, tsub=512,
                    evac=None):
    p = ctx.p
    ps = [ctx.psum(f"gm_{tag}", [128, tsub], F32) for _ in range(2)]
    s_mm = ctx.sem()
    s_ev = ctx.sem()
    ntt = (T + tsub - 1) // tsub
    it = 0
    for n in range(nw // 128):
        for tt in range(ntt):
            t0, t1 = tt * tsub, min((tt + 1) * tsub, T)
            pb = ps[it % 2]
            wv = [(s_ev, it - 1)] if it >= 2 else []
            for k in range(8):
                last = (k == 7) and bias_row is None
                p.add("tensor", lambda e, k=k, n=n, t0=t0, t1=t1, pb=pb,
                      last=last: e.matmul(pb[:, 0:t1 - t0],
                                          wsb[k][:, n * 128:(n + 1) * 128],
                                          xfm[k][:, t0:t1], start=(k == 0),
                                          stop=last),
                      waits=wv if k == 0 else [],
                      inc=(s_mm, 1) if last else None)
            if bias_row is not None:
                p.add("tensor", lambda e, n=n, t0=t0, t1=t1, pb=pb: e.matmul(
                    pb[:, 0:t1 - t0], bias_row[:, n * 128:(n + 1) * 128],
                    ctx.ones_sb[:, 0:t1 - t0], start=False, stop=True),
                    inc=(s_mm, 1))
            it += 1
            if evac is None:
                p.add("vector", lambda e, n=n, t0=t0, t1=t1, pb=pb, v=it:
                      e.tensor_copy(out=out[n][:, t0:t1],
                                    in_=pb[:, 0:t1 - t0]),
                      waits=[(s_mm, it)], inc=(s_ev, 1))
            else:
                p.add("vector", lambda e, n=n, t0=t0, t1=t1, pb=pb: evac(
                    e, pb[:, 0:t1 - t0], (n, t0, t1)),
                    waits=[(s_mm, it)], inc=(s_ev, 1))
    p.barrier()
    ctx.free(ps)


def emit_gemm_fm_tm(ctx, xfm, T, wsb, nw, bias_row, tag, evac, nsub=512,
                    evac_eng="vector"):
    p = ctx.p
    ps = [ctx.psum(f"gt_{tag}", [128, nsub], F32) for _ in range(2)]
    s_mm = ctx.sem()
    s_ev = ctx.sem()
    nkc = len(xfm)
    it = 0
    for tt in range(T // 128):
        for n in range((nw + nsub - 1) // nsub):
            n0, n1 = n * nsub, min((n + 1) * nsub, nw)
            pb = ps[it % 2]
            wv = [(s_ev, it - 1)] if it >= 2 else []
            for k in range(nkc):
                last = (k == nkc - 1) and bias_row is None
                p.add("tensor", lambda e, k=k, tt=tt, n0=n0, n1=n1, pb=pb,
                      last=last: e.matmul(
                          pb[:, 0:n1 - n0], xfm[k][:, tt * 128:(tt + 1) * 128],
                          wsb[k][:, n0:n1], start=(k == 0), stop=last),
                      waits=wv if k == 0 else [],
                      inc=(s_mm, 1) if last else None)
            if bias_row is not None:
                p.add("tensor", lambda e, tt=tt, n0=n0, n1=n1, pb=pb: e.matmul(
                    pb[:, 0:n1 - n0], ctx.ones_sb[0:1, 0:128],
                    bias_row[:, n0:n1], start=False, stop=True),
                    inc=(s_mm, 1))
            it += 1
            p.add(evac_eng, lambda e, tt=tt, n0=n0, n1=n1, pb=pb: evac(
                e, pb[:, 0:n1 - n0], (tt, n0, n1)),
                waits=[(s_mm, it)], inc=(s_ev, 1))
    p.barrier()
    ctx.free(ps)


def emit_tm_to_fm(ctx, tm_tiles, T, tag, out_dtype=BF16, scale_cols=None,
                  in_ready=None):
    p = ctx.p
    nt = T // 128
    fm = [ctx.sbuf(f"fm_{tag}{c}", [128, T], out_dtype) for c in range(8)]
    tp = [ctx.psum(f"tp_{tag}", [128, 128], out_dtype) for _ in range(2)]
    s_sc = ctx.sem()
    s_tp = ctx.sem()
    s_ev = ctx.sem()
    for i in range(nt):
        if scale_cols is not None:
            p.add("vector", lambda e, i=i: e.tensor_scalar(
                out=tm_tiles[i][:], in0=tm_tiles[i][:],
                scalar1=scale_cols[:, i:i + 1], scalar2=None, op0=ALU.mult),
                waits=in_ready or [], inc=(s_sc, 1))
        for c in range(8):
            j = i * 8 + c
            wv = []
            if scale_cols is not None:
                wv.append((s_sc, i + 1))
            elif in_ready and j == 0:
                wv.extend(in_ready)
            if j >= 2:
                wv.append((s_ev, j - 1))
            p.add("tensor", lambda e, i=i, c=c, j=j: e.transpose(
                tp[j % 2][:], tm_tiles[i][:, c * 128:(c + 1) * 128],
                ctx.ident_sb[:]), waits=wv, inc=(s_tp, 1))
            p.add("vector", lambda e, i=i, c=c, j=j: e.tensor_copy(
                out=fm[c][:, i * 128:(i + 1) * 128], in_=tp[j % 2][:]),
                waits=[(s_tp, j + 1)], inc=(s_ev, 1))
    p.barrier()
    ctx.free(tp)
    return fm


def slot_qrange(j):
    r0 = max(0, 256 * (j - 1))
    r1 = min(OWN, 256 * (j + 1))
    return r0, r1


def build_l0b():
    ctx = make_ctx("l0b")
    nc = ctx.nc
    io = {}
    io["q"] = nc.dram_tensor("q", [8, 128, OWN], BF16,
                             kind="ExternalInput").ap()
    io["k"] = nc.dram_tensor("k", [8, 128, EXT], BF16,
                             kind="ExternalInput").ap()
    io["v"] = nc.dram_tensor("v", [EXT, 1040], BF16,
                             kind="ExternalInput").ap()
    io["posT"] = nc.dram_tensor("posT", [16, 512, 512], F32,
                                kind="ExternalInput").ap()
    io["slot_w"] = nc.dram_tensor("slot_w", [128, 9], F32,
                                  kind="ExternalInput").ap()
    io["invcnt"] = nc.dram_tensor("invcnt", [128, OWN // 128], F32,
                                  kind="ExternalInput").ap()
    io["wo"] = nc.dram_tensor("wo", [1024, 1024], F32,
                              kind="ExternalInput").ap()
    io["bo"] = nc.dram_tensor("bo", [1, 1024], F32,
                              kind="ExternalInput").ap()
    io["s0"] = nc.dram_tensor("s0", [OWN, 1024], BF16,
                              kind="ExternalOutput").ap()
    emit_l0b(ctx, io)
    ctx.p.finalize()
    ctx.close()
    return nc


def emit_l0b(ctx, io):
    nc, p = ctx.nc, ctx.p
    q_d, k_d, v_d, pos_d = io["q"], io["k"], io["v"], io["posT"]
    sw_d, ic_d, wo_d, bo_d = io["slot_w"], io["invcnt"], io["wo"], io["bo"]
    s_o = io["s0"]

    qfm = [ctx.sbuf(f"qfm{c}", [128, OWN], BF16) for c in range(8)]
    kfm = [ctx.sbuf(f"kfm{c}", [128, EXT], BF16) for c in range(8)]
    vtm = [ctx.sbuf("vtm", [128, 1040], BF16) for _ in range(EXT // 128)]
    ctxa = [ctx.sbuf("ctxa", [128, 1024], BF16) for _ in range(OWN // 128)]
    sw = ctx.sbuf("sw", [128, 9], F32)
    ic = ctx.sbuf("ic", [128, OWN // 128], F32)
    s_ld = ctx.sem()
    nld = 0
    for c in range(8):
        p.add("sync", lambda e, c=c: e.dma_start(out=qfm[c][:], in_=q_d[c]),
              inc=(s_ld, 16)); nld += 1
        p.add("sync", lambda e, c=c: e.dma_start(out=kfm[c][:], in_=k_d[c]),
              inc=(s_ld, 16)); nld += 1
    for i in range(EXT // 128):
        p.add("sync", lambda e, i=i: e.dma_start(
            out=vtm[i][:], in_=v_d[i * 128:(i + 1) * 128, :]),
            inc=(s_ld, 16)); nld += 1
    p.add("sync", lambda e: e.dma_start(out=sw[:], in_=sw_d),
          inc=(s_ld, 16)); nld += 1
    p.add("sync", lambda e: e.dma_start(out=ic[:], in_=ic_d),
          inc=(s_ld, 16)); nld += 1
    p.raw_wait("tensor", s_ld, 16 * nld)
    p.raw_wait("vector", s_ld, 16 * nld)
    p.raw_wait("scalar", s_ld, 16 * nld)
    p.barrier()

    pstg = [ctx.sbuf("pstg", [128, 512], F32) for _ in range(2)]
    psb = [[ctx.sbuf("psb", [128, 512], BF16) for _ in range(4)]
           for _ in range(2)]
    sc = [ctx.psum("sc", [128, 512], F32) for _ in range(2)]
    owp = [ctx.psum("owp", [128, 65], F32) for _ in range(4)]
    exp_sb = [[ctx.sbuf("exp", [128, 512], BF16) for _ in range(4)]
              for _ in range(2)]
    rcol = ctx.sbuf("rcol", [128, 2], F32)
    s_plds = [ctx.sem(), ctx.sem()]
    s_pcv = ctx.sem()
    s_sc = ctx.sem()
    s_ex = ctx.sem()
    s_ow = ctx.sem()
    s_mg = ctx.sem()

    npld = [0]

    def load_pos(h):
        buf = h % 2
        for ksub in range(4):
            wv = []
            if npld[0] >= 2:
                wv.append((s_pcv, npld[0] - 1))
            p.add("sync", lambda e, h=h, ksub=ksub, n=npld[0]: e.dma_start(
                out=pstg[n % 2][:],
                in_=pos_d[h, ksub * 128:(ksub + 1) * 128, :]),
                waits=wv, inc=(s_plds[npld[0] % 2], 16))
            wv2 = [(s_plds[npld[0] % 2], 16 * (npld[0] // 2 + 1))]
            if h >= 2:
                wv2.append((s_sc, sc_cnt_at_head_end[h - 2]))
            p.add("vector", lambda e, buf=buf, ksub=ksub, n=npld[0]:
                  e.tensor_copy(out=psb[buf][ksub][:], in_=pstg[n % 2][:]),
                  waits=wv2, inc=(s_pcv, 1))
            npld[0] += 1

    sc_cnt_at_head_end = {}
    cnt = 0
    for h in range(16):
        cnt += 9 * 4
        sc_cnt_at_head_end[h] = cnt

    ow_tile_last_mg = [0, 0, 0, 0]
    first_touch = {}
    gk = [0]
    gow = [0]
    ow_end_of_slot = {}

    def emit_scores(h, j, sl):
        ch, rb = h // 2, (h % 2) * 64
        r0, r1 = slot_qrange(j)
        qlen = r1 - r0
        qoff = r0 + 256 - 256 * j
        for ksub in range(4):
            scb = sc[gk[0] % 2]
            wv = [(s_pcv, 4 * h + ksub + 1)]
            if gk[0] >= 2:
                wv.append((s_ex, gk[0] - 1))
            p.add("tensor", lambda e, ch=ch, rb=rb, j=j, ksub=ksub,
                  r0=r0, qlen=qlen, scb=scb: e.matmul(
                      scb[:, 0:qlen],
                      kfm[ch][rb:rb + 64,
                              256 * j + 128 * ksub:256 * j + 128 * ksub + 128],
                      qfm[ch][rb:rb + 64, r0:r0 + qlen],
                      start=True, stop=False), waits=wv)
            p.add("tensor", lambda e, h=h, ksub=ksub, qoff=qoff,
                  qlen=qlen, scb=scb: e.matmul(
                      scb[:, 0:qlen], ctx.ident_sb[:],
                      psb[h % 2][ksub][:, qoff:qoff + qlen],
                      start=False, stop=True), inc=(s_sc, 1))
            ebuf = exp_sb[sl % 2][ksub]
            wv = [(s_sc, p.val(s_sc))]
            if sl >= 2:
                wv.append((s_ow, ow_end_of_slot[sl - 2]))
            p.add("scalar", lambda e, ebuf=ebuf, qlen=qlen, scb=scb:
                  e.activation(out=ebuf[:, 0:qlen], in_=scb[:, 0:qlen],
                               func=AF.Exp), waits=wv, inc=(s_ex, 1))
            gk[0] += 1

    def emit_ow(h, j, sl):
        r0, r1 = slot_qrange(j)
        qlen = r1 - r0
        exps = exp_sb[sl % 2]
        ex_done = 4 * (sl + 1)
        for qq in range(qlen // 128):
            ti = r0 // 128 + qq
            ob = owp[gow[0] % 4]
            wv = [(s_ex, ex_done), (s_mg, ow_tile_last_mg[gow[0] % 4])]
            for ksub in range(4):
                p.add("tensor", lambda e, exps=exps, ksub=ksub, qq=qq,
                      j=j, h=h, ob=ob: e.matmul(
                          ob[:],
                          exps[ksub][:, qq * 128:qq * 128 + 128],
                          vtm[2 * j + ksub][:, 65 * h:65 * h + 65],
                          start=(ksub == 0), stop=(ksub == 3)),
                      waits=wv if ksub == 0 else [],
                      inc=(s_ow, 1) if ksub == 3 else None)
            gow[0] += 1
            wv = [(s_ow, p.val(s_ow))]
            p.add_r("vector", lambda e, ob=ob: e.reciprocal(
                out=rcol[:, 0:1], in_=ob[:, 64:65]), waits=wv)
            p.add_r("vector", lambda e, j=j: e.tensor_tensor(
                out=rcol[:, 1:2], in0=rcol[:, 0:1], in1=sw[:, j:j + 1],
                op=ALU.mult))
            ft = (ti, h)
            if ft not in first_touch:
                first_touch[ft] = True
                p.add("vector", lambda e, ob=ob, ti=ti, h=h:
                      e.tensor_scalar(
                          out=ctxa[ti][:, 64 * h:64 * h + 64],
                          in0=ob[:, 0:64], scalar1=rcol[:, 1:2],
                          scalar2=None, op0=ALU.mult), inc=(s_mg, 1))
            else:
                p.add("vector", lambda e, ob=ob, ti=ti, h=h:
                      e.scalar_tensor_tensor(
                          out=ctxa[ti][:, 64 * h:64 * h + 64],
                          in0=ob[:, 0:64], scalar=rcol[:, 1:2],
                          in1=ctxa[ti][:, 64 * h:64 * h + 64],
                          op0=ALU.mult, op1=ALU.add), inc=(s_mg, 1))
            ow_tile_last_mg[(gow[0] - 1) % 4] = p.val(s_mg)
        ow_end_of_slot[sl] = p.val(s_ow)

    load_pos(0)
    load_pos(1)
    slots_list = [(h, j) for h in range(16) for j in range(9)]
    for sl, (h, j) in enumerate(slots_list):
        emit_scores(h, j, sl)
        if sl >= 1:
            emit_ow(*slots_list[sl - 1], sl - 1)
        if j == 8 and h + 2 < 16:
            load_pos(h + 2)
    emit_ow(*slots_list[-1], len(slots_list) - 1)
    p.barrier()
    ctx.free(qfm, kfm, vtm, pstg, psb, exp_sb, owp, sc)

    ctx_fm = emit_tm_to_fm(ctx, ctxa, OWN, "cx", scale_cols=ic)
    ctx.free(ctxa)
    wo_sb = load_weight_bf16(ctx, "wo", wo_d, 1024, 1024)
    bo_sb = load_bias_bf16(ctx, "bo", bo_d, 1024)
    p.barrier()
    s0 = [ctx.sbuf("s0", [128, 1024], BF16) for _ in range(OWN // 128)]

    def s_evac(e, ps_ap, key):
        tt, n0, n1 = key
        return e.tensor_copy(out=s0[tt][:, n0:n1], in_=ps_ap)
    emit_gemm_fm_tm(ctx, ctx_fm, OWN, wo_sb, 1024, bo_sb, "s0", s_evac)
    st = ctx.sem(swdge=True)
    for i in range(OWN // 128):
        p.add("gpsimd", lambda e, i=i: e.dma_start(
            out=s_o[i * 128:(i + 1) * 128, :], in_=s0[i][:]), inc=(st, 16))
    p.raw_wait("gpsimd", st, 16 * (OWN // 128))
    p.barrier()
    ctx.free(s0, ctx_fm, wo_sb, bo_sb, sw, ic)


def store_and_free(ctx, pairs, also_free=()):
    p = ctx.p
    st = ctx.sem(swdge=True)
    n = 0
    for dram_ap, sb in pairs:
        p.add("gpsimd", lambda e, d=dram_ap, s=sb: e.dma_start(out=d, in_=s),
              inc=(st, 16))
        n += 1
    p.raw_wait("gpsimd", st, 16 * n)
    p.barrier()
    frees = []
    for _, sb in pairs:
        pass
    for t in also_free:
        ctx.free(t)


def build_l0a():
    ctx = make_ctx("l0a")
    nc = ctx.nc
    io = {}
    io["x_ext"] = nc.dram_tensor("x_ext", [EXT, 1024], F32,
                                 kind="ExternalInput").ap()
    for nm_ in ("wq", "wk", "wv", "wi", "wiT"):
        io[nm_] = nc.dram_tensor(nm_, [1024, 1024], F32,
                                 kind="ExternalInput").ap()
    for nm_ in ("bq", "bk", "bv", "bi"):
        io[nm_] = nc.dram_tensor(nm_, [1, 1024], F32,
                                 kind="ExternalInput").ap()
    io["wg"] = nc.dram_tensor("wg", [1024, 8], F32, kind="ExternalInput").ap()
    io["bg"] = nc.dram_tensor("bg", [1, 8], F32, kind="ExternalInput").ap()
    io["b_in_col"] = nc.dram_tensor("b_in_col", [1024, 1], F32,
                                    kind="ExternalInput").ap()
    io["q"] = nc.dram_tensor("q", [8, 128, OWN], BF16,
                             kind="ExternalOutput").ap()
    io["k"] = nc.dram_tensor("k", [8, 128, OWN], BF16,
                             kind="ExternalOutput").ap()
    io["v"] = nc.dram_tensor("v", [OWN, 1040], BF16,
                             kind="ExternalOutput").ap()
    io["t"] = nc.dram_tensor("t", [8, 128, OWN], BF16,
                             kind="ExternalOutput").ap()
    io["route"] = nc.dram_tensor("route", [OWN, 8], F32,
                                 kind="ExternalOutput").ap()
    emit_l0a(ctx, io, ext_kv=False)
    ctx.p.finalize()
    ctx.close()
    return nc


def emit_l0a(ctx, io, ext_kv):
    nc, p = ctx.nc, ctx.p
    x = io["x_ext"]
    wq_d, wk_d, wv_d, wi_d = io["wq"], io["wk"], io["wv"], io["wi"]
    bq_d, bk_d, bv_d, bi_d = io["bq"], io["bk"], io["bv"], io["bi"]
    wg_d, bg_d, wiT_d, bic_d = io["wg"], io["bg"], io["wiT"], io["b_in_col"]
    q_o, k_o, v_o, t_o, r_o = io["q"], io["k"], io["v"], io["t"], io["route"]

    SCL = 1.0 / 8.0
    xn, _ = emit_ln_to_fm(ctx, x, EXT, "a", keep_stats=True)
    xn_own = [c[:, HALO:HALO + OWN] for c in xn]

    wk_sb = load_weight_bf16(ctx, "wk", wk_d, 1024, 1024)
    bk_sb = load_bias_bf16(ctx, "bk", bk_d, 1024)
    p.barrier()
    kfm = [ctx.sbuf(f"kfm{c}", [128, EXT], BF16) for c in range(8)]
    emit_gemm_fm_fm(ctx, xn, EXT, wk_sb, 1024, bk_sb, kfm, "k")
    if ext_kv:
        store_and_free(ctx, [(k_o[c], kfm[c][:]) for c in range(8)])
    else:
        store_and_free(ctx, [(k_o[c], kfm[c][:, HALO:HALO + OWN])
                             for c in range(8)])
    ctx.free(kfm, wk_sb, bk_sb)

    wq_sb = load_weight_bf16(ctx, "wq", wq_d, 1024, 1024, scale=SCL)
    bq_sb = load_bias_bf16(ctx, "bq", bq_d, 1024, scale=SCL)
    p.barrier()
    qfm = [ctx.sbuf(f"qfm{c}", [128, OWN], BF16) for c in range(8)]
    emit_gemm_fm_fm(ctx, xn_own, OWN, wq_sb, 1024, bq_sb, qfm, "q")
    store_and_free(ctx, [(q_o[c], qfm[c][:]) for c in range(8)])
    ctx.free(qfm, wq_sb, bq_sb)

    wv_sb = load_weight_bf16(ctx, "wv", wv_d, 1024, 1024)
    bv_sb = load_bias_bf16(ctx, "bv", bv_d, 1024)
    p.barrier()
    VR = EXT if ext_kv else OWN
    xn_v = xn if ext_kv else xn_own
    nvt = VR // 128
    vtm = [ctx.sbuf("vtm", [128, 1040], BF16) for _ in range(nvt)]
    s_vz = ctx.sem()
    for i in range(nvt):
        p.add("vector", lambda e, i=i: e.memset(
            vtm[i].rearrange("p (h x) -> p h x", x=65)[:, :, 64:65], 1.0),
            inc=(s_vz, 1))
    p.raw_wait("vector", s_vz, nvt)

    def v_evac(e, ps_ap, key):
        tt, n0, n1 = key
        hbase = n0 // 64
        dst = vtm[tt].rearrange("p (h x) -> p h x", x=65)[
            :, hbase:hbase + (n1 - n0) // 64, 0:64]
        return e.tensor_copy(
            out=dst, in_=ps_ap.rearrange("p (h x) -> p h x", x=64))
    emit_gemm_fm_tm(ctx, xn_v, VR, wv_sb, 1024, bv_sb, "v", v_evac)
    store_and_free(ctx, [(v_o[i * 128:(i + 1) * 128, :], vtm[i][:])
                         for i in range(nvt)])
    ctx.free(vtm, wv_sb, bv_sb)

    wi_sb = load_weight_bf16(ctx, "wi", wi_d, 1024, 1024)
    bi_sb = load_bias_bf16(ctx, "bi", bi_d, 1024)
    p.barrier()
    tfm = [ctx.sbuf(f"tfm{c}", [128, OWN], BF16) for c in range(8)]
    emit_gemm_fm_fm(ctx, xn_own, OWN, wi_sb, 1024, bi_sb, tfm, "t")
    store_and_free(ctx, [(t_o[c], tfm[c][:]) for c in range(8)])
    ctx.free(wi_sb, bi_sb, xn)

    xt_ln, nm_ln, rstd_ln = ctx.ln_stats
    ntt = OWN // 128
    route = ctx.sbuf("route_sb", [128, 8 * ntt], F32)
    wg32 = [ctx.sbuf("wg32", [128, 8], F32) for _ in range(8)]
    bg32 = ctx.sbuf("bg32", [1, 8], F32)
    bic = [ctx.sbuf("bic", [128, 1], F32) for _ in range(8)]
    onesr32 = ctx.sbuf("onesr32", [1, 128], F32)
    bcomb = ctx.sbuf("bcomb", [1, 8], F32)
    wcomb = [ctx.sbuf("wcomb", [128, 8], F32) for _ in range(8)]
    s_ld2 = ctx.sem()
    nld2 = 0
    p.add("vector", lambda e: e.memset(onesr32[:], 1.0))
    for k in range(8):
        p.add("sync", lambda e, k=k: e.dma_start(
            out=wg32[k][:], in_=wg_d[k * 128:(k + 1) * 128, :]),
            inc=(s_ld2, 16)); nld2 += 1
        p.add("sync", lambda e, k=k: e.dma_start(
            out=bic[k][:], in_=bic_d[k * 128:(k + 1) * 128, :]),
            inc=(s_ld2, 16)); nld2 += 1
    p.add("sync", lambda e: e.dma_start(out=bg32[:], in_=bg_d),
          inc=(s_ld2, 16)); nld2 += 1
    p.raw_wait("tensor", s_ld2, 16 * nld2)
    p.barrier()
    bc_ps = ctx.psum("bc", [1, 8], F32)
    s_wc = ctx.sem()
    for k in range(8):
        p.add("tensor", lambda e, k=k: e.matmul(
            bc_ps[:], bic[k][:], wg32[k][:], start=(k == 0), stop=False))
    p.add("tensor", lambda e: e.matmul(
        bc_ps[:], onesr32[0:1, 0:1], bg32[:], start=False, stop=True),
        inc=(s_wc, 1))
    p.add("vector", lambda e: e.tensor_copy(out=bcomb[:], in_=bc_ps[:]),
          waits=[(s_wc, 1)], inc=(s_wc, 1))
    ctx.free(bc_ps)
    wiT_sb = [ctx.sbuf("wiTs", [128, 1024], F32) for _ in range(2)]
    s_wl = [ctx.sem(), ctx.sem()]
    s_wi = ctx.sem()
    wc_ps = [ctx.psum("wcp", [128, 8], F32) for _ in range(8)]
    for k in range(8):
        p.add("sync", lambda e, k=k: e.dma_start(
            out=wiT_sb[k % 2][:], in_=wiT_d[k * 128:(k + 1) * 128, :]),
            waits=[(s_wi, k - 1)] if k >= 2 else [],
            inc=(s_wl[k % 2], 16))
        for m in range(8):
            wvm = [(s_wl[k % 2], 16 * (k // 2 + 1))] if m == 0 else []
            if k == 0 and m == 0:
                wvm.append((s_wc, 2))
            p.add("tensor", lambda e, k=k, m=m: e.matmul(
                wc_ps[m][:], wiT_sb[k % 2][:, m * 128:(m + 1) * 128],
                wg32[k][:], start=(k == 0), stop=(k == 7)),
                waits=wvm,
                inc=(s_wi, 1) if m == 7 else None)
    p.raw_wait("vector", s_wi, 8)
    p.add("vector", lambda e: e.tensor_copy(out=ctx.junk[:, 2:3],
                                            in_=ctx.cpool[:, 0:1]),
          inc=(s_wc, 1))
    for m in range(8):
        p.add("vector", lambda e, m=m: e.tensor_copy(
            out=wcomb[m][:], in_=wc_ps[m][:]), inc=(s_wc, 1))
    p.barrier()
    ctx.free(wc_ps, wiT_sb)

    lgw = ctx.sbuf("rt_lgw", [128, 8 * ntt], F32)
    xn32 = [ctx.sbuf("xn32", [128, 1024], F32) for _ in range(2)]
    xfm32 = [ctx.sbuf("xfm32", [128, 128], F32) for _ in range(8)]
    tp32 = [ctx.psum("tp32", [128, 128], F32) for _ in range(2)]
    lg = [ctx.psum("lg", [128, 8], F32) for _ in range(2)]
    s_n32 = ctx.sem()
    s_t32 = ctx.sem()
    s_x32 = ctx.sem()
    s_lgm = ctx.sem()
    s_rt = ctx.sem()
    for tt in range(ntt):
        i_ext = tt + HALO // 128
        wvn = [(s_t32, 8 * (tt - 1))] if tt >= 2 else []
        p.add("vector", lambda e, tt=tt, i=i_ext: e.tensor_scalar(
            out=xn32[tt % 2][:], in0=xt_ln[i][:], scalar1=nm_ln[:, i:i + 1],
            scalar2=rstd_ln[:, i:i + 1], op0=ALU.add, op1=ALU.mult),
            waits=wvn, inc=(s_n32, 1))
        for c in range(8):
            j = tt * 8 + c
            wv = [(s_n32, tt + 1)]
            if j >= 2:
                wv.append((s_x32, j - 1))
            p.add("tensor", lambda e, tt=tt, c=c, j=j: e.transpose(
                tp32[j % 2][:], xn32[tt % 2][:, c * 128:(c + 1) * 128],
                ctx.ident_f32[:]), waits=wv, inc=(s_t32, 1))
            p.add("vector", lambda e, c=c, j=j, tt=tt: e.tensor_copy(
                out=xfm32[c][:], in_=tp32[j % 2][:]),
                waits=[(s_t32, j + 1), (s_lgm, tt)], inc=(s_x32, 1))
        pb = lg[tt % 2]
        wv0 = [(s_x32, 8 * (tt + 1))]
        if tt >= 2:
            wv0.append((s_rt, p.val(s_rt)))
        for k in range(8):
            p.add("tensor", lambda e, k=k, pb=pb: e.matmul(
                pb[:], xfm32[k][:], wcomb[k][:], start=(k == 0), stop=False),
                waits=wv0 if k == 0 else [])
        p.add("tensor", lambda e, pb=pb: e.matmul(
            pb[:], onesr32[:], bcomb[:], start=False, stop=True),
            inc=(s_lgm, 1))
        fn = p.add_r if tt == ntt - 1 else p.add
        fn("vector", lambda e, tt=tt, pb=pb: e.tensor_copy(
            out=lgw[:, tt * 8:(tt + 1) * 8], in_=pb[:]),
            waits=[(s_lgm, tt + 1)], inc=(s_rt, 1))
    exps = ctx.sbuf("rt_exp", [128, 8 * ntt], F32)
    prb = ctx.sbuf("rt_prb", [128, 8 * ntt], F32)
    msk = ctx.sbuf("rt_msk", [128, 8 * ntt], F32)
    pm2 = ctx.sbuf("rt_pm2", [128, 8 * ntt], F32)
    ru = ctx.sbuf("rt_ru", [128, 8 * ntt], F32)
    mcol = ctx.sbuf("rt_mcol", [128, 6 * ntt], F32)
    mx, ssum, rs_, m1, m2, den = [mcol[:, i * ntt:(i + 1) * ntt]
                                  for i in range(6)]
    A = p.add_r
    A("vector", lambda e: e.reduce_max(out=seg(mx, 1), in_=seg(lgw[:], 8),
                                       axis=AX.X))
    A("vector", lambda e: e.tensor_tensor(out=seg(exps[:], 8),
                                          in0=seg(lgw[:], 8),
                                          in1=bcast(mx, 8),
                                          op=ALU.subtract), inc=(s_rt, 1))
    p.add_r("scalar", lambda e: e.activation(out=exps[:], in_=exps[:],
                                             func=AF.Exp),
            waits=[(s_rt, p.val(s_rt))], inc=(s_rt, 1))
    p.raw_wait("vector", s_rt, p.val(s_rt))
    A("vector", lambda e: e.reduce_sum(out=seg(ssum, 1), in_=seg(exps[:], 8),
                                       axis=AX.X))
    A("vector", lambda e: e.reciprocal(out=rs_, in_=ssum))
    A("vector", lambda e: e.tensor_tensor(out=seg(prb[:], 8),
                                          in0=seg(exps[:], 8),
                                          in1=bcast(rs_, 8), op=ALU.mult))
    A("vector", lambda e: e.reduce_max(out=seg(m1, 1), in_=seg(prb[:], 8),
                                       axis=AX.X))
    A("vector", lambda e: e.tensor_tensor(out=seg(msk[:], 8),
                                          in0=seg(prb[:], 8),
                                          in1=bcast(m1, 8), op=ALU.is_lt))
    A("vector", lambda e: e.tensor_tensor(out=pm2[:], in0=msk[:], in1=prb[:],
                                          op=ALU.mult))
    A("vector", lambda e: e.reduce_max(out=seg(m2, 1), in_=seg(pm2[:], 8),
                                       axis=AX.X))
    A("vector", lambda e: e.tensor_tensor(out=seg(msk[:], 8),
                                          in0=seg(prb[:], 8),
                                          in1=bcast(m2, 8), op=ALU.is_ge))
    A("vector", lambda e: e.tensor_tensor(out=ru[:], in0=msk[:], in1=prb[:],
                                          op=ALU.mult))
    A("vector", lambda e: e.reduce_sum(out=seg(den, 1), in_=seg(ru[:], 8),
                                       axis=AX.X))
    A("vector", lambda e: e.reciprocal(out=rs_, in_=den))
    A("vector", lambda e: e.tensor_tensor(out=seg(route[:], 8),
                                          in0=seg(ru[:], 8),
                                          in1=bcast(rs_, 8), op=ALU.mult))
    p.barrier()
    ctx.ln_stats_free()
    p.barrier()
    st = ctx.sem(swdge=True)
    for i in range(OWN // 128):
        p.add("gpsimd", lambda e, i=i: e.dma_start(
            out=r_o[i * 128:(i + 1) * 128, :],
            in_=route[:, i * 8:(i + 1) * 8]), inc=(st, 16))
    p.raw_wait("gpsimd", st, 16 * (OWN // 128))
    p.barrier()
    ctx.free(lg, tp32, tfm, route, lgw, exps, prb, msk, pm2, ru, mcol,
             xn32, xfm32, wg32, bg32, bic, onesr32, bcomb, wcomb)


def build_l0c():
    ctx = make_ctx("l0c")
    nc = ctx.nc
    io = {}
    io["t"] = nc.dram_tensor("t", [8, 128, OWN], BF16,
                             kind="ExternalInput").ap()
    io["route"] = nc.dram_tensor("route", [OWN, 8], F32,
                                 kind="ExternalInput").ap()
    io["w1"] = nc.dram_tensor("w1", [8, 1024, 4096], F32,
                              kind="ExternalInput").ap()
    io["b1"] = nc.dram_tensor("b1", [8, 4096], F32,
                              kind="ExternalInput").ap()
    io["w2"] = nc.dram_tensor("w2", [8, 4096, 1024], F32,
                              kind="ExternalInput").ap()
    io["b2"] = nc.dram_tensor("b2", [8, 1024], F32,
                              kind="ExternalInput").ap()
    io["wout"] = nc.dram_tensor("wout", [1024, 1024], F32,
                                kind="ExternalInput").ap()
    io["bout"] = nc.dram_tensor("bout", [1, 1024], F32,
                                kind="ExternalInput").ap()
    io["s0"] = nc.dram_tensor("s0", [OWN, 1024], BF16,
                              kind="ExternalInput").ap()
    io["x"] = nc.dram_tensor("x", [OWN, 1024], F32,
                             kind="ExternalInput").ap()
    io["x1"] = nc.dram_tensor("x1", [OWN, 1024], F32,
                              kind="ExternalOutput").ap()
    emit_l0c(ctx, io)
    ctx.p.finalize()
    ctx.close()
    return nc


def emit_l0c(ctx, io):
    nc, p = ctx.nc, ctx.p
    t_d, r_d = io["t"], io["route"]
    w1_d, b1_d, w2_d, b2_d = io["w1"], io["b1"], io["w2"], io["b2"]
    wo_d, bo_d = io["wout"], io["bout"]
    s0_d, x_d, x1_d = io["s0"], io["x"], io["x1"]

    NT = OWN // 128
    tfm = [ctx.sbuf(f"tfm{c}", [128, OWN], BF16) for c in range(8)]
    acc = [ctx.sbuf("acc", [128, 1024], BF16) for _ in range(NT)]
    rts = ctx.sbuf("rts", [128, 8 * NT], F32)
    s_ld = ctx.sem()
    nld = 0
    for c in range(8):
        p.add("sync", lambda e, c=c: e.dma_start(out=tfm[c][:], in_=t_d[c]),
              inc=(s_ld, 16)); nld += 1
    for i in range(NT):
        p.add("sync", lambda e, i=i: e.dma_start(
            out=rts[:, i * 8:(i + 1) * 8], in_=r_d[i * 128:(i + 1) * 128, :]),
            inc=(s_ld, 16)); nld += 1
    p.raw_wait("tensor", s_ld, 16 * nld)
    p.raw_wait("vector", s_ld, 16 * nld)
    p.barrier()

    slots = [[ctx.sbuf("w1q", [128, 1024], BF16) for _ in range(8)] +
             [ctx.sbuf("w2q", [128, 1024], BF16) for _ in range(8)] +
             [ctx.sbuf("b1q", [1, 1024], BF16), ctx.sbuf("b2r", [1, 1024],
                                                         BF16)]
             for _ in range(2)]
    WND = 4
    wstg = [ctx.sbuf("wstg", [128, 1024], F32) for _ in range(WND)]
    s_wld = [ctx.sem() for _ in range(WND)]
    s_wcv = ctx.sem()
    s_mm = ctx.sem()
    s_rel = ctx.sem()
    s_eo = ctx.sem()
    mid_ps = [ctx.psum("midp", [128, 512], F32) for _ in range(2)]
    eo_ps = [ctx.psum("eop", [128, 1024], F32) for _ in range(2)]
    mid_sb = [[ctx.sbuf("mid", [128, 512], BF16) for _ in range(8)]
              for _ in range(2)]
    nw = [0]
    mm_end_of_job = {}
    mid_evac_cnt = {}
    eo_evac_cnt = {}
    NPART = 18

    def load_job(j):
        e_, qq = divmod(j, 4)
        for part in range(NPART):
            g = nw[0]
            wv = [(s_wcv, g - WND + 1)] if g >= WND else []
            if part < 8:
                src = w1_d[e_, part * 128:(part + 1) * 128,
                           qq * 1024:(qq + 1) * 1024]
                dst_stage = wstg[g % WND][:]
                cast_src = wstg[g % WND][:]
            elif part < 16:
                kk = part - 8
                src = w2_d[e_, qq * 1024 + kk * 128:
                           qq * 1024 + kk * 128 + 128, :]
                dst_stage = wstg[g % WND][:]
                cast_src = wstg[g % WND][:]
            elif part == 16:
                src = b1_d[e_:e_ + 1, qq * 1024:(qq + 1) * 1024]
                dst_stage = wstg[g % WND][0:1, :]
                cast_src = wstg[g % WND][0:1, :]
            else:
                src = b2_d[e_:e_ + 1, :]
                dst_stage = wstg[g % WND][0:1, :]
                cast_src = wstg[g % WND][0:1, :]
            p.add("sync", lambda e, src=src, d=dst_stage: e.dma_start(
                out=d, in_=src), waits=wv, inc=(s_wld[g % WND], 16))
            wv2 = [(s_wld[g % WND], 16 * (g // WND + 1))]
            if j >= 2:
                wv2.append((s_mm, mm_end_of_job[j - 2]))
            dst = slots[j % 2][part][:] if part < 16 else \
                slots[j % 2][part][0:1, :]
            p.add("vector", lambda e, dst=dst, csrc=cast_src: e.tensor_copy(
                out=dst, in_=csrc), waits=wv2, inc=(s_wcv, 1))
            nw[0] += 1

    load_job(0)
    load_job(1)
    mid_cnt = [0]
    eo_done = {}
    for j in range(32):
        e_, qq = divmod(j, 4)
        sl = slots[j % 2]
        wcv_need = NPART * (j + 1)
        for tth in range(8):
            t0 = tth * 256

            def emit_mid(n):
                gmid = mid_cnt[0]
                mp = mid_ps[gmid % 2]
                wv = [(s_wcv, wcv_need)]
                if gmid >= 2:
                    wv.append((s_rel, gmid - 1))
                for k in range(8):
                    p.add("tensor", lambda e, sl=sl, k=k, n=n, t0=t0, mp=mp:
                          e.matmul(mp[:, 0:256],
                                   sl[k][:, n * 128:(n + 1) * 128],
                                   tfm[k][:, t0:t0 + 256],
                                   start=(k == 0), stop=False),
                          waits=wv if k == 0 else [])
                p.add("tensor", lambda e, sl=sl, n=n, mp=mp: e.matmul(
                    mp[:, 0:256], sl[16][:, n * 128:(n + 1) * 128],
                    ctx.ones_sb[:, 0:256], start=False, stop=True),
                    inc=(s_mm, 1))
                p.add("scalar", lambda e, mp=mp, gmid=gmid: e.activation(
                    out=mid_sb[gmid % 2][0][:, 0:256], in_=mp[:, 0:256],
                    func=AF.Relu),
                    waits=[(s_mm, p.val(s_mm))], inc=(s_rel, 1))
                mid_cnt[0] += 1
                return gmid

            def emit_eout(n, gmid):
                wv = [(s_rel, gmid + 1)]
                if n == 0:
                    geo = j * 8 + tth
                    if geo >= 1:
                        wv.append((s_eo, eo_done[geo - 1]))
                first_mm = True
                for tq2 in range(2):
                    ep = eo_ps[tq2]
                    for half in range(2):
                        last = (n == 7 and qq != 0)
                        p.add("tensor", lambda e, gmid=gmid, tq2=tq2,
                              half=half, n=n, ep=ep, sl=sl, last=last:
                              e.matmul(
                                  ep[:, half * 512:(half + 1) * 512],
                                  mid_sb[gmid % 2][0][:, tq2 * 128:
                                                      tq2 * 128 + 128],
                                  sl[8 + n][:, half * 512:(half + 1) * 512],
                                  start=(n == 0), stop=last),
                              waits=wv if first_mm else [],
                              inc=(s_mm, 1)
                              if (last and tq2 == 1 and half == 1) else None)
                        first_mm = False

            gm_prev = None
            for n in range(8):
                gm = emit_mid(n)
                if n >= 1:
                    emit_eout(n - 1, gm_prev)
                gm_prev = gm
            emit_eout(7, gm_prev)
            if qq == 0:
                for tq2 in range(2):
                    ep = eo_ps[tq2]
                    for half in range(2):
                        p.add("tensor", lambda e, sl=sl, half=half, ep=ep:
                              e.matmul(
                                  ep[:, half * 512:(half + 1) * 512],
                                  ctx.ones_sb[0:1, 0:128],
                                  sl[17][0:1, half * 512:(half + 1) * 512],
                                  start=False, stop=True),
                              inc=(s_mm, 1) if half == 1 else None)
            mmv = p.val(s_mm)
            for tq2 in range(2):
                ep = eo_ps[tq2]
                ti = tth * 2 + tq2
                rcol = rts[:, ti * 8 + e_:ti * 8 + e_ + 1]
                if j == 0:
                    p.add("vector", lambda e, ep=ep, ti=ti, rcol=rcol:
                          e.tensor_scalar(out=acc[ti][:], in0=ep[:],
                                          scalar1=rcol, scalar2=None,
                                          op0=ALU.mult),
                          waits=[(s_mm, mmv)], inc=(s_eo, 1))
                else:
                    p.add("vector", lambda e, ep=ep, ti=ti, rcol=rcol:
                          e.scalar_tensor_tensor(out=acc[ti][:], in0=ep[:],
                                                 scalar=rcol, in1=acc[ti][:],
                                                 op0=ALU.mult, op1=ALU.add),
                          waits=[(s_mm, mmv)], inc=(s_eo, 1))
            eo_done[j * 8 + tth] = p.val(s_eo)
        mm_end_of_job[j] = p.val(s_mm)
        if j + 2 < 32:
            load_job(j + 2)
    p.barrier()
    ctx.free(mid_sb, wstg, slots, eo_ps, mid_ps, tfm)

    acc_fm = emit_tm_to_fm(ctx, acc, OWN, "mo")
    ctx.free(acc)
    wo_sb = load_weight_bf16(ctx, "wout", wo_d, 1024, 1024)
    bo_sb = load_bias_bf16(ctx, "bout", bo_d, 1024)
    p.barrier()
    g_tm = [ctx.sbuf("g", [128, 1024], BF16) for _ in range(NT)]

    def g_evac(e, ps_ap, key):
        tt, n0, n1 = key
        return e.activation(out=g_tm[tt][:, n0:n1], in_=ps_ap, func=AF.Silu)
    emit_gemm_fm_tm(ctx, acc_fm, OWN, wo_sb, 1024, bo_sb, "g", g_evac,
                    evac_eng="scalar")
    ctx.free(wo_sb, bo_sb, acc_fm)

    xs = [ctx.sbuf("xs", [128, 1024], F32) for _ in range(2)]
    ss = [ctx.sbuf("ss", [128, 1024], BF16) for _ in range(2)]
    x1t = [ctx.sbuf("x1t", [128, 1024], F32) for _ in range(2)]
    s_xl = [ctx.sem(), ctx.sem()]
    s_cb = ctx.sem()
    s_st2 = [ctx.sem(swdge=True), ctx.sem(swdge=True)]
    for i in range(NT):
        wvl = [(s_cb, 2 * (i - 2) + 2)] if i >= 2 else []
        p.add("sync", lambda e, i=i: e.dma_start(
            out=xs[i % 2][:], in_=x_d[i * 128:(i + 1) * 128, :]),
            waits=wvl, inc=(s_xl[i % 2], 16))
        p.add("sync", lambda e, i=i: e.dma_start(
            out=ss[i % 2][:], in_=s0_d[i * 128:(i + 1) * 128, :]),
            inc=(s_xl[i % 2], 16))
        wv = [(s_xl[i % 2], 32 * (i // 2 + 1))]
        if i >= 2:
            wv.append((s_st2[i % 2], 16 * (i // 2)))
        p.add("vector", lambda e, i=i: e.tensor_tensor(
            out=x1t[i % 2][:], in0=ss[i % 2][:], in1=g_tm[i][:],
            op=ALU.mult), waits=wv, inc=(s_cb, 1))
        p.flush("vector")
        p.add("vector", lambda e, i=i: e.tensor_tensor(
            out=x1t[i % 2][:], in0=x1t[i % 2][:], in1=xs[i % 2][:],
            op=ALU.add), inc=(s_cb, 1))
        p.add("gpsimd", lambda e, i=i: e.dma_start(
            out=x1_d[i * 128:(i + 1) * 128, :], in_=x1t[i % 2][:]),
            waits=[(s_cb, 2 * i + 2)], inc=(s_st2[i % 2], 16))
    p.raw_wait("gpsimd", s_st2[0], 16 * (NT // 2))
    p.raw_wait("gpsimd", s_st2[1], 16 * (NT // 2))
    p.barrier()
    ctx.free(xs, ss, x1t, g_tm, rts)


def emit_fm_to_tm(ctx, fm_chunks, T, tag, out_dtype=BF16):
    p = ctx.p
    nt = T // 128
    tm = [ctx.sbuf(f"tm_{tag}", [128, 1024], out_dtype) for _ in range(nt)]
    tp = [ctx.psum(f"tp_{tag}", [128, 128], out_dtype) for _ in range(2)]
    s_tp = ctx.sem()
    s_ev = ctx.sem()
    j = 0
    for i in range(nt):
        for c in range(8):
            wv = [(s_ev, j - 1)] if j >= 2 else []
            p.add("tensor", lambda e, i=i, c=c, j=j: e.transpose(
                tp[j % 2][:], fm_chunks[c][:, i * 128:(i + 1) * 128],
                ctx.ident_sb[:]), waits=wv, inc=(s_tp, 1))
            p.add("vector", lambda e, i=i, c=c, j=j: e.tensor_copy(
                out=tm[i][:, c * 128:(c + 1) * 128], in_=tp[j % 2][:]),
                waits=[(s_tp, j + 1)], inc=(s_ev, 1))
            j += 1
    p.barrier()
    ctx.free(tp)
    return tm


def load_weight_mp(ctx, name, dram_ap, gain_sb, cmul):
    p, nc = ctx.p, ctx.nc
    out = [ctx.sbuf(f"w_{name}{k}", [128, 1024], BF16) for k in range(8)]
    fcol = ctx.sbuf(f"fc_{name}", [128, 8], F32)
    stage = [ctx.sbuf(f"wst_{name}", [128, 1024], F32) for _ in range(2)]
    sq = [ctx.sbuf(f"sq_{name}", [128, 1024], F32) for _ in range(2)]
    frow = ctx.sbuf(f"fr_{name}", [1, 1024], F32)
    s2p = [ctx.psum(f"s2_{name}", [1, 512], F32) for _ in range(2)]
    fcp = ctx.psum(f"fcp_{name}", [128, 8], F32)
    s_lds = [ctx.sem(), ctx.sem()]
    s_cv = ctx.sem()
    s_sq = ctx.sem()
    s_s2 = ctx.sem()
    s_f = ctx.sem()
    for k in range(8):
        buf = stage[k % 2]
        s_ld = s_lds[k % 2]
        w = [(s_sq, 2 * (k - 2) + 2), (s_cv, k - 1)] if k >= 2 else []
        p.add("sync", lambda e, buf=buf, k=k: e.dma_start(
            out=buf[:], in_=dram_ap[k * 128:(k + 1) * 128, :]),
            waits=w, inc=(s_ld, 16))
        wvc = [(s_ld, 16 * (k // 2 + 1))]
        p.add("vector", lambda e, k=k, buf=buf: e.tensor_copy(
            out=out[k][:], in_=buf[:]), waits=wvc, inc=(s_cv, 1))
        p.add("scalar", lambda e, k=k, buf=buf: e.activation(
            out=sq[k % 2][:], in_=buf[:], func=AF.Square),
            waits=wvc + ([(s_s2, 2 * (k - 1))] if k >= 2 else []),
            inc=(s_sq, 2))
        for hh in range(2):
            p.add("tensor", lambda e, k=k, hh=hh: e.matmul(
                s2p[hh][:], ctx.ones_col128[:, 0:1],
                sq[k % 2][:, hh * 512:(hh + 1) * 512],
                start=(k == 0), stop=(k == 7)),
                waits=[(s_sq, 2 * k + 2)] if hh == 0 else [],
                inc=(s_s2, 1))
    c_inv = ctx.ccol(1.0 / 1024.0)
    p.add("scalar", lambda e: e.activation(
        out=frow[:, 0:512], in_=s2p[0][:], func=AF.Sqrt,
        scale=c_inv[0:1, :]), waits=[(s_s2, 16)])
    p.add_r("scalar", lambda e: e.activation(
        out=frow[:, 512:1024], in_=s2p[1][:], func=AF.Sqrt,
        scale=c_inv[0:1, :]))
    p.add("vector", lambda e: e.tensor_scalar(
        out=frow[:], in0=frow[:], scalar1=EPS_MP, scalar2=None, op0=ALU.add),
        waits=[(p.rsync["scalar"], p.val(p.rsync["scalar"]))])
    p.flush("vector")
    p.add_r("vector", lambda e: e.reciprocal(out=frow[:], in_=frow[:]))
    p.add("vector", lambda e: e.tensor_scalar(
        out=frow[:], in0=frow[:], scalar1=gain_sb[0:1, 0:1], scalar2=cmul,
        op0=ALU.mult, op1=ALU.mult), inc=(s_f, 1))
    for c in range(8):
        p.add("tensor", lambda e, c=c: e.matmul(
            fcp[:, c:c + 1], frow[:, c * 128:(c + 1) * 128],
            ctx.ones_f32[0:1, 0:1], start=True, stop=True),
            waits=[(s_f, p.val(s_f))] if c == 0 else [],
            inc=(s_f, 1) if c == 7 else None)
    p.add("vector", lambda e: e.tensor_copy(out=fcol[:], in_=fcp[:]),
          waits=[(s_f, p.val(s_f))], inc=(s_f, 1))
    p.barrier()
    ctx.free(fcp, s2p, frow, sq, stage)
    return out, fcol


def emit_mp_norm_tm(ctx, tm_tiles, nt, tag, extra=1.0):
    p = ctx.p
    sq = [ctx.sbuf(f"nsq_{tag}", [128, 1024], F32) for _ in range(2)]
    ss = ctx.sbuf(f"nss_{tag}", [128, 16], F32)
    fs = ctx.sbuf(f"nfs_{tag}", [128, 16], F32)
    s_sq = ctx.sem()
    s_ss = ctx.sem()
    c64 = ctx.ccol(1.0 / 64.0)
    for i in range(nt):
        p.add("scalar", lambda e, i=i: e.activation(
            out=sq[i % 2][:], in_=tm_tiles[i][:], func=AF.Square),
            waits=[(s_ss, max(0, 3 * (i - 1)))], inc=(s_sq, 1))
        p.add_r("vector", lambda e, i=i: e.reduce_sum(
            out=seg(ss[:], 1), in_=seg(sq[i % 2][:], 64), axis=AX.X),
            waits=[(s_sq, i + 1)])
        p.add_r("scalar", lambda e: e.activation(
            out=fs[:], in_=ss[:], func=AF.Sqrt, scale=c64),
            waits=[(p.rsync["vector"], p.val(p.rsync["vector"]))])
        p.add("vector", lambda e: e.tensor_scalar(
            out=fs[:], in0=fs[:], scalar1=EPS_MP, scalar2=None, op0=ALU.add),
            waits=[(p.rsync["scalar"], p.val(p.rsync["scalar"]))])
        p.flush("vector")
        p.add_r("vector", lambda e: e.reciprocal(out=fs[:], in_=fs[:]))
        if extra != 1.0:
            p.add("vector", lambda e: e.tensor_scalar(
                out=fs[:], in0=fs[:], scalar1=extra, scalar2=None,
                op0=ALU.mult))
            p.flush("vector")
        p.add("vector", lambda e, i=i: e.tensor_tensor(
            out=seg(tm_tiles[i][:], 64), in0=seg(tm_tiles[i][:], 64),
            in1=bcast(fs[:], 64), op=ALU.mult), inc=(s_ss, 3))
        p.flush("vector")
    p.barrier()
    ctx.free(fs, ss, sq)


def build_l1a():
    ctx = make_ctx("l1a")
    nc = ctx.nc
    io = {}
    io["x1"] = nc.dram_tensor("x1", [OWN, 1024], F32,
                              kind="ExternalInput").ap()
    for nm_ in ("wqT", "wkT", "wvT"):
        io[nm_] = nc.dram_tensor(nm_, [1024, 1024], F32,
                                 kind="ExternalInput").ap()
    io["gain"] = nc.dram_tensor("gain", [1, 1], F32,
                                kind="ExternalInput").ap()
    io["xn1"] = nc.dram_tensor("xn1", [OWN, 1024], BF16,
                               kind="ExternalOutput").ap()
    io["v1"] = nc.dram_tensor("v1", [8, 128, OWN], BF16,
                              kind="ExternalOutput").ap()
    io["gram"] = nc.dram_tensor("gram", [128, 512], F32,
                                kind="ExternalOutput").ap()
    emit_l1a(ctx, io)
    ctx.p.finalize()
    ctx.close()
    return nc


def emit_l1a(ctx, io):
    nc, p = ctx.nc, ctx.p
    ctx.ones_col128 = ctx.sbuf("ones_col", [128, 1], F32)
    ctx.ones_f32 = ctx.sbuf("ones_f32", [1, 8], F32)
    p.add("vector", lambda e: e.memset(ctx.ones_col128[:], 1.0))
    p.add("vector", lambda e: e.memset(ctx.ones_f32[:], 1.0))
    x1_d, gn_d = io["x1"], io["gain"]
    wq_d, wk_d, wv_d = io["wqT"], io["wkT"], io["wvT"]
    xn_o, v_o, g_o = io["xn1"], io["v1"], io["gram"]

    gain_sb = ctx.sbuf("gain", [1, 1], F32)
    s_g = ctx.sem()
    p.add("sync", lambda e: e.dma_start(out=gain_sb[:], in_=gn_d),
          inc=(s_g, 16))
    p.raw_wait("vector", s_g, 16)
    p.barrier()

    xn_fm, xn_tm = emit_ln_to_fm(ctx, x1_d, OWN, "l1", keep_tm=True)
    st = ctx.sem(swdge=True)
    nst = [0]
    for i in range(OWN // 128):
        p.add("gpsimd", lambda e, i=i: e.dma_start(
            out=xn_o[i * 128:(i + 1) * 128, :], in_=xn_tm[i][:]),
            inc=(st, 16)); nst[0] += 1

    NT = OWN // 128
    SQS = 1.0 / math.sqrt(float(S))

    def proj(tagn, w_d, cmul=1.0):
        wsb, fcol = load_weight_mp(ctx, tagn, w_d, gain_sb, 1.0 / 32.0 * cmul)
        p.barrier()
        outfm = [ctx.sbuf(f"{tagn}fm{c}", [128, OWN], BF16) for c in range(8)]

        def evac(e, ps_ap, key):
            n, t0, t1 = key
            return e.tensor_scalar(out=outfm[n][:, t0:t1], in0=ps_ap,
                                   scalar1=fcol[:, n:n + 1], scalar2=None,
                                   op0=ALU.mult)
        emit_gemm_fm_fm(ctx, xn_fm, OWN, wsb, 1024, None, None, tagn,
                        evac=evac)
        ctx.free(wsb)
        return outfm, fcol

    vfm, fv = proj("wv", wv_d)
    for c in range(8):
        p.add("gpsimd", lambda e, c=c: e.dma_start(
            out=v_o[c], in_=vfm[c][:]), inc=(st, 16)); nst[0] += 1
    p.raw_wait("gpsimd", st, 16 * nst[0])
    p.barrier()
    ctx.free(vfm, fv)

    qfm, fq = proj("wq", wq_d)
    q_tm = emit_fm_to_tm(ctx, qfm, OWN, "q")
    ctx.free(qfm, fq)
    kfm, fk = proj("wk", wk_d)
    k_tm = emit_fm_to_tm(ctx, kfm, OWN, "k")
    ctx.free(kfm, fk)
    emit_mp_norm_tm(ctx, q_tm, NT, "q")
    emit_mp_norm_tm(ctx, k_tm, NT, "k", extra=SQS)

    gps = [ctx.psum("gram", [128, 64], F32) for _ in range(8)]
    gsb = ctx.sbuf("gramsb", [128, 512], F32)
    s_gm = ctx.sem()
    for h in range(16):
        hp, rb = h // 2, (h % 2) * 64
        for i in range(NT):
            p.add("tensor", lambda e, h=h, hp=hp, rb=rb, i=i: e.matmul(
                gps[hp][rb:rb + 64, :],
                q_tm[i][:, h * 64:h * 64 + 64],
                k_tm[i][:, h * 64:h * 64 + 64],
                start=(i == 0), stop=(i == NT - 1)),
                inc=(s_gm, 1) if i == NT - 1 else None)
    for hp in range(8):
        p.add("vector", lambda e, hp=hp: e.tensor_copy(
            out=gsb[:, hp * 64:(hp + 1) * 64], in_=gps[hp][:]),
            waits=[(s_gm, 16)], inc=(s_gm, 1))
    st2 = ctx.sem(swdge=True)
    p.add("gpsimd", lambda e: e.dma_start(out=g_o, in_=gsb[:]),
          waits=[(s_gm, 24)], inc=(st2, 16))
    p.raw_wait("gpsimd", st2, 16)
    p.barrier()
    ctx.free(gsb, gps, q_tm, k_tm, xn_fm, xn_tm)


def build_l1b():
    ctx = make_ctx("l1b")
    nc = ctx.nc
    io = {}
    io["grams"] = nc.dram_tensor("grams", [4, 128, 512], F32,
                                 kind="ExternalInput").ap()
    io["v1"] = nc.dram_tensor("v1", [8, 128, OWN], BF16,
                              kind="ExternalInput").ap()
    io["xn1"] = nc.dram_tensor("xn1", [OWN, 1024], BF16,
                               kind="ExternalInput").ap()
    io["woT"] = nc.dram_tensor("woT", [1024, 1024], F32,
                               kind="ExternalInput").ap()
    io["gain"] = nc.dram_tensor("gain", [1, 1], F32,
                                kind="ExternalInput").ap()
    io["s1"] = nc.dram_tensor("s1", [OWN, 1024], BF16,
                              kind="ExternalOutput").ap()
    emit_l1b(ctx, io, npart=4)
    ctx.p.finalize()
    ctx.close()
    return nc


def emit_l1b(ctx, io, npart):
    nc, p = ctx.nc, ctx.p
    ctx.ones_col128 = ctx.sbuf("ones_col", [128, 1], F32)
    ctx.ones_f32 = ctx.sbuf("ones_f32", [1, 8], F32)
    p.add("vector", lambda e: e.memset(ctx.ones_col128[:], 1.0))
    p.add("vector", lambda e: e.memset(ctx.ones_f32[:], 1.0))
    gr_d, v_d, xn_d = io["grams"], io["v1"], io["xn1"]
    wo_d, gn_d, s_o = io["woT"], io["gain"], io["s1"]

    NT = OWN // 128
    gain_sb = ctx.sbuf("gain", [1, 1], F32)
    grs = [ctx.sbuf("grs", [128, 512], F32) for _ in range(npart)]
    vfm = [ctx.sbuf(f"vfm{c}", [128, OWN], BF16) for c in range(8)]
    s_ld = ctx.sem()
    nld = 0
    p.add("sync", lambda e: e.dma_start(out=gain_sb[:], in_=gn_d),
          inc=(s_ld, 16)); nld += 1
    for i in range(npart):
        p.add("sync", lambda e, i=i: e.dma_start(
            out=grs[i][:], in_=gr_d[i] if npart > 1 else gr_d),
            inc=(s_ld, 16)); nld += 1
    for c in range(8):
        p.add("sync", lambda e, c=c: e.dma_start(out=vfm[c][:], in_=v_d[c]),
              inc=(s_ld, 16)); nld += 1
    p.raw_wait("vector", s_ld, 16 * nld)
    p.raw_wait("tensor", s_ld, 16 * nld)
    p.barrier()

    att = ctx.sbuf("att", [128, 512], BF16)
    w0 = ctx.sbuf("smw0", [128, 512], F32)
    w1 = ctx.sbuf("smw1", [128, 512], F32)
    mcol = ctx.sbuf("smc", [128, 24], F32)
    if npart > 1:
        p.add("vector", lambda e: e.tensor_tensor(out=w0[:], in0=grs[0][:],
                                                  in1=grs[1][:], op=ALU.add))
        p.add("vector", lambda e: e.tensor_tensor(out=w1[:], in0=grs[2][:],
                                                  in1=grs[3][:], op=ALU.add))
        p.flush("vector")
        p.add("vector", lambda e: e.tensor_tensor(out=w0[:], in0=w0[:],
                                                  in1=w1[:], op=ALU.add))
        p.flush("vector")
    else:
        p.add("vector", lambda e: e.tensor_copy(out=w0[:], in_=grs[0][:]))
        p.flush("vector")
    A = p.add_r
    mx, sm, rc = mcol[:, 0:8], mcol[:, 8:16], mcol[:, 16:24]
    s_sm = ctx.sem()
    A("vector", lambda e: e.reduce_max(out=seg(mx, 1), in_=seg(w0[:], 64),
                                       axis=AX.X))
    A("vector", lambda e: e.tensor_tensor(out=seg(w1[:], 64),
                                          in0=seg(w0[:], 64),
                                          in1=bcast(mx, 64),
                                          op=ALU.subtract), inc=(s_sm, 1))
    p.add_r("scalar", lambda e: e.activation(out=w1[:], in_=w1[:],
                                             func=AF.Exp),
            waits=[(s_sm, 1)], inc=(s_sm, 1))
    p.raw_wait("vector", s_sm, 2)
    A("vector", lambda e: e.reduce_sum(out=seg(sm, 1), in_=seg(w1[:], 64),
                                       axis=AX.X))
    A("vector", lambda e: e.reciprocal(out=rc, in_=sm))
    A("vector", lambda e: e.tensor_tensor(out=seg(att[:], 64),
                                          in0=seg(w1[:], 64),
                                          in1=bcast(rc, 64), op=ALU.mult))
    p.barrier()

    atT = ctx.sbuf("atT", [128, 512], BF16)
    tp = [ctx.psum("atp", [128, 64], BF16) for _ in range(2)]
    s_tp = ctx.sem()
    s_te = ctx.sem()
    for h in range(16):
        rb, cb = (h % 2) * 64, (h // 2) * 64
        wv = [(s_te, h - 1)] if h >= 2 else []
        p.add("tensor", lambda e, h=h, rb=rb, cb=cb: e.transpose(
            tp[h % 2][rb:rb + 64, :], att[rb:rb + 64, cb:cb + 64],
            ctx.ident_sb[rb:rb + 64, rb:rb + 64]), waits=wv, inc=(s_tp, 1))
        p.add("vector", lambda e, h=h, rb=rb, cb=cb: e.tensor_copy(
            out=atT[rb:rb + 64, cb:cb + 64], in_=tp[h % 2][rb:rb + 64, :]),
            waits=[(s_tp, h + 1)], inc=(s_te, 1))
    p.barrier()
    ctx.free(tp, w0, w1, mcol, grs)

    ofm = [ctx.sbuf(f"ofm{c}", [128, OWN], BF16) for c in range(8)]
    ops_ = [ctx.psum("op", [128, 512], F32) for _ in range(2)]
    s_mm = ctx.sem()
    s_ev = ctx.sem()
    it = 0
    for hp in range(4):
        for tt in range(OWN // 512):
            pb = ops_[it % 2]
            wv = [(s_ev, it - 1)] if it >= 2 else []
            for sub in range(2):
                h = hp * 2 + sub
                rb = (h % 2) * 64
                p.add("tensor", lambda e, h=h, hp=hp, rb=rb, tt=tt, pb=pb:
                      e.matmul(pb[rb:rb + 64, :],
                               atT[rb:rb + 64, (h // 2) * 64:(h // 2) * 64 + 64],
                               vfm[h // 2][rb:rb + 64,
                                           tt * 512:(tt + 1) * 512],
                               start=True, stop=True),
                      waits=wv if sub == 0 else [],
                      inc=(s_mm, 1) if sub == 1 else None)
            it += 1
            p.add("vector", lambda e, hp=hp, tt=tt, pb=pb: e.tensor_copy(
                out=ofm[hp * 2 // 2][:, tt * 512:(tt + 1) * 512]
                if False else ofm[hp][:, tt * 512:(tt + 1) * 512], in_=pb[:]),
                waits=[(s_mm, it)], inc=(s_ev, 1))
    p.barrier()
    for hp in range(4, 8):
        for tt in range(OWN // 512):
            pb = ops_[it % 2]
            wv = [(s_ev, it - 1)] if it >= 2 else []
            for sub in range(2):
                h = hp * 2 + sub
                rb = (h % 2) * 64
                p.add("tensor", lambda e, h=h, rb=rb, tt=tt, pb=pb:
                      e.matmul(pb[rb:rb + 64, :],
                               atT[rb:rb + 64, (h // 2) * 64:(h // 2) * 64 + 64],
                               vfm[h // 2][rb:rb + 64,
                                           tt * 512:(tt + 1) * 512],
                               start=True, stop=True),
                      waits=wv if sub == 0 else [],
                      inc=(s_mm, 1) if sub == 1 else None)
            it += 1
            p.add("vector", lambda e, hp=hp, tt=tt, pb=pb: e.tensor_copy(
                out=ofm[hp][:, tt * 512:(tt + 1) * 512], in_=pb[:]),
                waits=[(s_mm, it)], inc=(s_ev, 1))
    p.barrier()
    ctx.free(ops_, atT, att, vfm)

    wsb, fcol = load_weight_mp(ctx, "wo", wo_d, gain_sb,
                               1.0 / 32.0 * 0.7071067811865476)
    p.barrier()
    opfm = [ctx.sbuf(f"opfm{c}", [128, OWN], BF16) for c in range(8)]

    def evac(e, ps_ap, key):
        n, t0, t1 = key
        return e.tensor_scalar(out=opfm[n][:, t0:t1], in0=ps_ap,
                               scalar1=fcol[:, n:n + 1], scalar2=None,
                               op0=ALU.mult)
    emit_gemm_fm_fm(ctx, ofm, OWN, wsb, 1024, None, None, "op", evac=evac)
    ctx.free(wsb, fcol, ofm)
    op_tm = emit_fm_to_tm(ctx, opfm, OWN, "op")
    ctx.free(opfm)

    xnt = [ctx.sbuf("xnt", [128, 1024], BF16) for _ in range(2)]
    s1t = [ctx.sbuf("s1t", [128, 1024], BF16) for _ in range(2)]
    xsc = [ctx.sbuf("xsc", [128, 1024], F32) for _ in range(2)]
    s_xl = [ctx.sem(), ctx.sem()]
    s_cbA = ctx.sem()
    s_cbB = ctx.sem()
    s_st = [ctx.sem(swdge=True), ctx.sem(swdge=True)]
    c07 = ctx.ccol(0.7071067811865476)
    for i in range(NT):
        wvl = [(s_cbA, i - 1)] if i >= 2 else []
        p.add("sync", lambda e, i=i: e.dma_start(
            out=xnt[i % 2][:], in_=xn_d[i * 128:(i + 1) * 128, :]),
            waits=wvl, inc=(s_xl[i % 2], 16))
        wv = [(s_xl[i % 2], 16 * (i // 2 + 1))]
        if i >= 2:
            wv.append((s_cbB, i - 1))
        p.add("scalar", lambda e, i=i: e.activation(
            out=xsc[i % 2][:], in_=xnt[i % 2][:], func=AF.Copy,
            scale=c07), waits=wv, inc=(s_cbA, 1))
        wv2 = [(s_cbA, i + 1)]
        if i >= 2:
            wv2.append((s_st[i % 2], 16 * (i // 2)))
        p.add("vector", lambda e, i=i: e.tensor_tensor(
            out=s1t[i % 2][:], in0=xsc[i % 2][:], in1=op_tm[i][:],
            op=ALU.add), waits=wv2, inc=(s_cbB, 1))
        p.flush("vector")
        p.add("gpsimd", lambda e, i=i: e.dma_start(
            out=s_o[i * 128:(i + 1) * 128, :], in_=s1t[i % 2][:]),
            waits=[(s_cbB, i + 1)], inc=(s_st[i % 2], 16))
    p.raw_wait("gpsimd", s_st[0], 16 * (NT // 2))
    p.raw_wait("gpsimd", s_st[1], 16 * (NT // 2))
    p.barrier()
    ctx.free(xnt, s1t, xsc, op_tm)


def build_l1c():
    ctx = make_ctx("l1c")
    nc = ctx.nc
    io = {}
    io["xn1"] = nc.dram_tensor("xn1", [OWN, 1024], BF16,
                               kind="ExternalInput").ap()
    io["wgT"] = nc.dram_tensor("wgT", [1024, FFN1_H], I32,
                               kind="ExternalInput").ap()
    io["wuT"] = nc.dram_tensor("wuT", [1024, FFN1_H], I32,
                               kind="ExternalInput").ap()
    io["wdT"] = nc.dram_tensor("wdT", [FFN1_H, 1024], I32,
                               kind="ExternalInput").ap()
    io["x1"] = nc.dram_tensor("x1", [OWN, 1024], F32,
                              kind="ExternalInput").ap()
    io["s1"] = nc.dram_tensor("s1", [OWN, 1024], BF16,
                              kind="ExternalInput").ap()
    io["x2"] = nc.dram_tensor("x2", [OWN, 1024], F32,
                              kind="ExternalOutput").ap()
    emit_l1c(ctx, io)
    ctx.p.finalize()
    ctx.close()
    return nc


def emit_l1c(ctx, io):
    nc, p = ctx.nc, ctx.p
    xn_d, wg_d, wu_d, wd_d = io["xn1"], io["wgT"], io["wuT"], io["wdT"]
    x1_d, s1_d, x2_d = io["x1"], io["s1"], io["x2"]
    NT = OWN // 128
    QS = 1.0 / 127.0

    xnt = [ctx.sbuf("xnt", [128, 1024], BF16) for _ in range(NT)]
    s_ld = ctx.sem()
    for i in range(NT):
        p.add("sync", lambda e, i=i: e.dma_start(
            out=xnt[i][:], in_=xn_d[i * 128:(i + 1) * 128, :]),
            inc=(s_ld, 16))
    p.raw_wait("tensor", s_ld, 16 * NT)
    p.raw_wait("vector", s_ld, 16 * NT)
    p.barrier()
    xn_fm = emit_tm_to_fm(ctx, xnt, OWN, "xn")
    ctx.free(xnt)

    wd_sb = [ctx.sbuf("wd", [128, 1024], BF16) for _ in range(24)]
    wstg = [ctx.sbuf("wstg", [128, 1024], I32) for _ in range(2)]
    s_wld = [ctx.sem(), ctx.sem()]
    s_wcv = ctx.sem()
    for k in range(24):
        g = k
        w = [(s_wcv, k - 1)] if k >= 2 else []
        p.add("sync", lambda e, k=k, g=g: e.dma_start(
            out=wstg[g % 2][:], in_=wd_d[k * 128:(k + 1) * 128, :]),
            waits=w, inc=(s_wld[g % 2], 16))
        p.add("vector", lambda e, k=k, g=g: e.tensor_scalar(
            out=wd_sb[k][:], in0=wstg[g % 2][:], scalar1=QS, scalar2=None,
            op0=ALU.mult), waits=[(s_wld[g % 2], 16 * (g // 2 + 1))],
            inc=(s_wcv, 1))
    p.raw_wait("tensor", s_wcv, 24)
    p.raw_wait("sync", s_wcv, 24)
    p.barrier()

    h_fm = [ctx.sbuf("hfm", [128, 1024], BF16) for _ in range(24)]
    g1 = [ctx.sbuf("g1", [128, 1024], BF16) for _ in range(NT)]
    gu_sb = [[ctx.sbuf("gusb", [128, 128], BF16) for _ in range(16)]
             for _ in range(2)]
    GDEPTH = 8
    gstg = [ctx.sbuf("gstg", [128, 128], I32) for _ in range(GDEPTH)]
    gps = ctx.psum("gate", [128, 1024], F32)
    ups = ctx.psum("up", [128, 1024], F32)
    dps = ctx.psum("down", [128, 1024], F32)
    gsb = ctx.sbuf("gsb", [128, 1024], BF16)
    s_gld = [ctx.sem() for _ in range(8)]
    s_gcv = ctx.sem()
    s_mm = ctx.sem()
    s_svg = ctx.sem()
    s_g1 = ctx.sem()
    s_hv = ctx.sem()
    ng = [0]
    mm_at_gu = {}

    def load_gu(tt, n, slot):
        for part in range(16):
            g = ng[0]
            w = [(s_gcv, g - GDEPTH + 1)] if g >= GDEPTH else []
            src = (wg_d if part < 8 else wu_d)[
                (part % 8) * 128:(part % 8) * 128 + 128,
                n * 128:(n + 1) * 128]
            p.add("sync", lambda e, src=src, g=g: e.dma_start(
                out=gstg[g % GDEPTH][:], in_=src), waits=w,
                inc=(s_gld[g % 8], 16))
            wv2 = [(s_gld[g % 8], 16 * (g // 8 + 1))]
            gun = ng[0] // 16
            if gun >= 2:
                wv2.append((s_mm, mm_at_gu[gun - 2]))
            p.add("vector", lambda e, slot=slot, part=part, g=g:
                  e.tensor_scalar(out=gu_sb[slot][part][:],
                                  in0=gstg[g % GDEPTH][:], scalar1=QS,
                                  scalar2=None, op0=ALU.mult),
                  waits=wv2, inc=(s_gcv, 1))
            ng[0] += 1

    load_gu(0, 0, 0)
    load_gu(0, 1, 1)
    mm_end_down = [0]
    for tt in range(2):
        TT = 1024
        for n in range(24):
            gun = tt * 24 + n
            sl = gu_sb[gun % 2]
            need_cv = 16 * (gun + 1)
            for half in range(2):
                for k in range(8):
                    p.add("tensor", lambda e, k=k, half=half, tt=tt, sl=sl:
                          e.matmul(gps[:, half * 512:(half + 1) * 512],
                                   sl[k][:],
                                   xn_fm[k][:, tt * TT + half * 512:
                                            tt * TT + half * 512 + 512],
                                   start=(k == 0), stop=(k == 7)),
                          waits=[(s_gcv, need_cv), (s_svg, gun),
                                 (s_hv, max(0, 2 * gun))]
                          if (k == 0 and half == 0) else [],
                          inc=(s_mm, 1) if k == 7 else None)
                for k in range(8):
                    p.add("tensor", lambda e, k=k, half=half, tt=tt, sl=sl:
                          e.matmul(ups[:, half * 512:(half + 1) * 512],
                                   sl[8 + k][:],
                                   xn_fm[k][:, tt * TT + half * 512:
                                            tt * TT + half * 512 + 512],
                                   start=(k == 0), stop=(k == 7)),
                          inc=(s_mm, 1) if k == 7 else None)
            wv = [(s_mm, p.val(s_mm)), (s_hv, 2 * gun)]
            p.add_r("scalar", lambda e: e.activation(
                out=gsb[:], in_=gps[:], func=AF.Silu), waits=wv,
                inc=(s_svg, 1))
            wvh = [(s_svg, gun + 1)]
            if tt == 1:
                wvh.append((s_mm, mm_end_down[0]))
            p.add("vector", lambda e, n=n: e.tensor_tensor(
                out=h_fm[n][:], in0=gsb[:], in1=ups[:], op=ALU.mult),
                waits=wvh, inc=(s_hv, 2))
            mm_at_gu[gun] = p.val(s_mm)
            if gun + 2 < 48:
                nxt = gun + 2
                load_gu(nxt // 24, nxt % 24, nxt % 2)
        hv_need = p.val(s_hv)
        for tq in range(8):
            ti = tt * 8 + tq
            wv = [(s_hv, hv_need), (s_g1, ti)]
            im = 0
            for k in range(24):
                for half in range(2):
                    im += 1
                    p.add("tensor", lambda e, k=k, half=half, tq=tq:
                          e.matmul(dps[:, half * 512:(half + 1) * 512],
                                   h_fm[k][:, tq * 128:(tq + 1) * 128],
                                   wd_sb[k][:, half * 512:(half + 1) * 512],
                                   start=(k == 0), stop=(k == 23)),
                          waits=wv if im == 1 else [],
                          inc=(s_mm, 1) if (k == 23 and half == 1) else None)
            p.add_r("scalar", lambda e, ti=ti: e.activation(
                out=g1[ti][:], in_=dps[:], func=AF.Silu),
                waits=[(s_mm, p.val(s_mm))], inc=(s_g1, 1))
        mm_end_down[0] = p.val(s_mm)
    p.barrier()
    ctx.free(gsb, dps, ups, gps, gstg, gu_sb, h_fm, wstg, wd_sb)

    xs = [ctx.sbuf("xs", [128, 1024], F32) for _ in range(2)]
    ss = [ctx.sbuf("ss", [128, 1024], BF16) for _ in range(2)]
    x2t = [ctx.sbuf("x2t", [128, 1024], F32) for _ in range(2)]
    s_xl = [ctx.sem(), ctx.sem()]
    s_cb = ctx.sem()
    s_st2 = [ctx.sem(swdge=True), ctx.sem(swdge=True)]
    for i in range(NT):
        wvl = [(s_cb, 2 * (i - 2) + 2)] if i >= 2 else []
        p.add("sync", lambda e, i=i: e.dma_start(
            out=xs[i % 2][:], in_=x1_d[i * 128:(i + 1) * 128, :]),
            waits=wvl, inc=(s_xl[i % 2], 16))
        p.add("sync", lambda e, i=i: e.dma_start(
            out=ss[i % 2][:], in_=s1_d[i * 128:(i + 1) * 128, :]),
            inc=(s_xl[i % 2], 16))
        wv = [(s_xl[i % 2], 32 * (i // 2 + 1))]
        if i >= 2:
            wv.append((s_st2[i % 2], 16 * (i // 2)))
        p.add("vector", lambda e, i=i: e.tensor_tensor(
            out=x2t[i % 2][:], in0=ss[i % 2][:], in1=g1[i][:],
            op=ALU.mult), waits=wv, inc=(s_cb, 1))
        p.flush("vector")
        p.add("vector", lambda e, i=i: e.tensor_tensor(
            out=x2t[i % 2][:], in0=x2t[i % 2][:], in1=xs[i % 2][:],
            op=ALU.add), inc=(s_cb, 1))
        p.flush("vector")
        p.add("gpsimd", lambda e, i=i: e.dma_start(
            out=x2_d[i * 128:(i + 1) * 128, :], in_=x2t[i % 2][:]),
            waits=[(s_cb, 2 * i + 2)], inc=(s_st2[i % 2], 16))
    p.raw_wait("gpsimd", s_st2[0], 16 * (NT // 2))
    p.raw_wait("gpsimd", s_st2[1], 16 * (NT // 2))
    p.barrier()
    ctx.free(x2t, ss, xs, g1, xn_fm)


def build_fused_a():
    ctx = make_ctx("fuseda")
    nc, p = ctx.nc, ctx.p
    E = lambda n, sh, dt: nc.dram_tensor(n, sh, dt, kind="ExternalInput").ap()
    I = lambda n, sh, dt: nc.dram_tensor(n, sh, dt, kind="Internal").ap()
    O = lambda n, sh, dt: nc.dram_tensor(n, sh, dt,
                                         kind="ExternalOutput").ap()
    io0 = {
        "x_ext": E("x_ext", [EXT, 1024], F32),
        "wq": E("wq", [1024, 1024], F32), "wk": E("wk", [1024, 1024], F32),
        "wv": E("wv", [1024, 1024], F32), "wi": E("wi", [1024, 1024], F32),
        "wiT": E("wiT", [1024, 1024], F32),
        "bq": E("bq", [1, 1024], F32), "bk": E("bk", [1, 1024], F32),
        "bv": E("bv", [1, 1024], F32), "bi": E("bi", [1, 1024], F32),
        "wg": E("wg", [1024, 8], F32), "bg": E("bg", [1, 8], F32),
        "b_in_col": E("b_in_col", [1024, 1], F32),
        "q": I("q_s", [8, 128, OWN], BF16),
        "k": I("k_s", [8, 128, EXT], BF16),
        "v": I("v_s", [EXT, 1040], BF16),
        "t": I("t_s", [8, 128, OWN], BF16),
        "route": I("route_s", [OWN, 8], F32),
    }
    emit_l0a(ctx, io0, ext_kv=True)
    p.barrier()
    io1 = {
        "q": io0["q"], "k": io0["k"], "v": io0["v"],
        "posT": E("posT", [16, 512, 512], F32),
        "slot_w": E("slot_w", [128, 9], F32),
        "invcnt": E("invcnt", [128, OWN // 128], F32),
        "wo": E("wo", [1024, 1024], F32), "bo": E("bo", [1, 1024], F32),
        "s0": I("s0_s", [OWN, 1024], BF16),
    }
    emit_l0b(ctx, io1)
    p.barrier()
    io2 = {
        "t": io0["t"], "route": io0["route"],
        "w1": E("w1", [8, 1024, 4096], F32), "b1": E("b1", [8, 4096], F32),
        "w2": E("w2", [8, 4096, 1024], F32), "b2": E("b2", [8, 1024], F32),
        "wout": E("wout", [1024, 1024], F32), "bout": E("bout", [1, 1024],
                                                        F32),
        "s0": io1["s0"], "x": io0["x_ext"][HALO:HALO + OWN, :],
        "x1": O("x1", [OWN, 1024], F32),
    }
    emit_l0c(ctx, io2)
    p.barrier()
    io3 = {
        "x1": io2["x1"],
        "wqT": E("wqT", [1024, 1024], F32),
        "wkT": E("wkT", [1024, 1024], F32),
        "wvT": E("wvT", [1024, 1024], F32),
        "gain": E("gain", [1, 1], F32),
        "xn1": O("xn1", [OWN, 1024], BF16),
        "v1": O("v1", [8, 128, OWN], BF16),
        "gram": O("gram", [128, 512], F32),
    }
    emit_l1a(ctx, io3)
    p.finalize()
    ctx.close()
    return nc


def build_fused_b():
    ctx = make_ctx("fusedb")
    nc, p = ctx.nc, ctx.p
    E = lambda n, sh, dt: nc.dram_tensor(n, sh, dt, kind="ExternalInput").ap()
    I = lambda n, sh, dt: nc.dram_tensor(n, sh, dt, kind="Internal").ap()
    io4 = {
        "grams": E("grams", [4, 128, 512], F32),
        "v1": E("v1", [8, 128, OWN], BF16),
        "xn1": E("xn1", [OWN, 1024], BF16),
        "woT": E("woT", [1024, 1024], F32),
        "gain": E("gain", [1, 1], F32),
        "s1": I("s1_s", [OWN, 1024], BF16),
    }
    emit_l1b(ctx, io4, npart=4)
    p.barrier()
    io5 = {
        "xn1": io4["xn1"],
        "wgT": E("wgT", [1024, FFN1_H], I32),
        "wuT": E("wuT", [1024, FFN1_H], I32),
        "wdT": E("wdT", [FFN1_H, 1024], I32),
        "x1": E("x1", [OWN, 1024], F32), "s1": io4["s1"],
        "x2": nc.dram_tensor("x2", [OWN, 1024], F32,
                             kind="ExternalOutput").ap(),
    }
    emit_l1c(ctx, io5)
    p.finalize()
    ctx.close()
    return nc


def build_fused():
    ctx = make_ctx("fused")
    nc, p = ctx.nc, ctx.p
    E = lambda n, sh, dt: nc.dram_tensor(n, sh, dt, kind="ExternalInput").ap()
    I = lambda n, sh, dt: nc.dram_tensor(n, sh, dt, kind="Internal").ap()
    io0 = {
        "x_ext": E("x_ext", [EXT, 1024], F32),
        "wq": E("wq", [1024, 1024], F32), "wk": E("wk", [1024, 1024], F32),
        "wv": E("wv", [1024, 1024], F32), "wi": E("wi", [1024, 1024], F32),
        "wiT": E("wiT", [1024, 1024], F32),
        "bq": E("bq", [1, 1024], F32), "bk": E("bk", [1, 1024], F32),
        "bv": E("bv", [1, 1024], F32), "bi": E("bi", [1, 1024], F32),
        "wg": E("wg", [1024, 8], F32), "bg": E("bg", [1, 8], F32),
        "b_in_col": E("b_in_col", [1024, 1], F32),
        "q": I("q_s", [8, 128, OWN], BF16),
        "k": I("k_s", [8, 128, EXT], BF16),
        "v": I("v_s", [EXT, 1040], BF16),
        "t": I("t_s", [8, 128, OWN], BF16),
        "route": I("route_s", [OWN, 8], F32),
    }
    emit_l0a(ctx, io0, ext_kv=True)
    p.barrier()
    io1 = {
        "q": io0["q"], "k": io0["k"], "v": io0["v"],
        "posT": E("posT", [16, 512, 512], F32),
        "slot_w": E("slot_w", [128, 9], F32),
        "invcnt": E("invcnt", [128, OWN // 128], F32),
        "wo": E("wo", [1024, 1024], F32), "bo": E("bo", [1, 1024], F32),
        "s0": I("s0_s", [OWN, 1024], BF16),
    }
    emit_l0b(ctx, io1)
    p.barrier()
    io2 = {
        "t": io0["t"], "route": io0["route"],
        "w1": E("w1", [8, 1024, 4096], F32), "b1": E("b1", [8, 4096], F32),
        "w2": E("w2", [8, 4096, 1024], F32), "b2": E("b2", [8, 1024], F32),
        "wout": E("wout", [1024, 1024], F32), "bout": E("bout", [1, 1024],
                                                        F32),
        "s0": io1["s0"], "x": io0["x_ext"][HALO:HALO + OWN, :],
        "x1": I("x1_s", [OWN, 1024], F32),
    }
    emit_l0c(ctx, io2)
    p.barrier()
    gram_in = I("gram_in", [128, 512], F32)
    gram_out = I("gram_out", [128, 512], F32)
    io3 = {
        "x1": io2["x1"],
        "wqT": E("wqT", [1024, 1024], F32),
        "wkT": E("wkT", [1024, 1024], F32),
        "wvT": E("wvT", [1024, 1024], F32),
        "gain": E("gain", [1, 1], F32),
        "xn1": I("xn1_s", [OWN, 1024], BF16),
        "v1": I("v1_s", [8, 128, OWN], BF16),
        "gram": gram_in,
    }
    emit_l1a(ctx, io3)
    p.barrier()
    s_cc = ctx.sem()
    p.add("gpsimd", lambda e: e.collective_compute(
        "AllReduce", ALU.add, [[0, 1, 2, 3], [4, 5, 6, 7]],
        [gram_in], [gram_out]), inc=(s_cc, 16))
    p.raw_wait("gpsimd", s_cc, 16)
    p.barrier()
    io4 = {
        "grams": gram_out, "v1": io3["v1"], "xn1": io3["xn1"],
        "woT": E("woT", [1024, 1024], F32), "gain": io3["gain"],
        "s1": I("s1_s", [OWN, 1024], BF16),
    }
    emit_l1b(ctx, io4, npart=1)
    p.barrier()
    io5 = {
        "xn1": io3["xn1"],
        "wgT": E("wgT", [1024, FFN1_H], I32),
        "wuT": E("wuT", [1024, FFN1_H], I32),
        "wdT": E("wdT", [FFN1_H, 1024], I32),
        "x1": io2["x1"], "s1": io4["s1"],
        "x2": nc.dram_tensor("x2", [OWN, 1024], F32,
                             kind="ExternalOutput").ap(),
    }
    emit_l1c(ctx, io5)
    p.finalize()
    ctx.close()
    return nc


def _core_slices():
    return [(c // 4, (c % 4) * OWN) for c in range(NCORE)]


def _ext_idx(s0):
    return np.clip(np.arange(s0 - HALO, s0 + OWN + HALO), 0, S - 1)


def run_l0a(inp):
    x = np.asarray(inp["L0_attn_hidden_states"])
    nc = build_l0a()
    in_maps = []
    shared = {
        "wq": np.asarray(inp["L0_attn_Wq"]),
        "wk": np.asarray(inp["L0_attn_Wk"]),
        "wv": np.asarray(inp["L0_attn_Wv"]),
        "wi": np.asarray(inp["L0_ffn_W_in"]),
        "bq": np.asarray(inp["L0_attn_bq"])[None],
        "bk": np.asarray(inp["L0_attn_bk"])[None],
        "bv": np.asarray(inp["L0_attn_bv"])[None],
        "bi": np.asarray(inp["L0_ffn_b_in"])[None],
        "wg": np.asarray(inp["L0_ffn_W_gate"]),
        "bg": np.asarray(inp["L0_ffn_b_gate"])[None],
        "wiT": np.ascontiguousarray(np.asarray(inp["L0_ffn_W_in"]).T),
        "b_in_col": np.ascontiguousarray(
            np.asarray(inp["L0_ffn_b_in"]).reshape(1024, 1)),
    }
    for (b, s0) in _core_slices():
        m = dict(shared)
        m["x_ext"] = np.ascontiguousarray(x[b][_ext_idx(s0)])
        in_maps.append(m)
    return run_bass_kernel_spmd(nc, in_maps, core_ids=list(range(NCORE)))


def _slot_w(s0):
    w = np.zeros((128, 9), np.float32)
    for j in range(9):
        sg = s0 - 256 + 256 * j
        if 0 <= sg <= S - W:
            w[:, j] = 1.0
    if s0 + OWN == S:
        w[:, 7] = 2.0
        w[:, 8] = 0.0
    return w


def _invcnt(s0):
    counts = np.full(S, 2.0, np.float32)
    counts[:256] = 1.0
    counts[7680:7936] = 3.0
    counts[7936:] = 2.0
    sl = counts[s0:s0 + OWN]
    return np.ascontiguousarray(1.0 / sl.reshape(OWN // 128, 128).T)


def run_l0b(per_core, shared):
    nc = build_l0b()
    in_maps = []
    for c, m in enumerate(per_core):
        d = dict(shared)
        d.update(m)
        in_maps.append(d)
    return run_bass_kernel_spmd(nc, in_maps, core_ids=list(range(NCORE)))


def kernel(**inp):
    inp = {k: np.asarray(v) for k, v in inp.items()}
    x = inp["L0_attn_hidden_states"]
    cores = _core_slices()
    R = list(range(NCORE))
    nc_a = build_fused_a()
    nc_b = build_fused_b()
    gainv = np.asarray(inp["L1_attn_gain"], np.float32).reshape(1, 1)
    shared = {
        "wq": inp["L0_attn_Wq"], "wk": inp["L0_attn_Wk"],
        "wv": inp["L0_attn_Wv"], "wi": inp["L0_ffn_W_in"],
        "bq": inp["L0_attn_bq"][None], "bk": inp["L0_attn_bk"][None],
        "bv": inp["L0_attn_bv"][None], "bi": inp["L0_ffn_b_in"][None],
        "wg": inp["L0_ffn_W_gate"], "bg": inp["L0_ffn_b_gate"][None],
        "wiT": np.ascontiguousarray(inp["L0_ffn_W_in"].T),
        "b_in_col": np.ascontiguousarray(
            inp["L0_ffn_b_in"].reshape(1024, 1)),
        "posT": np.ascontiguousarray(
            inp["L0_attn_pos_bias"].transpose(0, 2, 1)),
        "wo": inp["L0_attn_Wo"], "bo": inp["L0_attn_bo"][None],
        "w1": inp["L0_ffn_W1"], "b1": inp["L0_ffn_b1"],
        "w2": inp["L0_ffn_W2"], "b2": inp["L0_ffn_b2"],
        "wout": inp["L0_ffn_W_out"], "bout": inp["L0_ffn_b_out"][None],
        "wqT": np.ascontiguousarray(inp["L1_attn_w_q"].T),
        "wkT": np.ascontiguousarray(inp["L1_attn_w_k"].T),
        "wvT": np.ascontiguousarray(inp["L1_attn_w_v"].T),
        "woT": np.ascontiguousarray(inp["L1_attn_w_o"].T),
        "gain": gainv,
        "wgT": np.ascontiguousarray(inp["L1_ffn_w_gate"].T),
        "wuT": np.ascontiguousarray(inp["L1_ffn_w_up"].T),
        "wdT": np.ascontiguousarray(inp["L1_ffn_w_down"].T),
    }
    akeys = {"wq", "wk", "wv", "wi", "wiT", "bq", "bk", "bv", "bi", "wg",
             "bg", "b_in_col", "posT", "wo", "bo", "w1", "b1", "w2", "b2",
             "wout", "bout", "wqT", "wkT", "wvT", "gain"}
    shared_a = {k_: v_ for k_, v_ in shared.items() if k_ in akeys}
    maps = []
    for (b, s0) in cores:
        m = dict(shared_a)
        m["x_ext"] = np.ascontiguousarray(x[b][_ext_idx(s0)])
        m["slot_w"] = _slot_w(s0)
        m["invcnt"] = _invcnt(s0)
        maps.append(m)
    ra = run_bass_kernel_spmd(nc_a, maps, core_ids=R).results
    bkeys = {"woT", "gain", "wgT", "wuT", "wdT"}
    shared_b = {k_: v_ for k_, v_ in shared.items() if k_ in bkeys}
    maps = []
    for c, (b, s0) in enumerate(cores):
        m = dict(shared_b)
        m["grams"] = np.stack([ra[b * 4 + j]["gram"] for j in range(4)])
        m["v1"] = ra[c]["v1"]
        m["xn1"] = ra[c]["xn1"]
        m["x1"] = ra[c]["x1"]
        maps.append(m)
    rb = run_bass_kernel_spmd(nc_b, maps, core_ids=R).results
    out = np.zeros((B, S, 1024), np.float32)
    for c, (b, s0) in enumerate(cores):
        out[b, s0:s0 + OWN] = rb[c]["x2"]
    return out


def kernel_unfused(**inp):
    inp = {k: np.asarray(v) for k, v in inp.items()}
    x = inp["L0_attn_hidden_states"]
    cores = _core_slices()
    R = list(range(NCORE))

    nc = build_l0a()
    shared = {
        "wq": inp["L0_attn_Wq"], "wk": inp["L0_attn_Wk"],
        "wv": inp["L0_attn_Wv"], "wi": inp["L0_ffn_W_in"],
        "bq": inp["L0_attn_bq"][None], "bk": inp["L0_attn_bk"][None],
        "bv": inp["L0_attn_bv"][None], "bi": inp["L0_ffn_b_in"][None],
        "wg": inp["L0_ffn_W_gate"], "bg": inp["L0_ffn_b_gate"][None],
        "wiT": np.ascontiguousarray(inp["L0_ffn_W_in"].T),
        "b_in_col": np.ascontiguousarray(
            inp["L0_ffn_b_in"].reshape(1024, 1)),
    }
    maps = []
    for (b, s0) in cores:
        m = dict(shared)
        m["x_ext"] = np.ascontiguousarray(x[b][_ext_idx(s0)])
        maps.append(m)
    r = run_bass_kernel_spmd(nc, maps, core_ids=R).results
    q_c = [r[c]["q"] for c in range(NCORE)]
    t_c = [r[c]["t"] for c in range(NCORE)]
    route_c = [r[c]["route"] for c in range(NCORE)]
    k_full = np.zeros((B, 8, 128, S), BF)
    v_full = np.zeros((B, S, 1040), BF)
    for c, (b, s0) in enumerate(cores):
        k_full[b][:, :, s0:s0 + OWN] = r[c]["k"]
        v_full[b][s0:s0 + OWN] = r[c]["v"]

    nc = build_l0b()
    shared = {"posT": np.ascontiguousarray(
        inp["L0_attn_pos_bias"].transpose(0, 2, 1)),
        "wo": inp["L0_attn_Wo"], "bo": inp["L0_attn_bo"][None]}
    maps = []
    for c, (b, s0) in enumerate(cores):
        idx = _ext_idx(s0)
        m = dict(shared)
        m["q"] = q_c[c]
        m["k"] = np.ascontiguousarray(k_full[b][:, :, idx])
        m["v"] = np.ascontiguousarray(v_full[b][idx])
        m["slot_w"] = _slot_w(s0)
        m["invcnt"] = _invcnt(s0)
        maps.append(m)
    r = run_bass_kernel_spmd(nc, maps, core_ids=R).results
    s0_c = [r[c]["s0"] for c in range(NCORE)]

    nc = build_l0c()
    shared = {"w1": inp["L0_ffn_W1"], "b1": inp["L0_ffn_b1"],
              "w2": inp["L0_ffn_W2"], "b2": inp["L0_ffn_b2"],
              "wout": inp["L0_ffn_W_out"], "bout": inp["L0_ffn_b_out"][None]}
    maps = []
    for c, (b, s0) in enumerate(cores):
        m = dict(shared)
        m["t"] = t_c[c]
        m["route"] = route_c[c]
        m["s0"] = s0_c[c]
        m["x"] = np.ascontiguousarray(x[b][s0:s0 + OWN])
        maps.append(m)
    r = run_bass_kernel_spmd(nc, maps, core_ids=R).results
    x1_c = [r[c]["x1"] for c in range(NCORE)]

    nc = build_l1a()
    gainv = np.asarray(inp["L1_attn_gain"], np.float32).reshape(1, 1)
    shared = {"wqT": np.ascontiguousarray(inp["L1_attn_w_q"].T),
              "wkT": np.ascontiguousarray(inp["L1_attn_w_k"].T),
              "wvT": np.ascontiguousarray(inp["L1_attn_w_v"].T),
              "gain": gainv}
    maps = []
    for c, (b, s0) in enumerate(cores):
        m = dict(shared)
        m["x1"] = x1_c[c]
        maps.append(m)
    r = run_bass_kernel_spmd(nc, maps, core_ids=R).results
    xn1_c = [r[c]["xn1"] for c in range(NCORE)]
    v1_c = [r[c]["v1"] for c in range(NCORE)]
    gram_c = [r[c]["gram"] for c in range(NCORE)]

    nc = build_l1b()
    shared = {"woT": np.ascontiguousarray(inp["L1_attn_w_o"].T),
              "gain": gainv}
    maps = []
    for c, (b, s0) in enumerate(cores):
        m = dict(shared)
        m["grams"] = np.stack([gram_c[b * 4 + j] for j in range(4)])
        m["v1"] = v1_c[c]
        m["xn1"] = xn1_c[c]
        maps.append(m)
    r = run_bass_kernel_spmd(nc, maps, core_ids=R).results
    s1_c = [r[c]["s1"] for c in range(NCORE)]

    nc = build_l1c()
    shared = {"wgT": np.ascontiguousarray(inp["L1_ffn_w_gate"].T),
              "wuT": np.ascontiguousarray(inp["L1_ffn_w_up"].T),
              "wdT": np.ascontiguousarray(inp["L1_ffn_w_down"].T)}
    maps = []
    for c, (b, s0) in enumerate(cores):
        m = dict(shared)
        m["xn1"] = xn1_c[c]
        m["x1"] = x1_c[c]
        m["s1"] = s1_c[c]
        maps.append(m)
    r = run_bass_kernel_spmd(nc, maps, core_ids=R).results

    out = np.zeros((B, S, 1024), np.float32)
    for c, (b, s0) in enumerate(cores):
        out[b, s0:s0 + OWN] = r[c]["x2"]
    return out


if __name__ == "__main__":
    pass


# version 08
# speedup vs baseline: 1.1765x; 1.0175x over previous
import math
import numpy as np
import ml_dtypes
import concourse.bass as bass
import concourse.mybir as mybir
from concourse.bass_utils import run_bass_kernel_spmd

F32 = mybir.dt.float32
BF16 = mybir.dt.bfloat16
I32 = mybir.dt.int32
AF = mybir.ActivationFunctionType
ALU = mybir.AluOpType
AX = mybir.AxisListType
BF = ml_dtypes.bfloat16

B, S, D = 2, 8192, 1024
OWN = 2048
HALO = 256
EXT = OWN + 2 * HALO
NCORE = 8
H, W, STRIDE = 16, 512, 256
HD = 64
N_EXP, D_MOE = 8, 4096
FFN1_H = 3072
EPS_LN = 1e-5
EPS_MP = 1e-4
TRUNC_L0B = False


class SemW:

    __slots__ = ("h", "base")

    def __init__(self, h, base):
        self.h = h
        self.base = base


class Pl:
    ENGS = ("sync", "vector", "scalar", "tensor", "gpsimd")

    def __init__(self, nc):
        self.nc = nc
        self.recs = {e: [] for e in self.ENGS}
        self.cnt = {}
        self.bar_sem = None
        self.rsync = None
        self.junk_ap = None

    def _h(self, s):
        return (s.h, s.base) if isinstance(s, SemW) else (s, 0)

    def val(self, s):
        h, base = self._h(s)
        return self.cnt.get(id(h), 0) - base

    def bump(self, s, n):
        h, _ = self._h(s)
        self.cnt[id(h)] = self.cnt.get(id(h), 0) + n
        return self.cnt[id(h)]

    def _tw(self, waits):
        out = []
        for sm, v in waits:
            h, base = self._h(sm)
            out.append((h, base + v))
        return out

    def _ti(self, inc):
        if inc is None:
            return None
        h, _ = self._h(inc[0])
        return (h, inc[1])

    def add(self, eng, build, waits=(), inc=None):
        if inc is not None:
            self.bump(inc[0], inc[1])
        self.recs[eng].append({"build": build, "waits": self._tw(waits),
                               "inc": self._ti(inc)})

    def raw_wait(self, eng, sm, v):
        self.recs[eng].append({"build": None, "waits": self._tw([(sm, v)]),
                               "inc": None})

    def add_r(self, eng, build, waits=(), inc=None):
        rs = self.rsync[eng]
        self.add(eng, build, waits=waits, inc=(rs, 1))
        self.raw_wait(eng, rs, self.val(rs))
        if inc is not None:
            self.bump(inc[0], inc[1])
            self.recs[eng].append({"build": None, "waits": [],
                                   "inc": None,
                                   "postinc": self._ti(inc)})

    def flush(self, eng):
        rs = self.rsync[eng]
        recs = self.recs[eng]
        prev = None
        for r in reversed(recs):
            if r["build"] is not None:
                prev = r
                break
        if prev is None:
            return
        if prev["inc"] is None:
            prev["inc"] = (rs, 1)
            self.bump(rs, 1)
            self.raw_wait(eng, rs, self.val(rs))
        else:
            self.raw_wait(eng, prev["inc"][0], self.val(prev["inc"][0]))

    def add_red(self, eng, build, waits=(), inc=None):
        self.flush(eng)
        self.add_r(eng, build, waits=waits, inc=inc)

    def barrier(self):
        b = self.bar_sem
        ja = self.junk_ap
        for _ in range(5):
            self.bump(b, 1)
        tgt = self.val(b)
        self.recs["vector"].append({
            "build": lambda eo: eo.memset(ja[:, 0:1], 0.0),
            "waits": [], "inc": (b, 1), "postwait": (b, tgt)})
        self.recs["scalar"].append({
            "build": lambda eo: eo.memzero(ja[:, 1:2]),
            "waits": [], "inc": (b, 1), "postwait": (b, tgt)})
        self.recs["tensor"].append({
            "build": lambda eo: eo.nop(),
            "waits": [], "inc": (b, 1), "postwait": (b, tgt)})
        for e in ("sync", "gpsimd"):
            self.recs[e].append({
                "build": None, "waits": [], "inc": None,
                "postinc": (b, 1), "postwait": (b, tgt)})

    def _run(self, eng, eo):
        for r in self.recs[eng]:
            for sm, v in r["waits"]:
                eo.wait_ge(sm, v)
            if r["build"] is not None:
                ins = r["build"](eo)
                if r["inc"] is not None:
                    ins.then_inc(r["inc"][0], r["inc"][1])
            if "postinc" in r:
                eo.sem_inc(r["postinc"][0], r["postinc"][1])
            if "postwait" in r:
                eo.wait_ge(r["postwait"][0], r["postwait"][1])

    def finalize(self):
        with self.nc.Block() as block:
            @block.sync
            def _(eng):
                self._run("sync", eng)

            @block.vector
            def _(eng):
                self._run("vector", eng)

            @block.scalar
            def _(eng):
                self._run("scalar", eng)

            @block.tensor
            def _(eng):
                self._run("tensor", eng)

            @block.gpsimd
            def _(eng):
                self._run("gpsimd", eng)


def seg(ap2d, e):
    return ap2d.rearrange("p (s e) -> p s e", e=e)


def bcast(ap2d, e):
    return bass.AP(tensor=ap2d.tensor, offset=ap2d.offset,
                   ap=[list(ap2d.ap[0]), list(ap2d.ap[1]), [0, e]])


class Ctx:

    def __init__(self, nc):
        self.nc = nc
        self.p = Pl(nc)
        self.exits = []
        self.sems = {}
        self.uid = 0
        base = (nc.sbuf_base + 63) & ~63
        self.freelist = [(base, nc.sbuf_top - base)]
        self.sizes = {}
        self.hiwater = 0

    def _name(self, base):
        self.uid += 1
        return f"{base}_{self.uid}"

    SEM_POOL = 78

    def sem(self, name=None, swdge=False):
        if swdge:
            cm = self.nc.semaphore(self._name("swdge"))
            h = cm.__enter__()
            self.exits.append((None, cm))
            return h
        if name is None:
            self.anon_cnt = getattr(self, "anon_cnt", 0)
            self.anon_pool = getattr(self, "anon_pool", [])
            if self.anon_cnt >= self.SEM_POOL:
                h = self.anon_pool[self.anon_cnt % self.SEM_POOL]
                self.anon_cnt += 1
                return SemW(h, self.p.val(h))
            name = self._name("sem")
            cm = self.nc.semaphore(self._name(name))
            h = cm.__enter__()
            self.exits.append((None, cm))
            self.anon_pool.append(h)
            self.anon_cnt += 1
            return h
        if name not in self.sems:
            cm = self.nc.semaphore(self._name(name))
            self.sems[name] = cm.__enter__()
            self.exits.append((None, cm))
        return self.sems[name]

    def sbuf(self, name, shape, dtype):
        nb = int(np.prod(shape[1:])) * mybir.dt.size(dtype)
        nb = (nb + 63) & ~63
        for i, (off, sz) in enumerate(self.freelist):
            if sz >= nb:
                if sz == nb:
                    self.freelist.pop(i)
                else:
                    self.freelist[i] = (off + nb, sz - nb)
                t = self.nc.alloc_sbuf_tensor_at(
                    self._name(name), shape, dtype, offset=off)
                self.sizes[id(t)] = (off, nb)
                used = self.nc.sbuf_top - sum(s for _, s in self.freelist) \
                    - ((self.nc.sbuf_base + 63) & ~63)
                self.hiwater = max(self.hiwater, used)
                return t
        raise MemoryError(
            f"sbuf arena out of space for {name} {shape} ({nb}B); "
            f"free={[s for _, s in self.freelist]}")

    def psum(self, name, shape, dtype=F32):
        cm = self.nc.psum_tensor(self._name(name), shape, dtype)
        t = cm.__enter__()
        self.exits.append((id(t), cm))
        return t

    def ccol(self, val):
        val = float(val)
        if val not in self.cmap:
            assert len(self.cvals) < self.cpool.shape[1]
            self.cmap[val] = len(self.cvals)
            self.cvals.append(val)
        return self.cpool[:, self.cmap[val]:self.cmap[val] + 1]

    def free(self, *tensors):
        for t in tensors:
            if isinstance(t, (list, tuple)):
                self.free(*reversed(t))
                continue
            key = id(t)
            if key in self.sizes:
                off, nb = self.sizes.pop(key)
                self.freelist.append((off, nb))
                self.freelist.sort()
                merged = []
                for o, s in self.freelist:
                    if merged and merged[-1][0] + merged[-1][1] == o:
                        merged[-1] = (merged[-1][0], merged[-1][1] + s)
                    else:
                        merged.append((o, s))
                self.freelist = [tuple(x) for x in merged]
                continue
            for i, (k, cm) in enumerate(self.exits):
                if k == key:
                    cm.__exit__(None, None, None)
                    self.exits.pop(i)
                    break
            else:
                raise KeyError("free: tensor not found")

    def close(self):
        for _, cm in reversed(self.exits):
            cm.__exit__(None, None, None)
        self.exits = []


def make_ctx(name):
    nc = bass.Bass("TRN2", target_bir_lowering=False)
    ctx = Ctx(nc)
    ctx.p.bar_sem = ctx.sem("bar")
    ctx.p.rsync = {e: ctx.sem(f"rsync_{e}")
                   for e in ("sync", "vector", "scalar", "tensor",
                             "gpsimd")}
    ident = nc.inline_tensor(np.eye(128, dtype=BF), name=ctx._name("identbf")).ap()
    ident32 = nc.inline_tensor(np.eye(128, dtype=np.float32),
                               name=ctx._name("identf32")).ap()
    ones = nc.inline_tensor(np.ones((1, 512), dtype=BF), name=ctx._name("onesbf")).ap()
    ctx.ident_sb = ctx.sbuf("ident_sb", [128, 128], BF16)
    ctx.ident_f32 = ctx.sbuf("ident_f32", [128, 128], F32)
    ctx.ones_sb = ctx.sbuf("ones_sb", [1, 512], BF16)
    ld = ctx.sem("ldconst")
    ctx.p.add("sync", lambda e: e.dma_start(out=ctx.ident_sb[:], in_=ident),
              inc=(ld, 16))
    ctx.p.add("sync", lambda e: e.dma_start(out=ctx.ident_f32[:], in_=ident32),
              inc=(ld, 16))
    ctx.p.add("sync", lambda e: e.dma_start(out=ctx.ones_sb[:], in_=ones),
              inc=(ld, 16))
    ctx.cvals, ctx.cmap = [], {}
    ctx.cpool = ctx.sbuf("cpool", [128, 32], F32)
    ctx.junk = ctx.sbuf("junk", [128, 4], F32)
    ctx.p.junk_ap = ctx.junk

    def _ld_cpool(e):
        vals = ctx.cvals or [0.0]
        arr = np.tile(np.asarray(vals, np.float32), (128, 1))
        return e.dma_start(out=ctx.cpool[:, 0:len(vals)],
                           in_=nc.inline_tensor(arr, name=ctx._name("cpool")).ap())
    ctx.p.add("sync", _ld_cpool, inc=(ld, 16))
    ctx.p.raw_wait("tensor", ld, 64)
    ctx.p.raw_wait("vector", ld, 64)
    ctx.p.raw_wait("scalar", ld, 64)
    ctx.p.barrier()
    return ctx


def load_weight_bf16(ctx, name, dram_ap, kin, nout, scale=None):
    p = ctx.p
    nk = (kin + 127) // 128
    ND = 4
    stage = [ctx.sbuf(f"wst_{name}", [128, nout], F32) for _ in range(ND)]
    out = [ctx.sbuf(f"w_{name}{k}", [128, nout], BF16) for k in range(nk)]
    s_lds = [ctx.sem() for _ in range(ND)]
    s_cv = ctx.sem()
    for k in range(nk):
        buf = stage[k % ND]
        s_ld = s_lds[k % ND]
        w = [(s_cv, k - ND + 1)] if k >= ND else []
        p.add("sync", lambda e, buf=buf, k=k: e.dma_start(
            out=buf[:], in_=dram_ap[k * 128:(k + 1) * 128, :]),
            waits=w, inc=(s_ld, 16))
        wv = [(s_ld, 16 * (k // ND + 1))]
        if scale is None:
            p.add("vector", lambda e, k=k, buf=buf: e.tensor_copy(
                out=out[k][:], in_=buf[:]), waits=wv, inc=(s_cv, 1))
        else:
            p.add("vector", lambda e, k=k, buf=buf: e.tensor_scalar(
                out=out[k][:], in0=buf[:], scalar1=scale, scalar2=None,
                op0=ALU.mult), waits=wv, inc=(s_cv, 1))
    p.raw_wait("tensor", s_cv, nk)
    p.raw_wait("scalar", s_cv, nk)
    p.raw_wait("sync", s_cv, nk)
    p.raw_wait("gpsimd", s_cv, nk)
    ctx.free(stage)
    return out


def load_bias_bf16(ctx, name, dram_ap, n, scale=None):
    p = ctx.p
    stage = ctx.sbuf(f"bst_{name}", [1, n], F32)
    out = ctx.sbuf(f"b_{name}", [1, n], BF16)
    s_ld = ctx.sem()
    s_cv = ctx.sem()
    p.add("sync", lambda e: e.dma_start(out=stage[:], in_=dram_ap),
          inc=(s_ld, 16))
    if scale is None:
        p.add("vector", lambda e: e.tensor_copy(out=out[:], in_=stage[:]),
              waits=[(s_ld, 16)], inc=(s_cv, 1))
    else:
        p.add("vector", lambda e: e.tensor_scalar(
            out=out[:], in0=stage[:], scalar1=scale, scalar2=None,
            op0=ALU.mult), waits=[(s_ld, 16)], inc=(s_cv, 1))
    p.raw_wait("tensor", s_cv, 1)
    p.raw_wait("sync", s_cv, 1)
    p.raw_wait("gpsimd", s_cv, 1)
    ctx.free(stage)
    return out


def emit_ln_to_fm(ctx, x_dram, nrows, tag, keep_tm=False, keep_stats=False):
    p = ctx.p
    nt = nrows // 128
    xt = [ctx.sbuf(f"lnx_{tag}", [128, 1024], F32) for _ in range(nt)]
    fm = [ctx.sbuf(f"xnfm_{tag}{c}", [128, nrows], BF16) for c in range(8)]
    nmr = ctx.sbuf(f"lnnmr_{tag}", [128, nt], F32)
    nm = ctx.sbuf(f"lnnm_{tag}", [128, nt], F32)
    ssq = ctx.sbuf(f"lnssq_{tag}", [128, nt], F32)
    rstd_a = ctx.sbuf(f"lnrstda_{tag}", [128, nt], F32)
    sq = [ctx.sbuf(f"lnsq_{tag}", [128, 1024], F32) for _ in range(2)]
    rstd = ctx.sbuf(f"lnrstd_{tag}", [128, nt], F32)
    if keep_tm:
        tmb = [ctx.sbuf(f"lntm_{tag}", [128, 1024], BF16) for _ in range(nt)]
    else:
        tmb = [ctx.sbuf(f"lntm_{tag}", [128, 1024], BF16) for _ in range(2)]
    tp = [ctx.psum(f"lntp_{tag}", [128, 128], BF16) for _ in range(2)]
    s_ld = ctx.sem()
    s_st = ctx.sem()
    s_sq = ctx.sem()
    s_no = ctx.sem()
    s_tp = ctx.sem()
    s_ev = ctx.sem()
    for i in range(nt):
        p.add("sync", lambda e, i=i: e.dma_start(
            out=xt[i][:], in_=x_dram[i * 128:(i + 1) * 128, :]),
            inc=(s_ld, 16))
    for i in range(nt):
        last = (i == nt - 1)
        fn = p.add_r if last else p.add
        fn("vector", lambda e, i=i: e.reduce_sum(
            out=nmr[:, i:i + 1], in_=xt[i][:], axis=AX.X),
            waits=[(s_ld, 16 * nt)])
    p.add("vector", lambda e: e.tensor_scalar(
        out=nm[:], in0=nmr[:], scalar1=-1.0 / 1024.0, scalar2=None,
        op0=ALU.mult), inc=(s_st, 1))
    for i in range(nt):
        p.add_r("scalar", lambda e, i=i: e.activation(
            out=sq[i % 2][:], in_=xt[i][:], func=AF.Square,
            bias=nm[:, i:i + 1], scale=1.0, accum_out=ssq[:, i:i + 1]),
            waits=[(s_st, 1)], inc=(s_sq, 1))
    c_eps, c_inv, c_half = ctx.ccol(EPS_LN), ctx.ccol(1.0 / 1024.0), \
        ctx.ccol(-0.5)
    p.add_r("scalar", lambda e: e.activation(
        out=rstd_a[:], in_=ssq[:], func=AF.Ln, bias=c_eps, scale=c_inv))
    p.add("scalar", lambda e: e.activation(
        out=rstd[:], in_=rstd_a[:], func=AF.Exp, scale=c_half),
        inc=(s_sq, 1))
    p.raw_wait("vector", s_sq, nt + 1)
    for i in range(nt):
        t = tmb[i] if keep_tm else tmb[i % 2]
        wv = []
        if not keep_tm and i >= 2:
            wv.append((s_tp, 8 * (i - 1)))
        p.add("vector", lambda e, i=i, t=t: e.tensor_scalar(
            out=t[:], in0=xt[i][:], scalar1=nm[:, i:i + 1],
            scalar2=rstd[:, i:i + 1], op0=ALU.add, op1=ALU.mult),
            waits=wv, inc=(s_no, 1))
        for c in range(8):
            j = i * 8 + c
            wv = [(s_no, i + 1)]
            if j >= 2:
                wv.append((s_ev, j - 1))
            p.add("tensor", lambda e, c=c, t=t, j=j: e.transpose(
                tp[j % 2][:], t[:, c * 128:(c + 1) * 128], ctx.ident_sb[:]),
                waits=wv, inc=(s_tp, 1))
            p.add("vector", lambda e, i=i, c=c, j=j: e.tensor_copy(
                out=fm[c][:, i * 128:(i + 1) * 128], in_=tp[j % 2][:]),
                waits=[(s_tp, j + 1)], inc=(s_ev, 1))
    p.barrier()
    if keep_stats:
        ctx.ln_stats = (xt, nm, rstd)
        ctx.ln_stats_free = lambda: ctx.free(xt, nmr, nm, ssq, rstd_a, sq,
                                             rstd)
        ctx.free(tp)
    else:
        ctx.free(xt, nmr, nm, ssq, rstd_a, sq, rstd, tp)
    if keep_tm:
        return fm, tmb
    ctx.free(tmb)
    return fm, None


def emit_gemm_fm_fm(ctx, xfm, T, wsb, nw, bias_row, out, tag, tsub=512,
                    evac=None):
    p = ctx.p
    ps = [ctx.psum(f"gm_{tag}", [128, tsub], F32) for _ in range(2)]
    s_mm = ctx.sem()
    s_ev = ctx.sem()
    ntt = (T + tsub - 1) // tsub
    it = 0
    for n in range(nw // 128):
        for tt in range(ntt):
            t0, t1 = tt * tsub, min((tt + 1) * tsub, T)
            pb = ps[it % 2]
            wv = [(s_ev, it - 1)] if it >= 2 else []
            for k in range(8):
                last = (k == 7) and bias_row is None
                p.add("tensor", lambda e, k=k, n=n, t0=t0, t1=t1, pb=pb,
                      last=last: e.matmul(pb[:, 0:t1 - t0],
                                          wsb[k][:, n * 128:(n + 1) * 128],
                                          xfm[k][:, t0:t1], start=(k == 0),
                                          stop=last),
                      waits=wv if k == 0 else [],
                      inc=(s_mm, 1) if last else None)
            if bias_row is not None:
                p.add("tensor", lambda e, n=n, t0=t0, t1=t1, pb=pb: e.matmul(
                    pb[:, 0:t1 - t0], bias_row[:, n * 128:(n + 1) * 128],
                    ctx.ones_sb[:, 0:t1 - t0], start=False, stop=True),
                    inc=(s_mm, 1))
            it += 1
            if evac is None:
                p.add("vector", lambda e, n=n, t0=t0, t1=t1, pb=pb, v=it:
                      e.tensor_copy(out=out[n][:, t0:t1],
                                    in_=pb[:, 0:t1 - t0]),
                      waits=[(s_mm, it)], inc=(s_ev, 1))
            else:
                p.add("vector", lambda e, n=n, t0=t0, t1=t1, pb=pb: evac(
                    e, pb[:, 0:t1 - t0], (n, t0, t1)),
                    waits=[(s_mm, it)], inc=(s_ev, 1))
    p.barrier()
    ctx.free(ps)


def emit_gemm_fm_tm(ctx, xfm, T, wsb, nw, bias_row, tag, evac, nsub=512,
                    evac_eng="vector"):
    p = ctx.p
    ps = [ctx.psum(f"gt_{tag}", [128, nsub], F32) for _ in range(2)]
    s_mm = ctx.sem()
    s_ev = ctx.sem()
    nkc = len(xfm)
    it = 0
    for tt in range(T // 128):
        for n in range((nw + nsub - 1) // nsub):
            n0, n1 = n * nsub, min((n + 1) * nsub, nw)
            pb = ps[it % 2]
            wv = [(s_ev, it - 1)] if it >= 2 else []
            for k in range(nkc):
                last = (k == nkc - 1) and bias_row is None
                p.add("tensor", lambda e, k=k, tt=tt, n0=n0, n1=n1, pb=pb,
                      last=last: e.matmul(
                          pb[:, 0:n1 - n0], xfm[k][:, tt * 128:(tt + 1) * 128],
                          wsb[k][:, n0:n1], start=(k == 0), stop=last),
                      waits=wv if k == 0 else [],
                      inc=(s_mm, 1) if last else None)
            if bias_row is not None:
                p.add("tensor", lambda e, tt=tt, n0=n0, n1=n1, pb=pb: e.matmul(
                    pb[:, 0:n1 - n0], ctx.ones_sb[0:1, 0:128],
                    bias_row[:, n0:n1], start=False, stop=True),
                    inc=(s_mm, 1))
            it += 1
            p.add(evac_eng, lambda e, tt=tt, n0=n0, n1=n1, pb=pb: evac(
                e, pb[:, 0:n1 - n0], (tt, n0, n1)),
                waits=[(s_mm, it)], inc=(s_ev, 1))
    p.barrier()
    ctx.free(ps)


def emit_tm_to_fm(ctx, tm_tiles, T, tag, out_dtype=BF16, scale_cols=None,
                  in_ready=None):
    p = ctx.p
    nt = T // 128
    fm = [ctx.sbuf(f"fm_{tag}{c}", [128, T], out_dtype) for c in range(8)]
    tp = [ctx.psum(f"tp_{tag}", [128, 128], out_dtype) for _ in range(2)]
    s_sc = ctx.sem()
    s_tp = ctx.sem()
    s_ev = ctx.sem()
    for i in range(nt):
        if scale_cols is not None:
            p.add("vector", lambda e, i=i: e.tensor_scalar(
                out=tm_tiles[i][:], in0=tm_tiles[i][:],
                scalar1=scale_cols[:, i:i + 1], scalar2=None, op0=ALU.mult),
                waits=in_ready or [], inc=(s_sc, 1))
        for c in range(8):
            j = i * 8 + c
            wv = []
            if scale_cols is not None:
                wv.append((s_sc, i + 1))
            elif in_ready and j == 0:
                wv.extend(in_ready)
            if j >= 2:
                wv.append((s_ev, j - 1))
            p.add("tensor", lambda e, i=i, c=c, j=j: e.transpose(
                tp[j % 2][:], tm_tiles[i][:, c * 128:(c + 1) * 128],
                ctx.ident_sb[:]), waits=wv, inc=(s_tp, 1))
            p.add("vector", lambda e, i=i, c=c, j=j: e.tensor_copy(
                out=fm[c][:, i * 128:(i + 1) * 128], in_=tp[j % 2][:]),
                waits=[(s_tp, j + 1)], inc=(s_ev, 1))
    p.barrier()
    ctx.free(tp)
    return fm


def slot_qrange(j):
    r0 = max(0, 256 * (j - 1))
    r1 = min(OWN, 256 * (j + 1))
    return r0, r1


def build_l0b():
    ctx = make_ctx("l0b")
    nc = ctx.nc
    io = {}
    io["q"] = nc.dram_tensor("q", [8, 128, OWN], BF16,
                             kind="ExternalInput").ap()
    io["k"] = nc.dram_tensor("k", [8, 128, EXT], BF16,
                             kind="ExternalInput").ap()
    io["v"] = nc.dram_tensor("v", [EXT, 1040], BF16,
                             kind="ExternalInput").ap()
    io["posT"] = nc.dram_tensor("posT", [16, 512, 512], F32,
                                kind="ExternalInput").ap()
    io["slot_w"] = nc.dram_tensor("slot_w", [128, 9], F32,
                                  kind="ExternalInput").ap()
    io["invcnt"] = nc.dram_tensor("invcnt", [128, OWN // 128], F32,
                                  kind="ExternalInput").ap()
    io["wo"] = nc.dram_tensor("wo", [1024, 1024], F32,
                              kind="ExternalInput").ap()
    io["bo"] = nc.dram_tensor("bo", [1, 1024], F32,
                              kind="ExternalInput").ap()
    io["s0"] = nc.dram_tensor("s0", [OWN, 1024], BF16,
                              kind="ExternalOutput").ap()
    emit_l0b(ctx, io)
    ctx.p.finalize()
    ctx.close()
    return nc


def emit_l0b(ctx, io):
    nc, p = ctx.nc, ctx.p
    q_d, k_d, v_d, pos_d = io["q"], io["k"], io["v"], io["posT"]
    sw_d, ic_d, wo_d, bo_d = io["slot_w"], io["invcnt"], io["wo"], io["bo"]
    s_o = io["s0"]

    qfm = [ctx.sbuf(f"qfm{c}", [128, OWN], BF16) for c in range(8)]
    kfm = [ctx.sbuf(f"kfm{c}", [128, EXT], BF16) for c in range(8)]
    vtm = [ctx.sbuf("vtm", [128, 1040], BF16) for _ in range(EXT // 128)]
    ctxa = [ctx.sbuf("ctxa", [128, 1024], BF16) for _ in range(OWN // 128)]
    sw = ctx.sbuf("sw", [128, 9], F32)
    ic = ctx.sbuf("ic", [128, OWN // 128], F32)
    s_ld = ctx.sem()
    nld = 0
    for c in range(8):
        p.add("sync", lambda e, c=c: e.dma_start(out=qfm[c][:], in_=q_d[c]),
              inc=(s_ld, 16)); nld += 1
        p.add("sync", lambda e, c=c: e.dma_start(out=kfm[c][:], in_=k_d[c]),
              inc=(s_ld, 16)); nld += 1
    for i in range(EXT // 128):
        p.add("sync", lambda e, i=i: e.dma_start(
            out=vtm[i][:], in_=v_d[i * 128:(i + 1) * 128, :]),
            inc=(s_ld, 16)); nld += 1
    p.add("sync", lambda e: e.dma_start(out=sw[:], in_=sw_d),
          inc=(s_ld, 16)); nld += 1
    p.add("sync", lambda e: e.dma_start(out=ic[:], in_=ic_d),
          inc=(s_ld, 16)); nld += 1
    p.raw_wait("tensor", s_ld, 16 * nld)
    p.raw_wait("vector", s_ld, 16 * nld)
    p.raw_wait("scalar", s_ld, 16 * nld)
    p.barrier()

    pstg = [ctx.sbuf("pstg", [128, 512], F32) for _ in range(2)]
    psb = [[ctx.sbuf("psb", [128, 512], BF16) for _ in range(4)]
           for _ in range(2)]
    sc = [ctx.psum("sc", [128, 512], F32) for _ in range(2)]
    owp = [ctx.psum("owp", [128, 65], F32) for _ in range(4)]
    exp_sb = [[ctx.sbuf("exp", [128, 512], BF16) for _ in range(4)]
              for _ in range(2)]
    rcol = ctx.sbuf("rcol", [128, 2], F32)
    s_plds = [ctx.sem(), ctx.sem()]
    s_pcv = ctx.sem()
    s_sc = ctx.sem()
    s_ex = ctx.sem()
    s_ow = ctx.sem()
    s_mg = ctx.sem()

    npld = [0]

    def load_pos(h):
        buf = h % 2
        for ksub in range(4):
            wv = []
            if npld[0] >= 2:
                wv.append((s_pcv, npld[0] - 1))
            p.add("sync", lambda e, h=h, ksub=ksub, n=npld[0]: e.dma_start(
                out=pstg[n % 2][:],
                in_=pos_d[h, ksub * 128:(ksub + 1) * 128, :]),
                waits=wv, inc=(s_plds[npld[0] % 2], 16))
            wv2 = [(s_plds[npld[0] % 2], 16 * (npld[0] // 2 + 1))]
            if h >= 2:
                wv2.append((s_sc, sc_cnt_at_head_end[h - 2]))
            p.add("vector", lambda e, buf=buf, ksub=ksub, n=npld[0]:
                  e.tensor_copy(out=psb[buf][ksub][:], in_=pstg[n % 2][:]),
                  waits=wv2, inc=(s_pcv, 1))
            npld[0] += 1

    sc_cnt_at_head_end = {}
    cnt = 0
    for h in range(16):
        cnt += 9 * 4
        sc_cnt_at_head_end[h] = cnt

    ow_tile_last_mg = [0, 0, 0, 0]
    first_touch = {}
    gk = [0]
    gow = [0]
    ow_end_of_slot = {}

    def emit_scores(h, j, sl):
        ch, rb = h // 2, (h % 2) * 64
        r0, r1 = slot_qrange(j)
        qlen = r1 - r0
        qoff = r0 + 256 - 256 * j
        for ksub in range(4):
            scb = sc[gk[0] % 2]
            wv = [(s_pcv, 4 * h + ksub + 1)]
            if gk[0] >= 2:
                wv.append((s_ex, gk[0] - 1))
            p.add("tensor", lambda e, ch=ch, rb=rb, j=j, ksub=ksub,
                  r0=r0, qlen=qlen, scb=scb: e.matmul(
                      scb[:, 0:qlen],
                      kfm[ch][rb:rb + 64,
                              256 * j + 128 * ksub:256 * j + 128 * ksub + 128],
                      qfm[ch][rb:rb + 64, r0:r0 + qlen],
                      start=True, stop=False), waits=wv)
            p.add("tensor", lambda e, h=h, ksub=ksub, qoff=qoff,
                  qlen=qlen, scb=scb: e.matmul(
                      scb[:, 0:qlen], ctx.ident_sb[:],
                      psb[h % 2][ksub][:, qoff:qoff + qlen],
                      start=False, stop=True), inc=(s_sc, 1))
            ebuf = exp_sb[sl % 2][ksub]
            wv = [(s_sc, p.val(s_sc))]
            if sl >= 2:
                wv.append((s_ow, ow_end_of_slot[sl - 2]))
            p.add("scalar", lambda e, ebuf=ebuf, qlen=qlen, scb=scb:
                  e.activation(out=ebuf[:, 0:qlen], in_=scb[:, 0:qlen],
                               func=AF.Exp), waits=wv, inc=(s_ex, 1))
            gk[0] += 1

    def emit_ow(h, j, sl):
        r0, r1 = slot_qrange(j)
        qlen = r1 - r0
        exps = exp_sb[sl % 2]
        ex_done = 4 * (sl + 1)
        for qq in range(qlen // 128):
            ti = r0 // 128 + qq
            ob = owp[gow[0] % 4]
            wv = [(s_ex, ex_done), (s_mg, ow_tile_last_mg[gow[0] % 4])]
            for ksub in range(4):
                p.add("tensor", lambda e, exps=exps, ksub=ksub, qq=qq,
                      j=j, h=h, ob=ob: e.matmul(
                          ob[:],
                          exps[ksub][:, qq * 128:qq * 128 + 128],
                          vtm[2 * j + ksub][:, 65 * h:65 * h + 65],
                          start=(ksub == 0), stop=(ksub == 3)),
                      waits=wv if ksub == 0 else [],
                      inc=(s_ow, 1) if ksub == 3 else None)
            gow[0] += 1
            wv = [(s_ow, p.val(s_ow))]
            p.add_r("vector", lambda e, ob=ob: e.reciprocal(
                out=rcol[:, 0:1], in_=ob[:, 64:65]), waits=wv)
            p.add_r("vector", lambda e, j=j: e.tensor_tensor(
                out=rcol[:, 1:2], in0=rcol[:, 0:1], in1=sw[:, j:j + 1],
                op=ALU.mult))
            ft = (ti, h)
            if ft not in first_touch:
                first_touch[ft] = True
                p.add("vector", lambda e, ob=ob, ti=ti, h=h:
                      e.tensor_scalar(
                          out=ctxa[ti][:, 64 * h:64 * h + 64],
                          in0=ob[:, 0:64], scalar1=rcol[:, 1:2],
                          scalar2=None, op0=ALU.mult), inc=(s_mg, 1))
            else:
                p.add("vector", lambda e, ob=ob, ti=ti, h=h:
                      e.scalar_tensor_tensor(
                          out=ctxa[ti][:, 64 * h:64 * h + 64],
                          in0=ob[:, 0:64], scalar=rcol[:, 1:2],
                          in1=ctxa[ti][:, 64 * h:64 * h + 64],
                          op0=ALU.mult, op1=ALU.add), inc=(s_mg, 1))
            ow_tile_last_mg[(gow[0] - 1) % 4] = p.val(s_mg)
        ow_end_of_slot[sl] = p.val(s_ow)

    load_pos(0)
    load_pos(1)
    slots_list = [(h, j) for h in range(16) for j in range(9)]
    for sl, (h, j) in enumerate(slots_list):
        emit_scores(h, j, sl)
        if sl >= 1:
            emit_ow(*slots_list[sl - 1], sl - 1)
        if j == 8 and h + 2 < 16:
            load_pos(h + 2)
    emit_ow(*slots_list[-1], len(slots_list) - 1)
    p.barrier()
    ctx.free(qfm, kfm, vtm, pstg, psb, exp_sb, owp, sc)

    ctx_fm = emit_tm_to_fm(ctx, ctxa, OWN, "cx", scale_cols=ic)
    ctx.free(ctxa)
    wo_sb = load_weight_bf16(ctx, "wo", wo_d, 1024, 1024)
    bo_sb = load_bias_bf16(ctx, "bo", bo_d, 1024)
    p.barrier()
    s0 = [ctx.sbuf("s0", [128, 1024], BF16) for _ in range(OWN // 128)]

    def s_evac(e, ps_ap, key):
        tt, n0, n1 = key
        return e.tensor_copy(out=s0[tt][:, n0:n1], in_=ps_ap)
    emit_gemm_fm_tm(ctx, ctx_fm, OWN, wo_sb, 1024, bo_sb, "s0", s_evac)
    st = ctx.sem(swdge=True)
    for i in range(OWN // 128):
        p.add("gpsimd", lambda e, i=i: e.dma_start(
            out=s_o[i * 128:(i + 1) * 128, :], in_=s0[i][:]), inc=(st, 16))
    p.raw_wait("gpsimd", st, 16 * (OWN // 128))
    p.barrier()
    ctx.free(s0, ctx_fm, wo_sb, bo_sb, sw, ic)


def store_and_free(ctx, pairs, also_free=()):
    p = ctx.p
    st = ctx.sem(swdge=True)
    n = 0
    for dram_ap, sb in pairs:
        p.add("gpsimd", lambda e, d=dram_ap, s=sb: e.dma_start(out=d, in_=s),
              inc=(st, 16))
        n += 1
    p.raw_wait("gpsimd", st, 16 * n)
    p.barrier()
    frees = []
    for _, sb in pairs:
        pass
    for t in also_free:
        ctx.free(t)


def build_l0a():
    ctx = make_ctx("l0a")
    nc = ctx.nc
    io = {}
    io["x_ext"] = nc.dram_tensor("x_ext", [EXT, 1024], F32,
                                 kind="ExternalInput").ap()
    for nm_ in ("wq", "wk", "wv", "wi", "wiT"):
        io[nm_] = nc.dram_tensor(nm_, [1024, 1024], F32,
                                 kind="ExternalInput").ap()
    for nm_ in ("bq", "bk", "bv", "bi"):
        io[nm_] = nc.dram_tensor(nm_, [1, 1024], F32,
                                 kind="ExternalInput").ap()
    io["wg"] = nc.dram_tensor("wg", [1024, 8], F32, kind="ExternalInput").ap()
    io["bg"] = nc.dram_tensor("bg", [1, 8], F32, kind="ExternalInput").ap()
    io["b_in_col"] = nc.dram_tensor("b_in_col", [1024, 1], F32,
                                    kind="ExternalInput").ap()
    io["q"] = nc.dram_tensor("q", [8, 128, OWN], BF16,
                             kind="ExternalOutput").ap()
    io["k"] = nc.dram_tensor("k", [8, 128, OWN], BF16,
                             kind="ExternalOutput").ap()
    io["v"] = nc.dram_tensor("v", [OWN, 1040], BF16,
                             kind="ExternalOutput").ap()
    io["t"] = nc.dram_tensor("t", [8, 128, OWN], BF16,
                             kind="ExternalOutput").ap()
    io["route"] = nc.dram_tensor("route", [OWN, 8], F32,
                                 kind="ExternalOutput").ap()
    emit_l0a(ctx, io, ext_kv=False)
    ctx.p.finalize()
    ctx.close()
    return nc


def emit_l0a(ctx, io, ext_kv):
    nc, p = ctx.nc, ctx.p
    x = io["x_ext"]
    wq_d, wk_d, wv_d, wi_d = io["wq"], io["wk"], io["wv"], io["wi"]
    bq_d, bk_d, bv_d, bi_d = io["bq"], io["bk"], io["bv"], io["bi"]
    wg_d, bg_d, wiT_d, bic_d = io["wg"], io["bg"], io["wiT"], io["b_in_col"]
    q_o, k_o, v_o, t_o, r_o = io["q"], io["k"], io["v"], io["t"], io["route"]

    SCL = 1.0 / 8.0
    xn, _ = emit_ln_to_fm(ctx, x, EXT, "a", keep_stats=True)
    xn_own = [c[:, HALO:HALO + OWN] for c in xn]

    wk_sb = load_weight_bf16(ctx, "wk", wk_d, 1024, 1024)
    bk_sb = load_bias_bf16(ctx, "bk", bk_d, 1024)
    p.barrier()
    kfm = [ctx.sbuf(f"kfm{c}", [128, EXT], BF16) for c in range(8)]
    emit_gemm_fm_fm(ctx, xn, EXT, wk_sb, 1024, bk_sb, kfm, "k")
    if ext_kv:
        store_and_free(ctx, [(k_o[c], kfm[c][:]) for c in range(8)])
    else:
        store_and_free(ctx, [(k_o[c], kfm[c][:, HALO:HALO + OWN])
                             for c in range(8)])
    ctx.free(kfm, wk_sb, bk_sb)

    wq_sb = load_weight_bf16(ctx, "wq", wq_d, 1024, 1024, scale=SCL)
    bq_sb = load_bias_bf16(ctx, "bq", bq_d, 1024, scale=SCL)
    p.barrier()
    qfm = [ctx.sbuf(f"qfm{c}", [128, OWN], BF16) for c in range(8)]
    emit_gemm_fm_fm(ctx, xn_own, OWN, wq_sb, 1024, bq_sb, qfm, "q")
    store_and_free(ctx, [(q_o[c], qfm[c][:]) for c in range(8)])
    ctx.free(qfm, wq_sb, bq_sb)

    wv_sb = load_weight_bf16(ctx, "wv", wv_d, 1024, 1024)
    bv_sb = load_bias_bf16(ctx, "bv", bv_d, 1024)
    p.barrier()
    VR = EXT if ext_kv else OWN
    xn_v = xn if ext_kv else xn_own
    nvt = VR // 128
    vtm = [ctx.sbuf("vtm", [128, 1040], BF16) for _ in range(nvt)]
    s_vz = ctx.sem()
    for i in range(nvt):
        p.add("vector", lambda e, i=i: e.memset(
            vtm[i].rearrange("p (h x) -> p h x", x=65)[:, :, 64:65], 1.0),
            inc=(s_vz, 1))
    p.raw_wait("vector", s_vz, nvt)

    def v_evac(e, ps_ap, key):
        tt, n0, n1 = key
        hbase = n0 // 64
        dst = vtm[tt].rearrange("p (h x) -> p h x", x=65)[
            :, hbase:hbase + (n1 - n0) // 64, 0:64]
        return e.tensor_copy(
            out=dst, in_=ps_ap.rearrange("p (h x) -> p h x", x=64))
    emit_gemm_fm_tm(ctx, xn_v, VR, wv_sb, 1024, bv_sb, "v", v_evac)
    store_and_free(ctx, [(v_o[i * 128:(i + 1) * 128, :], vtm[i][:])
                         for i in range(nvt)])
    ctx.free(vtm, wv_sb, bv_sb)

    wi_sb = load_weight_bf16(ctx, "wi", wi_d, 1024, 1024)
    bi_sb = load_bias_bf16(ctx, "bi", bi_d, 1024)
    p.barrier()
    tfm = [ctx.sbuf(f"tfm{c}", [128, OWN], BF16) for c in range(8)]
    emit_gemm_fm_fm(ctx, xn_own, OWN, wi_sb, 1024, bi_sb, tfm, "t")
    store_and_free(ctx, [(t_o[c], tfm[c][:]) for c in range(8)])
    ctx.free(wi_sb, bi_sb, xn)

    xt_ln, nm_ln, rstd_ln = ctx.ln_stats
    ntt = OWN // 128
    route = ctx.sbuf("route_sb", [128, 8 * ntt], F32)
    wg32 = [ctx.sbuf("wg32", [128, 8], F32) for _ in range(8)]
    bg32 = ctx.sbuf("bg32", [1, 8], F32)
    bic = [ctx.sbuf("bic", [128, 1], F32) for _ in range(8)]
    onesr32 = ctx.sbuf("onesr32", [1, 128], F32)
    bcomb = ctx.sbuf("bcomb", [1, 8], F32)
    wcomb = [ctx.sbuf("wcomb", [128, 8], F32) for _ in range(8)]
    s_ld2 = ctx.sem()
    nld2 = 0
    p.add("vector", lambda e: e.memset(onesr32[:], 1.0))
    for k in range(8):
        p.add("sync", lambda e, k=k: e.dma_start(
            out=wg32[k][:], in_=wg_d[k * 128:(k + 1) * 128, :]),
            inc=(s_ld2, 16)); nld2 += 1
        p.add("sync", lambda e, k=k: e.dma_start(
            out=bic[k][:], in_=bic_d[k * 128:(k + 1) * 128, :]),
            inc=(s_ld2, 16)); nld2 += 1
    p.add("sync", lambda e: e.dma_start(out=bg32[:], in_=bg_d),
          inc=(s_ld2, 16)); nld2 += 1
    p.raw_wait("tensor", s_ld2, 16 * nld2)
    p.barrier()
    bc_ps = ctx.psum("bc", [1, 8], F32)
    s_wc = ctx.sem()
    for k in range(8):
        p.add("tensor", lambda e, k=k: e.matmul(
            bc_ps[:], bic[k][:], wg32[k][:], start=(k == 0), stop=False))
    p.add("tensor", lambda e: e.matmul(
        bc_ps[:], onesr32[0:1, 0:1], bg32[:], start=False, stop=True),
        inc=(s_wc, 1))
    p.add("vector", lambda e: e.tensor_copy(out=bcomb[:], in_=bc_ps[:]),
          waits=[(s_wc, 1)], inc=(s_wc, 1))
    ctx.free(bc_ps)
    wiT_sb = [ctx.sbuf("wiTs", [128, 1024], F32) for _ in range(2)]
    s_wl = [ctx.sem(), ctx.sem()]
    s_wi = ctx.sem()
    wc_ps = [ctx.psum("wcp", [128, 8], F32) for _ in range(8)]
    for k in range(8):
        p.add("sync", lambda e, k=k: e.dma_start(
            out=wiT_sb[k % 2][:], in_=wiT_d[k * 128:(k + 1) * 128, :]),
            waits=[(s_wi, k - 1)] if k >= 2 else [],
            inc=(s_wl[k % 2], 16))
        for m in range(8):
            wvm = [(s_wl[k % 2], 16 * (k // 2 + 1))] if m == 0 else []
            if k == 0 and m == 0:
                wvm.append((s_wc, 2))
            p.add("tensor", lambda e, k=k, m=m: e.matmul(
                wc_ps[m][:], wiT_sb[k % 2][:, m * 128:(m + 1) * 128],
                wg32[k][:], start=(k == 0), stop=(k == 7)),
                waits=wvm,
                inc=(s_wi, 1) if m == 7 else None)
    p.raw_wait("vector", s_wi, 8)
    p.add("vector", lambda e: e.tensor_copy(out=ctx.junk[:, 2:3],
                                            in_=ctx.cpool[:, 0:1]),
          inc=(s_wc, 1))
    for m in range(8):
        p.add("vector", lambda e, m=m: e.tensor_copy(
            out=wcomb[m][:], in_=wc_ps[m][:]), inc=(s_wc, 1))
    p.barrier()
    ctx.free(wc_ps, wiT_sb)

    lgw = ctx.sbuf("rt_lgw", [128, 8 * ntt], F32)
    xn32 = [ctx.sbuf("xn32", [128, 1024], F32) for _ in range(2)]
    xfm32 = [ctx.sbuf("xfm32", [128, 128], F32) for _ in range(8)]
    tp32 = [ctx.psum("tp32", [128, 128], F32) for _ in range(2)]
    lg = [ctx.psum("lg", [128, 8], F32) for _ in range(2)]
    s_n32 = ctx.sem()
    s_t32 = ctx.sem()
    s_x32 = ctx.sem()
    s_lgm = ctx.sem()
    s_rt = ctx.sem()
    for tt in range(ntt):
        i_ext = tt + HALO // 128
        wvn = [(s_t32, 8 * (tt - 1))] if tt >= 2 else []
        p.add("vector", lambda e, tt=tt, i=i_ext: e.tensor_scalar(
            out=xn32[tt % 2][:], in0=xt_ln[i][:], scalar1=nm_ln[:, i:i + 1],
            scalar2=rstd_ln[:, i:i + 1], op0=ALU.add, op1=ALU.mult),
            waits=wvn, inc=(s_n32, 1))
        for c in range(8):
            j = tt * 8 + c
            wv = [(s_n32, tt + 1)]
            if j >= 2:
                wv.append((s_x32, j - 1))
            p.add("tensor", lambda e, tt=tt, c=c, j=j: e.transpose(
                tp32[j % 2][:], xn32[tt % 2][:, c * 128:(c + 1) * 128],
                ctx.ident_f32[:]), waits=wv, inc=(s_t32, 1))
            p.add("vector", lambda e, c=c, j=j, tt=tt: e.tensor_copy(
                out=xfm32[c][:], in_=tp32[j % 2][:]),
                waits=[(s_t32, j + 1), (s_lgm, tt)], inc=(s_x32, 1))
        pb = lg[tt % 2]
        wv0 = [(s_x32, 8 * (tt + 1))]
        if tt >= 2:
            wv0.append((s_rt, p.val(s_rt)))
        for k in range(8):
            p.add("tensor", lambda e, k=k, pb=pb: e.matmul(
                pb[:], xfm32[k][:], wcomb[k][:], start=(k == 0), stop=False),
                waits=wv0 if k == 0 else [])
        p.add("tensor", lambda e, pb=pb: e.matmul(
            pb[:], onesr32[:], bcomb[:], start=False, stop=True),
            inc=(s_lgm, 1))
        fn = p.add_r if tt == ntt - 1 else p.add
        fn("vector", lambda e, tt=tt, pb=pb: e.tensor_copy(
            out=lgw[:, tt * 8:(tt + 1) * 8], in_=pb[:]),
            waits=[(s_lgm, tt + 1)], inc=(s_rt, 1))
    exps = ctx.sbuf("rt_exp", [128, 8 * ntt], F32)
    prb = ctx.sbuf("rt_prb", [128, 8 * ntt], F32)
    msk = ctx.sbuf("rt_msk", [128, 8 * ntt], F32)
    pm2 = ctx.sbuf("rt_pm2", [128, 8 * ntt], F32)
    ru = ctx.sbuf("rt_ru", [128, 8 * ntt], F32)
    mcol = ctx.sbuf("rt_mcol", [128, 6 * ntt], F32)
    mx, ssum, rs_, m1, m2, den = [mcol[:, i * ntt:(i + 1) * ntt]
                                  for i in range(6)]
    A = p.add_r
    A("vector", lambda e: e.reduce_max(out=seg(mx, 1), in_=seg(lgw[:], 8),
                                       axis=AX.X))
    A("vector", lambda e: e.tensor_tensor(out=seg(exps[:], 8),
                                          in0=seg(lgw[:], 8),
                                          in1=bcast(mx, 8),
                                          op=ALU.subtract), inc=(s_rt, 1))
    p.add_r("scalar", lambda e: e.activation(out=exps[:], in_=exps[:],
                                             func=AF.Exp),
            waits=[(s_rt, p.val(s_rt))], inc=(s_rt, 1))
    p.raw_wait("vector", s_rt, p.val(s_rt))
    A("vector", lambda e: e.reduce_sum(out=seg(ssum, 1), in_=seg(exps[:], 8),
                                       axis=AX.X))
    A("vector", lambda e: e.reciprocal(out=rs_, in_=ssum))
    A("vector", lambda e: e.tensor_tensor(out=seg(prb[:], 8),
                                          in0=seg(exps[:], 8),
                                          in1=bcast(rs_, 8), op=ALU.mult))
    A("vector", lambda e: e.reduce_max(out=seg(m1, 1), in_=seg(prb[:], 8),
                                       axis=AX.X))
    A("vector", lambda e: e.tensor_tensor(out=seg(msk[:], 8),
                                          in0=seg(prb[:], 8),
                                          in1=bcast(m1, 8), op=ALU.is_lt))
    A("vector", lambda e: e.tensor_tensor(out=pm2[:], in0=msk[:], in1=prb[:],
                                          op=ALU.mult))
    A("vector", lambda e: e.reduce_max(out=seg(m2, 1), in_=seg(pm2[:], 8),
                                       axis=AX.X))
    A("vector", lambda e: e.tensor_tensor(out=seg(msk[:], 8),
                                          in0=seg(prb[:], 8),
                                          in1=bcast(m2, 8), op=ALU.is_ge))
    A("vector", lambda e: e.tensor_tensor(out=ru[:], in0=msk[:], in1=prb[:],
                                          op=ALU.mult))
    A("vector", lambda e: e.reduce_sum(out=seg(den, 1), in_=seg(ru[:], 8),
                                       axis=AX.X))
    A("vector", lambda e: e.reciprocal(out=rs_, in_=den))
    A("vector", lambda e: e.tensor_tensor(out=seg(route[:], 8),
                                          in0=seg(ru[:], 8),
                                          in1=bcast(rs_, 8), op=ALU.mult))
    p.barrier()
    ctx.ln_stats_free()
    p.barrier()
    st = ctx.sem(swdge=True)
    for i in range(OWN // 128):
        p.add("gpsimd", lambda e, i=i: e.dma_start(
            out=r_o[i * 128:(i + 1) * 128, :],
            in_=route[:, i * 8:(i + 1) * 8]), inc=(st, 16))
    p.raw_wait("gpsimd", st, 16 * (OWN // 128))
    p.barrier()
    ctx.free(lg, tp32, tfm, route, lgw, exps, prb, msk, pm2, ru, mcol,
             xn32, xfm32, wg32, bg32, bic, onesr32, bcomb, wcomb)


def build_l0c():
    ctx = make_ctx("l0c")
    nc = ctx.nc
    io = {}
    io["t"] = nc.dram_tensor("t", [8, 128, OWN], BF16,
                             kind="ExternalInput").ap()
    io["route"] = nc.dram_tensor("route", [OWN, 8], F32,
                                 kind="ExternalInput").ap()
    io["w1"] = nc.dram_tensor("w1", [8, 1024, 4096], F32,
                              kind="ExternalInput").ap()
    io["b1"] = nc.dram_tensor("b1", [8, 4096], F32,
                              kind="ExternalInput").ap()
    io["w2"] = nc.dram_tensor("w2", [8, 4096, 1024], F32,
                              kind="ExternalInput").ap()
    io["b2"] = nc.dram_tensor("b2", [8, 1024], F32,
                              kind="ExternalInput").ap()
    io["wout"] = nc.dram_tensor("wout", [1024, 1024], F32,
                                kind="ExternalInput").ap()
    io["bout"] = nc.dram_tensor("bout", [1, 1024], F32,
                                kind="ExternalInput").ap()
    io["s0"] = nc.dram_tensor("s0", [OWN, 1024], BF16,
                              kind="ExternalInput").ap()
    io["x"] = nc.dram_tensor("x", [OWN, 1024], F32,
                             kind="ExternalInput").ap()
    io["x1"] = nc.dram_tensor("x1", [OWN, 1024], F32,
                              kind="ExternalOutput").ap()
    emit_l0c(ctx, io)
    ctx.p.finalize()
    ctx.close()
    return nc


def emit_l0c(ctx, io):
    nc, p = ctx.nc, ctx.p
    t_d, r_d = io["t"], io["route"]
    w1_d, b1_d, w2_d, b2_d = io["w1"], io["b1"], io["w2"], io["b2"]
    wo_d, bo_d = io["wout"], io["bout"]
    s0_d, x_d, x1_d = io["s0"], io["x"], io["x1"]

    NT = OWN // 128
    tfm = [ctx.sbuf(f"tfm{c}", [128, OWN], BF16) for c in range(8)]
    acc = [ctx.sbuf("acc", [128, 1024], BF16) for _ in range(NT)]
    rts = ctx.sbuf("rts", [128, 8 * NT], F32)
    s_ld = ctx.sem()
    nld = 0
    for c in range(8):
        p.add("sync", lambda e, c=c: e.dma_start(out=tfm[c][:], in_=t_d[c]),
              inc=(s_ld, 16)); nld += 1
    for i in range(NT):
        p.add("sync", lambda e, i=i: e.dma_start(
            out=rts[:, i * 8:(i + 1) * 8], in_=r_d[i * 128:(i + 1) * 128, :]),
            inc=(s_ld, 16)); nld += 1
    p.raw_wait("tensor", s_ld, 16 * nld)
    p.raw_wait("vector", s_ld, 16 * nld)
    p.barrier()

    slots = [[ctx.sbuf("w1q", [128, 1024], BF16) for _ in range(8)] +
             [ctx.sbuf("w2q", [128, 1024], BF16) for _ in range(8)] +
             [ctx.sbuf("b1q", [1, 1024], BF16), ctx.sbuf("b2r", [1, 1024],
                                                         BF16)]
             for _ in range(2)]
    WND = 4
    wstg = [ctx.sbuf("wstg", [128, 1024], F32) for _ in range(WND)]
    s_wld = [ctx.sem() for _ in range(WND)]
    s_wcv = ctx.sem()
    s_mm = ctx.sem()
    s_rel = ctx.sem()
    s_eo = ctx.sem()
    mid_ps = [ctx.psum("midp", [128, 512], F32) for _ in range(2)]
    eo_ps = [ctx.psum("eop", [128, 1024], F32) for _ in range(2)]
    mid_sb = [[ctx.sbuf("mid", [128, 512], BF16) for _ in range(8)]
              for _ in range(2)]
    nw = [0]
    mm_end_of_job = {}
    mid_evac_cnt = {}
    eo_evac_cnt = {}
    NPART = 18

    def load_job(j):
        e_, qq = divmod(j, 4)
        for part in range(NPART):
            g = nw[0]
            wv = [(s_wcv, g - WND + 1)] if g >= WND else []
            if part < 8:
                src = w1_d[e_, part * 128:(part + 1) * 128,
                           qq * 1024:(qq + 1) * 1024]
                dst_stage = wstg[g % WND][:]
                cast_src = wstg[g % WND][:]
            elif part < 16:
                kk = part - 8
                src = w2_d[e_, qq * 1024 + kk * 128:
                           qq * 1024 + kk * 128 + 128, :]
                dst_stage = wstg[g % WND][:]
                cast_src = wstg[g % WND][:]
            elif part == 16:
                src = b1_d[e_:e_ + 1, qq * 1024:(qq + 1) * 1024]
                dst_stage = wstg[g % WND][0:1, :]
                cast_src = wstg[g % WND][0:1, :]
            else:
                src = b2_d[e_:e_ + 1, :]
                dst_stage = wstg[g % WND][0:1, :]
                cast_src = wstg[g % WND][0:1, :]
            p.add("sync", lambda e, src=src, d=dst_stage: e.dma_start(
                out=d, in_=src), waits=wv, inc=(s_wld[g % WND], 16))
            wv2 = [(s_wld[g % WND], 16 * (g // WND + 1))]
            if j >= 2:
                wv2.append((s_mm, mm_end_of_job[j - 2]))
            dst = slots[j % 2][part][:] if part < 16 else \
                slots[j % 2][part][0:1, :]
            p.add("vector", lambda e, dst=dst, csrc=cast_src: e.tensor_copy(
                out=dst, in_=csrc), waits=wv2, inc=(s_wcv, 1))
            nw[0] += 1

    load_job(0)
    load_job(1)
    mid_cnt = [0]
    eo_done = {}
    for j in range(32):
        e_, qq = divmod(j, 4)
        sl = slots[j % 2]
        wcv_need = NPART * (j + 1)
        for tth in range(8):
            t0 = tth * 256

            def emit_mid(n):
                gmid = mid_cnt[0]
                mp = mid_ps[gmid % 2]
                wv = [(s_wcv, wcv_need)]
                if gmid >= 2:
                    wv.append((s_rel, gmid - 1))
                for k in range(8):
                    p.add("tensor", lambda e, sl=sl, k=k, n=n, t0=t0, mp=mp:
                          e.matmul(mp[:, 0:256],
                                   sl[k][:, n * 128:(n + 1) * 128],
                                   tfm[k][:, t0:t0 + 256],
                                   start=(k == 0), stop=False),
                          waits=wv if k == 0 else [])
                p.add("tensor", lambda e, sl=sl, n=n, mp=mp: e.matmul(
                    mp[:, 0:256], sl[16][:, n * 128:(n + 1) * 128],
                    ctx.ones_sb[:, 0:256], start=False, stop=True),
                    inc=(s_mm, 1))
                p.add("scalar", lambda e, mp=mp, gmid=gmid: e.activation(
                    out=mid_sb[gmid % 2][0][:, 0:256], in_=mp[:, 0:256],
                    func=AF.Relu),
                    waits=[(s_mm, p.val(s_mm))], inc=(s_rel, 1))
                mid_cnt[0] += 1
                return gmid

            def emit_eout(n, gmid):
                wv = [(s_rel, gmid + 1)]
                if n == 0:
                    geo = j * 8 + tth
                    if geo >= 1:
                        wv.append((s_eo, eo_done[geo - 1]))
                first_mm = True
                for tq2 in range(2):
                    ep = eo_ps[tq2]
                    for half in range(2):
                        last = (n == 7 and qq != 0)
                        p.add("tensor", lambda e, gmid=gmid, tq2=tq2,
                              half=half, n=n, ep=ep, sl=sl, last=last:
                              e.matmul(
                                  ep[:, half * 512:(half + 1) * 512],
                                  mid_sb[gmid % 2][0][:, tq2 * 128:
                                                      tq2 * 128 + 128],
                                  sl[8 + n][:, half * 512:(half + 1) * 512],
                                  start=(n == 0), stop=last),
                              waits=wv if first_mm else [],
                              inc=(s_mm, 1)
                              if (last and tq2 == 1 and half == 1) else None)
                        first_mm = False

            gm_prev = None
            for n in range(8):
                gm = emit_mid(n)
                if n >= 1:
                    emit_eout(n - 1, gm_prev)
                gm_prev = gm
            emit_eout(7, gm_prev)
            if qq == 0:
                for tq2 in range(2):
                    ep = eo_ps[tq2]
                    for half in range(2):
                        p.add("tensor", lambda e, sl=sl, half=half, ep=ep:
                              e.matmul(
                                  ep[:, half * 512:(half + 1) * 512],
                                  ctx.ones_sb[0:1, 0:128],
                                  sl[17][0:1, half * 512:(half + 1) * 512],
                                  start=False, stop=True),
                              inc=(s_mm, 1) if half == 1 else None)
            mmv = p.val(s_mm)
            for tq2 in range(2):
                ep = eo_ps[tq2]
                ti = tth * 2 + tq2
                rcol = rts[:, ti * 8 + e_:ti * 8 + e_ + 1]
                if j == 0:
                    p.add("vector", lambda e, ep=ep, ti=ti, rcol=rcol:
                          e.tensor_scalar(out=acc[ti][:], in0=ep[:],
                                          scalar1=rcol, scalar2=None,
                                          op0=ALU.mult),
                          waits=[(s_mm, mmv)], inc=(s_eo, 1))
                else:
                    p.add("vector", lambda e, ep=ep, ti=ti, rcol=rcol:
                          e.scalar_tensor_tensor(out=acc[ti][:], in0=ep[:],
                                                 scalar=rcol, in1=acc[ti][:],
                                                 op0=ALU.mult, op1=ALU.add),
                          waits=[(s_mm, mmv)], inc=(s_eo, 1))
            eo_done[j * 8 + tth] = p.val(s_eo)
        mm_end_of_job[j] = p.val(s_mm)
        if j + 2 < 32:
            load_job(j + 2)
    p.barrier()
    ctx.free(mid_sb, wstg, slots, eo_ps, mid_ps, tfm)

    acc_fm = emit_tm_to_fm(ctx, acc, OWN, "mo")
    ctx.free(acc)
    wo_sb = load_weight_bf16(ctx, "wout", wo_d, 1024, 1024)
    bo_sb = load_bias_bf16(ctx, "bout", bo_d, 1024)
    p.barrier()
    g_tm = [ctx.sbuf("g", [128, 1024], BF16) for _ in range(NT)]

    def g_evac(e, ps_ap, key):
        tt, n0, n1 = key
        return e.activation(out=g_tm[tt][:, n0:n1], in_=ps_ap, func=AF.Silu)
    emit_gemm_fm_tm(ctx, acc_fm, OWN, wo_sb, 1024, bo_sb, "g", g_evac,
                    evac_eng="scalar")
    ctx.free(wo_sb, bo_sb, acc_fm)

    xs = [ctx.sbuf("xs", [128, 1024], F32) for _ in range(2)]
    ss = [ctx.sbuf("ss", [128, 1024], BF16) for _ in range(2)]
    x1t = [ctx.sbuf("x1t", [128, 1024], F32) for _ in range(2)]
    s_xl = [ctx.sem(), ctx.sem()]
    s_cb = ctx.sem()
    s_st2 = [ctx.sem(swdge=True), ctx.sem(swdge=True)]
    for i in range(NT):
        wvl = [(s_cb, 2 * (i - 2) + 2)] if i >= 2 else []
        p.add("sync", lambda e, i=i: e.dma_start(
            out=xs[i % 2][:], in_=x_d[i * 128:(i + 1) * 128, :]),
            waits=wvl, inc=(s_xl[i % 2], 16))
        p.add("sync", lambda e, i=i: e.dma_start(
            out=ss[i % 2][:], in_=s0_d[i * 128:(i + 1) * 128, :]),
            inc=(s_xl[i % 2], 16))
        wv = [(s_xl[i % 2], 32 * (i // 2 + 1))]
        if i >= 2:
            wv.append((s_st2[i % 2], 16 * (i // 2)))
        p.add("vector", lambda e, i=i: e.tensor_tensor(
            out=x1t[i % 2][:], in0=ss[i % 2][:], in1=g_tm[i][:],
            op=ALU.mult), waits=wv, inc=(s_cb, 1))
        p.flush("vector")
        p.add("vector", lambda e, i=i: e.tensor_tensor(
            out=x1t[i % 2][:], in0=x1t[i % 2][:], in1=xs[i % 2][:],
            op=ALU.add), inc=(s_cb, 1))
        p.add("gpsimd", lambda e, i=i: e.dma_start(
            out=x1_d[i * 128:(i + 1) * 128, :], in_=x1t[i % 2][:]),
            waits=[(s_cb, 2 * i + 2)], inc=(s_st2[i % 2], 16))
    p.raw_wait("gpsimd", s_st2[0], 16 * (NT // 2))
    p.raw_wait("gpsimd", s_st2[1], 16 * (NT // 2))
    p.barrier()
    ctx.free(xs, ss, x1t, g_tm, rts)


def emit_fm_to_tm(ctx, fm_chunks, T, tag, out_dtype=BF16):
    p = ctx.p
    nt = T // 128
    tm = [ctx.sbuf(f"tm_{tag}", [128, 1024], out_dtype) for _ in range(nt)]
    tp = [ctx.psum(f"tp_{tag}", [128, 128], out_dtype) for _ in range(2)]
    s_tp = ctx.sem()
    s_ev = ctx.sem()
    j = 0
    for i in range(nt):
        for c in range(8):
            wv = [(s_ev, j - 1)] if j >= 2 else []
            p.add("tensor", lambda e, i=i, c=c, j=j: e.transpose(
                tp[j % 2][:], fm_chunks[c][:, i * 128:(i + 1) * 128],
                ctx.ident_sb[:]), waits=wv, inc=(s_tp, 1))
            p.add("vector", lambda e, i=i, c=c, j=j: e.tensor_copy(
                out=tm[i][:, c * 128:(c + 1) * 128], in_=tp[j % 2][:]),
                waits=[(s_tp, j + 1)], inc=(s_ev, 1))
            j += 1
    p.barrier()
    ctx.free(tp)
    return tm


def load_weight_mp(ctx, name, dram_ap, gain_sb, cmul):
    p, nc = ctx.p, ctx.nc
    out = [ctx.sbuf(f"w_{name}{k}", [128, 1024], BF16) for k in range(8)]
    fcol = ctx.sbuf(f"fc_{name}", [128, 8], F32)
    stage = [ctx.sbuf(f"wst_{name}", [128, 1024], F32) for _ in range(2)]
    sq = [ctx.sbuf(f"sq_{name}", [128, 1024], F32) for _ in range(2)]
    frow = ctx.sbuf(f"fr_{name}", [1, 1024], F32)
    s2p = [ctx.psum(f"s2_{name}", [1, 512], F32) for _ in range(2)]
    fcp = ctx.psum(f"fcp_{name}", [128, 8], F32)
    s_lds = [ctx.sem(), ctx.sem()]
    s_cv = ctx.sem()
    s_sq = ctx.sem()
    s_s2 = ctx.sem()
    s_f = ctx.sem()
    for k in range(8):
        buf = stage[k % 2]
        s_ld = s_lds[k % 2]
        w = [(s_sq, 2 * (k - 2) + 2), (s_cv, k - 1)] if k >= 2 else []
        p.add("sync", lambda e, buf=buf, k=k: e.dma_start(
            out=buf[:], in_=dram_ap[k * 128:(k + 1) * 128, :]),
            waits=w, inc=(s_ld, 16))
        wvc = [(s_ld, 16 * (k // 2 + 1))]
        p.add("vector", lambda e, k=k, buf=buf: e.tensor_copy(
            out=out[k][:], in_=buf[:]), waits=wvc, inc=(s_cv, 1))
        p.add("scalar", lambda e, k=k, buf=buf: e.activation(
            out=sq[k % 2][:], in_=buf[:], func=AF.Square),
            waits=wvc + ([(s_s2, 2 * (k - 1))] if k >= 2 else []),
            inc=(s_sq, 2))
        for hh in range(2):
            p.add("tensor", lambda e, k=k, hh=hh: e.matmul(
                s2p[hh][:], ctx.ones_col128[:, 0:1],
                sq[k % 2][:, hh * 512:(hh + 1) * 512],
                start=(k == 0), stop=(k == 7)),
                waits=[(s_sq, 2 * k + 2)] if hh == 0 else [],
                inc=(s_s2, 1))
    c_inv = ctx.ccol(1.0 / 1024.0)
    p.add("scalar", lambda e: e.activation(
        out=frow[:, 0:512], in_=s2p[0][:], func=AF.Sqrt,
        scale=c_inv[0:1, :]), waits=[(s_s2, 16)])
    p.add_r("scalar", lambda e: e.activation(
        out=frow[:, 512:1024], in_=s2p[1][:], func=AF.Sqrt,
        scale=c_inv[0:1, :]))
    p.add("vector", lambda e: e.tensor_scalar(
        out=frow[:], in0=frow[:], scalar1=EPS_MP, scalar2=None, op0=ALU.add),
        waits=[(p.rsync["scalar"], p.val(p.rsync["scalar"]))])
    p.flush("vector")
    p.add_r("vector", lambda e: e.reciprocal(out=frow[:], in_=frow[:]))
    p.add("vector", lambda e: e.tensor_scalar(
        out=frow[:], in0=frow[:], scalar1=gain_sb[0:1, 0:1], scalar2=cmul,
        op0=ALU.mult, op1=ALU.mult), inc=(s_f, 1))
    for c in range(8):
        p.add("tensor", lambda e, c=c: e.matmul(
            fcp[:, c:c + 1], frow[:, c * 128:(c + 1) * 128],
            ctx.ones_f32[0:1, 0:1], start=True, stop=True),
            waits=[(s_f, p.val(s_f))] if c == 0 else [],
            inc=(s_f, 1) if c == 7 else None)
    p.add("vector", lambda e: e.tensor_copy(out=fcol[:], in_=fcp[:]),
          waits=[(s_f, p.val(s_f))], inc=(s_f, 1))
    p.barrier()
    ctx.free(fcp, s2p, frow, sq, stage)
    return out, fcol


def emit_mp_norm_tm(ctx, tm_tiles, nt, tag, extra=1.0):
    p = ctx.p
    sq = [ctx.sbuf(f"nsq_{tag}", [128, 1024], F32) for _ in range(2)]
    ss = ctx.sbuf(f"nss_{tag}", [128, 16], F32)
    fs = ctx.sbuf(f"nfs_{tag}", [128, 16], F32)
    s_sq = ctx.sem()
    s_ss = ctx.sem()
    c64 = ctx.ccol(1.0 / 64.0)
    for i in range(nt):
        p.add("scalar", lambda e, i=i: e.activation(
            out=sq[i % 2][:], in_=tm_tiles[i][:], func=AF.Square),
            waits=[(s_ss, max(0, 3 * (i - 1)))], inc=(s_sq, 1))
        p.add_r("vector", lambda e, i=i: e.reduce_sum(
            out=seg(ss[:], 1), in_=seg(sq[i % 2][:], 64), axis=AX.X),
            waits=[(s_sq, i + 1)])
        p.add_r("scalar", lambda e: e.activation(
            out=fs[:], in_=ss[:], func=AF.Sqrt, scale=c64),
            waits=[(p.rsync["vector"], p.val(p.rsync["vector"]))])
        p.add("vector", lambda e: e.tensor_scalar(
            out=fs[:], in0=fs[:], scalar1=EPS_MP, scalar2=None, op0=ALU.add),
            waits=[(p.rsync["scalar"], p.val(p.rsync["scalar"]))])
        p.flush("vector")
        p.add_r("vector", lambda e: e.reciprocal(out=fs[:], in_=fs[:]))
        if extra != 1.0:
            p.add("vector", lambda e: e.tensor_scalar(
                out=fs[:], in0=fs[:], scalar1=extra, scalar2=None,
                op0=ALU.mult))
            p.flush("vector")
        p.add("vector", lambda e, i=i: e.tensor_tensor(
            out=seg(tm_tiles[i][:], 64), in0=seg(tm_tiles[i][:], 64),
            in1=bcast(fs[:], 64), op=ALU.mult), inc=(s_ss, 3))
        p.flush("vector")
    p.barrier()
    ctx.free(fs, ss, sq)


def build_l1a():
    ctx = make_ctx("l1a")
    nc = ctx.nc
    io = {}
    io["x1"] = nc.dram_tensor("x1", [OWN, 1024], F32,
                              kind="ExternalInput").ap()
    for nm_ in ("wqT", "wkT", "wvT"):
        io[nm_] = nc.dram_tensor(nm_, [1024, 1024], F32,
                                 kind="ExternalInput").ap()
    io["gain"] = nc.dram_tensor("gain", [1, 1], F32,
                                kind="ExternalInput").ap()
    io["xn1"] = nc.dram_tensor("xn1", [OWN, 1024], BF16,
                               kind="ExternalOutput").ap()
    io["v1"] = nc.dram_tensor("v1", [8, 128, OWN], BF16,
                              kind="ExternalOutput").ap()
    io["gram"] = nc.dram_tensor("gram", [128, 512], F32,
                                kind="ExternalOutput").ap()
    emit_l1a(ctx, io)
    ctx.p.finalize()
    ctx.close()
    return nc


def emit_l1a(ctx, io):
    nc, p = ctx.nc, ctx.p
    ctx.ones_col128 = ctx.sbuf("ones_col", [128, 1], F32)
    ctx.ones_f32 = ctx.sbuf("ones_f32", [1, 8], F32)
    p.add("vector", lambda e: e.memset(ctx.ones_col128[:], 1.0))
    p.add("vector", lambda e: e.memset(ctx.ones_f32[:], 1.0))
    x1_d, gn_d = io["x1"], io["gain"]
    wq_d, wk_d, wv_d = io["wqT"], io["wkT"], io["wvT"]
    xn_o, v_o, g_o = io["xn1"], io["v1"], io["gram"]

    gain_sb = ctx.sbuf("gain", [1, 1], F32)
    s_g = ctx.sem()
    p.add("sync", lambda e: e.dma_start(out=gain_sb[:], in_=gn_d),
          inc=(s_g, 16))
    p.raw_wait("vector", s_g, 16)
    p.barrier()

    wk_pack = {}
    for tagn, w_d in (("wv", wv_d), ("wq", wq_d), ("wk", wk_d)):
        wk_pack[tagn] = load_weight_mp(ctx, tagn, w_d, gain_sb, 1.0 / 32.0)
    p.barrier()
    xn_fm, xn_tm = emit_ln_to_fm(ctx, x1_d, OWN, "l1", keep_tm=True)
    st = ctx.sem(swdge=True)
    nst = [0]
    for i in range(OWN // 128):
        p.add("gpsimd", lambda e, i=i: e.dma_start(
            out=xn_o[i * 128:(i + 1) * 128, :], in_=xn_tm[i][:]),
            inc=(st, 16)); nst[0] += 1

    NT = OWN // 128
    SQS = 1.0 / math.sqrt(float(S))

    def proj(tagn, w_d, cmul=1.0):
        wsb, fcol = wk_pack[tagn]
        outfm = [ctx.sbuf(f"{tagn}fm{c}", [128, OWN], BF16) for c in range(8)]

        def evac(e, ps_ap, key):
            n, t0, t1 = key
            return e.tensor_scalar(out=outfm[n][:, t0:t1], in0=ps_ap,
                                   scalar1=fcol[:, n:n + 1], scalar2=None,
                                   op0=ALU.mult)
        emit_gemm_fm_fm(ctx, xn_fm, OWN, wsb, 1024, None, None, tagn,
                        evac=evac)
        ctx.free(wsb)
        return outfm, fcol

    vfm, fv = proj("wv", wv_d)
    for c in range(8):
        p.add("gpsimd", lambda e, c=c: e.dma_start(
            out=v_o[c], in_=vfm[c][:]), inc=(st, 16)); nst[0] += 1
    p.raw_wait("gpsimd", st, 16 * nst[0])
    p.barrier()
    ctx.free(vfm, fv)

    qfm, fq = proj("wq", wq_d)
    q_tm = emit_fm_to_tm(ctx, qfm, OWN, "q")
    ctx.free(qfm, fq)
    kfm, fk = proj("wk", wk_d)
    k_tm = emit_fm_to_tm(ctx, kfm, OWN, "k")
    ctx.free(kfm, fk)
    emit_mp_norm_tm(ctx, q_tm, NT, "q")
    emit_mp_norm_tm(ctx, k_tm, NT, "k", extra=SQS)

    gps = [ctx.psum("gram", [128, 64], F32) for _ in range(8)]
    gsb = ctx.sbuf("gramsb", [128, 512], F32)
    s_gm = ctx.sem()
    for h in range(16):
        hp, rb = h // 2, (h % 2) * 64
        for i in range(NT):
            p.add("tensor", lambda e, h=h, hp=hp, rb=rb, i=i: e.matmul(
                gps[hp][rb:rb + 64, :],
                q_tm[i][:, h * 64:h * 64 + 64],
                k_tm[i][:, h * 64:h * 64 + 64],
                start=(i == 0), stop=(i == NT - 1)),
                inc=(s_gm, 1) if i == NT - 1 else None)
    for hp in range(8):
        p.add("vector", lambda e, hp=hp: e.tensor_copy(
            out=gsb[:, hp * 64:(hp + 1) * 64], in_=gps[hp][:]),
            waits=[(s_gm, 16)], inc=(s_gm, 1))
    st2 = ctx.sem(swdge=True)
    p.add("gpsimd", lambda e: e.dma_start(out=g_o, in_=gsb[:]),
          waits=[(s_gm, 24)], inc=(st2, 16))
    p.raw_wait("gpsimd", st2, 16)
    p.barrier()
    ctx.free(gsb, gps, q_tm, k_tm, xn_fm, xn_tm)


def build_l1b():
    ctx = make_ctx("l1b")
    nc = ctx.nc
    io = {}
    io["grams"] = nc.dram_tensor("grams", [4, 128, 512], F32,
                                 kind="ExternalInput").ap()
    io["v1"] = nc.dram_tensor("v1", [8, 128, OWN], BF16,
                              kind="ExternalInput").ap()
    io["xn1"] = nc.dram_tensor("xn1", [OWN, 1024], BF16,
                               kind="ExternalInput").ap()
    io["woT"] = nc.dram_tensor("woT", [1024, 1024], F32,
                               kind="ExternalInput").ap()
    io["gain"] = nc.dram_tensor("gain", [1, 1], F32,
                                kind="ExternalInput").ap()
    io["s1"] = nc.dram_tensor("s1", [OWN, 1024], BF16,
                              kind="ExternalOutput").ap()
    emit_l1b(ctx, io, npart=4)
    ctx.p.finalize()
    ctx.close()
    return nc


def emit_l1b(ctx, io, npart):
    nc, p = ctx.nc, ctx.p
    ctx.ones_col128 = ctx.sbuf("ones_col", [128, 1], F32)
    ctx.ones_f32 = ctx.sbuf("ones_f32", [1, 8], F32)
    p.add("vector", lambda e: e.memset(ctx.ones_col128[:], 1.0))
    p.add("vector", lambda e: e.memset(ctx.ones_f32[:], 1.0))
    gr_d, v_d, xn_d = io["grams"], io["v1"], io["xn1"]
    wo_d, gn_d, s_o = io["woT"], io["gain"], io["s1"]

    NT = OWN // 128
    gain_sb = ctx.sbuf("gain", [1, 1], F32)
    grs = [ctx.sbuf("grs", [128, 512], F32) for _ in range(npart)]
    vfm = [ctx.sbuf(f"vfm{c}", [128, OWN], BF16) for c in range(8)]
    s_ld = ctx.sem()
    nld = 0
    p.add("sync", lambda e: e.dma_start(out=gain_sb[:], in_=gn_d),
          inc=(s_ld, 16)); nld += 1
    for i in range(npart):
        p.add("sync", lambda e, i=i: e.dma_start(
            out=grs[i][:], in_=gr_d[i] if npart > 1 else gr_d),
            inc=(s_ld, 16)); nld += 1
    for c in range(8):
        p.add("sync", lambda e, c=c: e.dma_start(out=vfm[c][:], in_=v_d[c]),
              inc=(s_ld, 16)); nld += 1
    p.raw_wait("vector", s_ld, 16 * nld)
    p.raw_wait("tensor", s_ld, 16 * nld)
    p.barrier()

    att = ctx.sbuf("att", [128, 512], BF16)
    w0 = ctx.sbuf("smw0", [128, 512], F32)
    w1 = ctx.sbuf("smw1", [128, 512], F32)
    mcol = ctx.sbuf("smc", [128, 24], F32)
    if npart > 1:
        p.add("vector", lambda e: e.tensor_tensor(out=w0[:], in0=grs[0][:],
                                                  in1=grs[1][:], op=ALU.add))
        p.add("vector", lambda e: e.tensor_tensor(out=w1[:], in0=grs[2][:],
                                                  in1=grs[3][:], op=ALU.add))
        p.flush("vector")
        p.add("vector", lambda e: e.tensor_tensor(out=w0[:], in0=w0[:],
                                                  in1=w1[:], op=ALU.add))
        p.flush("vector")
    else:
        p.add("vector", lambda e: e.tensor_copy(out=w0[:], in_=grs[0][:]))
        p.flush("vector")
    A = p.add_r
    mx, sm, rc = mcol[:, 0:8], mcol[:, 8:16], mcol[:, 16:24]
    s_sm = ctx.sem()
    A("vector", lambda e: e.reduce_max(out=seg(mx, 1), in_=seg(w0[:], 64),
                                       axis=AX.X))
    A("vector", lambda e: e.tensor_tensor(out=seg(w1[:], 64),
                                          in0=seg(w0[:], 64),
                                          in1=bcast(mx, 64),
                                          op=ALU.subtract), inc=(s_sm, 1))
    p.add_r("scalar", lambda e: e.activation(out=w1[:], in_=w1[:],
                                             func=AF.Exp),
            waits=[(s_sm, 1)], inc=(s_sm, 1))
    p.raw_wait("vector", s_sm, 2)
    A("vector", lambda e: e.reduce_sum(out=seg(sm, 1), in_=seg(w1[:], 64),
                                       axis=AX.X))
    A("vector", lambda e: e.reciprocal(out=rc, in_=sm))
    A("vector", lambda e: e.tensor_tensor(out=seg(att[:], 64),
                                          in0=seg(w1[:], 64),
                                          in1=bcast(rc, 64), op=ALU.mult))
    p.barrier()

    atT = ctx.sbuf("atT", [128, 512], BF16)
    tp = [ctx.psum("atp", [128, 64], BF16) for _ in range(2)]
    s_tp = ctx.sem()
    s_te = ctx.sem()
    for h in range(16):
        rb, cb = (h % 2) * 64, (h // 2) * 64
        wv = [(s_te, h - 1)] if h >= 2 else []
        p.add("tensor", lambda e, h=h, rb=rb, cb=cb: e.transpose(
            tp[h % 2][rb:rb + 64, :], att[rb:rb + 64, cb:cb + 64],
            ctx.ident_sb[rb:rb + 64, rb:rb + 64]), waits=wv, inc=(s_tp, 1))
        p.add("vector", lambda e, h=h, rb=rb, cb=cb: e.tensor_copy(
            out=atT[rb:rb + 64, cb:cb + 64], in_=tp[h % 2][rb:rb + 64, :]),
            waits=[(s_tp, h + 1)], inc=(s_te, 1))
    p.barrier()
    ctx.free(tp, w0, w1, mcol, grs)

    ofm = [ctx.sbuf(f"ofm{c}", [128, OWN], BF16) for c in range(8)]
    ops_ = [ctx.psum("op", [128, 512], F32) for _ in range(2)]
    s_mm = ctx.sem()
    s_ev = ctx.sem()
    it = 0
    for hp in range(4):
        for tt in range(OWN // 512):
            pb = ops_[it % 2]
            wv = [(s_ev, it - 1)] if it >= 2 else []
            for sub in range(2):
                h = hp * 2 + sub
                rb = (h % 2) * 64
                p.add("tensor", lambda e, h=h, hp=hp, rb=rb, tt=tt, pb=pb:
                      e.matmul(pb[rb:rb + 64, :],
                               atT[rb:rb + 64, (h // 2) * 64:(h // 2) * 64 + 64],
                               vfm[h // 2][rb:rb + 64,
                                           tt * 512:(tt + 1) * 512],
                               start=True, stop=True),
                      waits=wv if sub == 0 else [],
                      inc=(s_mm, 1) if sub == 1 else None)
            it += 1
            p.add("vector", lambda e, hp=hp, tt=tt, pb=pb: e.tensor_copy(
                out=ofm[hp * 2 // 2][:, tt * 512:(tt + 1) * 512]
                if False else ofm[hp][:, tt * 512:(tt + 1) * 512], in_=pb[:]),
                waits=[(s_mm, it)], inc=(s_ev, 1))
    p.barrier()
    for hp in range(4, 8):
        for tt in range(OWN // 512):
            pb = ops_[it % 2]
            wv = [(s_ev, it - 1)] if it >= 2 else []
            for sub in range(2):
                h = hp * 2 + sub
                rb = (h % 2) * 64
                p.add("tensor", lambda e, h=h, rb=rb, tt=tt, pb=pb:
                      e.matmul(pb[rb:rb + 64, :],
                               atT[rb:rb + 64, (h // 2) * 64:(h // 2) * 64 + 64],
                               vfm[h // 2][rb:rb + 64,
                                           tt * 512:(tt + 1) * 512],
                               start=True, stop=True),
                      waits=wv if sub == 0 else [],
                      inc=(s_mm, 1) if sub == 1 else None)
            it += 1
            p.add("vector", lambda e, hp=hp, tt=tt, pb=pb: e.tensor_copy(
                out=ofm[hp][:, tt * 512:(tt + 1) * 512], in_=pb[:]),
                waits=[(s_mm, it)], inc=(s_ev, 1))
    p.barrier()
    ctx.free(ops_, atT, att, vfm)

    wsb, fcol = load_weight_mp(ctx, "wo", wo_d, gain_sb,
                               1.0 / 32.0 * 0.7071067811865476)
    p.barrier()
    opfm = [ctx.sbuf(f"opfm{c}", [128, OWN], BF16) for c in range(8)]

    def evac(e, ps_ap, key):
        n, t0, t1 = key
        return e.tensor_scalar(out=opfm[n][:, t0:t1], in0=ps_ap,
                               scalar1=fcol[:, n:n + 1], scalar2=None,
                               op0=ALU.mult)
    emit_gemm_fm_fm(ctx, ofm, OWN, wsb, 1024, None, None, "op", evac=evac)
    ctx.free(wsb, fcol, ofm)
    op_tm = emit_fm_to_tm(ctx, opfm, OWN, "op")
    ctx.free(opfm)

    xnt = [ctx.sbuf("xnt", [128, 1024], BF16) for _ in range(2)]
    s1t = [ctx.sbuf("s1t", [128, 1024], BF16) for _ in range(2)]
    xsc = [ctx.sbuf("xsc", [128, 1024], F32) for _ in range(2)]
    s_xl = [ctx.sem(), ctx.sem()]
    s_cbA = ctx.sem()
    s_cbB = ctx.sem()
    s_st = [ctx.sem(swdge=True), ctx.sem(swdge=True)]
    c07 = ctx.ccol(0.7071067811865476)
    for i in range(NT):
        wvl = [(s_cbA, i - 1)] if i >= 2 else []
        p.add("sync", lambda e, i=i: e.dma_start(
            out=xnt[i % 2][:], in_=xn_d[i * 128:(i + 1) * 128, :]),
            waits=wvl, inc=(s_xl[i % 2], 16))
        wv = [(s_xl[i % 2], 16 * (i // 2 + 1))]
        if i >= 2:
            wv.append((s_cbB, i - 1))
        p.add("scalar", lambda e, i=i: e.activation(
            out=xsc[i % 2][:], in_=xnt[i % 2][:], func=AF.Copy,
            scale=c07), waits=wv, inc=(s_cbA, 1))
        wv2 = [(s_cbA, i + 1)]
        if i >= 2:
            wv2.append((s_st[i % 2], 16 * (i // 2)))
        p.add("vector", lambda e, i=i: e.tensor_tensor(
            out=s1t[i % 2][:], in0=xsc[i % 2][:], in1=op_tm[i][:],
            op=ALU.add), waits=wv2, inc=(s_cbB, 1))
        p.flush("vector")
        p.add("gpsimd", lambda e, i=i: e.dma_start(
            out=s_o[i * 128:(i + 1) * 128, :], in_=s1t[i % 2][:]),
            waits=[(s_cbB, i + 1)], inc=(s_st[i % 2], 16))
    p.raw_wait("gpsimd", s_st[0], 16 * (NT // 2))
    p.raw_wait("gpsimd", s_st[1], 16 * (NT // 2))
    p.barrier()
    ctx.free(xnt, s1t, xsc, op_tm)


def build_l1c():
    ctx = make_ctx("l1c")
    nc = ctx.nc
    io = {}
    io["xn1"] = nc.dram_tensor("xn1", [OWN, 1024], BF16,
                               kind="ExternalInput").ap()
    io["wgT"] = nc.dram_tensor("wgT", [1024, FFN1_H], I32,
                               kind="ExternalInput").ap()
    io["wuT"] = nc.dram_tensor("wuT", [1024, FFN1_H], I32,
                               kind="ExternalInput").ap()
    io["wdT"] = nc.dram_tensor("wdT", [FFN1_H, 1024], I32,
                               kind="ExternalInput").ap()
    io["x1"] = nc.dram_tensor("x1", [OWN, 1024], F32,
                              kind="ExternalInput").ap()
    io["s1"] = nc.dram_tensor("s1", [OWN, 1024], BF16,
                              kind="ExternalInput").ap()
    io["x2"] = nc.dram_tensor("x2", [OWN, 1024], F32,
                              kind="ExternalOutput").ap()
    emit_l1c(ctx, io)
    ctx.p.finalize()
    ctx.close()
    return nc


def emit_l1c(ctx, io):
    nc, p = ctx.nc, ctx.p
    xn_d, wg_d, wu_d, wd_d = io["xn1"], io["wgT"], io["wuT"], io["wdT"]
    x1_d, s1_d, x2_d = io["x1"], io["s1"], io["x2"]
    NT = OWN // 128
    QS = 1.0 / 127.0

    xnt = [ctx.sbuf("xnt", [128, 1024], BF16) for _ in range(NT)]
    s_ld = ctx.sem()
    for i in range(NT):
        p.add("sync", lambda e, i=i: e.dma_start(
            out=xnt[i][:], in_=xn_d[i * 128:(i + 1) * 128, :]),
            inc=(s_ld, 16))
    p.raw_wait("tensor", s_ld, 16 * NT)
    p.raw_wait("vector", s_ld, 16 * NT)
    p.barrier()
    xn_fm = emit_tm_to_fm(ctx, xnt, OWN, "xn")
    ctx.free(xnt)

    wd_sb = [ctx.sbuf("wd", [128, 1024], BF16) for _ in range(24)]
    WDD = 6
    wstg = [ctx.sbuf("wstg", [128, 1024], I32) for _ in range(WDD)]
    s_wld = [ctx.sem() for _ in range(WDD)]
    s_wcv = ctx.sem()
    for k in range(24):
        g = k
        w = [(s_wcv, k - WDD + 1)] if k >= WDD else []
        p.add("sync", lambda e, k=k, g=g: e.dma_start(
            out=wstg[g % WDD][:], in_=wd_d[k * 128:(k + 1) * 128, :]),
            waits=w, inc=(s_wld[g % WDD], 16))
        p.add("vector", lambda e, k=k, g=g: e.tensor_scalar(
            out=wd_sb[k][:], in0=wstg[g % WDD][:], scalar1=QS, scalar2=None,
            op0=ALU.mult), waits=[(s_wld[g % WDD], 16 * (g // WDD + 1))],
            inc=(s_wcv, 1))
    p.raw_wait("tensor", s_wcv, 24)
    p.raw_wait("sync", s_wcv, 24)
    p.barrier()

    h_fm = [ctx.sbuf("hfm", [128, 1024], BF16) for _ in range(24)]
    g1 = [ctx.sbuf("g1", [128, 1024], BF16) for _ in range(NT)]
    gu_sb = [[ctx.sbuf("gusb", [128, 128], BF16) for _ in range(16)]
             for _ in range(2)]
    GDEPTH = 8
    gstg = [ctx.sbuf("gstg", [128, 128], I32) for _ in range(GDEPTH)]
    gps = ctx.psum("gate", [128, 1024], F32)
    ups = ctx.psum("up", [128, 1024], F32)
    dps = ctx.psum("down", [128, 1024], F32)
    gsb = ctx.sbuf("gsb", [128, 1024], BF16)
    s_gld = [ctx.sem() for _ in range(8)]
    s_gcv = ctx.sem()
    s_mm = ctx.sem()
    s_svg = ctx.sem()
    s_g1 = ctx.sem()
    s_hv = ctx.sem()
    ng = [0]
    mm_at_gu = {}

    def load_gu(tt, n, slot):
        for part in range(16):
            g = ng[0]
            w = [(s_gcv, g - GDEPTH + 1)] if g >= GDEPTH else []
            src = (wg_d if part < 8 else wu_d)[
                (part % 8) * 128:(part % 8) * 128 + 128,
                n * 128:(n + 1) * 128]
            p.add("sync", lambda e, src=src, g=g: e.dma_start(
                out=gstg[g % GDEPTH][:], in_=src), waits=w,
                inc=(s_gld[g % 8], 16))
            wv2 = [(s_gld[g % 8], 16 * (g // 8 + 1))]
            gun = ng[0] // 16
            if gun >= 2:
                wv2.append((s_mm, mm_at_gu[gun - 2]))
            p.add("vector", lambda e, slot=slot, part=part, g=g:
                  e.tensor_scalar(out=gu_sb[slot][part][:],
                                  in0=gstg[g % GDEPTH][:], scalar1=QS,
                                  scalar2=None, op0=ALU.mult),
                  waits=wv2, inc=(s_gcv, 1))
            ng[0] += 1

    load_gu(0, 0, 0)
    load_gu(0, 1, 1)
    mm_end_down = [0]
    for tt in range(2):
        TT = 1024
        for n in range(24):
            gun = tt * 24 + n
            sl = gu_sb[gun % 2]
            need_cv = 16 * (gun + 1)
            for half in range(2):
                for k in range(8):
                    p.add("tensor", lambda e, k=k, half=half, tt=tt, sl=sl:
                          e.matmul(gps[:, half * 512:(half + 1) * 512],
                                   sl[k][:],
                                   xn_fm[k][:, tt * TT + half * 512:
                                            tt * TT + half * 512 + 512],
                                   start=(k == 0), stop=(k == 7)),
                          waits=[(s_gcv, need_cv), (s_svg, gun),
                                 (s_hv, max(0, 2 * gun))]
                          if (k == 0 and half == 0) else [],
                          inc=(s_mm, 1) if k == 7 else None)
                for k in range(8):
                    p.add("tensor", lambda e, k=k, half=half, tt=tt, sl=sl:
                          e.matmul(ups[:, half * 512:(half + 1) * 512],
                                   sl[8 + k][:],
                                   xn_fm[k][:, tt * TT + half * 512:
                                            tt * TT + half * 512 + 512],
                                   start=(k == 0), stop=(k == 7)),
                          inc=(s_mm, 1) if k == 7 else None)
            wv = [(s_mm, p.val(s_mm)), (s_hv, 2 * gun)]
            p.add_r("scalar", lambda e: e.activation(
                out=gsb[:], in_=gps[:], func=AF.Silu), waits=wv,
                inc=(s_svg, 1))
            wvh = [(s_svg, gun + 1)]
            if tt == 1:
                wvh.append((s_mm, mm_end_down[0]))
            p.add("vector", lambda e, n=n: e.tensor_tensor(
                out=h_fm[n][:], in0=gsb[:], in1=ups[:], op=ALU.mult),
                waits=wvh, inc=(s_hv, 2))
            mm_at_gu[gun] = p.val(s_mm)
            if gun + 2 < 48:
                nxt = gun + 2
                load_gu(nxt // 24, nxt % 24, nxt % 2)
        hv_need = p.val(s_hv)
        for tq in range(8):
            ti = tt * 8 + tq
            wv = [(s_hv, hv_need), (s_g1, ti)]
            im = 0
            for k in range(24):
                for half in range(2):
                    im += 1
                    p.add("tensor", lambda e, k=k, half=half, tq=tq:
                          e.matmul(dps[:, half * 512:(half + 1) * 512],
                                   h_fm[k][:, tq * 128:(tq + 1) * 128],
                                   wd_sb[k][:, half * 512:(half + 1) * 512],
                                   start=(k == 0), stop=(k == 23)),
                          waits=wv if im == 1 else [],
                          inc=(s_mm, 1) if (k == 23 and half == 1) else None)
            p.add_r("scalar", lambda e, ti=ti: e.activation(
                out=g1[ti][:], in_=dps[:], func=AF.Silu),
                waits=[(s_mm, p.val(s_mm))], inc=(s_g1, 1))
        mm_end_down[0] = p.val(s_mm)
    p.barrier()
    ctx.free(gsb, dps, ups, gps, gstg, gu_sb, h_fm, wstg, wd_sb)

    xs = [ctx.sbuf("xs", [128, 1024], F32) for _ in range(2)]
    ss = [ctx.sbuf("ss", [128, 1024], BF16) for _ in range(2)]
    x2t = [ctx.sbuf("x2t", [128, 1024], F32) for _ in range(2)]
    s_xl = [ctx.sem(), ctx.sem()]
    s_cb = ctx.sem()
    s_st2 = [ctx.sem(swdge=True), ctx.sem(swdge=True)]
    for i in range(NT):
        wvl = [(s_cb, 2 * (i - 2) + 2)] if i >= 2 else []
        p.add("sync", lambda e, i=i: e.dma_start(
            out=xs[i % 2][:], in_=x1_d[i * 128:(i + 1) * 128, :]),
            waits=wvl, inc=(s_xl[i % 2], 16))
        p.add("sync", lambda e, i=i: e.dma_start(
            out=ss[i % 2][:], in_=s1_d[i * 128:(i + 1) * 128, :]),
            inc=(s_xl[i % 2], 16))
        wv = [(s_xl[i % 2], 32 * (i // 2 + 1))]
        if i >= 2:
            wv.append((s_st2[i % 2], 16 * (i // 2)))
        p.add("vector", lambda e, i=i: e.tensor_tensor(
            out=x2t[i % 2][:], in0=ss[i % 2][:], in1=g1[i][:],
            op=ALU.mult), waits=wv, inc=(s_cb, 1))
        p.flush("vector")
        p.add("vector", lambda e, i=i: e.tensor_tensor(
            out=x2t[i % 2][:], in0=x2t[i % 2][:], in1=xs[i % 2][:],
            op=ALU.add), inc=(s_cb, 1))
        p.flush("vector")
        p.add("gpsimd", lambda e, i=i: e.dma_start(
            out=x2_d[i * 128:(i + 1) * 128, :], in_=x2t[i % 2][:]),
            waits=[(s_cb, 2 * i + 2)], inc=(s_st2[i % 2], 16))
    p.raw_wait("gpsimd", s_st2[0], 16 * (NT // 2))
    p.raw_wait("gpsimd", s_st2[1], 16 * (NT // 2))
    p.barrier()
    ctx.free(x2t, ss, xs, g1, xn_fm)


def build_fused_a():
    ctx = make_ctx("fuseda")
    nc, p = ctx.nc, ctx.p
    E = lambda n, sh, dt: nc.dram_tensor(n, sh, dt, kind="ExternalInput").ap()
    I = lambda n, sh, dt: nc.dram_tensor(n, sh, dt, kind="Internal").ap()
    O = lambda n, sh, dt: nc.dram_tensor(n, sh, dt,
                                         kind="ExternalOutput").ap()
    io0 = {
        "x_ext": E("x_ext", [EXT, 1024], F32),
        "wq": E("wq", [1024, 1024], F32), "wk": E("wk", [1024, 1024], F32),
        "wv": E("wv", [1024, 1024], F32), "wi": E("wi", [1024, 1024], F32),
        "wiT": E("wiT", [1024, 1024], F32),
        "bq": E("bq", [1, 1024], F32), "bk": E("bk", [1, 1024], F32),
        "bv": E("bv", [1, 1024], F32), "bi": E("bi", [1, 1024], F32),
        "wg": E("wg", [1024, 8], F32), "bg": E("bg", [1, 8], F32),
        "b_in_col": E("b_in_col", [1024, 1], F32),
        "q": I("q_s", [8, 128, OWN], BF16),
        "k": I("k_s", [8, 128, EXT], BF16),
        "v": I("v_s", [EXT, 1040], BF16),
        "t": I("t_s", [8, 128, OWN], BF16),
        "route": I("route_s", [OWN, 8], F32),
    }
    emit_l0a(ctx, io0, ext_kv=True)
    p.barrier()
    io1 = {
        "q": io0["q"], "k": io0["k"], "v": io0["v"],
        "posT": E("posT", [16, 512, 512], F32),
        "slot_w": E("slot_w", [128, 9], F32),
        "invcnt": E("invcnt", [128, OWN // 128], F32),
        "wo": E("wo", [1024, 1024], F32), "bo": E("bo", [1, 1024], F32),
        "s0": I("s0_s", [OWN, 1024], BF16),
    }
    emit_l0b(ctx, io1)
    p.barrier()
    io2 = {
        "t": io0["t"], "route": io0["route"],
        "w1": E("w1", [8, 1024, 4096], F32), "b1": E("b1", [8, 4096], F32),
        "w2": E("w2", [8, 4096, 1024], F32), "b2": E("b2", [8, 1024], F32),
        "wout": E("wout", [1024, 1024], F32), "bout": E("bout", [1, 1024],
                                                        F32),
        "s0": io1["s0"], "x": io0["x_ext"][HALO:HALO + OWN, :],
        "x1": O("x1", [OWN, 1024], F32),
    }
    emit_l0c(ctx, io2)
    p.barrier()
    io3 = {
        "x1": io2["x1"],
        "wqT": E("wqT", [1024, 1024], F32),
        "wkT": E("wkT", [1024, 1024], F32),
        "wvT": E("wvT", [1024, 1024], F32),
        "gain": E("gain", [1, 1], F32),
        "xn1": O("xn1", [OWN, 1024], BF16),
        "v1": O("v1", [8, 128, OWN], BF16),
        "gram": O("gram", [128, 512], F32),
    }
    emit_l1a(ctx, io3)
    p.finalize()
    ctx.close()
    return nc


def build_fused_b():
    ctx = make_ctx("fusedb")
    nc, p = ctx.nc, ctx.p
    E = lambda n, sh, dt: nc.dram_tensor(n, sh, dt, kind="ExternalInput").ap()
    I = lambda n, sh, dt: nc.dram_tensor(n, sh, dt, kind="Internal").ap()
    io4 = {
        "grams": E("grams", [4, 128, 512], F32),
        "v1": E("v1", [8, 128, OWN], BF16),
        "xn1": E("xn1", [OWN, 1024], BF16),
        "woT": E("woT", [1024, 1024], F32),
        "gain": E("gain", [1, 1], F32),
        "s1": I("s1_s", [OWN, 1024], BF16),
    }
    emit_l1b(ctx, io4, npart=4)
    p.barrier()
    io5 = {
        "xn1": io4["xn1"],
        "wgT": E("wgT", [1024, FFN1_H], I32),
        "wuT": E("wuT", [1024, FFN1_H], I32),
        "wdT": E("wdT", [FFN1_H, 1024], I32),
        "x1": E("x1", [OWN, 1024], F32), "s1": io4["s1"],
        "x2": nc.dram_tensor("x2", [OWN, 1024], F32,
                             kind="ExternalOutput").ap(),
    }
    emit_l1c(ctx, io5)
    p.finalize()
    ctx.close()
    return nc


def build_fused():
    ctx = make_ctx("fused")
    nc, p = ctx.nc, ctx.p
    E = lambda n, sh, dt: nc.dram_tensor(n, sh, dt, kind="ExternalInput").ap()
    I = lambda n, sh, dt: nc.dram_tensor(n, sh, dt, kind="Internal").ap()
    io0 = {
        "x_ext": E("x_ext", [EXT, 1024], F32),
        "wq": E("wq", [1024, 1024], F32), "wk": E("wk", [1024, 1024], F32),
        "wv": E("wv", [1024, 1024], F32), "wi": E("wi", [1024, 1024], F32),
        "wiT": E("wiT", [1024, 1024], F32),
        "bq": E("bq", [1, 1024], F32), "bk": E("bk", [1, 1024], F32),
        "bv": E("bv", [1, 1024], F32), "bi": E("bi", [1, 1024], F32),
        "wg": E("wg", [1024, 8], F32), "bg": E("bg", [1, 8], F32),
        "b_in_col": E("b_in_col", [1024, 1], F32),
        "q": I("q_s", [8, 128, OWN], BF16),
        "k": I("k_s", [8, 128, EXT], BF16),
        "v": I("v_s", [EXT, 1040], BF16),
        "t": I("t_s", [8, 128, OWN], BF16),
        "route": I("route_s", [OWN, 8], F32),
    }
    emit_l0a(ctx, io0, ext_kv=True)
    p.barrier()
    io1 = {
        "q": io0["q"], "k": io0["k"], "v": io0["v"],
        "posT": E("posT", [16, 512, 512], F32),
        "slot_w": E("slot_w", [128, 9], F32),
        "invcnt": E("invcnt", [128, OWN // 128], F32),
        "wo": E("wo", [1024, 1024], F32), "bo": E("bo", [1, 1024], F32),
        "s0": I("s0_s", [OWN, 1024], BF16),
    }
    emit_l0b(ctx, io1)
    p.barrier()
    io2 = {
        "t": io0["t"], "route": io0["route"],
        "w1": E("w1", [8, 1024, 4096], F32), "b1": E("b1", [8, 4096], F32),
        "w2": E("w2", [8, 4096, 1024], F32), "b2": E("b2", [8, 1024], F32),
        "wout": E("wout", [1024, 1024], F32), "bout": E("bout", [1, 1024],
                                                        F32),
        "s0": io1["s0"], "x": io0["x_ext"][HALO:HALO + OWN, :],
        "x1": I("x1_s", [OWN, 1024], F32),
    }
    emit_l0c(ctx, io2)
    p.barrier()
    gram_in = I("gram_in", [128, 512], F32)
    gram_out = I("gram_out", [128, 512], F32)
    io3 = {
        "x1": io2["x1"],
        "wqT": E("wqT", [1024, 1024], F32),
        "wkT": E("wkT", [1024, 1024], F32),
        "wvT": E("wvT", [1024, 1024], F32),
        "gain": E("gain", [1, 1], F32),
        "xn1": I("xn1_s", [OWN, 1024], BF16),
        "v1": I("v1_s", [8, 128, OWN], BF16),
        "gram": gram_in,
    }
    emit_l1a(ctx, io3)
    p.barrier()
    s_cc = ctx.sem()
    p.add("gpsimd", lambda e: e.collective_compute(
        "AllReduce", ALU.add, [[0, 1, 2, 3], [4, 5, 6, 7]],
        [gram_in], [gram_out]), inc=(s_cc, 16))
    p.raw_wait("gpsimd", s_cc, 16)
    p.barrier()
    io4 = {
        "grams": gram_out, "v1": io3["v1"], "xn1": io3["xn1"],
        "woT": E("woT", [1024, 1024], F32), "gain": io3["gain"],
        "s1": I("s1_s", [OWN, 1024], BF16),
    }
    emit_l1b(ctx, io4, npart=1)
    p.barrier()
    io5 = {
        "xn1": io3["xn1"],
        "wgT": E("wgT", [1024, FFN1_H], I32),
        "wuT": E("wuT", [1024, FFN1_H], I32),
        "wdT": E("wdT", [FFN1_H, 1024], I32),
        "x1": io2["x1"], "s1": io4["s1"],
        "x2": nc.dram_tensor("x2", [OWN, 1024], F32,
                             kind="ExternalOutput").ap(),
    }
    emit_l1c(ctx, io5)
    p.finalize()
    ctx.close()
    return nc


def _core_slices():
    return [(c // 4, (c % 4) * OWN) for c in range(NCORE)]


def _ext_idx(s0):
    return np.clip(np.arange(s0 - HALO, s0 + OWN + HALO), 0, S - 1)


def run_l0a(inp):
    x = np.asarray(inp["L0_attn_hidden_states"])
    nc = build_l0a()
    in_maps = []
    shared = {
        "wq": np.asarray(inp["L0_attn_Wq"]),
        "wk": np.asarray(inp["L0_attn_Wk"]),
        "wv": np.asarray(inp["L0_attn_Wv"]),
        "wi": np.asarray(inp["L0_ffn_W_in"]),
        "bq": np.asarray(inp["L0_attn_bq"])[None],
        "bk": np.asarray(inp["L0_attn_bk"])[None],
        "bv": np.asarray(inp["L0_attn_bv"])[None],
        "bi": np.asarray(inp["L0_ffn_b_in"])[None],
        "wg": np.asarray(inp["L0_ffn_W_gate"]),
        "bg": np.asarray(inp["L0_ffn_b_gate"])[None],
        "wiT": np.ascontiguousarray(np.asarray(inp["L0_ffn_W_in"]).T),
        "b_in_col": np.ascontiguousarray(
            np.asarray(inp["L0_ffn_b_in"]).reshape(1024, 1)),
    }
    for (b, s0) in _core_slices():
        m = dict(shared)
        m["x_ext"] = np.ascontiguousarray(x[b][_ext_idx(s0)])
        in_maps.append(m)
    return run_bass_kernel_spmd(nc, in_maps, core_ids=list(range(NCORE)))


def _slot_w(s0):
    w = np.zeros((128, 9), np.float32)
    for j in range(9):
        sg = s0 - 256 + 256 * j
        if 0 <= sg <= S - W:
            w[:, j] = 1.0
    if s0 + OWN == S:
        w[:, 7] = 2.0
        w[:, 8] = 0.0
    return w


def _invcnt(s0):
    counts = np.full(S, 2.0, np.float32)
    counts[:256] = 1.0
    counts[7680:7936] = 3.0
    counts[7936:] = 2.0
    sl = counts[s0:s0 + OWN]
    return np.ascontiguousarray(1.0 / sl.reshape(OWN // 128, 128).T)


def run_l0b(per_core, shared):
    nc = build_l0b()
    in_maps = []
    for c, m in enumerate(per_core):
        d = dict(shared)
        d.update(m)
        in_maps.append(d)
    return run_bass_kernel_spmd(nc, in_maps, core_ids=list(range(NCORE)))


def kernel(**inp):
    inp = {k: np.asarray(v) for k, v in inp.items()}
    x = inp["L0_attn_hidden_states"]
    cores = _core_slices()
    R = list(range(NCORE))
    nc_a = build_fused_a()
    nc_b = build_fused_b()
    gainv = np.asarray(inp["L1_attn_gain"], np.float32).reshape(1, 1)
    shared = {
        "wq": inp["L0_attn_Wq"], "wk": inp["L0_attn_Wk"],
        "wv": inp["L0_attn_Wv"], "wi": inp["L0_ffn_W_in"],
        "bq": inp["L0_attn_bq"][None], "bk": inp["L0_attn_bk"][None],
        "bv": inp["L0_attn_bv"][None], "bi": inp["L0_ffn_b_in"][None],
        "wg": inp["L0_ffn_W_gate"], "bg": inp["L0_ffn_b_gate"][None],
        "wiT": np.ascontiguousarray(inp["L0_ffn_W_in"].T),
        "b_in_col": np.ascontiguousarray(
            inp["L0_ffn_b_in"].reshape(1024, 1)),
        "posT": np.ascontiguousarray(
            inp["L0_attn_pos_bias"].transpose(0, 2, 1)),
        "wo": inp["L0_attn_Wo"], "bo": inp["L0_attn_bo"][None],
        "w1": inp["L0_ffn_W1"], "b1": inp["L0_ffn_b1"],
        "w2": inp["L0_ffn_W2"], "b2": inp["L0_ffn_b2"],
        "wout": inp["L0_ffn_W_out"], "bout": inp["L0_ffn_b_out"][None],
        "wqT": np.ascontiguousarray(inp["L1_attn_w_q"].T),
        "wkT": np.ascontiguousarray(inp["L1_attn_w_k"].T),
        "wvT": np.ascontiguousarray(inp["L1_attn_w_v"].T),
        "woT": np.ascontiguousarray(inp["L1_attn_w_o"].T),
        "gain": gainv,
        "wgT": np.ascontiguousarray(inp["L1_ffn_w_gate"].T),
        "wuT": np.ascontiguousarray(inp["L1_ffn_w_up"].T),
        "wdT": np.ascontiguousarray(inp["L1_ffn_w_down"].T),
    }
    akeys = {"wq", "wk", "wv", "wi", "wiT", "bq", "bk", "bv", "bi", "wg",
             "bg", "b_in_col", "posT", "wo", "bo", "w1", "b1", "w2", "b2",
             "wout", "bout", "wqT", "wkT", "wvT", "gain"}
    shared_a = {k_: v_ for k_, v_ in shared.items() if k_ in akeys}
    maps = []
    for (b, s0) in cores:
        m = dict(shared_a)
        m["x_ext"] = np.ascontiguousarray(x[b][_ext_idx(s0)])
        m["slot_w"] = _slot_w(s0)
        m["invcnt"] = _invcnt(s0)
        maps.append(m)
    ra = run_bass_kernel_spmd(nc_a, maps, core_ids=R).results
    bkeys = {"woT", "gain", "wgT", "wuT", "wdT"}
    shared_b = {k_: v_ for k_, v_ in shared.items() if k_ in bkeys}
    maps = []
    for c, (b, s0) in enumerate(cores):
        m = dict(shared_b)
        m["grams"] = np.stack([ra[b * 4 + j]["gram"] for j in range(4)])
        m["v1"] = ra[c]["v1"]
        m["xn1"] = ra[c]["xn1"]
        m["x1"] = ra[c]["x1"]
        maps.append(m)
    rb = run_bass_kernel_spmd(nc_b, maps, core_ids=R).results
    out = np.zeros((B, S, 1024), np.float32)
    for c, (b, s0) in enumerate(cores):
        out[b, s0:s0 + OWN] = rb[c]["x2"]
    return out


def kernel_unfused(**inp):
    inp = {k: np.asarray(v) for k, v in inp.items()}
    x = inp["L0_attn_hidden_states"]
    cores = _core_slices()
    R = list(range(NCORE))

    nc = build_l0a()
    shared = {
        "wq": inp["L0_attn_Wq"], "wk": inp["L0_attn_Wk"],
        "wv": inp["L0_attn_Wv"], "wi": inp["L0_ffn_W_in"],
        "bq": inp["L0_attn_bq"][None], "bk": inp["L0_attn_bk"][None],
        "bv": inp["L0_attn_bv"][None], "bi": inp["L0_ffn_b_in"][None],
        "wg": inp["L0_ffn_W_gate"], "bg": inp["L0_ffn_b_gate"][None],
        "wiT": np.ascontiguousarray(inp["L0_ffn_W_in"].T),
        "b_in_col": np.ascontiguousarray(
            inp["L0_ffn_b_in"].reshape(1024, 1)),
    }
    maps = []
    for (b, s0) in cores:
        m = dict(shared)
        m["x_ext"] = np.ascontiguousarray(x[b][_ext_idx(s0)])
        maps.append(m)
    r = run_bass_kernel_spmd(nc, maps, core_ids=R).results
    q_c = [r[c]["q"] for c in range(NCORE)]
    t_c = [r[c]["t"] for c in range(NCORE)]
    route_c = [r[c]["route"] for c in range(NCORE)]
    k_full = np.zeros((B, 8, 128, S), BF)
    v_full = np.zeros((B, S, 1040), BF)
    for c, (b, s0) in enumerate(cores):
        k_full[b][:, :, s0:s0 + OWN] = r[c]["k"]
        v_full[b][s0:s0 + OWN] = r[c]["v"]

    nc = build_l0b()
    shared = {"posT": np.ascontiguousarray(
        inp["L0_attn_pos_bias"].transpose(0, 2, 1)),
        "wo": inp["L0_attn_Wo"], "bo": inp["L0_attn_bo"][None]}
    maps = []
    for c, (b, s0) in enumerate(cores):
        idx = _ext_idx(s0)
        m = dict(shared)
        m["q"] = q_c[c]
        m["k"] = np.ascontiguousarray(k_full[b][:, :, idx])
        m["v"] = np.ascontiguousarray(v_full[b][idx])
        m["slot_w"] = _slot_w(s0)
        m["invcnt"] = _invcnt(s0)
        maps.append(m)
    r = run_bass_kernel_spmd(nc, maps, core_ids=R).results
    s0_c = [r[c]["s0"] for c in range(NCORE)]

    nc = build_l0c()
    shared = {"w1": inp["L0_ffn_W1"], "b1": inp["L0_ffn_b1"],
              "w2": inp["L0_ffn_W2"], "b2": inp["L0_ffn_b2"],
              "wout": inp["L0_ffn_W_out"], "bout": inp["L0_ffn_b_out"][None]}
    maps = []
    for c, (b, s0) in enumerate(cores):
        m = dict(shared)
        m["t"] = t_c[c]
        m["route"] = route_c[c]
        m["s0"] = s0_c[c]
        m["x"] = np.ascontiguousarray(x[b][s0:s0 + OWN])
        maps.append(m)
    r = run_bass_kernel_spmd(nc, maps, core_ids=R).results
    x1_c = [r[c]["x1"] for c in range(NCORE)]

    nc = build_l1a()
    gainv = np.asarray(inp["L1_attn_gain"], np.float32).reshape(1, 1)
    shared = {"wqT": np.ascontiguousarray(inp["L1_attn_w_q"].T),
              "wkT": np.ascontiguousarray(inp["L1_attn_w_k"].T),
              "wvT": np.ascontiguousarray(inp["L1_attn_w_v"].T),
              "gain": gainv}
    maps = []
    for c, (b, s0) in enumerate(cores):
        m = dict(shared)
        m["x1"] = x1_c[c]
        maps.append(m)
    r = run_bass_kernel_spmd(nc, maps, core_ids=R).results
    xn1_c = [r[c]["xn1"] for c in range(NCORE)]
    v1_c = [r[c]["v1"] for c in range(NCORE)]
    gram_c = [r[c]["gram"] for c in range(NCORE)]

    nc = build_l1b()
    shared = {"woT": np.ascontiguousarray(inp["L1_attn_w_o"].T),
              "gain": gainv}
    maps = []
    for c, (b, s0) in enumerate(cores):
        m = dict(shared)
        m["grams"] = np.stack([gram_c[b * 4 + j] for j in range(4)])
        m["v1"] = v1_c[c]
        m["xn1"] = xn1_c[c]
        maps.append(m)
    r = run_bass_kernel_spmd(nc, maps, core_ids=R).results
    s1_c = [r[c]["s1"] for c in range(NCORE)]

    nc = build_l1c()
    shared = {"wgT": np.ascontiguousarray(inp["L1_ffn_w_gate"].T),
              "wuT": np.ascontiguousarray(inp["L1_ffn_w_up"].T),
              "wdT": np.ascontiguousarray(inp["L1_ffn_w_down"].T)}
    maps = []
    for c, (b, s0) in enumerate(cores):
        m = dict(shared)
        m["xn1"] = xn1_c[c]
        m["x1"] = x1_c[c]
        m["s1"] = s1_c[c]
        maps.append(m)
    r = run_bass_kernel_spmd(nc, maps, core_ids=R).results

    out = np.zeros((B, S, 1024), np.float32)
    for c, (b, s0) in enumerate(cores):
        out[b, s0:s0 + OWN] = r[c]["x2"]
    return out


if __name__ == "__main__":
    pass


# version 09 (final)
# speedup vs baseline: 1.2712x; 1.0805x over previous
import math
import numpy as np
import ml_dtypes
import concourse.bass as bass
import concourse.mybir as mybir
from concourse.bass_utils import run_bass_kernel_spmd

F32 = mybir.dt.float32
BF16 = mybir.dt.bfloat16
I32 = mybir.dt.int32
AF = mybir.ActivationFunctionType
ALU = mybir.AluOpType
AX = mybir.AxisListType
BF = ml_dtypes.bfloat16

B, S, D = 2, 8192, 1024
OWN = 2048
HALO = 256
EXT = OWN + 2 * HALO
NCORE = 8
H, W, STRIDE = 16, 512, 256
HD = 64
N_EXP, D_MOE = 8, 4096
FFN1_H = 3072
EPS_LN = 1e-5
EPS_MP = 1e-4
TRUNC_L0B = False


class SemW:

    __slots__ = ("h", "base")

    def __init__(self, h, base):
        self.h = h
        self.base = base


class Pl:
    ENGS = ("sync", "vector", "scalar", "tensor", "gpsimd")

    def __init__(self, nc):
        self.nc = nc
        self.recs = {e: [] for e in self.ENGS}
        self.cnt = {}
        self.bar_sem = None
        self.rsync = None
        self.junk_ap = None

    def _h(self, s):
        return (s.h, s.base) if isinstance(s, SemW) else (s, 0)

    def val(self, s):
        h, base = self._h(s)
        return self.cnt.get(id(h), 0) - base

    def bump(self, s, n):
        h, _ = self._h(s)
        self.cnt[id(h)] = self.cnt.get(id(h), 0) + n
        return self.cnt[id(h)]

    def _tw(self, waits):
        out = []
        for sm, v in waits:
            h, base = self._h(sm)
            out.append((h, base + v))
        return out

    def _ti(self, inc):
        if inc is None:
            return None
        h, _ = self._h(inc[0])
        return (h, inc[1])

    def add(self, eng, build, waits=(), inc=None):
        if inc is not None:
            self.bump(inc[0], inc[1])
        self.recs[eng].append({"build": build, "waits": self._tw(waits),
                               "inc": self._ti(inc)})

    def raw_wait(self, eng, sm, v):
        self.recs[eng].append({"build": None, "waits": self._tw([(sm, v)]),
                               "inc": None})

    def add_r(self, eng, build, waits=(), inc=None):
        rs = self.rsync[eng]
        self.add(eng, build, waits=waits, inc=(rs, 1))
        self.raw_wait(eng, rs, self.val(rs))
        if inc is not None:
            self.bump(inc[0], inc[1])
            self.recs[eng].append({"build": None, "waits": [],
                                   "inc": None,
                                   "postinc": self._ti(inc)})

    def flush(self, eng):
        rs = self.rsync[eng]
        recs = self.recs[eng]
        prev = None
        for r in reversed(recs):
            if r["build"] is not None:
                prev = r
                break
        if prev is None:
            return
        if prev["inc"] is None:
            prev["inc"] = (rs, 1)
            self.bump(rs, 1)
            self.raw_wait(eng, rs, self.val(rs))
        else:
            self.raw_wait(eng, prev["inc"][0], self.val(prev["inc"][0]))

    def add_red(self, eng, build, waits=(), inc=None):
        self.flush(eng)
        self.add_r(eng, build, waits=waits, inc=inc)

    def barrier(self):
        b = self.bar_sem
        ja = self.junk_ap
        for _ in range(5):
            self.bump(b, 1)
        tgt = self.val(b)
        self.recs["vector"].append({
            "build": lambda eo: eo.memset(ja[:, 0:1], 0.0),
            "waits": [], "inc": (b, 1), "postwait": (b, tgt)})
        self.recs["scalar"].append({
            "build": lambda eo: eo.memzero(ja[:, 1:2]),
            "waits": [], "inc": (b, 1), "postwait": (b, tgt)})
        self.recs["tensor"].append({
            "build": lambda eo: eo.nop(),
            "waits": [], "inc": (b, 1), "postwait": (b, tgt)})
        for e in ("sync", "gpsimd"):
            self.recs[e].append({
                "build": None, "waits": [], "inc": None,
                "postinc": (b, 1), "postwait": (b, tgt)})

    def _run(self, eng, eo):
        for r in self.recs[eng]:
            for sm, v in r["waits"]:
                eo.wait_ge(sm, v)
            if r["build"] is not None:
                ins = r["build"](eo)
                if r["inc"] is not None:
                    ins.then_inc(r["inc"][0], r["inc"][1])
            if "postinc" in r:
                eo.sem_inc(r["postinc"][0], r["postinc"][1])
            if "postwait" in r:
                eo.wait_ge(r["postwait"][0], r["postwait"][1])

    def finalize(self):
        with self.nc.Block() as block:
            @block.sync
            def _(eng):
                self._run("sync", eng)

            @block.vector
            def _(eng):
                self._run("vector", eng)

            @block.scalar
            def _(eng):
                self._run("scalar", eng)

            @block.tensor
            def _(eng):
                self._run("tensor", eng)

            @block.gpsimd
            def _(eng):
                self._run("gpsimd", eng)


def seg(ap2d, e):
    return ap2d.rearrange("p (s e) -> p s e", e=e)


def bcast(ap2d, e):
    return bass.AP(tensor=ap2d.tensor, offset=ap2d.offset,
                   ap=[list(ap2d.ap[0]), list(ap2d.ap[1]), [0, e]])


class Ctx:

    def __init__(self, nc):
        self.nc = nc
        self.p = Pl(nc)
        self.exits = []
        self.sems = {}
        self.uid = 0
        base = (nc.sbuf_base + 63) & ~63
        self.freelist = [(base, nc.sbuf_top - base)]
        self.sizes = {}
        self.hiwater = 0

    def _name(self, base):
        self.uid += 1
        return f"{base}_{self.uid}"

    SEM_POOL = 78

    def sem(self, name=None, swdge=False):
        if swdge:
            cm = self.nc.semaphore(self._name("swdge"))
            h = cm.__enter__()
            self.exits.append((None, cm))
            return h
        if name is None:
            self.anon_cnt = getattr(self, "anon_cnt", 0)
            self.anon_pool = getattr(self, "anon_pool", [])
            if self.anon_cnt >= self.SEM_POOL:
                h = self.anon_pool[self.anon_cnt % self.SEM_POOL]
                self.anon_cnt += 1
                return SemW(h, self.p.val(h))
            name = self._name("sem")
            cm = self.nc.semaphore(self._name(name))
            h = cm.__enter__()
            self.exits.append((None, cm))
            self.anon_pool.append(h)
            self.anon_cnt += 1
            return h
        if name not in self.sems:
            cm = self.nc.semaphore(self._name(name))
            self.sems[name] = cm.__enter__()
            self.exits.append((None, cm))
        return self.sems[name]

    def sbuf(self, name, shape, dtype):
        nb = int(np.prod(shape[1:])) * mybir.dt.size(dtype)
        nb = (nb + 63) & ~63
        for i, (off, sz) in enumerate(self.freelist):
            if sz >= nb:
                if sz == nb:
                    self.freelist.pop(i)
                else:
                    self.freelist[i] = (off + nb, sz - nb)
                t = self.nc.alloc_sbuf_tensor_at(
                    self._name(name), shape, dtype, offset=off)
                self.sizes[id(t)] = (off, nb)
                used = self.nc.sbuf_top - sum(s for _, s in self.freelist) \
                    - ((self.nc.sbuf_base + 63) & ~63)
                self.hiwater = max(self.hiwater, used)
                return t
        raise MemoryError(
            f"sbuf arena out of space for {name} {shape} ({nb}B); "
            f"free={[s for _, s in self.freelist]}")

    def psum(self, name, shape, dtype=F32):
        cm = self.nc.psum_tensor(self._name(name), shape, dtype)
        t = cm.__enter__()
        self.exits.append((id(t), cm))
        return t

    def ccol(self, val):
        val = float(val)
        if val not in self.cmap:
            assert len(self.cvals) < self.cpool.shape[1]
            self.cmap[val] = len(self.cvals)
            self.cvals.append(val)
        return self.cpool[:, self.cmap[val]:self.cmap[val] + 1]

    def free(self, *tensors):
        for t in tensors:
            if isinstance(t, (list, tuple)):
                self.free(*reversed(t))
                continue
            key = id(t)
            if key in self.sizes:
                off, nb = self.sizes.pop(key)
                self.freelist.append((off, nb))
                self.freelist.sort()
                merged = []
                for o, s in self.freelist:
                    if merged and merged[-1][0] + merged[-1][1] == o:
                        merged[-1] = (merged[-1][0], merged[-1][1] + s)
                    else:
                        merged.append((o, s))
                self.freelist = [tuple(x) for x in merged]
                continue
            for i, (k, cm) in enumerate(self.exits):
                if k == key:
                    cm.__exit__(None, None, None)
                    self.exits.pop(i)
                    break
            else:
                raise KeyError("free: tensor not found")

    def close(self):
        for _, cm in reversed(self.exits):
            cm.__exit__(None, None, None)
        self.exits = []


def make_ctx(name):
    nc = bass.Bass("TRN2", target_bir_lowering=False)
    ctx = Ctx(nc)
    ctx.p.bar_sem = ctx.sem("bar")
    ctx.p.rsync = {e: ctx.sem(f"rsync_{e}")
                   for e in ("sync", "vector", "scalar", "tensor",
                             "gpsimd")}
    ident = nc.inline_tensor(np.eye(128, dtype=BF), name=ctx._name("identbf")).ap()
    ident32 = nc.inline_tensor(np.eye(128, dtype=np.float32),
                               name=ctx._name("identf32")).ap()
    ones = nc.inline_tensor(np.ones((1, 512), dtype=BF), name=ctx._name("onesbf")).ap()
    ctx.ident_sb = ctx.sbuf("ident_sb", [128, 128], BF16)
    ctx.ident_f32 = ctx.sbuf("ident_f32", [128, 128], F32)
    ctx.ones_sb = ctx.sbuf("ones_sb", [1, 512], BF16)
    ld = ctx.sem("ldconst")
    ctx.p.add("sync", lambda e: e.dma_start(out=ctx.ident_sb[:], in_=ident),
              inc=(ld, 16))
    ctx.p.add("sync", lambda e: e.dma_start(out=ctx.ident_f32[:], in_=ident32),
              inc=(ld, 16))
    ctx.p.add("sync", lambda e: e.dma_start(out=ctx.ones_sb[:], in_=ones),
              inc=(ld, 16))
    ctx.cvals, ctx.cmap = [], {}
    ctx.cpool = ctx.sbuf("cpool", [128, 32], F32)
    ctx.junk = ctx.sbuf("junk", [128, 4], F32)
    ctx.p.junk_ap = ctx.junk

    def _ld_cpool(e):
        vals = ctx.cvals or [0.0]
        arr = np.tile(np.asarray(vals, np.float32), (128, 1))
        return e.dma_start(out=ctx.cpool[:, 0:len(vals)],
                           in_=nc.inline_tensor(arr, name=ctx._name("cpool")).ap())
    ctx.p.add("sync", _ld_cpool, inc=(ld, 16))
    ctx.p.raw_wait("tensor", ld, 64)
    ctx.p.raw_wait("vector", ld, 64)
    ctx.p.raw_wait("scalar", ld, 64)
    ctx.p.barrier()
    return ctx


def load_weight_bf16(ctx, name, dram_ap, kin, nout, scale=None):
    p = ctx.p
    nk = (kin + 127) // 128
    ND = 4
    stage = [ctx.sbuf(f"wst_{name}", [128, nout], F32) for _ in range(ND)]
    out = [ctx.sbuf(f"w_{name}{k}", [128, nout], BF16) for k in range(nk)]
    s_lds = [ctx.sem() for _ in range(ND)]
    s_cv = ctx.sem()
    for k in range(nk):
        buf = stage[k % ND]
        s_ld = s_lds[k % ND]
        w = [(s_cv, k - ND + 1)] if k >= ND else []
        p.add("sync", lambda e, buf=buf, k=k: e.dma_start(
            out=buf[:], in_=dram_ap[k * 128:(k + 1) * 128, :]),
            waits=w, inc=(s_ld, 16))
        wv = [(s_ld, 16 * (k // ND + 1))]
        if scale is None:
            p.add("vector", lambda e, k=k, buf=buf: e.tensor_copy(
                out=out[k][:], in_=buf[:]), waits=wv, inc=(s_cv, 1))
        else:
            p.add("vector", lambda e, k=k, buf=buf: e.tensor_scalar(
                out=out[k][:], in0=buf[:], scalar1=scale, scalar2=None,
                op0=ALU.mult), waits=wv, inc=(s_cv, 1))
    p.raw_wait("tensor", s_cv, nk)
    p.raw_wait("scalar", s_cv, nk)
    p.raw_wait("sync", s_cv, nk)
    p.raw_wait("gpsimd", s_cv, nk)
    ctx.free(stage)
    return out


def load_bias_bf16(ctx, name, dram_ap, n, scale=None):
    p = ctx.p
    stage = ctx.sbuf(f"bst_{name}", [1, n], F32)
    out = ctx.sbuf(f"b_{name}", [1, n], BF16)
    s_ld = ctx.sem()
    s_cv = ctx.sem()
    p.add("sync", lambda e: e.dma_start(out=stage[:], in_=dram_ap),
          inc=(s_ld, 16))
    if scale is None:
        p.add("vector", lambda e: e.tensor_copy(out=out[:], in_=stage[:]),
              waits=[(s_ld, 16)], inc=(s_cv, 1))
    else:
        p.add("vector", lambda e: e.tensor_scalar(
            out=out[:], in0=stage[:], scalar1=scale, scalar2=None,
            op0=ALU.mult), waits=[(s_ld, 16)], inc=(s_cv, 1))
    p.raw_wait("tensor", s_cv, 1)
    p.raw_wait("sync", s_cv, 1)
    p.raw_wait("gpsimd", s_cv, 1)
    ctx.free(stage)
    return out


def emit_ln_to_fm(ctx, x_dram, nrows, tag, keep_tm=False, keep_stats=False):
    p = ctx.p
    nt = nrows // 128
    xt = [ctx.sbuf(f"lnx_{tag}", [128, 1024], F32) for _ in range(nt)]
    fm = [ctx.sbuf(f"xnfm_{tag}{c}", [128, nrows], BF16) for c in range(8)]
    nmr = ctx.sbuf(f"lnnmr_{tag}", [128, nt], F32)
    nm = ctx.sbuf(f"lnnm_{tag}", [128, nt], F32)
    ssq = ctx.sbuf(f"lnssq_{tag}", [128, nt], F32)
    rstd_a = ctx.sbuf(f"lnrstda_{tag}", [128, nt], F32)
    sq = [ctx.sbuf(f"lnsq_{tag}", [128, 1024], F32) for _ in range(2)]
    rstd = ctx.sbuf(f"lnrstd_{tag}", [128, nt], F32)
    if keep_tm:
        tmb = [ctx.sbuf(f"lntm_{tag}", [128, 1024], BF16) for _ in range(nt)]
    else:
        tmb = [ctx.sbuf(f"lntm_{tag}", [128, 1024], BF16) for _ in range(2)]
    tp = [ctx.psum(f"lntp_{tag}", [128, 128], BF16) for _ in range(2)]
    s_ld = ctx.sem()
    s_st = ctx.sem()
    s_sq = ctx.sem()
    s_no = ctx.sem()
    s_tp = ctx.sem()
    s_ev = ctx.sem()
    for i in range(nt):
        p.add("sync", lambda e, i=i: e.dma_start(
            out=xt[i][:], in_=x_dram[i * 128:(i + 1) * 128, :]),
            inc=(s_ld, 16))
    for i in range(nt):
        last = (i == nt - 1)
        fn = p.add_r if last else p.add
        fn("vector", lambda e, i=i: e.reduce_sum(
            out=nmr[:, i:i + 1], in_=xt[i][:], axis=AX.X),
            waits=[(s_ld, 16 * nt)])
    p.add("vector", lambda e: e.tensor_scalar(
        out=nm[:], in0=nmr[:], scalar1=-1.0 / 1024.0, scalar2=None,
        op0=ALU.mult), inc=(s_st, 1))
    for i in range(nt):
        p.add_r("scalar", lambda e, i=i: e.activation(
            out=sq[i % 2][:], in_=xt[i][:], func=AF.Square,
            bias=nm[:, i:i + 1], scale=1.0, accum_out=ssq[:, i:i + 1]),
            waits=[(s_st, 1)], inc=(s_sq, 1))
    c_eps, c_inv, c_half = ctx.ccol(EPS_LN), ctx.ccol(1.0 / 1024.0), \
        ctx.ccol(-0.5)
    p.add_r("scalar", lambda e: e.activation(
        out=rstd_a[:], in_=ssq[:], func=AF.Ln, bias=c_eps, scale=c_inv))
    p.add("scalar", lambda e: e.activation(
        out=rstd[:], in_=rstd_a[:], func=AF.Exp, scale=c_half),
        inc=(s_sq, 1))
    p.raw_wait("vector", s_sq, nt + 1)
    for i in range(nt):
        t = tmb[i] if keep_tm else tmb[i % 2]
        wv = []
        if not keep_tm and i >= 2:
            wv.append((s_tp, 8 * (i - 1)))
        p.add("vector", lambda e, i=i, t=t: e.tensor_scalar(
            out=t[:], in0=xt[i][:], scalar1=nm[:, i:i + 1],
            scalar2=rstd[:, i:i + 1], op0=ALU.add, op1=ALU.mult),
            waits=wv, inc=(s_no, 1))
        for c in range(8):
            j = i * 8 + c
            wv = [(s_no, i + 1)]
            if j >= 2:
                wv.append((s_ev, j - 1))
            p.add("tensor", lambda e, c=c, t=t, j=j: e.transpose(
                tp[j % 2][:], t[:, c * 128:(c + 1) * 128], ctx.ident_sb[:]),
                waits=wv, inc=(s_tp, 1))
            p.add("vector", lambda e, i=i, c=c, j=j: e.tensor_copy(
                out=fm[c][:, i * 128:(i + 1) * 128], in_=tp[j % 2][:]),
                waits=[(s_tp, j + 1)], inc=(s_ev, 1))
    p.barrier()
    if keep_stats:
        ctx.ln_stats = (xt, nm, rstd)
        ctx.ln_stats_free = lambda: ctx.free(xt, nmr, nm, ssq, rstd_a, sq,
                                             rstd)
        ctx.free(tp)
    else:
        ctx.free(xt, nmr, nm, ssq, rstd_a, sq, rstd, tp)
    if keep_tm:
        return fm, tmb
    ctx.free(tmb)
    return fm, None


def emit_gemm_fm_fm(ctx, xfm, T, wsb, nw, bias_row, out, tag, tsub=512,
                    evac=None):
    p = ctx.p
    ps = [ctx.psum(f"gm_{tag}", [128, tsub], F32) for _ in range(2)]
    s_mm = ctx.sem()
    s_ev = ctx.sem()
    ntt = (T + tsub - 1) // tsub
    it = 0
    for n in range(nw // 128):
        for tt in range(ntt):
            t0, t1 = tt * tsub, min((tt + 1) * tsub, T)
            pb = ps[it % 2]
            wv = [(s_ev, it - 1)] if it >= 2 else []
            for k in range(8):
                last = (k == 7) and bias_row is None
                p.add("tensor", lambda e, k=k, n=n, t0=t0, t1=t1, pb=pb,
                      last=last: e.matmul(pb[:, 0:t1 - t0],
                                          wsb[k][:, n * 128:(n + 1) * 128],
                                          xfm[k][:, t0:t1], start=(k == 0),
                                          stop=last),
                      waits=wv if k == 0 else [],
                      inc=(s_mm, 1) if last else None)
            if bias_row is not None:
                p.add("tensor", lambda e, n=n, t0=t0, t1=t1, pb=pb: e.matmul(
                    pb[:, 0:t1 - t0], bias_row[:, n * 128:(n + 1) * 128],
                    ctx.ones_sb[:, 0:t1 - t0], start=False, stop=True),
                    inc=(s_mm, 1))
            it += 1
            if evac is None:
                p.add("vector", lambda e, n=n, t0=t0, t1=t1, pb=pb, v=it:
                      e.tensor_copy(out=out[n][:, t0:t1],
                                    in_=pb[:, 0:t1 - t0]),
                      waits=[(s_mm, it)], inc=(s_ev, 1))
            else:
                p.add("vector", lambda e, n=n, t0=t0, t1=t1, pb=pb: evac(
                    e, pb[:, 0:t1 - t0], (n, t0, t1)),
                    waits=[(s_mm, it)], inc=(s_ev, 1))
    p.barrier()
    ctx.free(ps)


def emit_gemm_fm_tm(ctx, xfm, T, wsb, nw, bias_row, tag, evac, nsub=512,
                    evac_eng="vector"):
    p = ctx.p
    ps = [ctx.psum(f"gt_{tag}", [128, nsub], F32) for _ in range(2)]
    s_mm = ctx.sem()
    s_ev = ctx.sem()
    nkc = len(xfm)
    it = 0
    for tt in range(T // 128):
        for n in range((nw + nsub - 1) // nsub):
            n0, n1 = n * nsub, min((n + 1) * nsub, nw)
            pb = ps[it % 2]
            wv = [(s_ev, it - 1)] if it >= 2 else []
            for k in range(nkc):
                last = (k == nkc - 1) and bias_row is None
                p.add("tensor", lambda e, k=k, tt=tt, n0=n0, n1=n1, pb=pb,
                      last=last: e.matmul(
                          pb[:, 0:n1 - n0], xfm[k][:, tt * 128:(tt + 1) * 128],
                          wsb[k][:, n0:n1], start=(k == 0), stop=last),
                      waits=wv if k == 0 else [],
                      inc=(s_mm, 1) if last else None)
            if bias_row is not None:
                p.add("tensor", lambda e, tt=tt, n0=n0, n1=n1, pb=pb: e.matmul(
                    pb[:, 0:n1 - n0], ctx.ones_sb[0:1, 0:128],
                    bias_row[:, n0:n1], start=False, stop=True),
                    inc=(s_mm, 1))
            it += 1
            p.add(evac_eng, lambda e, tt=tt, n0=n0, n1=n1, pb=pb: evac(
                e, pb[:, 0:n1 - n0], (tt, n0, n1)),
                waits=[(s_mm, it)], inc=(s_ev, 1))
    p.barrier()
    ctx.free(ps)


def emit_tm_to_fm(ctx, tm_tiles, T, tag, out_dtype=BF16, scale_cols=None,
                  in_ready=None):
    p = ctx.p
    nt = T // 128
    fm = [ctx.sbuf(f"fm_{tag}{c}", [128, T], out_dtype) for c in range(8)]
    tp = [ctx.psum(f"tp_{tag}", [128, 128], out_dtype) for _ in range(2)]
    s_sc = ctx.sem()
    s_tp = ctx.sem()
    s_ev = ctx.sem()
    for i in range(nt):
        if scale_cols is not None:
            p.add("vector", lambda e, i=i: e.tensor_scalar(
                out=tm_tiles[i][:], in0=tm_tiles[i][:],
                scalar1=scale_cols[:, i:i + 1], scalar2=None, op0=ALU.mult),
                waits=in_ready or [], inc=(s_sc, 1))
        for c in range(8):
            j = i * 8 + c
            wv = []
            if scale_cols is not None:
                wv.append((s_sc, i + 1))
            elif in_ready and j == 0:
                wv.extend(in_ready)
            if j >= 2:
                wv.append((s_ev, j - 1))
            p.add("tensor", lambda e, i=i, c=c, j=j: e.transpose(
                tp[j % 2][:], tm_tiles[i][:, c * 128:(c + 1) * 128],
                ctx.ident_sb[:]), waits=wv, inc=(s_tp, 1))
            p.add("vector", lambda e, i=i, c=c, j=j: e.tensor_copy(
                out=fm[c][:, i * 128:(i + 1) * 128], in_=tp[j % 2][:]),
                waits=[(s_tp, j + 1)], inc=(s_ev, 1))
    p.barrier()
    ctx.free(tp)
    return fm


def slot_qrange(j):
    r0 = max(0, 256 * (j - 1))
    r1 = min(OWN, 256 * (j + 1))
    return r0, r1


def build_l0b():
    ctx = make_ctx("l0b")
    nc = ctx.nc
    io = {}
    io["q"] = nc.dram_tensor("q", [8, 128, OWN], BF16,
                             kind="ExternalInput").ap()
    io["k"] = nc.dram_tensor("k", [8, 128, EXT], BF16,
                             kind="ExternalInput").ap()
    io["v"] = nc.dram_tensor("v", [EXT, 1040], BF16,
                             kind="ExternalInput").ap()
    io["posT"] = nc.dram_tensor("posT", [16, 512, 512], F32,
                                kind="ExternalInput").ap()
    io["slot_w"] = nc.dram_tensor("slot_w", [128, 9], F32,
                                  kind="ExternalInput").ap()
    io["invcnt"] = nc.dram_tensor("invcnt", [128, OWN // 128], F32,
                                  kind="ExternalInput").ap()
    io["wo"] = nc.dram_tensor("wo", [1024, 1024], F32,
                              kind="ExternalInput").ap()
    io["bo"] = nc.dram_tensor("bo", [1, 1024], F32,
                              kind="ExternalInput").ap()
    io["s0"] = nc.dram_tensor("s0", [OWN, 1024], BF16,
                              kind="ExternalOutput").ap()
    emit_l0b(ctx, io)
    ctx.p.finalize()
    ctx.close()
    return nc


def emit_l0b(ctx, io):
    nc, p = ctx.nc, ctx.p
    q_d, k_d, v_d, pos_d = io["q"], io["k"], io["v"], io["posT"]
    sw_d, ic_d, wo_d, bo_d = io["slot_w"], io["invcnt"], io["wo"], io["bo"]
    s_o = io["s0"]

    qfm = [ctx.sbuf(f"qfm{c}", [128, OWN], BF16) for c in range(8)]
    kfm = [ctx.sbuf(f"kfm{c}", [128, EXT], BF16) for c in range(8)]
    vtm = [ctx.sbuf("vtm", [128, 1040], BF16) for _ in range(EXT // 128)]
    ctxa = [ctx.sbuf("ctxa", [128, 1024], BF16) for _ in range(OWN // 128)]
    sw = ctx.sbuf("sw", [128, 9], F32)
    ic = ctx.sbuf("ic", [128, OWN // 128], F32)
    s_ld = ctx.sem()
    nld = 0
    for c in range(8):
        p.add("sync", lambda e, c=c: e.dma_start(out=qfm[c][:], in_=q_d[c]),
              inc=(s_ld, 16)); nld += 1
        p.add("sync", lambda e, c=c: e.dma_start(out=kfm[c][:], in_=k_d[c]),
              inc=(s_ld, 16)); nld += 1
    for i in range(EXT // 128):
        p.add("sync", lambda e, i=i: e.dma_start(
            out=vtm[i][:], in_=v_d[i * 128:(i + 1) * 128, :]),
            inc=(s_ld, 16)); nld += 1
    p.add("sync", lambda e: e.dma_start(out=sw[:], in_=sw_d),
          inc=(s_ld, 16)); nld += 1
    p.add("sync", lambda e: e.dma_start(out=ic[:], in_=ic_d),
          inc=(s_ld, 16)); nld += 1
    p.raw_wait("tensor", s_ld, 16 * nld)
    p.raw_wait("vector", s_ld, 16 * nld)
    p.raw_wait("scalar", s_ld, 16 * nld)
    p.barrier()

    pstg = [ctx.sbuf("pstg", [128, 512], F32) for _ in range(2)]
    psb = [[ctx.sbuf("psb", [128, 512], BF16) for _ in range(4)]
           for _ in range(2)]
    sc = [ctx.psum("sc", [128, 512], F32) for _ in range(2)]
    owp = [ctx.psum("owp", [128, 65], F32) for _ in range(4)]
    exp_sb = [[ctx.sbuf("exp", [128, 512], BF16) for _ in range(4)]
              for _ in range(2)]
    rcol = ctx.sbuf("rcol", [128, 2], F32)
    s_plds = [ctx.sem(), ctx.sem()]
    s_pcv = ctx.sem()
    s_sc = ctx.sem()
    s_ex = ctx.sem()
    s_ow = ctx.sem()
    s_mg = ctx.sem()

    npld = [0]

    def load_pos(h):
        buf = h % 2
        for ksub in range(4):
            wv = []
            if npld[0] >= 2:
                wv.append((s_pcv, npld[0] - 1))
            p.add("sync", lambda e, h=h, ksub=ksub, n=npld[0]: e.dma_start(
                out=pstg[n % 2][:],
                in_=pos_d[h, ksub * 128:(ksub + 1) * 128, :]),
                waits=wv, inc=(s_plds[npld[0] % 2], 16))
            wv2 = [(s_plds[npld[0] % 2], 16 * (npld[0] // 2 + 1))]
            if h >= 2:
                wv2.append((s_sc, sc_cnt_at_head_end[h - 2]))
            p.add("vector", lambda e, buf=buf, ksub=ksub, n=npld[0]:
                  e.tensor_copy(out=psb[buf][ksub][:], in_=pstg[n % 2][:]),
                  waits=wv2, inc=(s_pcv, 1))
            npld[0] += 1

    sc_cnt_at_head_end = {}
    cnt = 0
    for h in range(16):
        cnt += 9 * 4
        sc_cnt_at_head_end[h] = cnt

    ow_tile_last_mg = [0, 0, 0, 0]
    first_touch = {}
    gk = [0]
    gow = [0]
    ow_end_of_slot = {}

    def emit_scores(h, j, sl):
        ch, rb = h // 2, (h % 2) * 64
        r0, r1 = slot_qrange(j)
        qlen = r1 - r0
        qoff = r0 + 256 - 256 * j
        for ksub in range(4):
            scb = sc[gk[0] % 2]
            wv = [(s_pcv, 4 * h + ksub + 1)]
            if gk[0] >= 2:
                wv.append((s_ex, gk[0] - 1))
            p.add("tensor", lambda e, ch=ch, rb=rb, j=j, ksub=ksub,
                  r0=r0, qlen=qlen, scb=scb: e.matmul(
                      scb[:, 0:qlen],
                      kfm[ch][rb:rb + 64,
                              256 * j + 128 * ksub:256 * j + 128 * ksub + 128],
                      qfm[ch][rb:rb + 64, r0:r0 + qlen],
                      start=True, stop=False), waits=wv)
            p.add("tensor", lambda e, h=h, ksub=ksub, qoff=qoff,
                  qlen=qlen, scb=scb: e.matmul(
                      scb[:, 0:qlen], ctx.ident_sb[:],
                      psb[h % 2][ksub][:, qoff:qoff + qlen],
                      start=False, stop=True), inc=(s_sc, 1))
            ebuf = exp_sb[sl % 2][ksub]
            wv = [(s_sc, p.val(s_sc))]
            if sl >= 2:
                wv.append((s_ow, ow_end_of_slot[sl - 2]))
            p.add("scalar", lambda e, ebuf=ebuf, qlen=qlen, scb=scb:
                  e.activation(out=ebuf[:, 0:qlen], in_=scb[:, 0:qlen],
                               func=AF.Exp), waits=wv, inc=(s_ex, 1))
            gk[0] += 1

    def emit_ow(h, j, sl):
        r0, r1 = slot_qrange(j)
        qlen = r1 - r0
        exps = exp_sb[sl % 2]
        ex_done = 4 * (sl + 1)
        for qq in range(qlen // 128):
            ti = r0 // 128 + qq
            ob = owp[gow[0] % 4]
            wv = [(s_ex, ex_done), (s_mg, ow_tile_last_mg[gow[0] % 4])]
            for ksub in range(4):
                p.add("tensor", lambda e, exps=exps, ksub=ksub, qq=qq,
                      j=j, h=h, ob=ob: e.matmul(
                          ob[:],
                          exps[ksub][:, qq * 128:qq * 128 + 128],
                          vtm[2 * j + ksub][:, 65 * h:65 * h + 65],
                          start=(ksub == 0), stop=(ksub == 3)),
                      waits=wv if ksub == 0 else [],
                      inc=(s_ow, 1) if ksub == 3 else None)
            gow[0] += 1
            wv = [(s_ow, p.val(s_ow))]
            p.add_r("vector", lambda e, ob=ob: e.reciprocal(
                out=rcol[:, 0:1], in_=ob[:, 64:65]), waits=wv)
            p.add_r("vector", lambda e, j=j: e.tensor_tensor(
                out=rcol[:, 1:2], in0=rcol[:, 0:1], in1=sw[:, j:j + 1],
                op=ALU.mult))
            ft = (ti, h)
            if ft not in first_touch:
                first_touch[ft] = True
                p.add("vector", lambda e, ob=ob, ti=ti, h=h:
                      e.tensor_scalar(
                          out=ctxa[ti][:, 64 * h:64 * h + 64],
                          in0=ob[:, 0:64], scalar1=rcol[:, 1:2],
                          scalar2=None, op0=ALU.mult), inc=(s_mg, 1))
            else:
                p.add("vector", lambda e, ob=ob, ti=ti, h=h:
                      e.scalar_tensor_tensor(
                          out=ctxa[ti][:, 64 * h:64 * h + 64],
                          in0=ob[:, 0:64], scalar=rcol[:, 1:2],
                          in1=ctxa[ti][:, 64 * h:64 * h + 64],
                          op0=ALU.mult, op1=ALU.add), inc=(s_mg, 1))
            ow_tile_last_mg[(gow[0] - 1) % 4] = p.val(s_mg)
        ow_end_of_slot[sl] = p.val(s_ow)

    load_pos(0)
    load_pos(1)
    slots_list = [(h, j) for h in range(16) for j in range(9)]
    for sl, (h, j) in enumerate(slots_list):
        emit_scores(h, j, sl)
        if sl >= 1:
            emit_ow(*slots_list[sl - 1], sl - 1)
        if j == 8 and h + 2 < 16:
            load_pos(h + 2)
    emit_ow(*slots_list[-1], len(slots_list) - 1)
    p.barrier()
    ctx.free(qfm, kfm, vtm, pstg, psb, exp_sb, owp, sc)

    ctx_fm = emit_tm_to_fm(ctx, ctxa, OWN, "cx", scale_cols=ic)
    ctx.free(ctxa)
    wo_sb = load_weight_bf16(ctx, "wo", wo_d, 1024, 1024)
    bo_sb = load_bias_bf16(ctx, "bo", bo_d, 1024)
    p.barrier()
    s0 = [ctx.sbuf("s0", [128, 1024], BF16) for _ in range(OWN // 128)]

    def s_evac(e, ps_ap, key):
        tt, n0, n1 = key
        return e.tensor_copy(out=s0[tt][:, n0:n1], in_=ps_ap)
    emit_gemm_fm_tm(ctx, ctx_fm, OWN, wo_sb, 1024, bo_sb, "s0", s_evac)
    st = ctx.sem(swdge=True)
    for i in range(OWN // 128):
        p.add("gpsimd", lambda e, i=i: e.dma_start(
            out=s_o[i * 128:(i + 1) * 128, :], in_=s0[i][:]), inc=(st, 16))
    p.raw_wait("gpsimd", st, 16 * (OWN // 128))
    p.barrier()
    ctx.free(s0, ctx_fm, wo_sb, bo_sb, sw, ic)


def store_and_free(ctx, pairs, also_free=()):
    p = ctx.p
    st = ctx.sem(swdge=True)
    n = 0
    for dram_ap, sb in pairs:
        p.add("gpsimd", lambda e, d=dram_ap, s=sb: e.dma_start(out=d, in_=s),
              inc=(st, 16))
        n += 1
    p.raw_wait("gpsimd", st, 16 * n)
    p.barrier()
    frees = []
    for _, sb in pairs:
        pass
    for t in also_free:
        ctx.free(t)


def build_l0a():
    ctx = make_ctx("l0a")
    nc = ctx.nc
    io = {}
    io["x_ext"] = nc.dram_tensor("x_ext", [EXT, 1024], F32,
                                 kind="ExternalInput").ap()
    for nm_ in ("wq", "wk", "wv", "wi", "wiT"):
        io[nm_] = nc.dram_tensor(nm_, [1024, 1024], F32,
                                 kind="ExternalInput").ap()
    for nm_ in ("bq", "bk", "bv", "bi"):
        io[nm_] = nc.dram_tensor(nm_, [1, 1024], F32,
                                 kind="ExternalInput").ap()
    io["wg"] = nc.dram_tensor("wg", [1024, 8], F32, kind="ExternalInput").ap()
    io["bg"] = nc.dram_tensor("bg", [1, 8], F32, kind="ExternalInput").ap()
    io["b_in_col"] = nc.dram_tensor("b_in_col", [1024, 1], F32,
                                    kind="ExternalInput").ap()
    io["q"] = nc.dram_tensor("q", [8, 128, OWN], BF16,
                             kind="ExternalOutput").ap()
    io["k"] = nc.dram_tensor("k", [8, 128, OWN], BF16,
                             kind="ExternalOutput").ap()
    io["v"] = nc.dram_tensor("v", [OWN, 1040], BF16,
                             kind="ExternalOutput").ap()
    io["t"] = nc.dram_tensor("t", [8, 128, OWN], BF16,
                             kind="ExternalOutput").ap()
    io["route"] = nc.dram_tensor("route", [OWN, 8], F32,
                                 kind="ExternalOutput").ap()
    emit_l0a(ctx, io, ext_kv=False)
    ctx.p.finalize()
    ctx.close()
    return nc


def emit_l0a(ctx, io, ext_kv):
    nc, p = ctx.nc, ctx.p
    x = io["x_ext"]
    wq_d, wk_d, wv_d, wi_d = io["wq"], io["wk"], io["wv"], io["wi"]
    bq_d, bk_d, bv_d, bi_d = io["bq"], io["bk"], io["bv"], io["bi"]
    wg_d, bg_d, wiT_d, bic_d = io["wg"], io["bg"], io["wiT"], io["b_in_col"]
    q_o, k_o, v_o, t_o, r_o = io["q"], io["k"], io["v"], io["t"], io["route"]

    SCL = 1.0 / 8.0
    xn, _ = emit_ln_to_fm(ctx, x, EXT, "a", keep_stats=True)
    xn_own = [c[:, HALO:HALO + OWN] for c in xn]

    wk_sb = load_weight_bf16(ctx, "wk", wk_d, 1024, 1024)
    bk_sb = load_bias_bf16(ctx, "bk", bk_d, 1024)
    p.barrier()
    kfm = [ctx.sbuf(f"kfm{c}", [128, EXT], BF16) for c in range(8)]
    emit_gemm_fm_fm(ctx, xn, EXT, wk_sb, 1024, bk_sb, kfm, "k")
    if ext_kv:
        store_and_free(ctx, [(k_o[c], kfm[c][:]) for c in range(8)])
    else:
        store_and_free(ctx, [(k_o[c], kfm[c][:, HALO:HALO + OWN])
                             for c in range(8)])
    ctx.free(kfm, wk_sb, bk_sb)

    wq_sb = load_weight_bf16(ctx, "wq", wq_d, 1024, 1024, scale=SCL)
    bq_sb = load_bias_bf16(ctx, "bq", bq_d, 1024, scale=SCL)
    p.barrier()
    qfm = [ctx.sbuf(f"qfm{c}", [128, OWN], BF16) for c in range(8)]
    emit_gemm_fm_fm(ctx, xn_own, OWN, wq_sb, 1024, bq_sb, qfm, "q")
    store_and_free(ctx, [(q_o[c], qfm[c][:]) for c in range(8)])
    ctx.free(qfm, wq_sb, bq_sb)

    wv_sb = load_weight_bf16(ctx, "wv", wv_d, 1024, 1024)
    bv_sb = load_bias_bf16(ctx, "bv", bv_d, 1024)
    p.barrier()
    VR = EXT if ext_kv else OWN
    xn_v = xn if ext_kv else xn_own
    nvt = VR // 128
    vtm = [ctx.sbuf("vtm", [128, 1040], BF16) for _ in range(nvt)]
    s_vz = ctx.sem()
    for i in range(nvt):
        p.add("vector", lambda e, i=i: e.memset(
            vtm[i].rearrange("p (h x) -> p h x", x=65)[:, :, 64:65], 1.0),
            inc=(s_vz, 1))
    p.raw_wait("vector", s_vz, nvt)

    def v_evac(e, ps_ap, key):
        tt, n0, n1 = key
        hbase = n0 // 64
        dst = vtm[tt].rearrange("p (h x) -> p h x", x=65)[
            :, hbase:hbase + (n1 - n0) // 64, 0:64]
        return e.tensor_copy(
            out=dst, in_=ps_ap.rearrange("p (h x) -> p h x", x=64))
    emit_gemm_fm_tm(ctx, xn_v, VR, wv_sb, 1024, bv_sb, "v", v_evac)
    store_and_free(ctx, [(v_o[i * 128:(i + 1) * 128, :], vtm[i][:])
                         for i in range(nvt)])
    ctx.free(vtm, wv_sb, bv_sb)

    wi_sb = load_weight_bf16(ctx, "wi", wi_d, 1024, 1024)
    bi_sb = load_bias_bf16(ctx, "bi", bi_d, 1024)
    p.barrier()
    tfm = [ctx.sbuf(f"tfm{c}", [128, OWN], BF16) for c in range(8)]
    emit_gemm_fm_fm(ctx, xn_own, OWN, wi_sb, 1024, bi_sb, tfm, "t")
    store_and_free(ctx, [(t_o[c], tfm[c][:]) for c in range(8)])
    ctx.free(wi_sb, bi_sb, xn)

    xt_ln, nm_ln, rstd_ln = ctx.ln_stats
    ntt = OWN // 128
    route = ctx.sbuf("route_sb", [128, 8 * ntt], F32)
    wg32 = [ctx.sbuf("wg32", [128, 8], F32) for _ in range(8)]
    bg32 = ctx.sbuf("bg32", [1, 8], F32)
    bic = [ctx.sbuf("bic", [128, 1], F32) for _ in range(8)]
    onesr32 = ctx.sbuf("onesr32", [1, 128], F32)
    bcomb = ctx.sbuf("bcomb", [1, 8], F32)
    wcomb = [ctx.sbuf("wcomb", [128, 8], F32) for _ in range(8)]
    s_ld2 = ctx.sem()
    nld2 = 0
    p.add("vector", lambda e: e.memset(onesr32[:], 1.0))
    for k in range(8):
        p.add("sync", lambda e, k=k: e.dma_start(
            out=wg32[k][:], in_=wg_d[k * 128:(k + 1) * 128, :]),
            inc=(s_ld2, 16)); nld2 += 1
        p.add("sync", lambda e, k=k: e.dma_start(
            out=bic[k][:], in_=bic_d[k * 128:(k + 1) * 128, :]),
            inc=(s_ld2, 16)); nld2 += 1
    p.add("sync", lambda e: e.dma_start(out=bg32[:], in_=bg_d),
          inc=(s_ld2, 16)); nld2 += 1
    p.raw_wait("tensor", s_ld2, 16 * nld2)
    p.barrier()
    bc_ps = ctx.psum("bc", [1, 8], F32)
    s_wc = ctx.sem()
    for k in range(8):
        p.add("tensor", lambda e, k=k: e.matmul(
            bc_ps[:], bic[k][:], wg32[k][:], start=(k == 0), stop=False))
    p.add("tensor", lambda e: e.matmul(
        bc_ps[:], onesr32[0:1, 0:1], bg32[:], start=False, stop=True),
        inc=(s_wc, 1))
    p.add("vector", lambda e: e.tensor_copy(out=bcomb[:], in_=bc_ps[:]),
          waits=[(s_wc, 1)], inc=(s_wc, 1))
    ctx.free(bc_ps)
    wiT_sb = [ctx.sbuf("wiTs", [128, 1024], F32) for _ in range(2)]
    s_wl = [ctx.sem(), ctx.sem()]
    s_wi = ctx.sem()
    wc_ps = [ctx.psum("wcp", [128, 8], F32) for _ in range(8)]
    for k in range(8):
        p.add("sync", lambda e, k=k: e.dma_start(
            out=wiT_sb[k % 2][:], in_=wiT_d[k * 128:(k + 1) * 128, :]),
            waits=[(s_wi, k - 1)] if k >= 2 else [],
            inc=(s_wl[k % 2], 16))
        for m in range(8):
            wvm = [(s_wl[k % 2], 16 * (k // 2 + 1))] if m == 0 else []
            if k == 0 and m == 0:
                wvm.append((s_wc, 2))
            p.add("tensor", lambda e, k=k, m=m: e.matmul(
                wc_ps[m][:], wiT_sb[k % 2][:, m * 128:(m + 1) * 128],
                wg32[k][:], start=(k == 0), stop=(k == 7)),
                waits=wvm,
                inc=(s_wi, 1) if m == 7 else None)
    p.raw_wait("vector", s_wi, 8)
    p.add("vector", lambda e: e.tensor_copy(out=ctx.junk[:, 2:3],
                                            in_=ctx.cpool[:, 0:1]),
          inc=(s_wc, 1))
    for m in range(8):
        p.add("vector", lambda e, m=m: e.tensor_copy(
            out=wcomb[m][:], in_=wc_ps[m][:]), inc=(s_wc, 1))
    p.barrier()
    ctx.free(wc_ps, wiT_sb)

    lgw = ctx.sbuf("rt_lgw", [128, 8 * ntt], F32)
    xn32 = [ctx.sbuf("xn32", [128, 1024], F32) for _ in range(2)]
    xfm32 = [ctx.sbuf("xfm32", [128, 128], F32) for _ in range(8)]
    tp32 = [ctx.psum("tp32", [128, 128], F32) for _ in range(2)]
    lg = [ctx.psum("lg", [128, 8], F32) for _ in range(2)]
    s_n32 = ctx.sem()
    s_t32 = ctx.sem()
    s_x32 = ctx.sem()
    s_lgm = ctx.sem()
    s_rt = ctx.sem()
    for tt in range(ntt):
        i_ext = tt + HALO // 128
        wvn = [(s_t32, 8 * (tt - 1))] if tt >= 2 else []
        p.add("vector", lambda e, tt=tt, i=i_ext: e.tensor_scalar(
            out=xn32[tt % 2][:], in0=xt_ln[i][:], scalar1=nm_ln[:, i:i + 1],
            scalar2=rstd_ln[:, i:i + 1], op0=ALU.add, op1=ALU.mult),
            waits=wvn, inc=(s_n32, 1))
        for c in range(8):
            j = tt * 8 + c
            wv = [(s_n32, tt + 1)]
            if j >= 2:
                wv.append((s_x32, j - 1))
            p.add("tensor", lambda e, tt=tt, c=c, j=j: e.transpose(
                tp32[j % 2][:], xn32[tt % 2][:, c * 128:(c + 1) * 128],
                ctx.ident_f32[:]), waits=wv, inc=(s_t32, 1))
            p.add("vector", lambda e, c=c, j=j, tt=tt: e.tensor_copy(
                out=xfm32[c][:], in_=tp32[j % 2][:]),
                waits=[(s_t32, j + 1), (s_lgm, tt)], inc=(s_x32, 1))
        pb = lg[tt % 2]
        wv0 = [(s_x32, 8 * (tt + 1))]
        if tt >= 2:
            wv0.append((s_rt, p.val(s_rt)))
        for k in range(8):
            p.add("tensor", lambda e, k=k, pb=pb: e.matmul(
                pb[:], xfm32[k][:], wcomb[k][:], start=(k == 0), stop=False),
                waits=wv0 if k == 0 else [])
        p.add("tensor", lambda e, pb=pb: e.matmul(
            pb[:], onesr32[:], bcomb[:], start=False, stop=True),
            inc=(s_lgm, 1))
        fn = p.add_r if tt == ntt - 1 else p.add
        fn("vector", lambda e, tt=tt, pb=pb: e.tensor_copy(
            out=lgw[:, tt * 8:(tt + 1) * 8], in_=pb[:]),
            waits=[(s_lgm, tt + 1)], inc=(s_rt, 1))
    exps = ctx.sbuf("rt_exp", [128, 8 * ntt], F32)
    prb = ctx.sbuf("rt_prb", [128, 8 * ntt], F32)
    msk = ctx.sbuf("rt_msk", [128, 8 * ntt], F32)
    pm2 = ctx.sbuf("rt_pm2", [128, 8 * ntt], F32)
    ru = ctx.sbuf("rt_ru", [128, 8 * ntt], F32)
    mcol = ctx.sbuf("rt_mcol", [128, 6 * ntt], F32)
    mx, ssum, rs_, m1, m2, den = [mcol[:, i * ntt:(i + 1) * ntt]
                                  for i in range(6)]
    A = p.add_r
    A("vector", lambda e: e.reduce_max(out=seg(mx, 1), in_=seg(lgw[:], 8),
                                       axis=AX.X))
    A("vector", lambda e: e.tensor_tensor(out=seg(exps[:], 8),
                                          in0=seg(lgw[:], 8),
                                          in1=bcast(mx, 8),
                                          op=ALU.subtract), inc=(s_rt, 1))
    p.add_r("scalar", lambda e: e.activation(out=exps[:], in_=exps[:],
                                             func=AF.Exp),
            waits=[(s_rt, p.val(s_rt))], inc=(s_rt, 1))
    p.raw_wait("vector", s_rt, p.val(s_rt))
    A("vector", lambda e: e.reduce_sum(out=seg(ssum, 1), in_=seg(exps[:], 8),
                                       axis=AX.X))
    A("vector", lambda e: e.reciprocal(out=rs_, in_=ssum))
    A("vector", lambda e: e.tensor_tensor(out=seg(prb[:], 8),
                                          in0=seg(exps[:], 8),
                                          in1=bcast(rs_, 8), op=ALU.mult))
    A("vector", lambda e: e.reduce_max(out=seg(m1, 1), in_=seg(prb[:], 8),
                                       axis=AX.X))
    A("vector", lambda e: e.tensor_tensor(out=seg(msk[:], 8),
                                          in0=seg(prb[:], 8),
                                          in1=bcast(m1, 8), op=ALU.is_lt))
    A("vector", lambda e: e.tensor_tensor(out=pm2[:], in0=msk[:], in1=prb[:],
                                          op=ALU.mult))
    A("vector", lambda e: e.reduce_max(out=seg(m2, 1), in_=seg(pm2[:], 8),
                                       axis=AX.X))
    A("vector", lambda e: e.tensor_tensor(out=seg(msk[:], 8),
                                          in0=seg(prb[:], 8),
                                          in1=bcast(m2, 8), op=ALU.is_ge))
    A("vector", lambda e: e.tensor_tensor(out=ru[:], in0=msk[:], in1=prb[:],
                                          op=ALU.mult))
    A("vector", lambda e: e.reduce_sum(out=seg(den, 1), in_=seg(ru[:], 8),
                                       axis=AX.X))
    A("vector", lambda e: e.reciprocal(out=rs_, in_=den))
    A("vector", lambda e: e.tensor_tensor(out=seg(route[:], 8),
                                          in0=seg(ru[:], 8),
                                          in1=bcast(rs_, 8), op=ALU.mult))
    p.barrier()
    ctx.ln_stats_free()
    p.barrier()
    st = ctx.sem(swdge=True)
    for i in range(OWN // 128):
        p.add("gpsimd", lambda e, i=i: e.dma_start(
            out=r_o[i * 128:(i + 1) * 128, :],
            in_=route[:, i * 8:(i + 1) * 8]), inc=(st, 16))
    p.raw_wait("gpsimd", st, 16 * (OWN // 128))
    p.barrier()
    ctx.free(lg, tp32, tfm, route, lgw, exps, prb, msk, pm2, ru, mcol,
             xn32, xfm32, wg32, bg32, bic, onesr32, bcomb, wcomb)


def build_l0c():
    ctx = make_ctx("l0c")
    nc = ctx.nc
    io = {}
    io["t"] = nc.dram_tensor("t", [8, 128, OWN], BF16,
                             kind="ExternalInput").ap()
    io["route"] = nc.dram_tensor("route", [OWN, 8], F32,
                                 kind="ExternalInput").ap()
    io["w1"] = nc.dram_tensor("w1", [8, 1024, 4096], F32,
                              kind="ExternalInput").ap()
    io["b1"] = nc.dram_tensor("b1", [8, 4096], F32,
                              kind="ExternalInput").ap()
    io["w2"] = nc.dram_tensor("w2", [8, 4096, 1024], F32,
                              kind="ExternalInput").ap()
    io["b2"] = nc.dram_tensor("b2", [8, 1024], F32,
                              kind="ExternalInput").ap()
    io["wout"] = nc.dram_tensor("wout", [1024, 1024], F32,
                                kind="ExternalInput").ap()
    io["bout"] = nc.dram_tensor("bout", [1, 1024], F32,
                                kind="ExternalInput").ap()
    io["s0"] = nc.dram_tensor("s0", [OWN, 1024], BF16,
                              kind="ExternalInput").ap()
    io["x"] = nc.dram_tensor("x", [OWN, 1024], F32,
                             kind="ExternalInput").ap()
    io["x1"] = nc.dram_tensor("x1", [OWN, 1024], F32,
                              kind="ExternalOutput").ap()
    emit_l0c(ctx, io)
    ctx.p.finalize()
    ctx.close()
    return nc


def emit_l0c(ctx, io):
    nc, p = ctx.nc, ctx.p
    t_d, r_d = io["t"], io["route"]
    w1_d, b1_d, w2_d, b2_d = io["w1"], io["b1"], io["w2"], io["b2"]
    wo_d, bo_d = io["wout"], io["bout"]
    s0_d, x_d, x1_d = io["s0"], io["x"], io["x1"]

    NT = OWN // 128
    tfm = [ctx.sbuf(f"tfm{c}", [128, OWN], BF16) for c in range(8)]
    acc = [ctx.sbuf("acc", [128, 1024], BF16) for _ in range(NT)]
    rts = ctx.sbuf("rts", [128, 8 * NT], F32)
    s_ld = ctx.sem()
    nld = 0
    for c in range(8):
        p.add("sync", lambda e, c=c: e.dma_start(out=tfm[c][:], in_=t_d[c]),
              inc=(s_ld, 16)); nld += 1
    for i in range(NT):
        p.add("sync", lambda e, i=i: e.dma_start(
            out=rts[:, i * 8:(i + 1) * 8], in_=r_d[i * 128:(i + 1) * 128, :]),
            inc=(s_ld, 16)); nld += 1
    p.raw_wait("tensor", s_ld, 16 * nld)
    p.raw_wait("vector", s_ld, 16 * nld)
    p.barrier()

    slots = [[ctx.sbuf("w1q", [128, 1024], BF16) for _ in range(8)] +
             [ctx.sbuf("w2q", [128, 1024], BF16) for _ in range(8)] +
             [ctx.sbuf("b1q", [1, 1024], BF16), ctx.sbuf("b2r", [1, 1024],
                                                         BF16)]
             for _ in range(2)]
    WND = 4
    wstg = [ctx.sbuf("wstg", [128, 1024], F32) for _ in range(WND)]
    s_wld = [ctx.sem() for _ in range(WND)]
    s_wcv = ctx.sem()
    s_mm = ctx.sem()
    s_rel = ctx.sem()
    s_eo = ctx.sem()
    mid_ps = [ctx.psum("midp", [128, 512], F32) for _ in range(2)]
    eo_ps = [ctx.psum("eop", [128, 512], F32) for _ in range(2)]
    mid_sb = [[ctx.sbuf("mid", [128, 512], BF16) for _ in range(8)]
              for _ in range(2)]
    nw = [0]
    mm_end_of_job = {}
    mid_evac_cnt = {}
    eo_evac_cnt = {}
    NPART = 18

    def load_job(j):
        e_, qq = divmod(j, 4)
        for part in range(NPART):
            g = nw[0]
            wv = [(s_wcv, g - WND + 1)] if g >= WND else []
            if part < 8:
                src = w1_d[e_, part * 128:(part + 1) * 128,
                           qq * 1024:(qq + 1) * 1024]
                dst_stage = wstg[g % WND][:]
                cast_src = wstg[g % WND][:]
            elif part < 16:
                kk = part - 8
                src = w2_d[e_, qq * 1024 + kk * 128:
                           qq * 1024 + kk * 128 + 128, :]
                dst_stage = wstg[g % WND][:]
                cast_src = wstg[g % WND][:]
            elif part == 16:
                src = b1_d[e_:e_ + 1, qq * 1024:(qq + 1) * 1024]
                dst_stage = wstg[g % WND][0:1, :]
                cast_src = wstg[g % WND][0:1, :]
            else:
                src = b2_d[e_:e_ + 1, :]
                dst_stage = wstg[g % WND][0:1, :]
                cast_src = wstg[g % WND][0:1, :]
            p.add("sync", lambda e, src=src, d=dst_stage: e.dma_start(
                out=d, in_=src), waits=wv, inc=(s_wld[g % WND], 16))
            wv2 = [(s_wld[g % WND], 16 * (g // WND + 1))]
            if j >= 2:
                wv2.append((s_mm, mm_end_of_job[j - 2]))
            dst = slots[j % 2][part][:] if part < 16 else \
                slots[j % 2][part][0:1, :]
            p.add("vector", lambda e, dst=dst, csrc=cast_src: e.tensor_copy(
                out=dst, in_=csrc), waits=wv2, inc=(s_wcv, 1))
            nw[0] += 1

    load_job(0)
    load_job(1)
    mid_cnt = [0]
    geo_cnt = [0]
    eo_done = {}
    for j in range(32):
        e_, qq = divmod(j, 4)
        sl = slots[j % 2]
        wcv_need = NPART * (j + 1)
        def emit_phaseA(tth):
            t0 = tth * 512
            for n in range(8):
                gmid = mid_cnt[0]
                mp = mid_ps[gmid % 2]
                wv = [(s_wcv, wcv_need)]
                if gmid >= 2:
                    wv.append((s_rel, gmid - 1))
                for k in range(8):
                    p.add("tensor", lambda e, sl=sl, k=k, n=n, t0=t0, mp=mp:
                          e.matmul(mp[:],
                                   sl[k][:, n * 128:(n + 1) * 128],
                                   tfm[k][:, t0:t0 + 512],
                                   start=(k == 0), stop=False),
                          waits=wv if k == 0 else [])
                p.add("tensor", lambda e, sl=sl, n=n, mp=mp: e.matmul(
                    mp[:], sl[16][:, n * 128:(n + 1) * 128],
                    ctx.ones_sb[:, 0:512], start=False, stop=True),
                    inc=(s_mm, 1))
                p.add("scalar", lambda e, mp=mp, n=n, tth=tth: e.activation(
                    out=mid_sb[tth % 2][n][:], in_=mp[:], func=AF.Relu),
                    waits=[(s_mm, p.val(s_mm))], inc=(s_rel, 1))
                mid_cnt[0] += 1

        def emit_phaseB(tth):
            rel_need = p.val(s_rel) - 8 * ((mid_cnt[0] // 8) -
                                           (tth_abs_of[tth] + 1)) \
                if False else rel_done_of[tth]
            for half in range(2):
                for tq in range(4):
                    geo = geo_cnt[0]
                    ep = eo_ps[geo % 2]
                    wv = [(s_rel, rel_need)]
                    if geo >= 2:
                        wv.append((s_eo, eo_done[geo - 2]))
                    first_mm = True
                    for n in range(8):
                        last = (n == 7 and qq != 0)
                        p.add("tensor", lambda e, tth=tth, tq=tq, half=half,
                              n=n, ep=ep, sl=sl, last=last: e.matmul(
                                  ep[:],
                                  mid_sb[tth % 2][n][:, tq * 128:
                                                     tq * 128 + 128],
                                  sl[8 + n][:, half * 512:(half + 1) * 512],
                                  start=(n == 0), stop=last),
                              waits=wv if first_mm else [],
                              inc=(s_mm, 1) if last else None)
                        first_mm = False
                    if qq == 0:
                        p.add("tensor", lambda e, sl=sl, half=half, ep=ep:
                              e.matmul(ep[:], ctx.ones_sb[0:1, 0:128],
                                       sl[17][0:1,
                                              half * 512:(half + 1) * 512],
                                       start=False, stop=True),
                              inc=(s_mm, 1))
                    ti = tth * 4 + tq
                    rcol = rts[:, ti * 8 + e_:ti * 8 + e_ + 1]
                    dst = acc[ti].rearrange("p (h x) -> p h x", x=512)[
                        :, half:half + 1, :]
                    mmv = p.val(s_mm)
                    if j == 0:
                        p.add("vector", lambda e, ep=ep, dst=dst, rcol=rcol:
                              e.tensor_scalar(out=dst, in0=ep[:],
                                              scalar1=rcol, scalar2=None,
                                              op0=ALU.mult),
                              waits=[(s_mm, mmv)], inc=(s_eo, 1))
                    else:
                        p.add("vector", lambda e, ep=ep, dst=dst, rcol=rcol:
                              e.scalar_tensor_tensor(out=dst, in0=ep[:],
                                                     scalar=rcol, in1=dst,
                                                     op0=ALU.mult,
                                                     op1=ALU.add),
                              waits=[(s_mm, mmv)], inc=(s_eo, 1))
                    eo_done[geo] = p.val(s_eo)
                    geo_cnt[0] += 1

        rel_done_of = {}
        tth_abs_of = {}
        for tth in range(4):
            emit_phaseA(tth)
            rel_done_of[tth] = p.val(s_rel)
            if tth >= 1:
                emit_phaseB(tth - 1)
        emit_phaseB(3)
        mm_end_of_job[j] = p.val(s_mm)
        if j + 2 < 32:
            load_job(j + 2)
    p.barrier()
    ctx.free(mid_sb, wstg, slots, eo_ps, mid_ps, tfm)

    acc_fm = emit_tm_to_fm(ctx, acc, OWN, "mo")
    ctx.free(acc)
    wo_sb = load_weight_bf16(ctx, "wout", wo_d, 1024, 1024)
    bo_sb = load_bias_bf16(ctx, "bout", bo_d, 1024)
    p.barrier()
    g_tm = [ctx.sbuf("g", [128, 1024], BF16) for _ in range(NT)]

    def g_evac(e, ps_ap, key):
        tt, n0, n1 = key
        return e.activation(out=g_tm[tt][:, n0:n1], in_=ps_ap, func=AF.Silu)
    emit_gemm_fm_tm(ctx, acc_fm, OWN, wo_sb, 1024, bo_sb, "g", g_evac,
                    evac_eng="scalar")
    ctx.free(wo_sb, bo_sb, acc_fm)

    xs = [ctx.sbuf("xs", [128, 1024], F32) for _ in range(2)]
    ss = [ctx.sbuf("ss", [128, 1024], BF16) for _ in range(2)]
    x1t = [ctx.sbuf("x1t", [128, 1024], F32) for _ in range(2)]
    s_xl = [ctx.sem(), ctx.sem()]
    s_cb = ctx.sem()
    s_st2 = [ctx.sem(swdge=True), ctx.sem(swdge=True)]
    for i in range(NT):
        wvl = [(s_cb, 2 * (i - 2) + 2)] if i >= 2 else []
        p.add("sync", lambda e, i=i: e.dma_start(
            out=xs[i % 2][:], in_=x_d[i * 128:(i + 1) * 128, :]),
            waits=wvl, inc=(s_xl[i % 2], 16))
        p.add("sync", lambda e, i=i: e.dma_start(
            out=ss[i % 2][:], in_=s0_d[i * 128:(i + 1) * 128, :]),
            inc=(s_xl[i % 2], 16))
        wv = [(s_xl[i % 2], 32 * (i // 2 + 1))]
        if i >= 2:
            wv.append((s_st2[i % 2], 16 * (i // 2)))
        p.add("vector", lambda e, i=i: e.tensor_tensor(
            out=x1t[i % 2][:], in0=ss[i % 2][:], in1=g_tm[i][:],
            op=ALU.mult), waits=wv, inc=(s_cb, 1))
        p.flush("vector")
        p.add("vector", lambda e, i=i: e.tensor_tensor(
            out=x1t[i % 2][:], in0=x1t[i % 2][:], in1=xs[i % 2][:],
            op=ALU.add), inc=(s_cb, 1))
        p.add("gpsimd", lambda e, i=i: e.dma_start(
            out=x1_d[i * 128:(i + 1) * 128, :], in_=x1t[i % 2][:]),
            waits=[(s_cb, 2 * i + 2)], inc=(s_st2[i % 2], 16))
    p.raw_wait("gpsimd", s_st2[0], 16 * (NT // 2))
    p.raw_wait("gpsimd", s_st2[1], 16 * (NT // 2))
    p.barrier()
    ctx.free(xs, ss, x1t, g_tm, rts)


def emit_fm_to_tm(ctx, fm_chunks, T, tag, out_dtype=BF16):
    p = ctx.p
    nt = T // 128
    tm = [ctx.sbuf(f"tm_{tag}", [128, 1024], out_dtype) for _ in range(nt)]
    tp = [ctx.psum(f"tp_{tag}", [128, 128], out_dtype) for _ in range(2)]
    s_tp = ctx.sem()
    s_ev = ctx.sem()
    j = 0
    for i in range(nt):
        for c in range(8):
            wv = [(s_ev, j - 1)] if j >= 2 else []
            p.add("tensor", lambda e, i=i, c=c, j=j: e.transpose(
                tp[j % 2][:], fm_chunks[c][:, i * 128:(i + 1) * 128],
                ctx.ident_sb[:]), waits=wv, inc=(s_tp, 1))
            p.add("vector", lambda e, i=i, c=c, j=j: e.tensor_copy(
                out=tm[i][:, c * 128:(c + 1) * 128], in_=tp[j % 2][:]),
                waits=[(s_tp, j + 1)], inc=(s_ev, 1))
            j += 1
    p.barrier()
    ctx.free(tp)
    return tm


def load_weight_mp(ctx, name, dram_ap, gain_sb, cmul):
    p, nc = ctx.p, ctx.nc
    out = [ctx.sbuf(f"w_{name}{k}", [128, 1024], BF16) for k in range(8)]
    fcol = ctx.sbuf(f"fc_{name}", [128, 8], F32)
    stage = [ctx.sbuf(f"wst_{name}", [128, 1024], F32) for _ in range(2)]
    sq = [ctx.sbuf(f"sq_{name}", [128, 1024], F32) for _ in range(2)]
    frow = ctx.sbuf(f"fr_{name}", [1, 1024], F32)
    s2p = [ctx.psum(f"s2_{name}", [1, 512], F32) for _ in range(2)]
    fcp = ctx.psum(f"fcp_{name}", [128, 8], F32)
    s_lds = [ctx.sem(), ctx.sem()]
    s_cv = ctx.sem()
    s_sq = ctx.sem()
    s_s2 = ctx.sem()
    s_f = ctx.sem()
    for k in range(8):
        buf = stage[k % 2]
        s_ld = s_lds[k % 2]
        w = [(s_sq, 2 * (k - 2) + 2), (s_cv, k - 1)] if k >= 2 else []
        p.add("sync", lambda e, buf=buf, k=k: e.dma_start(
            out=buf[:], in_=dram_ap[k * 128:(k + 1) * 128, :]),
            waits=w, inc=(s_ld, 16))
        wvc = [(s_ld, 16 * (k // 2 + 1))]
        p.add("vector", lambda e, k=k, buf=buf: e.tensor_copy(
            out=out[k][:], in_=buf[:]), waits=wvc, inc=(s_cv, 1))
        p.add("scalar", lambda e, k=k, buf=buf: e.activation(
            out=sq[k % 2][:], in_=buf[:], func=AF.Square),
            waits=wvc + ([(s_s2, 2 * (k - 1))] if k >= 2 else []),
            inc=(s_sq, 2))
        for hh in range(2):
            p.add("tensor", lambda e, k=k, hh=hh: e.matmul(
                s2p[hh][:], ctx.ones_col128[:, 0:1],
                sq[k % 2][:, hh * 512:(hh + 1) * 512],
                start=(k == 0), stop=(k == 7)),
                waits=[(s_sq, 2 * k + 2)] if hh == 0 else [],
                inc=(s_s2, 1))
    c_inv = ctx.ccol(1.0 / 1024.0)
    p.add("scalar", lambda e: e.activation(
        out=frow[:, 0:512], in_=s2p[0][:], func=AF.Sqrt,
        scale=c_inv[0:1, :]), waits=[(s_s2, 16)])
    p.add_r("scalar", lambda e: e.activation(
        out=frow[:, 512:1024], in_=s2p[1][:], func=AF.Sqrt,
        scale=c_inv[0:1, :]))
    p.add("vector", lambda e: e.tensor_scalar(
        out=frow[:], in0=frow[:], scalar1=EPS_MP, scalar2=None, op0=ALU.add),
        waits=[(p.rsync["scalar"], p.val(p.rsync["scalar"]))])
    p.flush("vector")
    p.add_r("vector", lambda e: e.reciprocal(out=frow[:], in_=frow[:]))
    p.add("vector", lambda e: e.tensor_scalar(
        out=frow[:], in0=frow[:], scalar1=gain_sb[0:1, 0:1], scalar2=cmul,
        op0=ALU.mult, op1=ALU.mult), inc=(s_f, 1))
    for c in range(8):
        p.add("tensor", lambda e, c=c: e.matmul(
            fcp[:, c:c + 1], frow[:, c * 128:(c + 1) * 128],
            ctx.ones_f32[0:1, 0:1], start=True, stop=True),
            waits=[(s_f, p.val(s_f))] if c == 0 else [],
            inc=(s_f, 1) if c == 7 else None)
    p.add("vector", lambda e: e.tensor_copy(out=fcol[:], in_=fcp[:]),
          waits=[(s_f, p.val(s_f))], inc=(s_f, 1))
    p.barrier()
    ctx.free(fcp, s2p, frow, sq, stage)
    return out, fcol


def emit_mp_norm_tm(ctx, tm_tiles, nt, tag, extra=1.0):
    p = ctx.p
    sq = [ctx.sbuf(f"nsq_{tag}", [128, 1024], F32) for _ in range(2)]
    ss = ctx.sbuf(f"nss_{tag}", [128, 16], F32)
    fs = ctx.sbuf(f"nfs_{tag}", [128, 16], F32)
    s_sq = ctx.sem()
    s_ss = ctx.sem()
    c64 = ctx.ccol(1.0 / 64.0)
    for i in range(nt):
        p.add("scalar", lambda e, i=i: e.activation(
            out=sq[i % 2][:], in_=tm_tiles[i][:], func=AF.Square),
            waits=[(s_ss, max(0, 3 * (i - 1)))], inc=(s_sq, 1))
        p.add_r("vector", lambda e, i=i: e.reduce_sum(
            out=seg(ss[:], 1), in_=seg(sq[i % 2][:], 64), axis=AX.X),
            waits=[(s_sq, i + 1)])
        p.add_r("scalar", lambda e: e.activation(
            out=fs[:], in_=ss[:], func=AF.Sqrt, scale=c64),
            waits=[(p.rsync["vector"], p.val(p.rsync["vector"]))])
        p.add("vector", lambda e: e.tensor_scalar(
            out=fs[:], in0=fs[:], scalar1=EPS_MP, scalar2=None, op0=ALU.add),
            waits=[(p.rsync["scalar"], p.val(p.rsync["scalar"]))])
        p.flush("vector")
        p.add_r("vector", lambda e: e.reciprocal(out=fs[:], in_=fs[:]))
        if extra != 1.0:
            p.add("vector", lambda e: e.tensor_scalar(
                out=fs[:], in0=fs[:], scalar1=extra, scalar2=None,
                op0=ALU.mult))
            p.flush("vector")
        p.add("vector", lambda e, i=i: e.tensor_tensor(
            out=seg(tm_tiles[i][:], 64), in0=seg(tm_tiles[i][:], 64),
            in1=bcast(fs[:], 64), op=ALU.mult), inc=(s_ss, 3))
        p.flush("vector")
    p.barrier()
    ctx.free(fs, ss, sq)


def build_l1a():
    ctx = make_ctx("l1a")
    nc = ctx.nc
    io = {}
    io["x1"] = nc.dram_tensor("x1", [OWN, 1024], F32,
                              kind="ExternalInput").ap()
    for nm_ in ("wqT", "wkT", "wvT"):
        io[nm_] = nc.dram_tensor(nm_, [1024, 1024], F32,
                                 kind="ExternalInput").ap()
    io["gain"] = nc.dram_tensor("gain", [1, 1], F32,
                                kind="ExternalInput").ap()
    io["xn1"] = nc.dram_tensor("xn1", [OWN, 1024], BF16,
                               kind="ExternalOutput").ap()
    io["v1"] = nc.dram_tensor("v1", [8, 128, OWN], BF16,
                              kind="ExternalOutput").ap()
    io["gram"] = nc.dram_tensor("gram", [128, 512], F32,
                                kind="ExternalOutput").ap()
    emit_l1a(ctx, io)
    ctx.p.finalize()
    ctx.close()
    return nc


def emit_l1a(ctx, io):
    nc, p = ctx.nc, ctx.p
    ctx.ones_col128 = ctx.sbuf("ones_col", [128, 1], F32)
    ctx.ones_f32 = ctx.sbuf("ones_f32", [1, 8], F32)
    p.add("vector", lambda e: e.memset(ctx.ones_col128[:], 1.0))
    p.add("vector", lambda e: e.memset(ctx.ones_f32[:], 1.0))
    x1_d, gn_d = io["x1"], io["gain"]
    wq_d, wk_d, wv_d = io["wqT"], io["wkT"], io["wvT"]
    xn_o, v_o, g_o = io["xn1"], io["v1"], io["gram"]

    gain_sb = ctx.sbuf("gain", [1, 1], F32)
    s_g = ctx.sem()
    p.add("sync", lambda e: e.dma_start(out=gain_sb[:], in_=gn_d),
          inc=(s_g, 16))
    p.raw_wait("vector", s_g, 16)
    p.barrier()

    wk_pack = {}
    for tagn, w_d in (("wv", wv_d), ("wq", wq_d), ("wk", wk_d)):
        wk_pack[tagn] = load_weight_mp(ctx, tagn, w_d, gain_sb, 1.0 / 32.0)
    p.barrier()
    xn_fm, xn_tm = emit_ln_to_fm(ctx, x1_d, OWN, "l1", keep_tm=True)
    st = ctx.sem(swdge=True)
    nst = [0]
    for i in range(OWN // 128):
        p.add("gpsimd", lambda e, i=i: e.dma_start(
            out=xn_o[i * 128:(i + 1) * 128, :], in_=xn_tm[i][:]),
            inc=(st, 16)); nst[0] += 1

    NT = OWN // 128
    SQS = 1.0 / math.sqrt(float(S))

    def proj(tagn, w_d, cmul=1.0):
        wsb, fcol = wk_pack[tagn]
        outfm = [ctx.sbuf(f"{tagn}fm{c}", [128, OWN], BF16) for c in range(8)]

        def evac(e, ps_ap, key):
            n, t0, t1 = key
            return e.tensor_scalar(out=outfm[n][:, t0:t1], in0=ps_ap,
                                   scalar1=fcol[:, n:n + 1], scalar2=None,
                                   op0=ALU.mult)
        emit_gemm_fm_fm(ctx, xn_fm, OWN, wsb, 1024, None, None, tagn,
                        evac=evac)
        ctx.free(wsb)
        return outfm, fcol

    vfm, fv = proj("wv", wv_d)
    for c in range(8):
        p.add("gpsimd", lambda e, c=c: e.dma_start(
            out=v_o[c], in_=vfm[c][:]), inc=(st, 16)); nst[0] += 1
    p.raw_wait("gpsimd", st, 16 * nst[0])
    p.barrier()
    ctx.free(vfm, fv)

    qfm, fq = proj("wq", wq_d)
    q_tm = emit_fm_to_tm(ctx, qfm, OWN, "q")
    ctx.free(qfm, fq)
    kfm, fk = proj("wk", wk_d)
    k_tm = emit_fm_to_tm(ctx, kfm, OWN, "k")
    ctx.free(kfm, fk)
    emit_mp_norm_tm(ctx, q_tm, NT, "q")
    emit_mp_norm_tm(ctx, k_tm, NT, "k", extra=SQS)

    gps = [ctx.psum("gram", [128, 64], F32) for _ in range(8)]
    gsb = ctx.sbuf("gramsb", [128, 512], F32)
    s_gm = ctx.sem()
    for h in range(16):
        hp, rb = h // 2, (h % 2) * 64
        for i in range(NT):
            p.add("tensor", lambda e, h=h, hp=hp, rb=rb, i=i: e.matmul(
                gps[hp][rb:rb + 64, :],
                q_tm[i][:, h * 64:h * 64 + 64],
                k_tm[i][:, h * 64:h * 64 + 64],
                start=(i == 0), stop=(i == NT - 1)),
                inc=(s_gm, 1) if i == NT - 1 else None)
    for hp in range(8):
        p.add("vector", lambda e, hp=hp: e.tensor_copy(
            out=gsb[:, hp * 64:(hp + 1) * 64], in_=gps[hp][:]),
            waits=[(s_gm, 16)], inc=(s_gm, 1))
    st2 = ctx.sem(swdge=True)
    p.add("gpsimd", lambda e: e.dma_start(out=g_o, in_=gsb[:]),
          waits=[(s_gm, 24)], inc=(st2, 16))
    p.raw_wait("gpsimd", st2, 16)
    p.barrier()
    ctx.free(gsb, gps, q_tm, k_tm, xn_fm, xn_tm)


def build_l1b():
    ctx = make_ctx("l1b")
    nc = ctx.nc
    io = {}
    io["grams"] = nc.dram_tensor("grams", [4, 128, 512], F32,
                                 kind="ExternalInput").ap()
    io["v1"] = nc.dram_tensor("v1", [8, 128, OWN], BF16,
                              kind="ExternalInput").ap()
    io["xn1"] = nc.dram_tensor("xn1", [OWN, 1024], BF16,
                               kind="ExternalInput").ap()
    io["woT"] = nc.dram_tensor("woT", [1024, 1024], F32,
                               kind="ExternalInput").ap()
    io["gain"] = nc.dram_tensor("gain", [1, 1], F32,
                                kind="ExternalInput").ap()
    io["s1"] = nc.dram_tensor("s1", [OWN, 1024], BF16,
                              kind="ExternalOutput").ap()
    emit_l1b(ctx, io, npart=4)
    ctx.p.finalize()
    ctx.close()
    return nc


def emit_l1b(ctx, io, npart):
    nc, p = ctx.nc, ctx.p
    ctx.ones_col128 = ctx.sbuf("ones_col", [128, 1], F32)
    ctx.ones_f32 = ctx.sbuf("ones_f32", [1, 8], F32)
    p.add("vector", lambda e: e.memset(ctx.ones_col128[:], 1.0))
    p.add("vector", lambda e: e.memset(ctx.ones_f32[:], 1.0))
    gr_d, v_d, xn_d = io["grams"], io["v1"], io["xn1"]
    wo_d, gn_d, s_o = io["woT"], io["gain"], io["s1"]

    NT = OWN // 128
    gain_sb = ctx.sbuf("gain", [1, 1], F32)
    grs = [ctx.sbuf("grs", [128, 512], F32) for _ in range(npart)]
    vfm = [ctx.sbuf(f"vfm{c}", [128, OWN], BF16) for c in range(8)]
    s_ld = ctx.sem()
    nld = 0
    p.add("sync", lambda e: e.dma_start(out=gain_sb[:], in_=gn_d),
          inc=(s_ld, 16)); nld += 1
    for i in range(npart):
        p.add("sync", lambda e, i=i: e.dma_start(
            out=grs[i][:], in_=gr_d[i] if npart > 1 else gr_d),
            inc=(s_ld, 16)); nld += 1
    for c in range(8):
        p.add("sync", lambda e, c=c: e.dma_start(out=vfm[c][:], in_=v_d[c]),
              inc=(s_ld, 16)); nld += 1
    p.raw_wait("vector", s_ld, 16 * nld)
    p.raw_wait("tensor", s_ld, 16 * nld)
    p.barrier()

    att = ctx.sbuf("att", [128, 512], BF16)
    w0 = ctx.sbuf("smw0", [128, 512], F32)
    w1 = ctx.sbuf("smw1", [128, 512], F32)
    mcol = ctx.sbuf("smc", [128, 24], F32)
    if npart > 1:
        p.add("vector", lambda e: e.tensor_tensor(out=w0[:], in0=grs[0][:],
                                                  in1=grs[1][:], op=ALU.add))
        p.add("vector", lambda e: e.tensor_tensor(out=w1[:], in0=grs[2][:],
                                                  in1=grs[3][:], op=ALU.add))
        p.flush("vector")
        p.add("vector", lambda e: e.tensor_tensor(out=w0[:], in0=w0[:],
                                                  in1=w1[:], op=ALU.add))
        p.flush("vector")
    else:
        p.add("vector", lambda e: e.tensor_copy(out=w0[:], in_=grs[0][:]))
        p.flush("vector")
    A = p.add_r
    mx, sm, rc = mcol[:, 0:8], mcol[:, 8:16], mcol[:, 16:24]
    s_sm = ctx.sem()
    A("vector", lambda e: e.reduce_max(out=seg(mx, 1), in_=seg(w0[:], 64),
                                       axis=AX.X))
    A("vector", lambda e: e.tensor_tensor(out=seg(w1[:], 64),
                                          in0=seg(w0[:], 64),
                                          in1=bcast(mx, 64),
                                          op=ALU.subtract), inc=(s_sm, 1))
    p.add_r("scalar", lambda e: e.activation(out=w1[:], in_=w1[:],
                                             func=AF.Exp),
            waits=[(s_sm, 1)], inc=(s_sm, 1))
    p.raw_wait("vector", s_sm, 2)
    A("vector", lambda e: e.reduce_sum(out=seg(sm, 1), in_=seg(w1[:], 64),
                                       axis=AX.X))
    A("vector", lambda e: e.reciprocal(out=rc, in_=sm))
    A("vector", lambda e: e.tensor_tensor(out=seg(att[:], 64),
                                          in0=seg(w1[:], 64),
                                          in1=bcast(rc, 64), op=ALU.mult))
    p.barrier()

    atT = ctx.sbuf("atT", [128, 512], BF16)
    tp = [ctx.psum("atp", [128, 64], BF16) for _ in range(2)]
    s_tp = ctx.sem()
    s_te = ctx.sem()
    for h in range(16):
        rb, cb = (h % 2) * 64, (h // 2) * 64
        wv = [(s_te, h - 1)] if h >= 2 else []
        p.add("tensor", lambda e, h=h, rb=rb, cb=cb: e.transpose(
            tp[h % 2][rb:rb + 64, :], att[rb:rb + 64, cb:cb + 64],
            ctx.ident_sb[rb:rb + 64, rb:rb + 64]), waits=wv, inc=(s_tp, 1))
        p.add("vector", lambda e, h=h, rb=rb, cb=cb: e.tensor_copy(
            out=atT[rb:rb + 64, cb:cb + 64], in_=tp[h % 2][rb:rb + 64, :]),
            waits=[(s_tp, h + 1)], inc=(s_te, 1))
    p.barrier()
    ctx.free(tp, w0, w1, mcol, grs)

    ofm = [ctx.sbuf(f"ofm{c}", [128, OWN], BF16) for c in range(8)]
    ops_ = [ctx.psum("op", [128, 512], F32) for _ in range(2)]
    s_mm = ctx.sem()
    s_ev = ctx.sem()
    it = 0
    for hp in range(4):
        for tt in range(OWN // 512):
            pb = ops_[it % 2]
            wv = [(s_ev, it - 1)] if it >= 2 else []
            for sub in range(2):
                h = hp * 2 + sub
                rb = (h % 2) * 64
                p.add("tensor", lambda e, h=h, hp=hp, rb=rb, tt=tt, pb=pb:
                      e.matmul(pb[rb:rb + 64, :],
                               atT[rb:rb + 64, (h // 2) * 64:(h // 2) * 64 + 64],
                               vfm[h // 2][rb:rb + 64,
                                           tt * 512:(tt + 1) * 512],
                               start=True, stop=True),
                      waits=wv if sub == 0 else [],
                      inc=(s_mm, 1) if sub == 1 else None)
            it += 1
            p.add("vector", lambda e, hp=hp, tt=tt, pb=pb: e.tensor_copy(
                out=ofm[hp * 2 // 2][:, tt * 512:(tt + 1) * 512]
                if False else ofm[hp][:, tt * 512:(tt + 1) * 512], in_=pb[:]),
                waits=[(s_mm, it)], inc=(s_ev, 1))
    p.barrier()
    for hp in range(4, 8):
        for tt in range(OWN // 512):
            pb = ops_[it % 2]
            wv = [(s_ev, it - 1)] if it >= 2 else []
            for sub in range(2):
                h = hp * 2 + sub
                rb = (h % 2) * 64
                p.add("tensor", lambda e, h=h, rb=rb, tt=tt, pb=pb:
                      e.matmul(pb[rb:rb + 64, :],
                               atT[rb:rb + 64, (h // 2) * 64:(h // 2) * 64 + 64],
                               vfm[h // 2][rb:rb + 64,
                                           tt * 512:(tt + 1) * 512],
                               start=True, stop=True),
                      waits=wv if sub == 0 else [],
                      inc=(s_mm, 1) if sub == 1 else None)
            it += 1
            p.add("vector", lambda e, hp=hp, tt=tt, pb=pb: e.tensor_copy(
                out=ofm[hp][:, tt * 512:(tt + 1) * 512], in_=pb[:]),
                waits=[(s_mm, it)], inc=(s_ev, 1))
    p.barrier()
    ctx.free(ops_, atT, att, vfm)

    wsb, fcol = load_weight_mp(ctx, "wo", wo_d, gain_sb,
                               1.0 / 32.0 * 0.7071067811865476)
    p.barrier()
    opfm = [ctx.sbuf(f"opfm{c}", [128, OWN], BF16) for c in range(8)]

    def evac(e, ps_ap, key):
        n, t0, t1 = key
        return e.tensor_scalar(out=opfm[n][:, t0:t1], in0=ps_ap,
                               scalar1=fcol[:, n:n + 1], scalar2=None,
                               op0=ALU.mult)
    emit_gemm_fm_fm(ctx, ofm, OWN, wsb, 1024, None, None, "op", evac=evac)
    ctx.free(wsb, fcol, ofm)
    op_tm = emit_fm_to_tm(ctx, opfm, OWN, "op")
    ctx.free(opfm)

    xnt = [ctx.sbuf("xnt", [128, 1024], BF16) for _ in range(2)]
    s1t = [ctx.sbuf("s1t", [128, 1024], BF16) for _ in range(2)]
    xsc = [ctx.sbuf("xsc", [128, 1024], F32) for _ in range(2)]
    s_xl = [ctx.sem(), ctx.sem()]
    s_cbA = ctx.sem()
    s_cbB = ctx.sem()
    s_st = [ctx.sem(swdge=True), ctx.sem(swdge=True)]
    c07 = ctx.ccol(0.7071067811865476)
    for i in range(NT):
        wvl = [(s_cbA, i - 1)] if i >= 2 else []
        p.add("sync", lambda e, i=i: e.dma_start(
            out=xnt[i % 2][:], in_=xn_d[i * 128:(i + 1) * 128, :]),
            waits=wvl, inc=(s_xl[i % 2], 16))
        wv = [(s_xl[i % 2], 16 * (i // 2 + 1))]
        if i >= 2:
            wv.append((s_cbB, i - 1))
        p.add("scalar", lambda e, i=i: e.activation(
            out=xsc[i % 2][:], in_=xnt[i % 2][:], func=AF.Copy,
            scale=c07), waits=wv, inc=(s_cbA, 1))
        wv2 = [(s_cbA, i + 1)]
        if i >= 2:
            wv2.append((s_st[i % 2], 16 * (i // 2)))
        p.add("vector", lambda e, i=i: e.tensor_tensor(
            out=s1t[i % 2][:], in0=xsc[i % 2][:], in1=op_tm[i][:],
            op=ALU.add), waits=wv2, inc=(s_cbB, 1))
        p.flush("vector")
        p.add("gpsimd", lambda e, i=i: e.dma_start(
            out=s_o[i * 128:(i + 1) * 128, :], in_=s1t[i % 2][:]),
            waits=[(s_cbB, i + 1)], inc=(s_st[i % 2], 16))
    p.raw_wait("gpsimd", s_st[0], 16 * (NT // 2))
    p.raw_wait("gpsimd", s_st[1], 16 * (NT // 2))
    p.barrier()
    ctx.free(xnt, s1t, xsc, op_tm)


def build_l1c():
    ctx = make_ctx("l1c")
    nc = ctx.nc
    io = {}
    io["xn1"] = nc.dram_tensor("xn1", [OWN, 1024], BF16,
                               kind="ExternalInput").ap()
    io["wgT"] = nc.dram_tensor("wgT", [1024, FFN1_H], I32,
                               kind="ExternalInput").ap()
    io["wuT"] = nc.dram_tensor("wuT", [1024, FFN1_H], I32,
                               kind="ExternalInput").ap()
    io["wdT"] = nc.dram_tensor("wdT", [FFN1_H, 1024], I32,
                               kind="ExternalInput").ap()
    io["x1"] = nc.dram_tensor("x1", [OWN, 1024], F32,
                              kind="ExternalInput").ap()
    io["s1"] = nc.dram_tensor("s1", [OWN, 1024], BF16,
                              kind="ExternalInput").ap()
    io["x2"] = nc.dram_tensor("x2", [OWN, 1024], F32,
                              kind="ExternalOutput").ap()
    emit_l1c(ctx, io)
    ctx.p.finalize()
    ctx.close()
    return nc


def emit_l1c(ctx, io):
    nc, p = ctx.nc, ctx.p
    xn_d, wg_d, wu_d, wd_d = io["xn1"], io["wgT"], io["wuT"], io["wdT"]
    x1_d, s1_d, x2_d = io["x1"], io["s1"], io["x2"]
    NT = OWN // 128
    QS = 1.0 / 127.0

    xnt = [ctx.sbuf("xnt", [128, 1024], BF16) for _ in range(NT)]
    s_ld = ctx.sem()
    for i in range(NT):
        p.add("sync", lambda e, i=i: e.dma_start(
            out=xnt[i][:], in_=xn_d[i * 128:(i + 1) * 128, :]),
            inc=(s_ld, 16))
    p.raw_wait("tensor", s_ld, 16 * NT)
    p.raw_wait("vector", s_ld, 16 * NT)
    p.barrier()
    xn_fm = emit_tm_to_fm(ctx, xnt, OWN, "xn")
    ctx.free(xnt)

    wd_sb = [ctx.sbuf("wd", [128, 1024], BF16) for _ in range(24)]
    WDD = 6
    wstg = [ctx.sbuf("wstg", [128, 1024], I32) for _ in range(WDD)]
    s_wld = [ctx.sem() for _ in range(WDD)]
    s_wcv = ctx.sem()
    for k in range(24):
        g = k
        w = [(s_wcv, k - WDD + 1)] if k >= WDD else []
        p.add("sync", lambda e, k=k, g=g: e.dma_start(
            out=wstg[g % WDD][:], in_=wd_d[k * 128:(k + 1) * 128, :]),
            waits=w, inc=(s_wld[g % WDD], 16))
        p.add("vector", lambda e, k=k, g=g: e.tensor_scalar(
            out=wd_sb[k][:], in0=wstg[g % WDD][:], scalar1=QS, scalar2=None,
            op0=ALU.mult), waits=[(s_wld[g % WDD], 16 * (g // WDD + 1))],
            inc=(s_wcv, 1))
    p.raw_wait("tensor", s_wcv, 24)
    p.raw_wait("sync", s_wcv, 24)
    p.barrier()

    h_fm = [ctx.sbuf("hfm", [128, 1024], BF16) for _ in range(24)]
    g1 = [ctx.sbuf("g1", [128, 1024], BF16) for _ in range(NT)]
    gu_sb = [[ctx.sbuf("gusb", [128, 128], BF16) for _ in range(16)]
             for _ in range(2)]
    GDEPTH = 8
    gstg = [ctx.sbuf("gstg", [128, 128], I32) for _ in range(GDEPTH)]
    gps = ctx.psum("gate", [128, 1024], F32)
    ups = ctx.psum("up", [128, 1024], F32)
    dps = ctx.psum("down", [128, 1024], F32)
    gsb = ctx.sbuf("gsb", [128, 1024], BF16)
    s_gld = [ctx.sem() for _ in range(8)]
    s_gcv = ctx.sem()
    s_mm = ctx.sem()
    s_svg = ctx.sem()
    s_g1 = ctx.sem()
    s_hv = ctx.sem()
    ng = [0]
    mm_at_gu = {}

    def load_gu(tt, n, slot):
        for part in range(16):
            g = ng[0]
            w = [(s_gcv, g - GDEPTH + 1)] if g >= GDEPTH else []
            src = (wg_d if part < 8 else wu_d)[
                (part % 8) * 128:(part % 8) * 128 + 128,
                n * 128:(n + 1) * 128]
            p.add("sync", lambda e, src=src, g=g: e.dma_start(
                out=gstg[g % GDEPTH][:], in_=src), waits=w,
                inc=(s_gld[g % 8], 16))
            wv2 = [(s_gld[g % 8], 16 * (g // 8 + 1))]
            gun = ng[0] // 16
            if gun >= 2:
                wv2.append((s_mm, mm_at_gu[gun - 2]))
            p.add("vector", lambda e, slot=slot, part=part, g=g:
                  e.tensor_scalar(out=gu_sb[slot][part][:],
                                  in0=gstg[g % GDEPTH][:], scalar1=QS,
                                  scalar2=None, op0=ALU.mult),
                  waits=wv2, inc=(s_gcv, 1))
            ng[0] += 1

    load_gu(0, 0, 0)
    load_gu(0, 1, 1)
    mm_end_down = [0]
    for tt in range(2):
        TT = 1024
        for n in range(24):
            gun = tt * 24 + n
            sl = gu_sb[gun % 2]
            need_cv = 16 * (gun + 1)
            for half in range(2):
                for k in range(8):
                    p.add("tensor", lambda e, k=k, half=half, tt=tt, sl=sl:
                          e.matmul(gps[:, half * 512:(half + 1) * 512],
                                   sl[k][:],
                                   xn_fm[k][:, tt * TT + half * 512:
                                            tt * TT + half * 512 + 512],
                                   start=(k == 0), stop=(k == 7)),
                          waits=[(s_gcv, need_cv), (s_svg, gun),
                                 (s_hv, max(0, 2 * gun))]
                          if (k == 0 and half == 0) else [],
                          inc=(s_mm, 1) if k == 7 else None)
                for k in range(8):
                    p.add("tensor", lambda e, k=k, half=half, tt=tt, sl=sl:
                          e.matmul(ups[:, half * 512:(half + 1) * 512],
                                   sl[8 + k][:],
                                   xn_fm[k][:, tt * TT + half * 512:
                                            tt * TT + half * 512 + 512],
                                   start=(k == 0), stop=(k == 7)),
                          inc=(s_mm, 1) if k == 7 else None)
            wv = [(s_mm, p.val(s_mm)), (s_hv, 2 * gun)]
            p.add_r("scalar", lambda e: e.activation(
                out=gsb[:], in_=gps[:], func=AF.Silu), waits=wv,
                inc=(s_svg, 1))
            wvh = [(s_svg, gun + 1)]
            if tt == 1:
                wvh.append((s_mm, mm_end_down[0]))
            p.add("vector", lambda e, n=n: e.tensor_tensor(
                out=h_fm[n][:], in0=gsb[:], in1=ups[:], op=ALU.mult),
                waits=wvh, inc=(s_hv, 2))
            mm_at_gu[gun] = p.val(s_mm)
            if gun + 2 < 48:
                nxt = gun + 2
                load_gu(nxt // 24, nxt % 24, nxt % 2)
        hv_need = p.val(s_hv)
        for tq in range(8):
            ti = tt * 8 + tq
            wv = [(s_hv, hv_need), (s_g1, ti)]
            im = 0
            for k in range(24):
                for half in range(2):
                    im += 1
                    p.add("tensor", lambda e, k=k, half=half, tq=tq:
                          e.matmul(dps[:, half * 512:(half + 1) * 512],
                                   h_fm[k][:, tq * 128:(tq + 1) * 128],
                                   wd_sb[k][:, half * 512:(half + 1) * 512],
                                   start=(k == 0), stop=(k == 23)),
                          waits=wv if im == 1 else [],
                          inc=(s_mm, 1) if (k == 23 and half == 1) else None)
            p.add_r("scalar", lambda e, ti=ti: e.activation(
                out=g1[ti][:], in_=dps[:], func=AF.Silu),
                waits=[(s_mm, p.val(s_mm))], inc=(s_g1, 1))
        mm_end_down[0] = p.val(s_mm)
    p.barrier()
    ctx.free(gsb, dps, ups, gps, gstg, gu_sb, h_fm, wstg, wd_sb)

    xs = [ctx.sbuf("xs", [128, 1024], F32) for _ in range(2)]
    ss = [ctx.sbuf("ss", [128, 1024], BF16) for _ in range(2)]
    x2t = [ctx.sbuf("x2t", [128, 1024], F32) for _ in range(2)]
    s_xl = [ctx.sem(), ctx.sem()]
    s_cb = ctx.sem()
    s_st2 = [ctx.sem(swdge=True), ctx.sem(swdge=True)]
    for i in range(NT):
        wvl = [(s_cb, 2 * (i - 2) + 2)] if i >= 2 else []
        p.add("sync", lambda e, i=i: e.dma_start(
            out=xs[i % 2][:], in_=x1_d[i * 128:(i + 1) * 128, :]),
            waits=wvl, inc=(s_xl[i % 2], 16))
        p.add("sync", lambda e, i=i: e.dma_start(
            out=ss[i % 2][:], in_=s1_d[i * 128:(i + 1) * 128, :]),
            inc=(s_xl[i % 2], 16))
        wv = [(s_xl[i % 2], 32 * (i // 2 + 1))]
        if i >= 2:
            wv.append((s_st2[i % 2], 16 * (i // 2)))
        p.add("vector", lambda e, i=i: e.tensor_tensor(
            out=x2t[i % 2][:], in0=ss[i % 2][:], in1=g1[i][:],
            op=ALU.mult), waits=wv, inc=(s_cb, 1))
        p.flush("vector")
        p.add("vector", lambda e, i=i: e.tensor_tensor(
            out=x2t[i % 2][:], in0=x2t[i % 2][:], in1=xs[i % 2][:],
            op=ALU.add), inc=(s_cb, 1))
        p.flush("vector")
        p.add("gpsimd", lambda e, i=i: e.dma_start(
            out=x2_d[i * 128:(i + 1) * 128, :], in_=x2t[i % 2][:]),
            waits=[(s_cb, 2 * i + 2)], inc=(s_st2[i % 2], 16))
    p.raw_wait("gpsimd", s_st2[0], 16 * (NT // 2))
    p.raw_wait("gpsimd", s_st2[1], 16 * (NT // 2))
    p.barrier()
    ctx.free(x2t, ss, xs, g1, xn_fm)


def build_fused_a():
    ctx = make_ctx("fuseda")
    nc, p = ctx.nc, ctx.p
    E = lambda n, sh, dt: nc.dram_tensor(n, sh, dt, kind="ExternalInput").ap()
    I = lambda n, sh, dt: nc.dram_tensor(n, sh, dt, kind="Internal").ap()
    O = lambda n, sh, dt: nc.dram_tensor(n, sh, dt,
                                         kind="ExternalOutput").ap()
    io0 = {
        "x_ext": E("x_ext", [EXT, 1024], F32),
        "wq": E("wq", [1024, 1024], F32), "wk": E("wk", [1024, 1024], F32),
        "wv": E("wv", [1024, 1024], F32), "wi": E("wi", [1024, 1024], F32),
        "wiT": E("wiT", [1024, 1024], F32),
        "bq": E("bq", [1, 1024], F32), "bk": E("bk", [1, 1024], F32),
        "bv": E("bv", [1, 1024], F32), "bi": E("bi", [1, 1024], F32),
        "wg": E("wg", [1024, 8], F32), "bg": E("bg", [1, 8], F32),
        "b_in_col": E("b_in_col", [1024, 1], F32),
        "q": I("q_s", [8, 128, OWN], BF16),
        "k": I("k_s", [8, 128, EXT], BF16),
        "v": I("v_s", [EXT, 1040], BF16),
        "t": I("t_s", [8, 128, OWN], BF16),
        "route": I("route_s", [OWN, 8], F32),
    }
    emit_l0a(ctx, io0, ext_kv=True)
    p.barrier()
    io1 = {
        "q": io0["q"], "k": io0["k"], "v": io0["v"],
        "posT": E("posT", [16, 512, 512], F32),
        "slot_w": E("slot_w", [128, 9], F32),
        "invcnt": E("invcnt", [128, OWN // 128], F32),
        "wo": E("wo", [1024, 1024], F32), "bo": E("bo", [1, 1024], F32),
        "s0": I("s0_s", [OWN, 1024], BF16),
    }
    emit_l0b(ctx, io1)
    p.barrier()
    io2 = {
        "t": io0["t"], "route": io0["route"],
        "w1": E("w1", [8, 1024, 4096], F32), "b1": E("b1", [8, 4096], F32),
        "w2": E("w2", [8, 4096, 1024], F32), "b2": E("b2", [8, 1024], F32),
        "wout": E("wout", [1024, 1024], F32), "bout": E("bout", [1, 1024],
                                                        F32),
        "s0": io1["s0"], "x": io0["x_ext"][HALO:HALO + OWN, :],
        "x1": O("x1", [OWN, 1024], F32),
    }
    emit_l0c(ctx, io2)
    p.barrier()
    io3 = {
        "x1": io2["x1"],
        "wqT": E("wqT", [1024, 1024], F32),
        "wkT": E("wkT", [1024, 1024], F32),
        "wvT": E("wvT", [1024, 1024], F32),
        "gain": E("gain", [1, 1], F32),
        "xn1": O("xn1", [OWN, 1024], BF16),
        "v1": O("v1", [8, 128, OWN], BF16),
        "gram": O("gram", [128, 512], F32),
    }
    emit_l1a(ctx, io3)
    p.finalize()
    ctx.close()
    return nc


def build_fused_b():
    ctx = make_ctx("fusedb")
    nc, p = ctx.nc, ctx.p
    E = lambda n, sh, dt: nc.dram_tensor(n, sh, dt, kind="ExternalInput").ap()
    I = lambda n, sh, dt: nc.dram_tensor(n, sh, dt, kind="Internal").ap()
    io4 = {
        "grams": E("grams", [4, 128, 512], F32),
        "v1": E("v1", [8, 128, OWN], BF16),
        "xn1": E("xn1", [OWN, 1024], BF16),
        "woT": E("woT", [1024, 1024], F32),
        "gain": E("gain", [1, 1], F32),
        "s1": I("s1_s", [OWN, 1024], BF16),
    }
    emit_l1b(ctx, io4, npart=4)
    p.barrier()
    io5 = {
        "xn1": io4["xn1"],
        "wgT": E("wgT", [1024, FFN1_H], I32),
        "wuT": E("wuT", [1024, FFN1_H], I32),
        "wdT": E("wdT", [FFN1_H, 1024], I32),
        "x1": E("x1", [OWN, 1024], F32), "s1": io4["s1"],
        "x2": nc.dram_tensor("x2", [OWN, 1024], F32,
                             kind="ExternalOutput").ap(),
    }
    emit_l1c(ctx, io5)
    p.finalize()
    ctx.close()
    return nc


def build_fused():
    ctx = make_ctx("fused")
    nc, p = ctx.nc, ctx.p
    E = lambda n, sh, dt: nc.dram_tensor(n, sh, dt, kind="ExternalInput").ap()
    I = lambda n, sh, dt: nc.dram_tensor(n, sh, dt, kind="Internal").ap()
    io0 = {
        "x_ext": E("x_ext", [EXT, 1024], F32),
        "wq": E("wq", [1024, 1024], F32), "wk": E("wk", [1024, 1024], F32),
        "wv": E("wv", [1024, 1024], F32), "wi": E("wi", [1024, 1024], F32),
        "wiT": E("wiT", [1024, 1024], F32),
        "bq": E("bq", [1, 1024], F32), "bk": E("bk", [1, 1024], F32),
        "bv": E("bv", [1, 1024], F32), "bi": E("bi", [1, 1024], F32),
        "wg": E("wg", [1024, 8], F32), "bg": E("bg", [1, 8], F32),
        "b_in_col": E("b_in_col", [1024, 1], F32),
        "q": I("q_s", [8, 128, OWN], BF16),
        "k": I("k_s", [8, 128, EXT], BF16),
        "v": I("v_s", [EXT, 1040], BF16),
        "t": I("t_s", [8, 128, OWN], BF16),
        "route": I("route_s", [OWN, 8], F32),
    }
    emit_l0a(ctx, io0, ext_kv=True)
    p.barrier()
    io1 = {
        "q": io0["q"], "k": io0["k"], "v": io0["v"],
        "posT": E("posT", [16, 512, 512], F32),
        "slot_w": E("slot_w", [128, 9], F32),
        "invcnt": E("invcnt", [128, OWN // 128], F32),
        "wo": E("wo", [1024, 1024], F32), "bo": E("bo", [1, 1024], F32),
        "s0": I("s0_s", [OWN, 1024], BF16),
    }
    emit_l0b(ctx, io1)
    p.barrier()
    io2 = {
        "t": io0["t"], "route": io0["route"],
        "w1": E("w1", [8, 1024, 4096], F32), "b1": E("b1", [8, 4096], F32),
        "w2": E("w2", [8, 4096, 1024], F32), "b2": E("b2", [8, 1024], F32),
        "wout": E("wout", [1024, 1024], F32), "bout": E("bout", [1, 1024],
                                                        F32),
        "s0": io1["s0"], "x": io0["x_ext"][HALO:HALO + OWN, :],
        "x1": I("x1_s", [OWN, 1024], F32),
    }
    emit_l0c(ctx, io2)
    p.barrier()
    gram_in = I("gram_in", [128, 512], F32)
    gram_out = I("gram_out", [128, 512], F32)
    io3 = {
        "x1": io2["x1"],
        "wqT": E("wqT", [1024, 1024], F32),
        "wkT": E("wkT", [1024, 1024], F32),
        "wvT": E("wvT", [1024, 1024], F32),
        "gain": E("gain", [1, 1], F32),
        "xn1": I("xn1_s", [OWN, 1024], BF16),
        "v1": I("v1_s", [8, 128, OWN], BF16),
        "gram": gram_in,
    }
    emit_l1a(ctx, io3)
    p.barrier()
    s_cc = ctx.sem()
    p.add("gpsimd", lambda e: e.collective_compute(
        "AllReduce", ALU.add, [[0, 1, 2, 3], [4, 5, 6, 7]],
        [gram_in], [gram_out]), inc=(s_cc, 16))
    p.raw_wait("gpsimd", s_cc, 16)
    p.barrier()
    io4 = {
        "grams": gram_out, "v1": io3["v1"], "xn1": io3["xn1"],
        "woT": E("woT", [1024, 1024], F32), "gain": io3["gain"],
        "s1": I("s1_s", [OWN, 1024], BF16),
    }
    emit_l1b(ctx, io4, npart=1)
    p.barrier()
    io5 = {
        "xn1": io3["xn1"],
        "wgT": E("wgT", [1024, FFN1_H], I32),
        "wuT": E("wuT", [1024, FFN1_H], I32),
        "wdT": E("wdT", [FFN1_H, 1024], I32),
        "x1": io2["x1"], "s1": io4["s1"],
        "x2": nc.dram_tensor("x2", [OWN, 1024], F32,
                             kind="ExternalOutput").ap(),
    }
    emit_l1c(ctx, io5)
    p.finalize()
    ctx.close()
    return nc


def _core_slices():
    return [(c // 4, (c % 4) * OWN) for c in range(NCORE)]


def _ext_idx(s0):
    return np.clip(np.arange(s0 - HALO, s0 + OWN + HALO), 0, S - 1)


def run_l0a(inp):
    x = np.asarray(inp["L0_attn_hidden_states"])
    nc = build_l0a()
    in_maps = []
    shared = {
        "wq": np.asarray(inp["L0_attn_Wq"]),
        "wk": np.asarray(inp["L0_attn_Wk"]),
        "wv": np.asarray(inp["L0_attn_Wv"]),
        "wi": np.asarray(inp["L0_ffn_W_in"]),
        "bq": np.asarray(inp["L0_attn_bq"])[None],
        "bk": np.asarray(inp["L0_attn_bk"])[None],
        "bv": np.asarray(inp["L0_attn_bv"])[None],
        "bi": np.asarray(inp["L0_ffn_b_in"])[None],
        "wg": np.asarray(inp["L0_ffn_W_gate"]),
        "bg": np.asarray(inp["L0_ffn_b_gate"])[None],
        "wiT": np.ascontiguousarray(np.asarray(inp["L0_ffn_W_in"]).T),
        "b_in_col": np.ascontiguousarray(
            np.asarray(inp["L0_ffn_b_in"]).reshape(1024, 1)),
    }
    for (b, s0) in _core_slices():
        m = dict(shared)
        m["x_ext"] = np.ascontiguousarray(x[b][_ext_idx(s0)])
        in_maps.append(m)
    return run_bass_kernel_spmd(nc, in_maps, core_ids=list(range(NCORE)))


def _slot_w(s0):
    w = np.zeros((128, 9), np.float32)
    for j in range(9):
        sg = s0 - 256 + 256 * j
        if 0 <= sg <= S - W:
            w[:, j] = 1.0
    if s0 + OWN == S:
        w[:, 7] = 2.0
        w[:, 8] = 0.0
    return w


def _invcnt(s0):
    counts = np.full(S, 2.0, np.float32)
    counts[:256] = 1.0
    counts[7680:7936] = 3.0
    counts[7936:] = 2.0
    sl = counts[s0:s0 + OWN]
    return np.ascontiguousarray(1.0 / sl.reshape(OWN // 128, 128).T)


def run_l0b(per_core, shared):
    nc = build_l0b()
    in_maps = []
    for c, m in enumerate(per_core):
        d = dict(shared)
        d.update(m)
        in_maps.append(d)
    return run_bass_kernel_spmd(nc, in_maps, core_ids=list(range(NCORE)))


def kernel(**inp):
    inp = {k: np.asarray(v) for k, v in inp.items()}
    x = inp["L0_attn_hidden_states"]
    cores = _core_slices()
    R = list(range(NCORE))
    nc_a = build_fused_a()
    nc_b = build_fused_b()
    gainv = np.asarray(inp["L1_attn_gain"], np.float32).reshape(1, 1)
    shared = {
        "wq": inp["L0_attn_Wq"], "wk": inp["L0_attn_Wk"],
        "wv": inp["L0_attn_Wv"], "wi": inp["L0_ffn_W_in"],
        "bq": inp["L0_attn_bq"][None], "bk": inp["L0_attn_bk"][None],
        "bv": inp["L0_attn_bv"][None], "bi": inp["L0_ffn_b_in"][None],
        "wg": inp["L0_ffn_W_gate"], "bg": inp["L0_ffn_b_gate"][None],
        "wiT": np.ascontiguousarray(inp["L0_ffn_W_in"].T),
        "b_in_col": np.ascontiguousarray(
            inp["L0_ffn_b_in"].reshape(1024, 1)),
        "posT": np.ascontiguousarray(
            inp["L0_attn_pos_bias"].transpose(0, 2, 1)),
        "wo": inp["L0_attn_Wo"], "bo": inp["L0_attn_bo"][None],
        "w1": inp["L0_ffn_W1"], "b1": inp["L0_ffn_b1"],
        "w2": inp["L0_ffn_W2"], "b2": inp["L0_ffn_b2"],
        "wout": inp["L0_ffn_W_out"], "bout": inp["L0_ffn_b_out"][None],
        "wqT": np.ascontiguousarray(inp["L1_attn_w_q"].T),
        "wkT": np.ascontiguousarray(inp["L1_attn_w_k"].T),
        "wvT": np.ascontiguousarray(inp["L1_attn_w_v"].T),
        "woT": np.ascontiguousarray(inp["L1_attn_w_o"].T),
        "gain": gainv,
        "wgT": np.ascontiguousarray(inp["L1_ffn_w_gate"].T),
        "wuT": np.ascontiguousarray(inp["L1_ffn_w_up"].T),
        "wdT": np.ascontiguousarray(inp["L1_ffn_w_down"].T),
    }
    akeys = {"wq", "wk", "wv", "wi", "wiT", "bq", "bk", "bv", "bi", "wg",
             "bg", "b_in_col", "posT", "wo", "bo", "w1", "b1", "w2", "b2",
             "wout", "bout", "wqT", "wkT", "wvT", "gain"}
    shared_a = {k_: v_ for k_, v_ in shared.items() if k_ in akeys}
    maps = []
    for (b, s0) in cores:
        m = dict(shared_a)
        m["x_ext"] = np.ascontiguousarray(x[b][_ext_idx(s0)])
        m["slot_w"] = _slot_w(s0)
        m["invcnt"] = _invcnt(s0)
        maps.append(m)
    ra = run_bass_kernel_spmd(nc_a, maps, core_ids=R).results
    bkeys = {"woT", "gain", "wgT", "wuT", "wdT"}
    shared_b = {k_: v_ for k_, v_ in shared.items() if k_ in bkeys}
    maps = []
    for c, (b, s0) in enumerate(cores):
        m = dict(shared_b)
        m["grams"] = np.stack([ra[b * 4 + j]["gram"] for j in range(4)])
        m["v1"] = ra[c]["v1"]
        m["xn1"] = ra[c]["xn1"]
        m["x1"] = ra[c]["x1"]
        maps.append(m)
    rb = run_bass_kernel_spmd(nc_b, maps, core_ids=R).results
    out = np.zeros((B, S, 1024), np.float32)
    for c, (b, s0) in enumerate(cores):
        out[b, s0:s0 + OWN] = rb[c]["x2"]
    return out


def kernel_unfused(**inp):
    inp = {k: np.asarray(v) for k, v in inp.items()}
    x = inp["L0_attn_hidden_states"]
    cores = _core_slices()
    R = list(range(NCORE))

    nc = build_l0a()
    shared = {
        "wq": inp["L0_attn_Wq"], "wk": inp["L0_attn_Wk"],
        "wv": inp["L0_attn_Wv"], "wi": inp["L0_ffn_W_in"],
        "bq": inp["L0_attn_bq"][None], "bk": inp["L0_attn_bk"][None],
        "bv": inp["L0_attn_bv"][None], "bi": inp["L0_ffn_b_in"][None],
        "wg": inp["L0_ffn_W_gate"], "bg": inp["L0_ffn_b_gate"][None],
        "wiT": np.ascontiguousarray(inp["L0_ffn_W_in"].T),
        "b_in_col": np.ascontiguousarray(
            inp["L0_ffn_b_in"].reshape(1024, 1)),
    }
    maps = []
    for (b, s0) in cores:
        m = dict(shared)
        m["x_ext"] = np.ascontiguousarray(x[b][_ext_idx(s0)])
        maps.append(m)
    r = run_bass_kernel_spmd(nc, maps, core_ids=R).results
    q_c = [r[c]["q"] for c in range(NCORE)]
    t_c = [r[c]["t"] for c in range(NCORE)]
    route_c = [r[c]["route"] for c in range(NCORE)]
    k_full = np.zeros((B, 8, 128, S), BF)
    v_full = np.zeros((B, S, 1040), BF)
    for c, (b, s0) in enumerate(cores):
        k_full[b][:, :, s0:s0 + OWN] = r[c]["k"]
        v_full[b][s0:s0 + OWN] = r[c]["v"]

    nc = build_l0b()
    shared = {"posT": np.ascontiguousarray(
        inp["L0_attn_pos_bias"].transpose(0, 2, 1)),
        "wo": inp["L0_attn_Wo"], "bo": inp["L0_attn_bo"][None]}
    maps = []
    for c, (b, s0) in enumerate(cores):
        idx = _ext_idx(s0)
        m = dict(shared)
        m["q"] = q_c[c]
        m["k"] = np.ascontiguousarray(k_full[b][:, :, idx])
        m["v"] = np.ascontiguousarray(v_full[b][idx])
        m["slot_w"] = _slot_w(s0)
        m["invcnt"] = _invcnt(s0)
        maps.append(m)
    r = run_bass_kernel_spmd(nc, maps, core_ids=R).results
    s0_c = [r[c]["s0"] for c in range(NCORE)]

    nc = build_l0c()
    shared = {"w1": inp["L0_ffn_W1"], "b1": inp["L0_ffn_b1"],
              "w2": inp["L0_ffn_W2"], "b2": inp["L0_ffn_b2"],
              "wout": inp["L0_ffn_W_out"], "bout": inp["L0_ffn_b_out"][None]}
    maps = []
    for c, (b, s0) in enumerate(cores):
        m = dict(shared)
        m["t"] = t_c[c]
        m["route"] = route_c[c]
        m["s0"] = s0_c[c]
        m["x"] = np.ascontiguousarray(x[b][s0:s0 + OWN])
        maps.append(m)
    r = run_bass_kernel_spmd(nc, maps, core_ids=R).results
    x1_c = [r[c]["x1"] for c in range(NCORE)]

    nc = build_l1a()
    gainv = np.asarray(inp["L1_attn_gain"], np.float32).reshape(1, 1)
    shared = {"wqT": np.ascontiguousarray(inp["L1_attn_w_q"].T),
              "wkT": np.ascontiguousarray(inp["L1_attn_w_k"].T),
              "wvT": np.ascontiguousarray(inp["L1_attn_w_v"].T),
              "gain": gainv}
    maps = []
    for c, (b, s0) in enumerate(cores):
        m = dict(shared)
        m["x1"] = x1_c[c]
        maps.append(m)
    r = run_bass_kernel_spmd(nc, maps, core_ids=R).results
    xn1_c = [r[c]["xn1"] for c in range(NCORE)]
    v1_c = [r[c]["v1"] for c in range(NCORE)]
    gram_c = [r[c]["gram"] for c in range(NCORE)]

    nc = build_l1b()
    shared = {"woT": np.ascontiguousarray(inp["L1_attn_w_o"].T),
              "gain": gainv}
    maps = []
    for c, (b, s0) in enumerate(cores):
        m = dict(shared)
        m["grams"] = np.stack([gram_c[b * 4 + j] for j in range(4)])
        m["v1"] = v1_c[c]
        m["xn1"] = xn1_c[c]
        maps.append(m)
    r = run_bass_kernel_spmd(nc, maps, core_ids=R).results
    s1_c = [r[c]["s1"] for c in range(NCORE)]

    nc = build_l1c()
    shared = {"wgT": np.ascontiguousarray(inp["L1_ffn_w_gate"].T),
              "wuT": np.ascontiguousarray(inp["L1_ffn_w_up"].T),
              "wdT": np.ascontiguousarray(inp["L1_ffn_w_down"].T)}
    maps = []
    for c, (b, s0) in enumerate(cores):
        m = dict(shared)
        m["xn1"] = xn1_c[c]
        m["x1"] = x1_c[c]
        m["s1"] = s1_c[c]
        maps.append(m)
    r = run_bass_kernel_spmd(nc, maps, core_ids=R).results

    out = np.zeros((B, S, 1024), np.float32)
    for c, (b, s0) in enumerate(cores):
        out[b, s0:s0 + OWN] = r[c]["x2"]
    return out


if __name__ == "__main__":
    pass
